# version 09
# baseline (speedup 1.0000x reference)
import contextlib
import numpy as np
import ml_dtypes
import concourse.bass as bass
import concourse.mybir as mybir
from concourse.bass_utils import run_bass_kernel_spmd
from concourse.alu_op_type import AluOpType as OP

F32 = mybir.dt.float32
BF16 = mybir.dt.bfloat16
AF = mybir.ActivationFunctionType
AX = mybir.AxisListType.X

H = 4096
DS = 4096
SN = 128
NH = 64
HD = 64
KC = 4
CONV = DS + 2 * SN
PROJ = 2 * DS + 2 * SN + NH
NCOL = PROJ + H + 32 + 4
T = 1024
NT = 8
TB = 1027
EPS = 1e-5

_cache = {}


def build_program():
    if "nc" in _cache:
        return _cache["nc"]
    nc = bass.Bass("TRN2", target_bir_lowering=False)

    xsegT = nc.dram_tensor("xsegT", [H, TB], F32, kind="ExternalInput").ap()
    xown = nc.dram_tensor("xown", [T, H], F32, kind="ExternalInput").ap()
    wcat = nc.dram_tensor("wcat", [H, NCOL], F32, kind="ExternalInput").ap()
    outwD = nc.dram_tensor("outwD", [DS, H], F32, kind="ExternalInput").ap()
    lorabD = nc.dram_tensor("lorabD", [32, H], F32, kind="ExternalInput").ap()
    convwT = nc.dram_tensor("convwT", [KC, CONV], F32, kind="ExternalInput").ap()
    mupvD = nc.dram_tensor("mupvD", [1, PROJ], F32, kind="ExternalInput").ap()
    lnwC = nc.dram_tensor("lnwC", [128, 32], F32, kind="ExternalInput").ap()
    normwD = nc.dram_tensor("normwD", [1, DS], F32, kind="ExternalInput").ap()
    alogD = nc.dram_tensor("alogD", [1, NH], F32, kind="ExternalInput").ap()
    dtbD = nc.dram_tensor("dtbD", [1, NH], F32, kind="ExternalInput").ap()
    dvecD = nc.dram_tensor("dvecD", [1, NH], F32, kind="ExternalInput").ap()
    ebiasD = nc.dram_tensor("ebiasD", [1, 4], F32, kind="ExternalInput").ap()
    identbD = nc.dram_tensor("identbD", [128, 128], BF16, kind="ExternalInput").ap()
    identfD = nc.dram_tensor("identfD", [128, 128], F32, kind="ExternalInput").ap()
    utonesD = nc.dram_tensor("utonesD", [128, 128], F32, kind="ExternalInput").ap()
    shiftD = nc.dram_tensor("shiftD", [128, 128 * 9], BF16, kind="ExternalInput").ap()
    ones1D = nc.dram_tensor("ones1D", [1, 128], F32, kind="ExternalInput").ap()
    cmaskD = nc.dram_tensor("cmaskD", [128, 3], F32, kind="ExternalInput").ap()
    gmaskD = nc.dram_tensor("gmaskD", [128, 3], F32, kind="ExternalInput").ap()
    gmaskiD = nc.dram_tensor("gmaskiD", [128, 3], F32, kind="ExternalInput").ap()

    oseg = nc.dram_tensor("oseg", [T, H], F32, kind="ExternalOutput").ap()

    convscr = nc.dram_tensor("convscr", [T, CONV], BF16, kind="Internal").ap()
    xnscr = nc.dram_tensor("xnscr", [128, 32 * T], BF16, kind="Internal").ap()
    zscr = nc.dram_tensor("zscr", [T, DS], F32, kind="Internal").ap()
    uscr = nc.dram_tensor("uscr", [T, H], F32, kind="Internal").ap()
    mupwscr = nc.dram_tensor("mupwscr", [128, CONV * KC], BF16, kind="Internal").ap()
    wscr = nc.dram_tensor("wscr", [H, NCOL], BF16, kind="Internal").ap()
    owscr = nc.dram_tensor("owscr", [DS, H], BF16, kind="Internal").ap()
    ccin = nc.dram_tensor("ccin", [128, DS + NH], F32, kind="Internal").ap()
    ccout = nc.dram_tensor("ccout", [512, DS + NH], F32, kind="Internal").ap()

    es = contextlib.ExitStack()
    sbt = lambda name, shape, dt: es.enter_context(nc.sbuf_tensor(name, shape, dt))

    sem_names = ["VS", "SS", "TS", "LDC", "LDX", "WCV", "OCV", "WLD", "XNR", "CPL", "ZUS",
                 "XLD", "ZRL", "URL", "XRL", "CC", "GDM", "STO", "LFD", "ROW",
                 "OWL", "GTH", "PUB", "MWS", "MWL"]
    SEM = {}
    CNT = {}
    QL = {"tensor": [], "vector": [], "scalar": [], "sync": [], "gpsimd": []}

    def emit(eng, fn, wait=(), inc=()):
        wait = tuple((s, v) for (s, v) in wait if v > 0)
        inc = tuple(inc)

        def step(e, _fn=fn, _w=wait, _i=inc):
            for s, v in _w:
                e.wait_ge(SEM[s], v)
            inst = _fn(e)
            for s, a in _i:
                inst = inst.then_inc(SEM[s], a)
        QL[eng].append(step)

    def bump(inc):
        for s, a in inc:
            CNT[s] += a

    def V(fn, wait=(), inc=()):
        w = [("VS", CNT["VS"])] + list(wait)
        i = [("VS", 1)] + list(inc)
        bump(i)
        emit("vector", fn, w, i)

    def SC(fn, wait=(), inc=()):
        w = [("SS", CNT["SS"])] + list(wait)
        i = [("SS", 1)] + list(inc)
        bump(i)
        emit("scalar", fn, w, i)

    def TE(fn, wait=(), inc=()):
        bump(inc)
        emit("tensor", fn, wait, inc)

    def DMA(eng, fn, wait=(), inc=()):
        bump(inc)
        emit(eng, fn, wait, inc)

    def WAITQ(eng, wait):
        emit(eng, lambda e: None, wait, ())

    s_identb = sbt("s_identb", [128, 128], BF16)
    s_identf = sbt("s_identf", [128, 128], F32)
    s_ut = sbt("s_ut", [128, 128], F32)
    s_shift = sbt("s_shift", [128, 128 * 9], BF16)
    s_ones1 = sbt("s_ones1", [1, 128], F32)
    s_onescol = sbt("s_onescol", [128, 1], F32)
    s_row64 = sbt("s_row64", [1, 512], F32)
    s_lnwc = sbt("s_lnwc", [128, 32], F32)
    s_dtbrep = sbt("s_dtbrep", [128, NH], F32)
    s_arep = sbt("s_arep", [128, NH], F32)
    s_drep = sbt("s_drep", [128, NH], F32)
    s_ebrep = sbt("s_ebrep", [128, 4], F32)
    s_cmask = sbt("s_cmask", [128, 3], F32)
    s_gmask = sbt("s_gmask", [128, 3], F32)
    s_gmaski = sbt("s_gmaski", [128, 3], F32)
    s_rmsall = sbt("s_rmsall", [128, 16], F32)
    s_dtsp = sbt("s_dtsp", [128, NH * NT], F32)
    s_dtaall = sbt("s_dtaall", [128, NH * NT], F32)
    s_lam = sbt("s_lam", [128, NH * NT], F32)
    s_lamfm = sbt("s_lamfm", [64, 128 * NT], F32)
    s_ftm = sbt("s_ftm", [128, NH * NT], F32)
    s_gamrep = sbt("s_gamrep", [128, NH * NT], F32)
    s_explam = sbt("s_explam", [128, NH * NT], F32)
    s_gtot = sbt("s_gtot", [128, NH], F32)
    s_glog = sbt("s_glog", [128, 4 * NT], F32)
    s_zlora = sbt("s_zlora", [128, 32 * NT], F32)
    s_zlt = sbt("s_zlt", [32, 128 * NT], BF16)
    s_hrun = sbt("s_hrun", [128, DS], F32)
    s_small = sbt("s_small", [128, 128], F32)
    s_normw = sbt("s_normw", [128, DS], BF16)
    s_stat = sbt("s_stat", [128, 32 * TB], BF16)
    s_mupdt = sbt("s_mupdt", [128, NH], F32)
    s_mupz = sbt("s_mupz", [128, DS], BF16)
    s_cnst = sbt("s_cnst", [128, 2], F32)

    psA = es.enter_context(nc.psum_tensor("psA", [128, 2048], F32))
    psB = es.enter_context(nc.psum_tensor("psB", [128, 2048], F32))

    for n in sem_names:
        SEM[n] = es.enter_context(nc.semaphore(n))
        CNT[n] = 0

    TS = lambda: CNT["TS"]
    VSn = lambda: CNT["VS"]
    SSn = lambda: CNT["SS"]

    esA = contextlib.ExitStack()
    sbA = lambda name, shape, dt: esA.enter_context(nc.sbuf_tensor(name, shape, dt))
    s_srow = sbA("s_srow", [1, 5200], F32)
    s_cwp = sbA("s_cwp", [1, 512], F32)
    s_cwq = sbA("s_cwq", [1, 512], F32)
    s_repp = sbA("s_repp", [128, 512], BF16)

    for dst, src in [(s_identb, identbD), (s_identf, identfD), (s_ut, utonesD),
                     (s_shift, shiftD), (s_ones1, ones1D), (s_cmask, cmaskD),
                     (s_gmask, gmaskD), (s_gmaski, gmaskiD), (s_lnwc, lnwC)]:
        DMA("sync", lambda e, d=dst, s=src: e.dma_start(out=d[:], in_=s),
            (), [("LDC", 16)])
    DMA("sync", lambda e: e.dma_start(out=s_row64[0:1, 0:64], in_=dtbD), (), [("LDC", 16)])
    DMA("sync", lambda e: e.dma_start(out=s_row64[0:1, 64:128], in_=alogD), (), [("LDC", 16)])
    DMA("sync", lambda e: e.dma_start(out=s_row64[0:1, 128:192], in_=dvecD), (), [("LDC", 16)])
    DMA("sync", lambda e: e.dma_start(out=s_row64[0:1, 192:196], in_=ebiasD), (), [("LDC", 16)])
    ldc_all = CNT["LDC"]
    NW_CHUNK = 1024
    wcv_marks = []
    for j in range((NCOL + NW_CHUNK - 1) // NW_CHUNK):
        c0 = j * NW_CHUNK
        cw = min(NW_CHUNK, NCOL - c0)
        DMA("gpsimd", lambda e, c0=c0, cw=cw: e.dma_start(
            out=wscr[:, c0:c0 + cw], in_=wcat[:, c0:c0 + cw]),
            (), [("WCV", 16)])
        wcv_marks.append((c0 + cw, CNT["WCV"]))

    def wcv_need(cend):
        for ce, mark in wcv_marks:
            if ce >= cend:
                return mark
        return wcv_marks[-1][1]
    WAITQ("vector", [("LDC", ldc_all)])
    WAITQ("scalar", [("LDC", ldc_all)])
    WAITQ("tensor", [("LDC", ldc_all)])

    V(lambda e: e.memset(s_onescol[:], 1.0))
    V(lambda e: e.memset(s_hrun[:], 0.0))
    V(lambda e: e.memset(s_gtot[:], 1.0))
    V(lambda e: e.memset(s_cnst[:, 0:1], EPS))
    V(lambda e: e.memset(s_cnst[:, 1:2], 1.0 / DS))
    nc.const_aps.register(F32, EPS, s_cnst[:, 0:1])
    nc.const_aps.register(F32, 1.0 / DS, s_cnst[:, 1:2])

    def k1rep(dst_ap, row_ap, width, ps, poff, sc_mul=None, wt=()):
        TE(lambda e, w=width: e.matmul(ps[:, poff:poff + w], s_ones1[0:1, 0:128],
                                       row_ap, start=True, stop=True),
           wait=[("VS", VSn()), ("SS", SSn())] + list(wt), inc=[("TS", 1)])
        ts = TS()
        if sc_mul is None:
            V(lambda e, w=width: e.tensor_copy(dst_ap, ps[:, poff:poff + w]),
              wait=[("TS", ts)])
        else:
            V(lambda e, w=width: e.tensor_scalar(dst_ap, ps[:, poff:poff + w],
                                                 sc_mul, None, OP.mult),
              wait=[("TS", ts)])

    SC(lambda e: e.activation(s_row64[0:1, 256:320], s_row64[0:1, 64:128], AF.Exp))
    k1rep(s_dtbrep[:, :], s_row64[0:1, 0:64], NH, psA, 0)
    k1rep(s_arep[:, :], s_row64[0:1, 256:320], NH, psA, 512, sc_mul=-1.0)
    k1rep(s_drep[:, :], s_row64[0:1, 128:192], NH, psA, 1024)
    k1rep(s_ebrep[:, :], s_row64[0:1, 192:196], 4, psA, 1536)

    mws_g = [0]
    for k in range(KC):
        for j in range(9):
            w = 512 if j < 8 else 256
            off = j * 512
            DMA("sync", lambda e, k=k, off=off, w=w: e.dma_start(
                out=s_cwp[0:1, 0:w], in_=convwT[k:k + 1, off:off + w]),
                wait=[("VS", mws_g[0])], inc=[("LDX", 16)])
            DMA("sync", lambda e, off=off, w=w: e.dma_start(
                out=s_cwq[0:1, 0:w], in_=mupvD[0:1, DS + off:DS + off + w]),
                wait=[("VS", mws_g[0])], inc=[("LDX", 16)])
            ldx = CNT["LDX"]
            V(lambda e, w=w: e.tensor_tensor(s_cwp[0:1, 0:w], s_cwp[0:1, 0:w],
                                             s_cwq[0:1, 0:w], OP.mult),
              wait=[("LDX", ldx)])
            k1rep(s_repp[:, 0:w], s_cwp[0:1, 0:w], w, psA, 0,
                  wt=[("MWS", CNT["MWS"])])
            vs = VSn()
            DMA("sync", lambda e, k=k, off=off, w=w: e.dma_start(
                out=mupwscr[:, k * CONV + off:k * CONV + off + w],
                in_=s_repp[:, 0:w]),
                wait=[("VS", vs)], inc=[("MWS", 16)])
            mws_g[0] = VSn()
    mws_all = CNT["MWS"]

    for j in range(8):
        DMA("sync", lambda e, j=j: e.dma_start(
            out=s_cwp[0:1, 0:512], in_=mupvD[0:1, j * 512:(j + 1) * 512]),
            wait=[("VS", mws_g[0])], inc=[("LDX", 16)])
        ldx = CNT["LDX"]
        WAITQ("tensor", [("LDX", ldx)])
        k1rep(s_mupz[:, j * 512:(j + 1) * 512], s_cwp[0:1, 0:512], 512,
              psA if j % 2 == 0 else psB, 0)
        mws_g[0] = VSn()
    DMA("sync", lambda e: e.dma_start(out=s_cwp[0:1, 0:64],
                                      in_=mupvD[0:1, PROJ - NH:PROJ]),
        wait=[("VS", mws_g[0])], inc=[("LDX", 16)])
    ldx = CNT["LDX"]
    WAITQ("tensor", [("LDX", ldx)])
    k1rep(s_mupdt[:, :], s_cwp[0:1, 0:64], NH, psA, 512)
    for j in range(8):
        DMA("sync", lambda e, j=j: e.dma_start(
            out=s_cwp[0:1, 0:512], in_=normwD[0:1, j * 512:(j + 1) * 512]),
            wait=[("VS", VSn())], inc=[("LDX", 16)])
        ldx = CNT["LDX"]
        WAITQ("tensor", [("LDX", ldx)])
        k1rep(s_normw[:, j * 512:(j + 1) * 512], s_cwp[0:1, 0:512], 512,
              psA if j % 2 == 0 else psB, 1024)

    s_xf = [sbA(f"s_xf{p}", [128, TB], F32) for p in range(4)]
    s_sqf = sbA("s_sqf", [128, TB], F32)
    s_mrep = sbA("s_mrep", [128, TB], F32)
    s_rrep = sbA("s_rrep", [128, TB], F32)
    s_xtmp = sbA("s_xtmp", [128, TB], F32)
    s_xnb = sbA("s_xnb", [128, TB], BF16)
    OM, OV, OR, OU, OT = 0, 1040, 2080, 3120, 4160

    xf_guard = [0, 0, 0, 0]
    for i in range(32):
        DMA("sync", lambda e, i=i: e.dma_start(
            out=s_xf[i % 4][:], in_=xsegT[128 * i:128 * (i + 1), :]),
            wait=[("TS", xf_guard[i % 4])], inc=[("LDX", 16)])
        ldx = CNT["LDX"]
        SC(lambda e, i=i: e.activation(s_sqf[:], s_xf[i % 4][:], AF.Square),
           wait=[("LDX", ldx)])
        ssq = SSn()
        for c in range(3):
            w = 512 if c < 2 else 3
            TE(lambda e, i=i, c=c, w=w: e.matmul(
                psA[0:1, c * 512:c * 512 + w], s_onescol[:, 0:1],
                s_xf[i % 4][:, c * 512:c * 512 + w], start=(i == 0), stop=(i == 31)),
               wait=[("LDX", ldx), ("VS", VSn() if i == 0 else 0)],
               inc=[("TS", 1)])
            TE(lambda e, i=i, c=c, w=w: e.matmul(
                psB[0:1, c * 512:c * 512 + w], s_onescol[:, 0:1],
                s_sqf[:, c * 512:c * 512 + w], start=(i == 0), stop=(i == 31)),
               wait=[("SS", ssq)], inc=[("TS", 1)])
        xf_guard[i % 4] = TS()
    ts_stats = TS()
    V(lambda e: e.tensor_scalar(s_srow[0:1, OM:OM + TB], psA[0:1, 0:TB],
                                1.0 / H, None, OP.mult), wait=[("TS", ts_stats)])
    V(lambda e: e.tensor_scalar(s_srow[0:1, OV:OV + TB], psB[0:1, 0:TB],
                                1.0 / H, None, OP.mult))
    V(lambda e: e.tensor_tensor(s_srow[0:1, OT:OT + TB], s_srow[0:1, OM:OM + TB],
                                s_srow[0:1, OM:OM + TB], OP.mult))
    V(lambda e: e.tensor_sub(s_srow[0:1, OV:OV + TB], s_srow[0:1, OV:OV + TB],
                             s_srow[0:1, OT:OT + TB]))
    V(lambda e: e.tensor_scalar(s_srow[0:1, OT:OT + TB], s_srow[0:1, OV:OV + TB],
                                EPS, None, OP.add))
    V(lambda e: e.reciprocal(s_srow[0:1, OR:OR + TB], s_srow[0:1, OT:OT + TB]))
    vs_r2 = VSn()
    SC(lambda e: e.activation(s_srow[0:1, OT:OT + TB], s_srow[0:1, OR:OR + TB],
                              AF.Sqrt), wait=[("VS", vs_r2)])
    ss_r = SSn()
    V(lambda e: e.tensor_tensor(s_srow[0:1, OU:OU + TB], s_srow[0:1, OV:OV + TB],
                                s_srow[0:1, OR:OR + TB], OP.mult),
      wait=[("SS", ss_r)])
    V(lambda e: e.tensor_copy(s_srow[0:1, OR:OR + TB], s_srow[0:1, OT:OT + TB]))
    vs_u = VSn()
    SC(lambda e: e.activation(s_srow[0:1, OT:OT + TB], s_srow[0:1, OU:OU + TB],
                              AF.Sqrt, bias=EPS, scale=1.0), wait=[("VS", vs_u)])
    ss_su = SSn()
    V(lambda e: e.reciprocal(s_srow[0:1, OU:OU + TB], s_srow[0:1, OT:OT + TB]),
      wait=[("SS", ss_su)])
    for c in range(3):
        w = 512 if c < 2 else 3
        k1rep(s_mrep[:, c * 512:c * 512 + w], s_srow[0:1, OM + c * 512:OM + c * 512 + w],
              w, psA, c * 512)
        k1rep(s_rrep[:, c * 512:c * 512 + w], s_srow[0:1, OR + c * 512:OR + c * 512 + w],
              w, psB, c * 512)
    for t in range(9):
        nr = 128 if t < 8 else 3
        so = OU + (3 + 128 * t if t < 8 else 0)
        TE(lambda e, nr=nr, so=so: e.transpose(
            psA[0:nr, 1536:1537], s_srow[0:1, so:so + nr], s_identf[0:1, 0:1]),
           wait=[("VS", VSn())], inc=[("TS", 1)])
        ts = TS()
        V(lambda e, t=t, nr=nr: e.tensor_copy(s_rmsall[0:nr, t:t + 1],
                                              psA[0:nr, 1536:1537]),
          wait=[("TS", ts)])

    vs_sub = {}
    xnr_g = [0]
    for i in range(32):
        DMA("sync", lambda e, i=i: e.dma_start(
            out=s_xf[i % 4][:], in_=xsegT[128 * i:128 * (i + 1), :]),
            wait=[("TS", xf_guard[i % 4]), ("VS", vs_sub.get(i - 4, 0))],
            inc=[("LDX", 16)])
        ldx = CNT["LDX"]
        V(lambda e, i=i: e.tensor_tensor(s_xtmp[:], s_xf[i % 4][:], s_mrep[:],
                                         OP.subtract), wait=[("LDX", ldx)])
        vs_sub[i] = VSn()
        V(lambda e: e.tensor_tensor(s_xnb[:], s_xtmp[:], s_rrep[:], OP.mult),
          wait=[("XNR", xnr_g[0])])
        vs_xn = VSn()
        DMA("sync", lambda e, i=i: e.dma_start(
            out=xnscr[:, i * T:(i + 1) * T], in_=s_xnb[:, 3:TB]),
            wait=[("VS", vs_xn)], inc=[("XNR", 16)])
        xnr_g[0] = CNT["XNR"]
        V(lambda e, i=i: e.tensor_scalar(
            s_stat[:, i * TB:(i + 1) * TB], s_xnb[:], s_lnwc[:, i:i + 1],
            None, OP.mult))
    vs_pln = VSn()
    xnr_pln = CNT["XNR"]
    esA.close()

    esB = contextlib.ExitStack()
    sbB = lambda name, shape, dt: esB.enter_context(nc.sbuf_tensor(name, shape, dt))
    s_w = [sbB(f"s_w{p}", [128, 32 * 256], BF16) for p in range(2)]
    s_stg = [sbB(f"s_stg{p}", [128, 512], F32) for p in range(2)]
    s_xt = [[sbB(f"s_xt{k}_{p}", [128, 512], BF16) for p in range(2)]
            for k in range(KC)]
    s_cstg = [sbB(f"s_cstg{p}", [128, 512], BF16) for p in range(2)]
    s_mws = [sbB(f"s_mws{p}", [128, KC * 512], BF16) for p in range(2)]
    s_dtt = sbB("s_dtt", [128, NH], F32)
    s_dtt2 = sbB("s_dtt2", [128, NH], F32)
    s_dtt3 = sbB("s_dtt3", [128, NH], F32)
    s_p2a = sbB("s_p2a", [128, 64], F32)
    s_p2b = sbB("s_p2b", [128, 32], BF16)

    def wsrc(hbm, c0, cw):
        return hbm.rearrange("(i p) c -> p i c", p=128)[:, :, c0:c0 + cw]

    bank_free = {b: 0 for b in range(8)}
    wbuf_free = {0: 0, 1: 0}
    stg_free = {0: 0, 1: 0}
    stg_cnt = [0]
    cstg_free = {0: 0, 1: 0}
    cstg_cnt = [0]
    xt_free = {0: 0, 1: 0}
    mws_free = {0: vs_pln, 1: vs_pln}
    half_idx = [0]

    def SH(ix):
        return s_shift[:, ix * 128:(ix + 1) * 128]

    def proj_group(g0, gw, conv_order, stat_off_fn, first_wait):
        nhal = (gw + 255) // 256
        bufs = []
        wlds = []
        for hf in range(nhal):
            c0 = g0 + hf * 256
            cw = min(256, g0 + gw - c0)
            buf = half_idx[0] % 2
            half_idx[0] += 1
            bufs.append((buf, c0, cw))
            DMA("sync", lambda e, c0=c0, cw=cw, buf=buf: e.dma_start(
                out=s_w[buf][:, 0:32 * 256].rearrange(
                    "p (i c) -> p i c", i=32)[:, :, 0:cw],
                in_=wsrc(wscr, c0, cw)),
                wait=[("TS", wbuf_free[buf]), ("WCV", wcv_need(c0 + cw))]
                + list(first_wait),
                inc=[("WLD", 16)])
            wlds.append(CNT["WLD"])
        is_conv = DS <= g0 < PROJ
        tiles = conv_order if is_conv else list(range(8))
        for t in tiles:
            nr = 128 if t < 8 else 3
            bank = t % 8
            ps = psA if bank < 4 else psB
            poff = (bank % 4) * 512
            for hj, (buf, c0, cw) in enumerate(bufs):
                sub = c0 - g0
                for i in range(32):
                    so = stat_off_fn(i, t)
                    fa = (hj == 0 and i == 0)
                    la = (hj == nhal - 1 and i == 31)
                    w = [("WLD", wlds[hj])]
                    if fa:
                        w.append(("VS", bank_free[bank]))
                    TE(lambda e, nr=nr, ps=ps, poff=poff, sub=sub, cw=cw,
                       so=so, buf=buf, i=i, fa=fa, la=la:
                       e.matmul(ps[0:nr, poff + sub:poff + sub + cw],
                                s_stat[:, so:so + nr],
                                s_w[buf][:, i * 256:i * 256 + cw],
                                start=fa, stop=la),
                       wait=w, inc=[("TS", 1)] if la else ())
            yield t, nr, ps, poff, bank, TS()
        for buf, _, _ in bufs:
            wbuf_free[buf] = TS()

    def evac_z(g0, t, ps, poff, bank, ts_mm):
        par = stg_cnt[0] % 2
        stg_cnt[0] += 1
        V(lambda e, t=t, ps=ps, poff=poff, par=par, g0=g0:
          e.scalar_tensor_tensor(s_stg[par][:, 0:512], ps[:, poff:poff + 512],
                                 s_rmsall[:, t:t + 1], s_mupz[:, g0:g0 + 512],
                                 OP.mult, OP.mult),
          wait=[("TS", ts_mm), ("ZUS", stg_free[par])])
        bank_free[bank] = VSn()
        vs = VSn()
        DMA("sync", lambda e, t=t, par=par, g0=g0: e.dma_start(
            out=zscr[128 * t:128 * (t + 1), g0:g0 + 512], in_=s_stg[par][:, 0:512]),
            wait=[("VS", vs)], inc=[("ZUS", 16)])
        stg_free[par] = CNT["ZUS"]

    def evac_conv(cc0, ccw, t, nr, ps, poff, bank, ts_mm, pos, mpar, mwl):
        par = pos % 2
        for k in range(KC):
            V(lambda e, k=k, ps=ps, poff=poff, ccw=ccw, nr=nr, t=t, par=par,
              mpar=mpar:
              e.scalar_tensor_tensor(
                  s_xt[k][par][0:nr, 0:ccw], ps[0:nr, poff:poff + ccw],
                  (s_rmsall[:, t:t + 1] if t < 8 else s_rmsall[0:3, 8:9]),
                  s_mws[mpar][0:nr, k * 512:k * 512 + ccw],
                  OP.mult, OP.mult),
              wait=([("TS", max(ts_mm, xt_free[par])), ("MWL", mwl)]
                    if k == 0 else []))
        vs_xt = VSn()
        bank_free[bank] = vs_xt
        if t == 8:
            return
        for k in range(KC):
            d = 3 - k
            lhs = s_identb[:, :] if d == 0 else SH(d - 1)
            fa = (k == 0)
            TE(lambda e, lhs=lhs, ps=ps, poff=poff, ccw=ccw, k=k, par=par,
               fa=fa: e.matmul(ps[:, poff:poff + ccw], lhs,
                               s_xt[k][par][:, 0:ccw], start=fa, stop=False),
               wait=[("VS", vs_xt)] if fa else [])
        for k in range(KC - 1):
            d = 3 - k
            lhs = SH(5 + d) if t == 0 else SH(2 + d)
            la = (k == KC - 2)
            TE(lambda e, lhs=lhs, ps=ps, poff=poff, ccw=ccw, k=k, par=par,
               la=la: e.matmul(ps[:, poff:poff + ccw], lhs,
                               s_xt[k][1 - par][:, 0:ccw], start=False, stop=la),
               wait=[], inc=[("TS", 1)] if la else ())
        ts_conv = TS()
        xt_free[1 - par] = ts_conv
        cpar = cstg_cnt[0] % 2
        cstg_cnt[0] += 1
        V(lambda e, ps=ps, poff=poff, ccw=ccw, cpar=cpar:
          e.tensor_copy(s_cstg[cpar][:, 0:ccw], ps[:, poff:poff + ccw]),
          wait=[("TS", ts_conv), ("CPL", cstg_free[cpar])])
        bank_free[bank] = VSn()
        vs = VSn()
        DMA("sync", lambda e, t=t, cc0=cc0, ccw=ccw, cpar=cpar: e.dma_start(
            out=convscr[128 * t:128 * (t + 1), cc0:cc0 + ccw],
            in_=s_cstg[cpar][:, 0:ccw]),
            wait=[("VS", vs)], inc=[("CPL", 16)])
        cstg_free[cpar] = CNT["CPL"]

    def evac_dt(t, ps, poff, ts_mm):
        V(lambda e, t=t, ps=ps, poff=poff: e.scalar_tensor_tensor(
            s_dtt[:, :], ps[:, poff + 256:poff + 320], s_rmsall[:, t:t + 1],
            s_mupdt[:, :], OP.mult, OP.mult), wait=[("TS", ts_mm)])
        V(lambda e: e.tensor_tensor(s_dtt[:, :], s_dtt[:, :], s_dtbrep[:, :],
                                    OP.add))
        vs_z = VSn()
        SC(lambda e: e.activation(s_dtt2[:, :], s_dtt[:, :], AF.Abs),
           wait=[("VS", vs_z)])
        SC(lambda e: e.activation(s_dtt3[:, :], s_dtt2[:, :], AF.Exp, scale=-1.0))
        SC(lambda e: e.activation(s_dtt2[:, :], s_dtt3[:, :], AF.Ln,
                                  bias=1.0, scale=1.0))
        SC(lambda e: e.activation(s_dtt3[:, :], s_dtt[:, :], AF.Relu))
        ss_sp = SSn()
        V(lambda e, t=t: e.tensor_add(s_dtsp[:, t * 64:(t + 1) * 64],
                                      s_dtt2[:, :], s_dtt3[:, :]),
          wait=[("SS", ss_sp)])
        V(lambda e, t=t: e.tensor_tensor(s_dtaall[:, t * 64:(t + 1) * 64],
                                         s_dtsp[:, t * 64:(t + 1) * 64],
                                         s_arep[:, :], OP.mult))

    def attn_stat_off(i, t):
        return i * TB + (3 + 128 * t if t < 8 else 0)

    conv_order = [8] + list(range(8))
    for k in range(KC):
        for p in range(2):
            V(lambda e, k=k, p=p: e.memset(s_xt[k][p][:, :], 0.0),
              wait=[("TS", TS())])
    first_wait = [("VS", vs_pln), ("MWS", mws_all)]
    mcnt = [0]
    for g in range(17):
        g0 = g * 512
        gw = 512 if g < 16 else 320
        is_conv = DS <= g0
        mpar = 0
        mwl = 0
        if is_conv:
            cc0 = g0 - DS
            ccw = min(gw, 512) if g < 16 else 256
            mpar = mcnt[0] % 2
            mcnt[0] += 1
            DMA("sync", lambda e, cc0=cc0, ccw=ccw, mpar=mpar: e.dma_start(
                out=s_mws[mpar][:, 0:KC * 512].rearrange(
                    "p (k c) -> p k c", k=KC)[:, :, 0:ccw],
                in_=mupwscr[:, :].rearrange(
                    "p (k c) -> p k c", k=KC)[:, :, cc0:cc0 + ccw]),
                wait=[("MWS", mws_all), ("VS", mws_free[mpar])],
                inc=[("MWL", 16)])
            mwl = CNT["MWL"]
        for t, nr, ps, poff, bank, ts_mm in proj_group(
                g0, gw, conv_order, attn_stat_off, first_wait):
            first_wait = []
            if not is_conv:
                evac_z(g0, t, ps, poff, bank, ts_mm)
            else:
                pos = conv_order.index(t)
                evac_conv(g0 - DS, ccw, t, nr, ps, poff, bank, ts_mm, pos,
                          mpar, mwl)
                if g == 16 and t != 8:
                    evac_dt(t, ps, poff, ts_mm)
        if is_conv:
            mws_free[mpar] = VSn()
    ts_p1a = TS()
    cpl_all = CNT["CPL"]
    zus_p1a = CNT["ZUS"]

    DMA("sync", lambda e: e.dma_start(
        out=s_stat[:, 0:32 * TB].rearrange("p (i c) -> p i c", c=TB)[:, :, 0:T],
        in_=xnscr[:, :].rearrange("p (i c) -> p i c", c=T)),
        wait=[("TS", ts_p1a), ("XNR", xnr_pln)], inc=[("XNR", 16)])
    xnr_re = CNT["XNR"]

    def ffn_stat_off(i, t):
        return i * TB + 128 * t

    def evac_u(g0, t, ps, poff, bank, ts_mm):
        par = stg_cnt[0] % 2
        stg_cnt[0] += 1
        u0 = g0 - PROJ
        V(lambda e, ps=ps, poff=poff, par=par:
          e.tensor_copy(s_stg[par][:, 0:512], ps[:, poff:poff + 512]),
          wait=[("TS", ts_mm), ("ZUS", stg_free[par])])
        bank_free[bank] = VSn()
        vs = VSn()
        DMA("sync", lambda e, t=t, par=par, u0=u0: e.dma_start(
            out=uscr[128 * t:128 * (t + 1), u0:u0 + 512],
            in_=s_stg[par][:, 0:512]),
            wait=[("VS", vs)], inc=[("ZUS", 16)])
        stg_free[par] = CNT["ZUS"]

    first_wait = [("XNR", xnr_re)]
    for g in range(8):
        g0 = PROJ + g * 512
        for t, nr, ps, poff, bank, ts_mm in proj_group(
                g0, 512, None, ffn_stat_off, first_wait):
            first_wait = []
            evac_u(g0, t, ps, poff, bank, ts_mm)
    for t, nr, ps, poff, bank, ts_mm in proj_group(
            PROJ + 4096, 36, None, ffn_stat_off, [("XNR", xnr_re)]):
        V(lambda e, ps=ps, poff=poff, t=t:
          e.tensor_copy(s_zlora[:, t * 32:(t + 1) * 32], ps[:, poff:poff + 32]),
          wait=[("TS", ts_mm)])
        V(lambda e, ps=ps, poff=poff, t=t:
          e.tensor_copy(s_glog[:, t * 4:(t + 1) * 4], ps[:, poff + 32:poff + 36]))
        bank_free[bank] = VSn()
    ts_p1b = TS()
    zus_all = CNT["ZUS"]

    for t in range(8):
        sm = s_small
        V(lambda e, t=t: e.tensor_tensor(sm[:, 0:4], s_glog[:, 4 * t:4 * t + 4],
                                         s_ebrep[:, :], OP.add))
        V(lambda e: e.reduce_max(sm[:, 4:5], sm[:, 0:4], axis=AX))
        for eix in range(4):
            V(lambda e, eix=eix: e.tensor_scalar(
                sm[:, 5 + eix:6 + eix], sm[:, eix:eix + 1], sm[:, 4:5], None,
                OP.is_ge))
        V(lambda e: e.tensor_copy(sm[:, 9:10], sm[:, 5:6]))
        V(lambda e: e.tensor_scalar(sm[:, 13:14], sm[:, 9:10], -1.0, 1.0,
                                    OP.mult, OP.add))
        V(lambda e: e.tensor_tensor(sm[:, 10:11], sm[:, 6:7], sm[:, 13:14],
                                    OP.mult))
        V(lambda e: e.tensor_scalar(sm[:, 14:15], sm[:, 10:11], -1.0, 1.0,
                                    OP.mult, OP.add))
        V(lambda e: e.tensor_tensor(sm[:, 14:15], sm[:, 14:15], sm[:, 13:14],
                                    OP.mult))
        V(lambda e: e.tensor_tensor(sm[:, 11:12], sm[:, 7:8], sm[:, 14:15],
                                    OP.mult))
        V(lambda e: e.tensor_tensor(sm[:, 15:16], sm[:, 9:10], sm[:, 10:11],
                                    OP.add))
        V(lambda e: e.tensor_tensor(sm[:, 15:16], sm[:, 15:16], sm[:, 11:12],
                                    OP.add))
        V(lambda e: e.tensor_scalar(sm[:, 12:13], sm[:, 15:16], -1.0, 1.0,
                                    OP.mult, OP.add))
        V(lambda e, t=t: e.tensor_tensor(
            s_p2a[:, 0:32].rearrange("p (a b) -> p a b", a=4),
            s_zlora[:, 32 * t:32 * (t + 1)].rearrange("p (a b) -> p a b", a=4),
            sm[:, 9:13].unsqueeze(2).broadcast_to([128, 4, 8]), OP.mult))
        V(lambda e: e.tensor_copy(s_p2b[:, 0:32], s_p2a[:, 0:32]))
        vs_zt = VSn()
        TE(lambda e: e.transpose(psA[0:32, 0:64].bitcast(BF16)[:, 0:128],
                                 s_p2b[:, 0:32], s_identb[:, :]),
           wait=[("VS", max(vs_zt, bank_free[0]))], inc=[("TS", 1)])
        ts = TS()
        V(lambda e, t=t: e.tensor_copy(s_zlt[:, 128 * t:128 * (t + 1)],
                                       psA[0:32, 0:64].bitcast(BF16)[:, 0:128]),
          wait=[("TS", ts)])
        bank_free[0] = VSn()
    vs_p2 = VSn()
    esB.close()

    esC = contextlib.ExitStack()
    sbC = lambda name, shape, dt: esC.enter_context(nc.sbuf_tensor(name, shape, dt))
    esCx = contextlib.ExitStack()
    s_xc = esCx.enter_context(nc.sbuf_tensor("s_xc", [128, CONV], BF16))
    s_xhat = esCx.enter_context(nc.sbuf_tensor("s_xhat", [128, DS], BF16))
    xc_guard = {"v": vs_p2, "t": 0, "s": 0}

    def load_conv_tile(c):
        DMA("sync", lambda e, c=c: e.dma_start(
            out=s_xc[:, :], in_=convscr[128 * c:128 * (c + 1), :]),
            wait=[("SS", xc_guard["s"]), ("VS", xc_guard["v"]),
                  ("TS", xc_guard["t"]), ("CPL", cpl_all)],
            inc=[("XLD", 16)])
        xld = CNT["XLD"]
        SC(lambda e: e.activation(s_xc[:, :], s_xc[:, :], AF.Silu),
           wait=[("XLD", xld)])
        xc_guard["s"] = SSn()
        return SSn()

    for c in range(NT):
        ss_xc = load_conv_tile(c)
        TE(lambda e, c=c: e.matmul(psA[:, 0:64], s_ut[:, :],
                                   s_dtaall[:, c * 64:(c + 1) * 64],
                                   start=True, stop=True),
           wait=[("VS", bank_free[0])], inc=[("TS", 1)])
        ts_lam = TS()
        V(lambda e, c=c: e.tensor_copy(s_lam[:, c * 64:(c + 1) * 64],
                                       psA[:, 0:64]), wait=[("TS", ts_lam)])
        vs_lam = VSn()
        TE(lambda e, c=c: e.transpose(psA[0:64, 512:640],
                                      s_lam[:, c * 64:(c + 1) * 64],
                                      s_identf[:, :]),
           wait=[("VS", vs_lam)], inc=[("TS", 1)])
        ts_lfm = TS()
        V(lambda e, c=c: e.tensor_copy(s_lamfm[:, c * 128:(c + 1) * 128],
                                       psA[0:64, 512:640]), wait=[("TS", ts_lfm)])
        DMA("sync", lambda e, c=c: e.dma_start(
            out=s_row64[0:1, 320:384], in_=s_lam[127:128, c * 64:(c + 1) * 64]),
            wait=[("VS", vs_lam), ("SS", SSn()), ("TS", TS())],
            inc=[("ROW", 16)])
        roww = CNT["ROW"]
        SC(lambda e: e.activation(s_row64[0:1, 384:448], s_row64[0:1, 320:384],
                                  AF.Exp), wait=[("ROW", roww)])
        ss_gam = SSn()
        TE(lambda e: e.matmul(psA[:, 1024:1088], s_ones1[0:1, 0:128],
                              s_row64[0:1, 320:384], start=True, stop=True),
           wait=[("ROW", roww), ("VS", VSn())], inc=[("TS", 1)])
        ts_lend = TS()
        TE(lambda e: e.matmul(psA[:, 1536:1600], s_ones1[0:1, 0:128],
                              s_row64[0:1, 384:448], start=True, stop=True),
           wait=[("SS", ss_gam)], inc=[("TS", 1)])
        ts_gam = TS()
        V(lambda e, c=c: e.tensor_copy(s_gamrep[:, c * 64:(c + 1) * 64],
                                       psA[:, 1536:1600]), wait=[("TS", ts_gam)])
        V(lambda e, c=c: e.tensor_tensor(s_small[:, 0:64],
                                         psA[:, 1024:1088],
                                         s_lam[:, c * 64:(c + 1) * 64],
                                         OP.subtract),
          wait=[("TS", ts_lend)])
        vs_f1 = VSn()
        SC(lambda e: e.activation(s_small[:, 64:128], s_small[:, 0:64], AF.Exp),
           wait=[("VS", vs_f1)])
        SC(lambda e, c=c: e.activation(s_explam[:, c * 64:(c + 1) * 64],
                                       s_lam[:, c * 64:(c + 1) * 64], AF.Exp))
        ss_f = SSn()
        V(lambda e, c=c: e.tensor_tensor(s_ftm[:, c * 64:(c + 1) * 64],
                                         s_small[:, 64:128],
                                         s_dtsp[:, c * 64:(c + 1) * 64], OP.mult),
          wait=[("SS", ss_f)])
        V(lambda e, c=c: e.tensor_tensor(s_gtot[:, :], s_gtot[:, :],
                                         s_gamrep[:, c * 64:(c + 1) * 64],
                                         OP.mult))
        V(lambda e, c=c: e.tensor_tensor(
            s_xhat[:, :].rearrange("p (a b) -> p a b", a=64),
            s_xc[:, 0:DS].rearrange("p (a b) -> p a b", a=64),
            s_ftm[:, c * 64:(c + 1) * 64].unsqueeze(2).broadcast_to([128, 64, 64]),
            OP.mult), wait=[("SS", ss_xc)])
        vs_xhat = VSn()
        for j in range(8):
            ps = psA if j < 4 else psB
            poff = (j % 4) * 512
            TE(lambda e, ps=ps, poff=poff, j=j:
               e.matmul(ps[:, poff:poff + 512], s_xc[:, DS:DS + 128],
                        s_xhat[:, j * 512:(j + 1) * 512], start=True, stop=True),
               wait=[("VS", vs_xhat)] if j == 0 else [],
               inc=[("TS", 1)] if j == 7 else ())
        ts_state = TS()
        V(lambda e, c=c: e.tensor_tensor(
            s_hrun[:, :].rearrange("p (a b) -> p a b", a=64),
            s_hrun[:, :].rearrange("p (a b) -> p a b", a=64),
            s_gamrep[:, c * 64:(c + 1) * 64].unsqueeze(2).broadcast_to([128, 64, 64]),
            OP.mult), wait=[("TS", ts_state)])
        V(lambda e: e.tensor_tensor(s_hrun[:, 0:2048], s_hrun[:, 0:2048],
                                    psA[:, :], OP.add))
        V(lambda e: e.tensor_tensor(s_hrun[:, 2048:4096], s_hrun[:, 2048:4096],
                                    psB[:, :], OP.add))
        for b in range(8):
            bank_free[b] = VSn()
        xc_guard["v"] = VSn()
        xc_guard["t"] = ts_state

    esC1 = contextlib.ExitStack()
    sbC1 = lambda name, shape, dt: esC1.enter_context(nc.sbuf_tensor(name, shape, dt))
    s_gath = sbC1("s_gath", [128, DS + NH], F32)
    s_prod = sbC1("s_prod", [128, NH], F32)
    s_wj = sbC1("s_wj", [128, NH], F32)
    s_hw = sbC1("s_hw", [128, DS], F32)
    vs_chain = VSn()
    DMA("sync", lambda e: e.dma_start(out=ccin[:, 0:DS], in_=s_hrun[:, :]),
        wait=[("VS", vs_chain)], inc=[("PUB", 16)])
    DMA("sync", lambda e: e.dma_start(out=ccin[:, DS:DS + NH], in_=s_gtot[:, :]),
        wait=[("VS", vs_chain)], inc=[("PUB", 16)])
    DMA("gpsimd", lambda e: e.collective_compute(
        "AllGather", OP.bypass, replica_groups=[[0, 1, 2, 3], [4, 5, 6, 7]],
        ins=[ccin], outs=[ccout]),
        wait=[("PUB", CNT["PUB"]), ("GDM", CNT["GDM"])], inc=[("CC", 1)])
    cc_done = CNT["CC"]
    for j in range(4):
        DMA("gpsimd", lambda e, j=j: e.dma_start(
            out=owscr[:, j * 1024:(j + 1) * 1024],
            in_=outwD[:, j * 1024:(j + 1) * 1024]),
            (), [("OCV", 16)])
    ocv_all = CNT["OCV"]
    V(lambda e: e.memset(s_hrun[:, :], 0.0), wait=[("CC", cc_done)])
    V(lambda e: e.memset(s_prod[:, :], 1.0))
    for j in [2, 1, 0]:
        vs_gprev = VSn()
        DMA("sync", lambda e, j=j: e.dma_start(
            out=s_gath[:, :], in_=ccout[128 * j:128 * (j + 1), :]),
            wait=[("CC", cc_done), ("VS", vs_gprev), ("GTH", CNT["GTH"])],
            inc=[("GTH", 16)])
        gth = CNT["GTH"]
        V(lambda e, j=j: e.tensor_scalar(s_wj[:, :], s_prod[:, :],
                                         s_cmask[:, j:j + 1], None, OP.mult))
        V(lambda e, j=j: e.tensor_scalar(s_small[:, 0:64],
                                         s_gath[:, DS:DS + NH],
                                         s_gmask[:, j:j + 1],
                                         s_gmaski[:, j:j + 1], OP.mult, OP.add),
          wait=[("GTH", gth)])
        V(lambda e: e.tensor_tensor(s_prod[:, :], s_prod[:, :], s_small[:, 0:64],
                                    OP.mult))
        V(lambda e, j=j: e.tensor_tensor(
            s_hw[:, :].rearrange("p (a b) -> p a b", a=64),
            s_gath[:, 0:DS].rearrange("p (a b) -> p a b", a=64),
            s_wj[:, :].unsqueeze(2).broadcast_to([128, 64, 64]), OP.mult))
        V(lambda e: e.tensor_tensor(s_hrun[:, :], s_hrun[:, :], s_hw[:, :],
                                    OP.add))
    vs_hinit = VSn()
    esC1.close()

    esC2 = contextlib.ExitStack()
    sbC2 = lambda name, shape, dt: esC2.enter_context(nc.sbuf_tensor(name, shape, dt))
    s_E = sbC2("s_E", [128, 2048], F32)
    s_L = sbC2("s_L", [128, 4096], BF16)
    s_stile = sbC2("s_stile", [128, 4096], BF16)
    s_y = sbC2("s_y", [128, DS], F32)
    s_zrl = sbC2("s_zrl", [128, 2048], F32)
    s_hcast = sbC2("s_hcast", [128, 2048], BF16)
    s_bfm = sbC2("s_bfm", [128, 128], BF16)
    s_cfm = sbC2("s_cfm", [128, 128], BF16)
    s_gbf = sbC2("s_gbf", [128, 128], BF16)
    s_lamq = sbC2("s_lamq", [1, 2048], F32)
    zrl_free = {0: vs_hinit, 1: vs_hinit}
    gA = [max(bank_free.values())]
    gB = [gA[0]]

    for c in range(NT):
        ss_xc = load_conv_tile(c)
        DMA("sync", lambda e, c=c: e.dma_start(
            out=s_zrl[:, :],
            in_=zscr[128 * c:128 * (c + 1), 0:2048]),
            wait=[("VS", zrl_free[0]), ("ZUS", zus_p1a)], inc=[("ZRL", 16)])
        zrl0 = CNT["ZRL"]
        TE(lambda e: e.transpose(psA[:, 0:64].bitcast(BF16)[:, 0:128],
                                 s_xc[:, DS:DS + 128], s_identb[:, :]),
           wait=[("SS", ss_xc), ("VS", gA[0])], inc=[("TS", 1)])
        ts_b = TS()
        TE(lambda e: e.transpose(psA[:, 512:576].bitcast(BF16)[:, 0:128],
                                 s_xc[:, DS + 128:DS + 256], s_identb[:, :]),
           inc=[("TS", 1)])
        ts_c = TS()
        V(lambda e: e.tensor_copy(s_bfm[:, :], psA[:, 0:64].bitcast(BF16)[:, 0:128]),
          wait=[("TS", ts_b)])
        V(lambda e: e.tensor_copy(s_cfm[:, :], psA[:, 512:576].bitcast(BF16)[:, 0:128]),
          wait=[("TS", ts_c)])
        vs_bc = VSn()
        TE(lambda e: e.matmul(psA[:, 1024:1152], s_bfm[:, :], s_cfm[:, :],
                              start=True, stop=True),
           wait=[("VS", vs_bc)], inc=[("TS", 1)])
        ts_g = TS()
        V(lambda e: e.tensor_tensor(s_gbf[:, :], psA[:, 1024:1152], s_ut[:, :],
                                    OP.mult), wait=[("TS", ts_g)])
        gA[0] = VSn()
        for half in range(2):
            for qq in range(2):
                q = 2 * half + qq
                ps = psA if qq == 0 else psB
                DMA("sync", lambda e, c=c, q=q: e.dma_start(
                    out=s_lamq[0:1, :],
                    in_=s_lamfm[16 * q:16 * (q + 1), c * 128:(c + 1) * 128]),
                    wait=[("LFD", CNT["LFD"]), ("VS", vs_hinit),
                          ("TS", TS())], inc=[("LFD", 16)])
                lfd = CNT["LFD"]
                for j in range(4):
                    TE(lambda e, ps=ps, j=j: e.matmul(
                        ps[:, j * 512:(j + 1) * 512], s_ones1[0:1, 0:128],
                        s_lamq[0:1, j * 512:(j + 1) * 512],
                        start=True, stop=True),
                       wait=[("LFD", lfd),
                             ("VS", gA[0] if qq == 0 else gB[0])]
                       if j == 0 else [],
                       inc=[("TS", 1)] if j == 3 else ())
                ts_q = TS()
                for hh in range(16):
                    h = q * 16 + hh
                    V(lambda e, ps=ps, hh=hh, h=h, c=c: e.tensor_scalar(
                        s_E[:, hh * 128:(hh + 1) * 128],
                        ps[:, hh * 128:(hh + 1) * 128],
                        s_lam[:, c * 64 + h:c * 64 + h + 1], 0.0,
                        OP.subtract, OP.min),
                      wait=[("TS", ts_q)] if hh == 0 else [])
                vs_e = VSn()
                if qq == 0:
                    gA[0] = vs_e
                else:
                    gB[0] = vs_e
                SC(lambda e, qq=qq: e.activation(
                    s_L[:, qq * 2048:(qq + 1) * 2048], s_E[:, :], AF.Exp),
                   wait=[("VS", vs_e)])
            ss_l = SSn()
            for hh in range(32):
                h = half * 32 + hh
                V(lambda e, hh=hh, h=h, c=c: e.scalar_tensor_tensor(
                    s_stile[:, hh * 128:(hh + 1) * 128],
                    s_L[:, hh * 128:(hh + 1) * 128],
                    s_dtsp[:, c * 64 + h:c * 64 + h + 1],
                    s_gbf[:, :], OP.mult, OP.mult),
                  wait=([("SS", ss_l)] if hh == 0 else []))
            vs_stile = VSn()
            V(lambda e, half=half: e.tensor_copy(
                s_hcast[:, :], s_hrun[:, half * 2048:(half + 1) * 2048]))
            vs_hc = VSn()
            for j in range(4):
                TE(lambda e, j=j: e.matmul(
                    psA[:, j * 512:(j + 1) * 512], s_cfm[:, :],
                    s_hcast[:, j * 512:(j + 1) * 512], start=True, stop=True),
                   wait=[("VS", max(vs_hc, gA[0]))] if j == 0 else [],
                   inc=[("TS", 1)] if j == 3 else ())
            ts_inter = TS()
            for hh in range(32):
                h = half * 32 + hh
                TE(lambda e, hh=hh, h=h, first=(hh % 8 == 0), last=(hh % 8 == 7):
                   e.matmul(psB[:, hh * 64:(hh + 1) * 64],
                            s_stile[:, hh * 128:(hh + 1) * 128],
                            s_xc[:, h * 64:(h + 1) * 64],
                            start=first, stop=last),
                   wait=[("VS", max(vs_stile, gB[0]))] if hh == 0 else [],
                   inc=[("TS", 1)] if hh == 31 else ())
            ts_intra = TS()
            V(lambda e, half=half, c=c: e.tensor_tensor(
                s_y[:, half * 2048:(half + 1) * 2048].rearrange(
                    "p (a b) -> p a b", a=32),
                psA[:, :].rearrange("p (a b) -> p a b", a=32),
                s_explam[:, c * 64 + half * 32:c * 64 + (half + 1) * 32]
                .unsqueeze(2).broadcast_to([128, 32, 64]), OP.mult),
              wait=[("TS", ts_inter)])
            V(lambda e, half=half: e.tensor_tensor(
                s_y[:, half * 2048:(half + 1) * 2048],
                s_y[:, half * 2048:(half + 1) * 2048], psB[:, :], OP.add),
              wait=[("TS", ts_intra)])
            gA[0] = VSn()
            gB[0] = VSn()
            V(lambda e, half=half, c=c: e.tensor_tensor(
                s_hcast[:, :].rearrange("p (a b) -> p a b", a=32),
                s_xc[:, half * 2048:(half + 1) * 2048].rearrange(
                    "p (a b) -> p a b", a=32),
                s_drep[:, half * 32:(half + 1) * 32].unsqueeze(2)
                .broadcast_to([128, 32, 64]), OP.mult))
            V(lambda e, half=half: e.tensor_tensor(
                s_y[:, half * 2048:(half + 1) * 2048],
                s_y[:, half * 2048:(half + 1) * 2048], s_hcast[:, :], OP.add))
        V(lambda e, c=c: e.tensor_tensor(
            s_xhat[:, :].rearrange("p (a b) -> p a b", a=64),
            s_xc[:, 0:DS].rearrange("p (a b) -> p a b", a=64),
            s_ftm[:, c * 64:(c + 1) * 64].unsqueeze(2).broadcast_to([128, 64, 64]),
            OP.mult))
        vs_xhat = VSn()
        for j in range(8):
            ps = psA if j < 4 else psB
            poff = (j % 4) * 512
            TE(lambda e, ps=ps, poff=poff, j=j:
               e.matmul(ps[:, poff:poff + 512], s_xc[:, DS:DS + 128],
                        s_xhat[:, j * 512:(j + 1) * 512], start=True, stop=True),
               wait=[("VS", max(vs_xhat, gA[0], gB[0]))] if j == 0 else [],
               inc=[("TS", 1)] if j == 7 else ())
        ts_state = TS()
        V(lambda e, c=c: e.tensor_tensor(
            s_hrun[:, :].rearrange("p (a b) -> p a b", a=64),
            s_hrun[:, :].rearrange("p (a b) -> p a b", a=64),
            s_gamrep[:, c * 64:(c + 1) * 64].unsqueeze(2).broadcast_to([128, 64, 64]),
            OP.mult), wait=[("TS", ts_state)])
        V(lambda e: e.tensor_tensor(s_hrun[:, 0:2048], s_hrun[:, 0:2048],
                                    psA[:, :], OP.add))
        V(lambda e: e.tensor_tensor(s_hrun[:, 2048:4096], s_hrun[:, 2048:4096],
                                    psB[:, :], OP.add))
        vs_hupd = VSn()
        SC(lambda e: e.activation(s_zrl[:, :], s_zrl[:, :], AF.Silu),
           wait=[("ZRL", zrl0)])
        ss_silu = SSn()
        V(lambda e: e.tensor_tensor(s_y[:, 0:2048], s_y[:, 0:2048],
                                    s_zrl[:, :], OP.mult),
          wait=[("SS", ss_silu)])
        vs_z0 = VSn()
        DMA("sync", lambda e, c=c: e.dma_start(
            out=s_zrl[:, :], in_=zscr[128 * c:128 * (c + 1), 2048:4096]),
            wait=[("VS", vs_z0), ("SS", SSn())], inc=[("ZRL", 16)])
        zrl1 = CNT["ZRL"]
        SC(lambda e: e.activation(s_zrl[:, :], s_zrl[:, :], AF.Silu),
           wait=[("ZRL", zrl1)])
        ss_silu2 = SSn()
        V(lambda e: e.tensor_tensor(s_y[:, 2048:4096], s_y[:, 2048:4096],
                                    s_zrl[:, :], OP.mult),
          wait=[("SS", ss_silu2)])
        zrl_free[0] = VSn()
        vs_gt = VSn()
        SC(lambda e: e.activation(s_E[:, :], s_y[:, 0:2048], AF.Square),
           wait=[("VS", vs_gt)])
        ss_sq1 = SSn()
        V(lambda e: e.reduce_sum(s_small[:, 100:101], s_E[:, :], axis=AX),
          wait=[("SS", ss_sq1)])
        vs_rs1 = VSn()
        SC(lambda e: e.activation(s_E[:, :], s_y[:, 2048:4096], AF.Square),
           wait=[("VS", vs_rs1)])
        ss_sq2 = SSn()
        V(lambda e: e.reduce_sum(s_small[:, 101:102], s_E[:, :], axis=AX),
          wait=[("SS", ss_sq2)])
        V(lambda e: e.tensor_tensor(s_small[:, 102:103], s_small[:, 100:101],
                                    s_small[:, 101:102], OP.add))
        vs_ss = VSn()
        SC(lambda e: e.activation(s_small[:, 103:104], s_small[:, 102:103],
                                  AF.Sqrt, bias=EPS, scale=1.0 / DS),
           wait=[("VS", vs_ss)])
        ss_sr = SSn()
        V(lambda e: e.reciprocal(s_small[:, 104:105], s_small[:, 103:104]),
          wait=[("SS", ss_sr)])
        V(lambda e: e.scalar_tensor_tensor(s_y[:, :], s_y[:, :],
                                           s_small[:, 104:105], s_normw[:, :],
                                           OP.mult, OP.mult))
        vs_yh = VSn()
        tb_free = {b: ("VS", vs_hupd) for b in range(8)}
        for j in range(32):
            bank = j % 8
            ps = psA if bank < 4 else psB
            poff = (bank % 4) * 512
            gsem, gval = tb_free[bank]
            TE(lambda e, j=j, ps=ps, poff=poff: e.transpose(
                ps[:, poff:poff + 128], s_y[:, j * 128:(j + 1) * 128],
                s_identf[:, :]),
               wait=[("VS", vs_yh), (gsem, gval)], inc=[("TS", 1)])
            ts = TS()
            SC(lambda e, j=j, c=c, ps=ps, poff=poff: e.copy(
                s_stat[:, j * TB + 128 * c:j * TB + 128 * (c + 1)],
                ps[:, poff:poff + 128]),
               wait=[("TS", ts)])
            tb_free[bank] = ("SS", SSn())
        V(lambda e: e.engine_nop(), wait=[("SS", SSn())])
        gA[0] = VSn()
        gB[0] = VSn()
        xc_guard["v"] = VSn()
        xc_guard["s"] = SSn()
        xc_guard["t"] = ts_state
    vs_p5 = VSn()
    ss_p5 = SSn()
    ts_p5 = TS()
    esC2.close()
    esCx.close()

    esC3 = contextlib.ExitStack()
    sbC3 = lambda name, shape, dt: esC3.enter_context(nc.sbuf_tensor(name, shape, dt))
    s_lorab = sbC3("s_lorab", [32, H], BF16)
    s_ow = [sbC3(f"s_ow{p}", [128, 32 * 256], BF16) for p in range(3)]
    s_url = [sbC3(f"s_url{p}", [128, 512], F32) for p in range(2)]
    s_xrl = [sbC3(f"s_xrl{p}", [128, 512], F32) for p in range(2)]
    s_ost = [sbC3(f"s_ost{p}", [128, 512], F32) for p in range(2)]
    DMA("gpsimd", lambda e: e.dma_start(out=s_lorab[:], in_=lorabD),
        wait=[("VS", vs_p5), ("SS", ss_p5)], inc=[("GDM", 16)])
    gdm_lb = CNT["GDM"]
    ow_free = {0: 0, 1: 0, 2: 0}
    url_free = {0: vs_p5, 1: vs_p5}
    xrl_free = {0: vs_p5, 1: vs_p5}
    ost_free = {0: 0, 1: 0}
    obank_free = {b: vs_p5 for b in range(8)}

    for og in range(8):
        owls = []
        obufs = []
        for hf in range(2):
            buf = (2 * og + hf) % 3
            obufs.append(buf)
            DMA("sync", lambda e, og=og, hf=hf, buf=buf: e.dma_start(
                out=s_ow[buf][:, 0:32 * 256].rearrange("p (i c) -> p i c", i=32),
                in_=wsrc(owscr, og * 512 + hf * 256, 256)),
                wait=[("TS", ow_free[buf]), ("VS", vs_p5), ("OCV", ocv_all)],
                inc=[("OWL", 16)])
            owls.append(CNT["OWL"])
        for t in range(8):
            rpar = t % 2
            DMA("sync", lambda e, og=og, t=t, rpar=rpar: e.dma_start(
                out=s_url[rpar][:, :],
                in_=uscr[128 * t:128 * (t + 1), og * 512:(og + 1) * 512]),
                wait=[("VS", url_free[rpar]), ("ZUS", zus_all)], inc=[("URL", 16)])
            DMA("sync", lambda e, og=og, t=t, rpar=rpar: e.dma_start(
                out=s_xrl[rpar][:, :],
                in_=xown[128 * t:128 * (t + 1), og * 512:(og + 1) * 512]),
                wait=[("VS", xrl_free[rpar])], inc=[("XRL", 16)])
            url = CNT["URL"]
            xrl = CNT["XRL"]
            bank = t % 8
            ps = psA if bank < 4 else psB
            poff = (bank % 4) * 512
            for hf in range(2):
                for i in range(32):
                    fa = (hf == 0 and i == 0)
                    w = []
                    if i == 0:
                        w.append(("OWL", owls[hf]))
                    if fa:
                        w += [("VS", obank_free[bank]), ("TS", ts_p5),
                              ("SS", ss_p5)]
                    TE(lambda e, ps=ps, poff=poff, i=i, t=t, hf=hf,
                       buf=obufs[hf], fa=fa:
                       e.matmul(ps[:, poff + hf * 256:poff + hf * 256 + 256],
                                s_stat[:, i * TB + 128 * t:i * TB + 128 * t + 128],
                                s_ow[buf][:, i * 256:(i + 1) * 256],
                                start=fa, stop=False),
                       wait=w)
            for hf in range(2):
                TE(lambda e, ps=ps, poff=poff, t=t, og=og, hf=hf: e.matmul(
                    ps[:, poff + hf * 256:poff + hf * 256 + 256],
                    s_zlt[:, 128 * t:128 * (t + 1)],
                    s_lorab[:, og * 512 + hf * 256:og * 512 + hf * 256 + 256],
                    start=False, stop=True),
                   wait=[("GDM", gdm_lb), ("VS", vs_p2)] if hf == 0 else [],
                   inc=[("TS", 1)] if hf == 1 else ())
            ts_o = TS()
            opar = (og * 8 + t) % 2
            V(lambda e, ps=ps, poff=poff, opar=opar, rpar=rpar:
              e.tensor_tensor(s_ost[opar][:, :], ps[:, poff:poff + 512],
                              s_url[rpar][:, :], OP.add),
              wait=[("TS", ts_o), ("URL", url), ("STO", ost_free[opar])])
            obank_free[bank] = VSn()
            url_free[rpar] = VSn()
            V(lambda e, opar=opar, rpar=rpar: e.tensor_tensor(
                s_ost[opar][:, :], s_ost[opar][:, :], s_xrl[rpar][:, :], OP.add),
              wait=[("XRL", xrl)])
            xrl_free[rpar] = VSn()
            vs_o = VSn()
            DMA("sync", lambda e, og=og, t=t, opar=opar: e.dma_start(
                out=oseg[128 * t:128 * (t + 1), og * 512:(og + 1) * 512],
                in_=s_ost[opar][:, :]),
                wait=[("VS", vs_o)], inc=[("STO", 16)])
            ost_free[opar] = CNT["STO"]
        for _b in obufs:
            ow_free[_b] = TS()

    WAITQ("gpsimd", [("STO", CNT["STO"])])
    esC3.close()
    esC.close()

    with nc.Block() as block:
        @block.sync
        def _(sync):
            for f in QL["sync"]:
                f(sync)

        @block.gpsimd
        def _(g):
            for f in QL["gpsimd"]:
                f(g)

        @block.tensor
        def _(t):
            for f in QL["tensor"]:
                f(t)

        @block.vector
        def _(v):
            for f in QL["vector"]:
                f(v)

        @block.scalar
        def _(s):
            for f in QL["scalar"]:
                f(s)

    es.close()
    _cache["nc"] = nc
    return nc


def _shift_consts():
    mats = []
    for d in (1, 2, 3):
        mats.append(np.eye(128, k=d))
    for d in (1, 2, 3):
        mats.append(np.eye(128, k=d - 128))
    for d in (1, 2, 3):
        m = np.zeros((128, 128))
        for t in range(d):
            m[3 + t - d, t] = 1.0
        mats.append(m)
    return np.ascontiguousarray(
        np.concatenate(mats, axis=1).astype(ml_dtypes.bfloat16))


def kernel(**inputs):
    x = np.asarray(inputs["L0_attn_hidden_states"], np.float32)
    inw = np.asarray(inputs["L0_attn_in_proj_w"], np.float32)
    wup = np.asarray(inputs["L0_ffn_W_up"], np.float32)
    loraA = np.asarray(inputs["L0_ffn_lora_A"], np.float32)
    loraB = np.asarray(inputs["L0_ffn_lora_B"], np.float32)
    gatew = np.asarray(inputs["L0_ffn_gate_w"], np.float32)
    bf = ml_dtypes.bfloat16

    wcat = np.ascontiguousarray(np.concatenate(
        [inw, wup.T, loraA.reshape(32, H).T, gatew.T], axis=1), dtype=np.float32)
    lnw = np.asarray(inputs["L0_attn_ln_w"], np.float32)

    shared = {
        "wcat": wcat,
        "outwD": np.ascontiguousarray(
            np.asarray(inputs["L0_attn_out_proj_w"], np.float32)),
        "lorabD": np.ascontiguousarray(
            loraB.transpose(0, 2, 1).reshape(32, H)),
        "convwT": np.ascontiguousarray(
            np.asarray(inputs["L0_attn_conv_w"], np.float32).T),
        "mupvD": np.asarray(inputs["L0_attn_mup_vector"], np.float32)[None, :],
        "lnwC": np.ascontiguousarray(lnw.reshape(32, 128).T),
        "normwD": np.asarray(inputs["L0_attn_norm_w"], np.float32)[None, :],
        "alogD": np.asarray(inputs["L0_attn_A_log"], np.float32)[None, :],
        "dtbD": np.asarray(inputs["L0_attn_dt_bias"], np.float32)[None, :],
        "dvecD": np.asarray(inputs["L0_attn_D"], np.float32)[None, :],
        "ebiasD": np.asarray(inputs["L0_ffn_expert_bias"], np.float32)[None, :],
        "identbD": np.eye(128, dtype=bf),
        "identfD": np.eye(128, dtype=np.float32),
        "utonesD": np.ascontiguousarray(np.triu(np.ones((128, 128), np.float32))),
        "shiftD": _shift_consts(),
        "ones1D": np.ones((1, 128), np.float32),
    }

    nc = build_program()
    in_maps = []
    for c in range(8):
        b, g = divmod(c, 4)
        own = x[b, g * T:(g + 1) * T]
        prev3 = (np.zeros((3, H), np.float32) if g == 0
                 else x[b, g * T - 3:g * T])
        xseg = np.concatenate([prev3, own], axis=0)
        cm = np.zeros((128, 3), np.float32)
        gm = np.zeros((128, 3), np.float32)
        for j in range(3):
            if j < g:
                cm[:, j] = 1.0
            if 1 <= j <= g - 1:
                gm[:, j] = 1.0
        m = dict(shared)
        m["xsegT"] = np.ascontiguousarray(xseg.T)
        m["xown"] = np.ascontiguousarray(own)
        m["cmaskD"] = cm
        m["gmaskD"] = gm
        m["gmaskiD"] = 1.0 - gm
        in_maps.append(m)

    res = run_bass_kernel_spmd(nc, in_maps, core_ids=list(range(8)))
    out = np.empty((2, 4096, H), np.float32)
    for c in range(8):
        b, g = divmod(c, 4)
        out[b, g * T:(g + 1) * T] = res.results[c]["oseg"]
    return out


# version 12
# speedup vs baseline: 1.0777x; 1.0777x over previous
import contextlib
import numpy as np
import ml_dtypes
import concourse.bass as bass
import concourse.mybir as mybir
from concourse.bass_utils import run_bass_kernel_spmd
from concourse.alu_op_type import AluOpType as OP

F32 = mybir.dt.float32
BF16 = mybir.dt.bfloat16
AF = mybir.ActivationFunctionType
AX = mybir.AxisListType.X

H = 4096
DS = 4096
SN = 128
NH = 64
HD = 64
KC = 4
CONV = DS + 2 * SN
PROJ = 2 * DS + 2 * SN + NH
NCOL = PROJ + H + 32 + 4
T = 1024
NT = 8
TB = 1027
EPS = 1e-5

_cache = {}


def build_program():
    if "nc" in _cache:
        return _cache["nc"]
    nc = bass.Bass("TRN2", target_bir_lowering=False)

    xsegT = nc.dram_tensor("xsegT", [H, TB], F32, kind="ExternalInput").ap()
    xown = nc.dram_tensor("xown", [T, H], F32, kind="ExternalInput").ap()
    wcat = nc.dram_tensor("wcat", [H, NCOL], F32, kind="ExternalInput").ap()
    outwD = nc.dram_tensor("outwD", [DS, H], F32, kind="ExternalInput").ap()
    lorabD = nc.dram_tensor("lorabD", [32, H], F32, kind="ExternalInput").ap()
    convwT = nc.dram_tensor("convwT", [KC, CONV], F32, kind="ExternalInput").ap()
    mupvD = nc.dram_tensor("mupvD", [1, PROJ], F32, kind="ExternalInput").ap()
    lnwC = nc.dram_tensor("lnwC", [128, 32], F32, kind="ExternalInput").ap()
    normwD = nc.dram_tensor("normwD", [1, DS], F32, kind="ExternalInput").ap()
    alogD = nc.dram_tensor("alogD", [1, NH], F32, kind="ExternalInput").ap()
    dtbD = nc.dram_tensor("dtbD", [1, NH], F32, kind="ExternalInput").ap()
    dvecD = nc.dram_tensor("dvecD", [1, NH], F32, kind="ExternalInput").ap()
    ebiasD = nc.dram_tensor("ebiasD", [1, 4], F32, kind="ExternalInput").ap()
    identbD = nc.dram_tensor("identbD", [128, 128], BF16, kind="ExternalInput").ap()
    identfD = nc.dram_tensor("identfD", [128, 128], F32, kind="ExternalInput").ap()
    utonesD = nc.dram_tensor("utonesD", [128, 128], F32, kind="ExternalInput").ap()
    shiftD = nc.dram_tensor("shiftD", [128, 128 * 9], BF16, kind="ExternalInput").ap()
    ones1D = nc.dram_tensor("ones1D", [1, 128], F32, kind="ExternalInput").ap()
    cmaskD = nc.dram_tensor("cmaskD", [128, 3], F32, kind="ExternalInput").ap()
    gmaskD = nc.dram_tensor("gmaskD", [128, 3], F32, kind="ExternalInput").ap()
    gmaskiD = nc.dram_tensor("gmaskiD", [128, 3], F32, kind="ExternalInput").ap()

    oseg = nc.dram_tensor("oseg", [T, H], F32, kind="ExternalOutput").ap()

    convscr = nc.dram_tensor("convscr", [T, CONV], BF16, kind="Internal").ap()
    xnscr = nc.dram_tensor("xnscr", [128, 32 * T], BF16, kind="Internal").ap()
    zscr = nc.dram_tensor("zscr", [T, DS], F32, kind="Internal").ap()
    uscr = nc.dram_tensor("uscr", [T, H], F32, kind="Internal").ap()
    mupwscr = nc.dram_tensor("mupwscr", [128, CONV * KC], BF16, kind="Internal").ap()
    wscr = nc.dram_tensor("wscr", [H, NCOL], BF16, kind="Internal").ap()
    owscr = nc.dram_tensor("owscr", [DS, H], BF16, kind="Internal").ap()
    ccin = nc.dram_tensor("ccin", [128, DS + NH], F32, kind="Internal").ap()
    ccout = nc.dram_tensor("ccout", [512, DS + NH], F32, kind="Internal").ap()

    es = contextlib.ExitStack()
    sbt = lambda name, shape, dt: es.enter_context(nc.sbuf_tensor(name, shape, dt))

    sem_names = ["VS", "SS", "TS", "LDC", "LDX", "LDY", "WCV", "OCV", "WLD", "XNR", "CPL", "ZUS",
                 "XLD", "ZRL", "URL", "XRL", "CC", "GDM", "STO", "LFD", "ROW",
                 "OWL", "GTH", "PUB", "MWS", "MWL", "GS"]
    SEM = {}
    CNT = {}
    QL = {"tensor": [], "vector": [], "scalar": [], "sync": [], "gpsimd": []}

    def emit(eng, fn, wait=(), inc=()):
        wait = tuple((s, v) for (s, v) in wait if v > 0)
        inc = tuple(inc)

        def step(e, _fn=fn, _w=wait, _i=inc):
            for s, v in _w:
                e.wait_ge(SEM[s], v)
            inst = _fn(e)
            for s, a in _i:
                inst = inst.then_inc(SEM[s], a)
        QL[eng].append(step)

    def bump(inc):
        for s, a in inc:
            CNT[s] += a

    def V(fn, wait=(), inc=()):
        w = [("VS", CNT["VS"])] + list(wait)
        i = [("VS", 1)] + list(inc)
        bump(i)
        emit("vector", fn, w, i)

    def SC(fn, wait=(), inc=()):
        w = [("SS", CNT["SS"])] + list(wait)
        i = [("SS", 1)] + list(inc)
        bump(i)
        emit("scalar", fn, w, i)

    def TE(fn, wait=(), inc=()):
        bump(inc)
        emit("tensor", fn, wait, inc)

    def DMA(eng, fn, wait=(), inc=()):
        bump(inc)
        emit(eng, fn, wait, inc)

    def G(fn, wait=(), inc=()):
        w = [("GS", CNT["GS"])] + list(wait)
        i = [("GS", 1)] + list(inc)
        bump(i)
        emit("gpsimd", fn, w, i)

    def WAITQ(eng, wait):
        emit(eng, lambda e: None, wait, ())

    s_identb = sbt("s_identb", [128, 128], BF16)
    s_identf = sbt("s_identf", [128, 128], F32)
    s_ut = sbt("s_ut", [128, 128], F32)
    s_shift = sbt("s_shift", [128, 128 * 9], BF16)
    s_ones1 = sbt("s_ones1", [1, 128], F32)
    s_onescol = sbt("s_onescol", [128, 1], F32)
    s_row64 = sbt("s_row64", [1, 512], F32)
    s_lnwc = sbt("s_lnwc", [128, 32], F32)
    s_dtbrep = sbt("s_dtbrep", [128, NH], F32)
    s_arep = sbt("s_arep", [128, NH], F32)
    s_drep = sbt("s_drep", [128, NH], F32)
    s_ebrep = sbt("s_ebrep", [128, 4], F32)
    s_cmask = sbt("s_cmask", [128, 3], F32)
    s_gmask = sbt("s_gmask", [128, 3], F32)
    s_gmaski = sbt("s_gmaski", [128, 3], F32)
    s_rmsall = sbt("s_rmsall", [128, 16], F32)
    s_dtsp = sbt("s_dtsp", [128, NH * NT], F32)
    s_dtaall = sbt("s_dtaall", [128, NH * NT], F32)
    s_lam = sbt("s_lam", [128, NH * NT], F32)
    s_lamfm = sbt("s_lamfm", [64, 128 * NT], F32)
    s_ftm = sbt("s_ftm", [128, NH * NT], F32)
    s_gamrep = sbt("s_gamrep", [128, NH * NT], F32)
    s_explam = sbt("s_explam", [128, NH * NT], F32)
    s_gtot = sbt("s_gtot", [128, NH], F32)
    s_glog = sbt("s_glog", [128, 4 * NT], F32)
    s_zlora = sbt("s_zlora", [128, 32 * NT], F32)
    s_zlt = sbt("s_zlt", [32, 128 * NT], BF16)
    s_hrun = sbt("s_hrun", [128, DS], F32)
    s_small = sbt("s_small", [128, 128], F32)
    s_normw = sbt("s_normw", [128, DS], BF16)
    s_stat = sbt("s_stat", [128, 32 * TB], BF16)
    s_mupdt = sbt("s_mupdt", [128, NH], F32)
    s_mupz = sbt("s_mupz", [128, DS], BF16)
    s_cnst = sbt("s_cnst", [128, 2], F32)

    psA = es.enter_context(nc.psum_tensor("psA", [128, 2048], F32))
    psB = es.enter_context(nc.psum_tensor("psB", [128, 2048], F32))

    for n in sem_names:
        SEM[n] = es.enter_context(nc.semaphore(n))
        CNT[n] = 0

    TS = lambda: CNT["TS"]
    VSn = lambda: CNT["VS"]
    SSn = lambda: CNT["SS"]

    esA = contextlib.ExitStack()
    sbA = lambda name, shape, dt: esA.enter_context(nc.sbuf_tensor(name, shape, dt))
    s_srow = sbA("s_srow", [1, 5200], F32)
    s_cwp = sbA("s_cwp", [1, 512], F32)
    s_cwq = sbA("s_cwq", [1, 512], F32)
    s_repp = sbA("s_repp", [128, 512], BF16)

    for dst, src in [(s_identb, identbD), (s_identf, identfD), (s_ut, utonesD),
                     (s_shift, shiftD), (s_ones1, ones1D), (s_cmask, cmaskD),
                     (s_gmask, gmaskD), (s_gmaski, gmaskiD), (s_lnwc, lnwC)]:
        DMA("sync", lambda e, d=dst, s=src: e.dma_start(out=d[:], in_=s),
            (), [("LDC", 16)])
    DMA("sync", lambda e: e.dma_start(out=s_row64[0:1, 0:64], in_=dtbD), (), [("LDC", 16)])
    DMA("sync", lambda e: e.dma_start(out=s_row64[0:1, 64:128], in_=alogD), (), [("LDC", 16)])
    DMA("sync", lambda e: e.dma_start(out=s_row64[0:1, 128:192], in_=dvecD), (), [("LDC", 16)])
    DMA("sync", lambda e: e.dma_start(out=s_row64[0:1, 192:196], in_=ebiasD), (), [("LDC", 16)])
    ldc_all = CNT["LDC"]
    WAITQ("vector", [("LDC", ldc_all)])
    WAITQ("scalar", [("LDC", ldc_all)])
    WAITQ("tensor", [("LDC", ldc_all)])

    V(lambda e: e.memset(s_onescol[:], 1.0))
    V(lambda e: e.memset(s_hrun[:], 0.0))
    V(lambda e: e.memset(s_gtot[:], 1.0))
    V(lambda e: e.memset(s_cnst[:, 0:1], EPS))
    V(lambda e: e.memset(s_cnst[:, 1:2], 1.0 / DS))
    nc.const_aps.register(F32, EPS, s_cnst[:, 0:1])
    nc.const_aps.register(F32, 1.0 / DS, s_cnst[:, 1:2])

    def k1rep(dst_ap, row_ap, width, ps, poff, sc_mul=None, wt=()):
        TE(lambda e, w=width: e.matmul(ps[:, poff:poff + w], s_ones1[0:1, 0:128],
                                       row_ap, start=True, stop=True),
           wait=[("VS", VSn()), ("SS", SSn())] + list(wt), inc=[("TS", 1)])
        ts = TS()
        if sc_mul is None:
            V(lambda e, w=width: e.tensor_copy(dst_ap, ps[:, poff:poff + w]),
              wait=[("TS", ts)])
        else:
            V(lambda e, w=width: e.tensor_scalar(dst_ap, ps[:, poff:poff + w],
                                                 sc_mul, None, OP.mult),
              wait=[("TS", ts)])

    SC(lambda e: e.activation(s_row64[0:1, 256:320], s_row64[0:1, 64:128], AF.Exp))
    k1rep(s_dtbrep[:, :], s_row64[0:1, 0:64], NH, psA, 0)
    k1rep(s_arep[:, :], s_row64[0:1, 256:320], NH, psA, 512, sc_mul=-1.0)
    k1rep(s_drep[:, :], s_row64[0:1, 128:192], NH, psA, 1024)
    k1rep(s_ebrep[:, :], s_row64[0:1, 192:196], 4, psA, 1536)

    mws_g = [0]
    for k in range(KC):
        for j in range(9):
            w = 512 if j < 8 else 256
            off = j * 512
            DMA("sync", lambda e, k=k, off=off, w=w: e.dma_start(
                out=s_cwp[0:1, 0:w], in_=convwT[k:k + 1, off:off + w]),
                wait=[("VS", mws_g[0])], inc=[("LDX", 16)])
            DMA("sync", lambda e, off=off, w=w: e.dma_start(
                out=s_cwq[0:1, 0:w], in_=mupvD[0:1, DS + off:DS + off + w]),
                wait=[("VS", mws_g[0])], inc=[("LDX", 16)])
            ldx = CNT["LDX"]
            V(lambda e, w=w: e.tensor_tensor(s_cwp[0:1, 0:w], s_cwp[0:1, 0:w],
                                             s_cwq[0:1, 0:w], OP.mult),
              wait=[("LDX", ldx)])
            k1rep(s_repp[:, 0:w], s_cwp[0:1, 0:w], w, psA, 0,
                  wt=[("MWS", CNT["MWS"])])
            vs = VSn()
            DMA("sync", lambda e, k=k, off=off, w=w: e.dma_start(
                out=mupwscr[:, k * CONV + off:k * CONV + off + w],
                in_=s_repp[:, 0:w]),
                wait=[("VS", vs)], inc=[("MWS", 16)])
            mws_g[0] = VSn()
    mws_all = CNT["MWS"]

    for j in range(8):
        DMA("sync", lambda e, j=j: e.dma_start(
            out=s_cwp[0:1, 0:512], in_=mupvD[0:1, j * 512:(j + 1) * 512]),
            wait=[("VS", mws_g[0])], inc=[("LDX", 16)])
        ldx = CNT["LDX"]
        WAITQ("tensor", [("LDX", ldx)])
        k1rep(s_mupz[:, j * 512:(j + 1) * 512], s_cwp[0:1, 0:512], 512,
              psA if j % 2 == 0 else psB, 0)
        mws_g[0] = VSn()
    DMA("sync", lambda e: e.dma_start(out=s_cwp[0:1, 0:64],
                                      in_=mupvD[0:1, PROJ - NH:PROJ]),
        wait=[("VS", mws_g[0])], inc=[("LDX", 16)])
    ldx = CNT["LDX"]
    WAITQ("tensor", [("LDX", ldx)])
    k1rep(s_mupdt[:, :], s_cwp[0:1, 0:64], NH, psA, 512)
    for j in range(8):
        DMA("sync", lambda e, j=j: e.dma_start(
            out=s_cwp[0:1, 0:512], in_=normwD[0:1, j * 512:(j + 1) * 512]),
            wait=[("VS", VSn())], inc=[("LDX", 16)])
        ldx = CNT["LDX"]
        WAITQ("tensor", [("LDX", ldx)])
        k1rep(s_normw[:, j * 512:(j + 1) * 512], s_cwp[0:1, 0:512], 512,
              psA if j % 2 == 0 else psB, 1024)

    s_xf = [sbA(f"s_xf{p}", [128, TB], F32) for p in range(4)]
    s_sqf = sbA("s_sqf", [128, TB], F32)
    s_mrep = sbA("s_mrep", [128, TB], F32)
    s_rrep = sbA("s_rrep", [128, TB], F32)
    s_xtmp = sbA("s_xtmp", [128, TB], F32)
    s_xnb = sbA("s_xnb", [128, TB], BF16)
    OM, OV, OR, OU, OT = 0, 1040, 2080, 3120, 4160

    xf_guard = [0, 0, 0, 0]
    sq_bufs = [s_sqf, s_xtmp]
    tsq_end = {}
    for i in range(32):
        qs = "LDX" if i % 2 == 0 else "LDY"
        DMA("sync" if i % 2 == 0 else "scalar", lambda e, i=i: e.dma_start(
            out=s_xf[i % 4][:], in_=xsegT[128 * i:128 * (i + 1), :]),
            wait=[("TS", xf_guard[i % 4])], inc=[(qs, 16)])
        ldx = CNT[qs]
        sq = sq_bufs[i % 2]
        SC(lambda e, i=i, sq=sq: e.activation(sq[:], s_xf[i % 4][:], AF.Square),
           wait=[(qs, ldx), ("TS", tsq_end.get(i - 2, 0))])
        ssq = SSn()
        for c in range(3):
            w = 512 if c < 2 else 3
            TE(lambda e, i=i, c=c, w=w: e.matmul(
                psA[0:1, c * 512:c * 512 + w], s_onescol[:, 0:1],
                s_xf[i % 4][:, c * 512:c * 512 + w], start=(i == 0), stop=(i == 31)),
               wait=[(qs, ldx), ("VS", VSn() if i == 0 else 0)],
               inc=[("TS", 1)])
            TE(lambda e, i=i, c=c, w=w, sq=sq: e.matmul(
                psB[0:1, c * 512:c * 512 + w], s_onescol[:, 0:1],
                sq[:, c * 512:c * 512 + w], start=(i == 0), stop=(i == 31)),
               wait=[("SS", ssq)], inc=[("TS", 1)])
        xf_guard[i % 4] = TS()
        tsq_end[i] = TS()
    ts_stats = TS()
    V(lambda e: e.tensor_scalar(s_srow[0:1, OM:OM + TB], psA[0:1, 0:TB],
                                1.0 / H, None, OP.mult), wait=[("TS", ts_stats)])
    V(lambda e: e.tensor_scalar(s_srow[0:1, OV:OV + TB], psB[0:1, 0:TB],
                                1.0 / H, None, OP.mult))
    V(lambda e: e.tensor_tensor(s_srow[0:1, OT:OT + TB], s_srow[0:1, OM:OM + TB],
                                s_srow[0:1, OM:OM + TB], OP.mult))
    V(lambda e: e.tensor_sub(s_srow[0:1, OV:OV + TB], s_srow[0:1, OV:OV + TB],
                             s_srow[0:1, OT:OT + TB]))
    V(lambda e: e.tensor_scalar(s_srow[0:1, OT:OT + TB], s_srow[0:1, OV:OV + TB],
                                EPS, None, OP.add))
    V(lambda e: e.reciprocal(s_srow[0:1, OR:OR + TB], s_srow[0:1, OT:OT + TB]))
    vs_r2 = VSn()
    SC(lambda e: e.activation(s_srow[0:1, OT:OT + TB], s_srow[0:1, OR:OR + TB],
                              AF.Sqrt), wait=[("VS", vs_r2)])
    ss_r = SSn()
    V(lambda e: e.tensor_tensor(s_srow[0:1, OU:OU + TB], s_srow[0:1, OV:OV + TB],
                                s_srow[0:1, OR:OR + TB], OP.mult),
      wait=[("SS", ss_r)])
    V(lambda e: e.tensor_copy(s_srow[0:1, OR:OR + TB], s_srow[0:1, OT:OT + TB]))
    vs_u = VSn()
    SC(lambda e: e.activation(s_srow[0:1, OT:OT + TB], s_srow[0:1, OU:OU + TB],
                              AF.Sqrt, bias=EPS, scale=1.0), wait=[("VS", vs_u)])
    ss_su = SSn()
    V(lambda e: e.reciprocal(s_srow[0:1, OU:OU + TB], s_srow[0:1, OT:OT + TB]),
      wait=[("SS", ss_su)])
    for c in range(3):
        w = 512 if c < 2 else 3
        k1rep(s_mrep[:, c * 512:c * 512 + w], s_srow[0:1, OM + c * 512:OM + c * 512 + w],
              w, psA, c * 512)
        k1rep(s_rrep[:, c * 512:c * 512 + w], s_srow[0:1, OR + c * 512:OR + c * 512 + w],
              w, psB, c * 512)
    for t in range(9):
        nr = 128 if t < 8 else 3
        so = OU + (3 + 128 * t if t < 8 else 0)
        TE(lambda e, nr=nr, so=so: e.transpose(
            psA[0:nr, 1536:1537], s_srow[0:1, so:so + nr], s_identf[0:1, 0:1]),
           wait=[("VS", VSn())], inc=[("TS", 1)])
        ts = TS()
        V(lambda e, t=t, nr=nr: e.tensor_copy(s_rmsall[0:nr, t:t + 1],
                                              psA[0:nr, 1536:1537]),
          wait=[("TS", ts)])

    vs_sub = {}
    xnr_g = [0]
    for i in range(32):
        qs = "LDX" if i % 2 == 0 else "LDY"
        DMA("sync" if i % 2 == 0 else "scalar", lambda e, i=i: e.dma_start(
            out=s_xf[i % 4][:], in_=xsegT[128 * i:128 * (i + 1), :]),
            wait=[("TS", xf_guard[i % 4]), ("VS", vs_sub.get(i - 4, 0))],
            inc=[(qs, 16)])
        ldx = CNT[qs]
        V(lambda e, i=i: e.tensor_tensor(s_xtmp[:], s_xf[i % 4][:], s_mrep[:],
                                         OP.subtract), wait=[(qs, ldx)])
        vs_sub[i] = VSn()
        V(lambda e: e.tensor_tensor(s_xnb[:], s_xtmp[:], s_rrep[:], OP.mult),
          wait=[("XNR", xnr_g[0])])
        vs_xn = VSn()
        DMA("sync", lambda e, i=i: e.dma_start(
            out=xnscr[:, i * T:(i + 1) * T], in_=s_xnb[:, 3:TB]),
            wait=[("VS", vs_xn)], inc=[("XNR", 16)])
        xnr_g[0] = CNT["XNR"]
        V(lambda e, i=i: e.tensor_scalar(
            s_stat[:, i * TB:(i + 1) * TB], s_xnb[:], s_lnwc[:, i:i + 1],
            None, OP.mult))
    vs_pln = VSn()
    xnr_pln = CNT["XNR"]
    esA.close()

    esB = contextlib.ExitStack()
    sbB = lambda name, shape, dt: esB.enter_context(nc.sbuf_tensor(name, shape, dt))
    s_w = [sbB(f"s_w{p}", [128, 32 * 256], BF16) for p in range(2)]
    s_stg = [sbB(f"s_stg{p}", [128, 512], F32) for p in range(2)]
    s_xt = [[sbB(f"s_xt{k}_{p}", [128, 512], BF16) for p in range(2)]
            for k in range(KC)]
    s_cstg = [sbB(f"s_cstg{p}", [128, 512], BF16) for p in range(2)]
    s_mws = [sbB(f"s_mws{p}", [128, KC * 512], BF16) for p in range(2)]
    s_dtt = sbB("s_dtt", [128, NH], F32)
    s_dtt2 = sbB("s_dtt2", [128, NH], F32)
    s_dtt3 = sbB("s_dtt3", [128, NH], F32)
    s_p2a = sbB("s_p2a", [128, 64], F32)
    s_p2b = sbB("s_p2b", [128, 32], BF16)

    def wsrc(hbm, c0, cw):
        return hbm.rearrange("(i p) c -> p i c", p=128)[:, :, c0:c0 + cw]

    bank_free = {b: 0 for b in range(8)}
    wbuf_free = {0: 0, 1: 0}
    stg_free = {0: 0, 1: 0}
    stg_cnt = [0]
    cstg_free = {0: 0, 1: 0}
    cstg_cnt = [0]
    xt_free = {0: 0, 1: 0}
    mws_free = {0: vs_pln, 1: vs_pln}
    half_idx = [0]

    def SH(ix):
        return s_shift[:, ix * 128:(ix + 1) * 128]

    def proj_group(g0, gw, conv_order, stat_off_fn, first_wait):
        nhal = (gw + 255) // 256
        bufs = []
        wlds = []
        for hf in range(nhal):
            c0 = g0 + hf * 256
            cw = min(256, g0 + gw - c0)
            buf = half_idx[0] % 2
            half_idx[0] += 1
            bufs.append((buf, c0, cw))
            DMA("gpsimd", lambda e, c0=c0, cw=cw, buf=buf: e.dma_start(
                out=s_w[buf][:, 0:32 * 256].rearrange(
                    "p (i c) -> p i c", i=32)[:, :, 0:cw],
                in_=wsrc(wcat, c0, cw)),
                wait=[("TS", wbuf_free[buf])] + list(first_wait),
                inc=[("WLD", 16)])
            wlds.append(CNT["WLD"])
        is_conv = DS <= g0 < PROJ
        tiles = conv_order if is_conv else list(range(8))
        for t in tiles:
            nr = 128 if t < 8 else 3
            bank = t % 8
            ps = psA if bank < 4 else psB
            poff = (bank % 4) * 512
            for hj, (buf, c0, cw) in enumerate(bufs):
                sub = c0 - g0
                for i in range(32):
                    so = stat_off_fn(i, t)
                    fa = (hj == 0 and i == 0)
                    la = (hj == nhal - 1 and i == 31)
                    w = [("WLD", wlds[hj])]
                    if fa:
                        w.append(("VS", bank_free[bank]))
                    TE(lambda e, nr=nr, ps=ps, poff=poff, sub=sub, cw=cw,
                       so=so, buf=buf, i=i, fa=fa, la=la:
                       e.matmul(ps[0:nr, poff + sub:poff + sub + cw],
                                s_stat[:, so:so + nr],
                                s_w[buf][:, i * 256:i * 256 + cw],
                                start=fa, stop=la),
                       wait=w, inc=[("TS", 1)] if la else ())
            yield t, nr, ps, poff, bank, TS()
        for buf, _, _ in bufs:
            wbuf_free[buf] = TS()

    def evac_z(g0, t, ps, poff, bank, ts_mm):
        par = stg_cnt[0] % 2
        stg_cnt[0] += 1
        V(lambda e, t=t, ps=ps, poff=poff, par=par, g0=g0:
          e.scalar_tensor_tensor(s_stg[par][:, 0:512], ps[:, poff:poff + 512],
                                 s_rmsall[:, t:t + 1], s_mupz[:, g0:g0 + 512],
                                 OP.mult, OP.mult),
          wait=[("TS", ts_mm), ("ZUS", stg_free[par])])
        bank_free[bank] = VSn()
        vs = VSn()
        DMA("sync", lambda e, t=t, par=par, g0=g0: e.dma_start(
            out=zscr[128 * t:128 * (t + 1), g0:g0 + 512], in_=s_stg[par][:, 0:512]),
            wait=[("VS", vs)], inc=[("ZUS", 16)])
        stg_free[par] = CNT["ZUS"]

    def evac_conv(cc0, ccw, t, nr, ps, poff, bank, ts_mm, pos, mpar, mwl):
        par = pos % 2
        for k in range(KC):
            V(lambda e, k=k, ps=ps, poff=poff, ccw=ccw, nr=nr, t=t, par=par,
              mpar=mpar:
              e.scalar_tensor_tensor(
                  s_xt[k][par][0:nr, 0:ccw], ps[0:nr, poff:poff + ccw],
                  (s_rmsall[:, t:t + 1] if t < 8 else s_rmsall[0:3, 8:9]),
                  s_mws[mpar][0:nr, k * 512:k * 512 + ccw],
                  OP.mult, OP.mult),
              wait=([("TS", max(ts_mm, xt_free[par])), ("MWL", mwl)]
                    if k == 0 else []))
        vs_xt = VSn()
        bank_free[bank] = vs_xt
        if t == 8:
            return
        for k in range(KC):
            d = 3 - k
            lhs = s_identb[:, :] if d == 0 else SH(d - 1)
            fa = (k == 0)
            TE(lambda e, lhs=lhs, ps=ps, poff=poff, ccw=ccw, k=k, par=par,
               fa=fa: e.matmul(ps[:, poff:poff + ccw], lhs,
                               s_xt[k][par][:, 0:ccw], start=fa, stop=False),
               wait=[("VS", vs_xt)] if fa else [])
        for k in range(KC - 1):
            d = 3 - k
            lhs = SH(5 + d) if t == 0 else SH(2 + d)
            la = (k == KC - 2)
            TE(lambda e, lhs=lhs, ps=ps, poff=poff, ccw=ccw, k=k, par=par,
               la=la: e.matmul(ps[:, poff:poff + ccw], lhs,
                               s_xt[k][1 - par][:, 0:ccw], start=False, stop=la),
               wait=[], inc=[("TS", 1)] if la else ())
        ts_conv = TS()
        xt_free[1 - par] = ts_conv
        cpar = cstg_cnt[0] % 2
        cstg_cnt[0] += 1
        V(lambda e, ps=ps, poff=poff, ccw=ccw, cpar=cpar:
          e.tensor_copy(s_cstg[cpar][:, 0:ccw], ps[:, poff:poff + ccw]),
          wait=[("TS", ts_conv), ("CPL", cstg_free[cpar])])
        bank_free[bank] = VSn()
        vs = VSn()
        DMA("sync", lambda e, t=t, cc0=cc0, ccw=ccw, cpar=cpar: e.dma_start(
            out=convscr[128 * t:128 * (t + 1), cc0:cc0 + ccw],
            in_=s_cstg[cpar][:, 0:ccw]),
            wait=[("VS", vs)], inc=[("CPL", 16)])
        cstg_free[cpar] = CNT["CPL"]

    def evac_dt(t, ps, poff, ts_mm):
        V(lambda e, t=t, ps=ps, poff=poff: e.scalar_tensor_tensor(
            s_dtt[:, :], ps[:, poff + 256:poff + 320], s_rmsall[:, t:t + 1],
            s_mupdt[:, :], OP.mult, OP.mult), wait=[("TS", ts_mm)])
        V(lambda e: e.tensor_tensor(s_dtt[:, :], s_dtt[:, :], s_dtbrep[:, :],
                                    OP.add))
        vs_z = VSn()
        SC(lambda e: e.activation(s_dtt2[:, :], s_dtt[:, :], AF.Abs),
           wait=[("VS", vs_z)])
        SC(lambda e: e.activation(s_dtt3[:, :], s_dtt2[:, :], AF.Exp, scale=-1.0))
        SC(lambda e: e.activation(s_dtt2[:, :], s_dtt3[:, :], AF.Ln,
                                  bias=1.0, scale=1.0))
        SC(lambda e: e.activation(s_dtt3[:, :], s_dtt[:, :], AF.Relu))
        ss_sp = SSn()
        V(lambda e, t=t: e.tensor_add(s_dtsp[:, t * 64:(t + 1) * 64],
                                      s_dtt2[:, :], s_dtt3[:, :]),
          wait=[("SS", ss_sp)])
        V(lambda e, t=t: e.tensor_tensor(s_dtaall[:, t * 64:(t + 1) * 64],
                                         s_dtsp[:, t * 64:(t + 1) * 64],
                                         s_arep[:, :], OP.mult))

    def attn_stat_off(i, t):
        return i * TB + (3 + 128 * t if t < 8 else 0)

    conv_order = [8] + list(range(8))
    for k in range(KC):
        for p in range(2):
            V(lambda e, k=k, p=p: e.memset(s_xt[k][p][:, :], 0.0),
              wait=[("TS", TS())])
    first_wait = [("VS", vs_pln), ("MWS", mws_all)]
    mcnt = [0]
    for g in range(17):
        g0 = g * 512
        gw = 512 if g < 16 else 320
        is_conv = DS <= g0
        mpar = 0
        mwl = 0
        if is_conv:
            cc0 = g0 - DS
            ccw = min(gw, 512) if g < 16 else 256
            mpar = mcnt[0] % 2
            mcnt[0] += 1
            DMA("sync", lambda e, cc0=cc0, ccw=ccw, mpar=mpar: e.dma_start(
                out=s_mws[mpar][:, 0:KC * 512].rearrange(
                    "p (k c) -> p k c", k=KC)[:, :, 0:ccw],
                in_=mupwscr[:, :].rearrange(
                    "p (k c) -> p k c", k=KC)[:, :, cc0:cc0 + ccw]),
                wait=[("MWS", mws_all), ("VS", mws_free[mpar])],
                inc=[("MWL", 16)])
            mwl = CNT["MWL"]
        for t, nr, ps, poff, bank, ts_mm in proj_group(
                g0, gw, conv_order, attn_stat_off, first_wait):
            first_wait = []
            if not is_conv:
                evac_z(g0, t, ps, poff, bank, ts_mm)
            else:
                pos = conv_order.index(t)
                evac_conv(g0 - DS, ccw, t, nr, ps, poff, bank, ts_mm, pos,
                          mpar, mwl)
                if g == 16 and t != 8:
                    evac_dt(t, ps, poff, ts_mm)
        if is_conv:
            mws_free[mpar] = VSn()
    ts_p1a = TS()
    cpl_all = CNT["CPL"]
    zus_p1a = CNT["ZUS"]

    DMA("sync", lambda e: e.dma_start(
        out=s_stat[:, 0:32 * TB].rearrange("p (i c) -> p i c", c=TB)[:, :, 0:T],
        in_=xnscr[:, :].rearrange("p (i c) -> p i c", c=T)),
        wait=[("TS", ts_p1a), ("XNR", xnr_pln)], inc=[("XNR", 16)])
    xnr_re = CNT["XNR"]

    def ffn_stat_off(i, t):
        return i * TB + 128 * t

    def evac_u(g0, t, ps, poff, bank, ts_mm):
        par = stg_cnt[0] % 2
        stg_cnt[0] += 1
        u0 = g0 - PROJ
        V(lambda e, ps=ps, poff=poff, par=par:
          e.tensor_copy(s_stg[par][:, 0:512], ps[:, poff:poff + 512]),
          wait=[("TS", ts_mm), ("ZUS", stg_free[par])])
        bank_free[bank] = VSn()
        vs = VSn()
        DMA("sync", lambda e, t=t, par=par, u0=u0: e.dma_start(
            out=uscr[128 * t:128 * (t + 1), u0:u0 + 512],
            in_=s_stg[par][:, 0:512]),
            wait=[("VS", vs)], inc=[("ZUS", 16)])
        stg_free[par] = CNT["ZUS"]

    first_wait = [("XNR", xnr_re)]
    for g in range(8):
        g0 = PROJ + g * 512
        for t, nr, ps, poff, bank, ts_mm in proj_group(
                g0, 512, None, ffn_stat_off, first_wait):
            first_wait = []
            evac_u(g0, t, ps, poff, bank, ts_mm)
    for t, nr, ps, poff, bank, ts_mm in proj_group(
            PROJ + 4096, 36, None, ffn_stat_off, [("XNR", xnr_re)]):
        V(lambda e, ps=ps, poff=poff, t=t:
          e.tensor_copy(s_zlora[:, t * 32:(t + 1) * 32], ps[:, poff:poff + 32]),
          wait=[("TS", ts_mm)])
        V(lambda e, ps=ps, poff=poff, t=t:
          e.tensor_copy(s_glog[:, t * 4:(t + 1) * 4], ps[:, poff + 32:poff + 36]))
        bank_free[bank] = VSn()
    ts_p1b = TS()
    zus_all = CNT["ZUS"]

    for t in range(8):
        sm = s_small
        V(lambda e, t=t: e.tensor_tensor(sm[:, 0:4], s_glog[:, 4 * t:4 * t + 4],
                                         s_ebrep[:, :], OP.add))
        V(lambda e: e.reduce_max(sm[:, 4:5], sm[:, 0:4], axis=AX))
        for eix in range(4):
            V(lambda e, eix=eix: e.tensor_scalar(
                sm[:, 5 + eix:6 + eix], sm[:, eix:eix + 1], sm[:, 4:5], None,
                OP.is_ge))
        V(lambda e: e.tensor_copy(sm[:, 9:10], sm[:, 5:6]))
        V(lambda e: e.tensor_scalar(sm[:, 13:14], sm[:, 9:10], -1.0, 1.0,
                                    OP.mult, OP.add))
        V(lambda e: e.tensor_tensor(sm[:, 10:11], sm[:, 6:7], sm[:, 13:14],
                                    OP.mult))
        V(lambda e: e.tensor_scalar(sm[:, 14:15], sm[:, 10:11], -1.0, 1.0,
                                    OP.mult, OP.add))
        V(lambda e: e.tensor_tensor(sm[:, 14:15], sm[:, 14:15], sm[:, 13:14],
                                    OP.mult))
        V(lambda e: e.tensor_tensor(sm[:, 11:12], sm[:, 7:8], sm[:, 14:15],
                                    OP.mult))
        V(lambda e: e.tensor_tensor(sm[:, 15:16], sm[:, 9:10], sm[:, 10:11],
                                    OP.add))
        V(lambda e: e.tensor_tensor(sm[:, 15:16], sm[:, 15:16], sm[:, 11:12],
                                    OP.add))
        V(lambda e: e.tensor_scalar(sm[:, 12:13], sm[:, 15:16], -1.0, 1.0,
                                    OP.mult, OP.add))
        V(lambda e, t=t: e.tensor_tensor(
            s_p2a[:, 0:32].rearrange("p (a b) -> p a b", a=4),
            s_zlora[:, 32 * t:32 * (t + 1)].rearrange("p (a b) -> p a b", a=4),
            sm[:, 9:13].unsqueeze(2).broadcast_to([128, 4, 8]), OP.mult))
        V(lambda e: e.tensor_copy(s_p2b[:, 0:32], s_p2a[:, 0:32]))
        vs_zt = VSn()
        TE(lambda e: e.transpose(psA[0:32, 0:64].bitcast(BF16)[:, 0:128],
                                 s_p2b[:, 0:32], s_identb[:, :]),
           wait=[("VS", max(vs_zt, bank_free[0]))], inc=[("TS", 1)])
        ts = TS()
        V(lambda e, t=t: e.tensor_copy(s_zlt[:, 128 * t:128 * (t + 1)],
                                       psA[0:32, 0:64].bitcast(BF16)[:, 0:128]),
          wait=[("TS", ts)])
        bank_free[0] = VSn()
    vs_p2 = VSn()
    esB.close()

    esC = contextlib.ExitStack()
    sbC = lambda name, shape, dt: esC.enter_context(nc.sbuf_tensor(name, shape, dt))
    esCx = contextlib.ExitStack()
    s_xc = esCx.enter_context(nc.sbuf_tensor("s_xc", [128, CONV], BF16))
    s_xhat = esCx.enter_context(nc.sbuf_tensor("s_xhat", [128, DS], BF16))
    xc_guard = {"v": vs_p2, "t": 0, "s": 0}

    def load_conv_tile(c):
        DMA("sync", lambda e, c=c: e.dma_start(
            out=s_xc[:, :], in_=convscr[128 * c:128 * (c + 1), :]),
            wait=[("SS", xc_guard["s"]), ("VS", xc_guard["v"]),
                  ("TS", xc_guard["t"]), ("CPL", cpl_all)],
            inc=[("XLD", 16)])
        xld = CNT["XLD"]
        SC(lambda e: e.activation(s_xc[:, :], s_xc[:, :], AF.Silu),
           wait=[("XLD", xld)])
        xc_guard["s"] = SSn()
        return SSn()

    for c in range(NT):
        ss_xc = load_conv_tile(c)
        TE(lambda e, c=c: e.matmul(psA[:, 0:64], s_ut[:, :],
                                   s_dtaall[:, c * 64:(c + 1) * 64],
                                   start=True, stop=True),
           wait=[("VS", bank_free[0])], inc=[("TS", 1)])
        ts_lam = TS()
        V(lambda e, c=c: e.tensor_copy(s_lam[:, c * 64:(c + 1) * 64],
                                       psA[:, 0:64]), wait=[("TS", ts_lam)])
        vs_lam = VSn()
        TE(lambda e, c=c: e.transpose(psA[0:64, 512:640],
                                      s_lam[:, c * 64:(c + 1) * 64],
                                      s_identf[:, :]),
           wait=[("VS", vs_lam)], inc=[("TS", 1)])
        ts_lfm = TS()
        V(lambda e, c=c: e.tensor_copy(s_lamfm[:, c * 128:(c + 1) * 128],
                                       psA[0:64, 512:640]), wait=[("TS", ts_lfm)])
        DMA("sync", lambda e, c=c: e.dma_start(
            out=s_row64[0:1, 320:384], in_=s_lam[127:128, c * 64:(c + 1) * 64]),
            wait=[("VS", vs_lam), ("SS", SSn()), ("TS", TS())],
            inc=[("ROW", 16)])
        roww = CNT["ROW"]
        SC(lambda e: e.activation(s_row64[0:1, 384:448], s_row64[0:1, 320:384],
                                  AF.Exp), wait=[("ROW", roww)])
        ss_gam = SSn()
        TE(lambda e: e.matmul(psA[:, 1024:1088], s_ones1[0:1, 0:128],
                              s_row64[0:1, 320:384], start=True, stop=True),
           wait=[("ROW", roww), ("VS", VSn())], inc=[("TS", 1)])
        ts_lend = TS()
        TE(lambda e: e.matmul(psA[:, 1536:1600], s_ones1[0:1, 0:128],
                              s_row64[0:1, 384:448], start=True, stop=True),
           wait=[("SS", ss_gam)], inc=[("TS", 1)])
        ts_gam = TS()
        V(lambda e, c=c: e.tensor_copy(s_gamrep[:, c * 64:(c + 1) * 64],
                                       psA[:, 1536:1600]), wait=[("TS", ts_gam)])
        V(lambda e, c=c: e.tensor_tensor(s_small[:, 0:64],
                                         psA[:, 1024:1088],
                                         s_lam[:, c * 64:(c + 1) * 64],
                                         OP.subtract),
          wait=[("TS", ts_lend)])
        vs_f1 = VSn()
        SC(lambda e: e.activation(s_small[:, 64:128], s_small[:, 0:64], AF.Exp),
           wait=[("VS", vs_f1)])
        SC(lambda e, c=c: e.activation(s_explam[:, c * 64:(c + 1) * 64],
                                       s_lam[:, c * 64:(c + 1) * 64], AF.Exp))
        ss_f = SSn()
        V(lambda e, c=c: e.tensor_tensor(s_ftm[:, c * 64:(c + 1) * 64],
                                         s_small[:, 64:128],
                                         s_dtsp[:, c * 64:(c + 1) * 64], OP.mult),
          wait=[("SS", ss_f)])
        V(lambda e, c=c: e.tensor_tensor(s_gtot[:, :], s_gtot[:, :],
                                         s_gamrep[:, c * 64:(c + 1) * 64],
                                         OP.mult))
        V(lambda e, c=c: e.tensor_tensor(
            s_xhat[:, :].rearrange("p (a b) -> p a b", a=64),
            s_xc[:, 0:DS].rearrange("p (a b) -> p a b", a=64),
            s_ftm[:, c * 64:(c + 1) * 64].unsqueeze(2).broadcast_to([128, 64, 64]),
            OP.mult), wait=[("SS", ss_xc)])
        vs_xhat = VSn()
        for j in range(8):
            ps = psA if j < 4 else psB
            poff = (j % 4) * 512
            TE(lambda e, ps=ps, poff=poff, j=j:
               e.matmul(ps[:, poff:poff + 512], s_xc[:, DS:DS + 128],
                        s_xhat[:, j * 512:(j + 1) * 512], start=True, stop=True),
               wait=[("VS", vs_xhat)] if j == 0 else [],
               inc=[("TS", 1)] if j == 7 else ())
        ts_state = TS()
        V(lambda e, c=c: e.tensor_tensor(
            s_hrun[:, :].rearrange("p (a b) -> p a b", a=64),
            s_hrun[:, :].rearrange("p (a b) -> p a b", a=64),
            s_gamrep[:, c * 64:(c + 1) * 64].unsqueeze(2).broadcast_to([128, 64, 64]),
            OP.mult), wait=[("TS", ts_state)])
        V(lambda e: e.tensor_tensor(s_hrun[:, 0:2048], s_hrun[:, 0:2048],
                                    psA[:, :], OP.add))
        V(lambda e: e.tensor_tensor(s_hrun[:, 2048:4096], s_hrun[:, 2048:4096],
                                    psB[:, :], OP.add))
        for b in range(8):
            bank_free[b] = VSn()
        xc_guard["v"] = VSn()
        xc_guard["t"] = ts_state

    esC1 = contextlib.ExitStack()
    sbC1 = lambda name, shape, dt: esC1.enter_context(nc.sbuf_tensor(name, shape, dt))
    s_gath = sbC1("s_gath", [128, DS + NH], F32)
    s_prod = sbC1("s_prod", [128, NH], F32)
    s_wj = sbC1("s_wj", [128, NH], F32)
    s_hw = sbC1("s_hw", [128, DS], F32)
    vs_chain = VSn()
    DMA("sync", lambda e: e.dma_start(out=ccin[:, 0:DS], in_=s_hrun[:, :]),
        wait=[("VS", vs_chain)], inc=[("PUB", 16)])
    DMA("sync", lambda e: e.dma_start(out=ccin[:, DS:DS + NH], in_=s_gtot[:, :]),
        wait=[("VS", vs_chain)], inc=[("PUB", 16)])
    DMA("gpsimd", lambda e: e.collective_compute(
        "AllGather", OP.bypass, replica_groups=[[0, 1, 2, 3], [4, 5, 6, 7]],
        ins=[ccin], outs=[ccout]),
        wait=[("PUB", CNT["PUB"]), ("GDM", CNT["GDM"])], inc=[("CC", 1)])
    cc_done = CNT["CC"]
    for j in range(4):
        DMA("gpsimd", lambda e, j=j: e.dma_start(
            out=owscr[:, j * 1024:(j + 1) * 1024],
            in_=outwD[:, j * 1024:(j + 1) * 1024]),
            (), [("OCV", 16)])
    ocv_all = CNT["OCV"]
    V(lambda e: e.memset(s_hrun[:, :], 0.0), wait=[("CC", cc_done)])
    V(lambda e: e.memset(s_prod[:, :], 1.0))
    for j in [2, 1, 0]:
        vs_gprev = VSn()
        DMA("sync", lambda e, j=j: e.dma_start(
            out=s_gath[:, :], in_=ccout[128 * j:128 * (j + 1), :]),
            wait=[("CC", cc_done), ("VS", vs_gprev), ("GTH", CNT["GTH"])],
            inc=[("GTH", 16)])
        gth = CNT["GTH"]
        V(lambda e, j=j: e.tensor_scalar(s_wj[:, :], s_prod[:, :],
                                         s_cmask[:, j:j + 1], None, OP.mult))
        V(lambda e, j=j: e.tensor_scalar(s_small[:, 0:64],
                                         s_gath[:, DS:DS + NH],
                                         s_gmask[:, j:j + 1],
                                         s_gmaski[:, j:j + 1], OP.mult, OP.add),
          wait=[("GTH", gth)])
        V(lambda e: e.tensor_tensor(s_prod[:, :], s_prod[:, :], s_small[:, 0:64],
                                    OP.mult))
        V(lambda e, j=j: e.tensor_tensor(
            s_hw[:, :].rearrange("p (a b) -> p a b", a=64),
            s_gath[:, 0:DS].rearrange("p (a b) -> p a b", a=64),
            s_wj[:, :].unsqueeze(2).broadcast_to([128, 64, 64]), OP.mult))
        V(lambda e: e.tensor_tensor(s_hrun[:, :], s_hrun[:, :], s_hw[:, :],
                                    OP.add))
    vs_hinit = VSn()
    esC1.close()

    esC2 = contextlib.ExitStack()
    sbC2 = lambda name, shape, dt: esC2.enter_context(nc.sbuf_tensor(name, shape, dt))
    s_E = sbC2("s_E", [128, 2048], F32)
    s_L = sbC2("s_L", [128, 4096], BF16)
    s_y = sbC2("s_y", [128, DS], F32)
    s_zrl = sbC2("s_zrl", [128, 2048], F32)
    s_hcast = sbC2("s_hcast", [128, 2048], BF16)
    s_bfm = sbC2("s_bfm", [128, 128], BF16)
    s_cfm = sbC2("s_cfm", [128, 128], BF16)
    s_gbf = sbC2("s_gbf", [128, 128], BF16)
    s_lamq = sbC2("s_lamq", [1, 2048], F32)
    s_dtb16 = sbC2("s_dtb16", [128, 32], BF16)
    zrl_free = {0: vs_hinit, 1: vs_hinit}
    gA = [max(bank_free.values())]
    gB = [gA[0]]

    for c in range(NT):
        ss_xc = load_conv_tile(c)
        DMA("sync", lambda e, c=c: e.dma_start(
            out=s_zrl[:, :],
            in_=zscr[128 * c:128 * (c + 1), 0:2048]),
            wait=[("VS", zrl_free[0]), ("ZUS", zus_p1a)], inc=[("ZRL", 16)])
        zrl0 = CNT["ZRL"]
        TE(lambda e: e.transpose(psA[:, 0:64].bitcast(BF16)[:, 0:128],
                                 s_xc[:, DS:DS + 128], s_identb[:, :]),
           wait=[("SS", ss_xc), ("VS", gA[0])], inc=[("TS", 1)])
        ts_b = TS()
        TE(lambda e: e.transpose(psA[:, 512:576].bitcast(BF16)[:, 0:128],
                                 s_xc[:, DS + 128:DS + 256], s_identb[:, :]),
           inc=[("TS", 1)])
        ts_c = TS()
        V(lambda e: e.tensor_copy(s_bfm[:, :], psA[:, 0:64].bitcast(BF16)[:, 0:128]),
          wait=[("TS", ts_b)])
        V(lambda e: e.tensor_copy(s_cfm[:, :], psA[:, 512:576].bitcast(BF16)[:, 0:128]),
          wait=[("TS", ts_c)])
        vs_bc = VSn()
        TE(lambda e: e.matmul(psA[:, 1024:1152], s_bfm[:, :], s_cfm[:, :],
                              start=True, stop=True),
           wait=[("VS", vs_bc)], inc=[("TS", 1)])
        ts_g = TS()
        V(lambda e: e.tensor_tensor(s_gbf[:, :], psA[:, 1024:1152], s_ut[:, :],
                                    OP.mult), wait=[("TS", ts_g)])
        gA[0] = VSn()
        for half in range(2):
            for qq in range(2):
                q = 2 * half + qq
                ps = psA if qq == 0 else psB
                DMA("sync", lambda e, c=c, q=q: e.dma_start(
                    out=s_lamq[0:1, :],
                    in_=s_lamfm[16 * q:16 * (q + 1), c * 128:(c + 1) * 128]),
                    wait=[("LFD", CNT["LFD"]), ("VS", vs_hinit),
                          ("TS", TS())], inc=[("LFD", 16)])
                lfd = CNT["LFD"]
                for j in range(4):
                    TE(lambda e, ps=ps, j=j: e.matmul(
                        ps[:, j * 512:(j + 1) * 512], s_ones1[0:1, 0:128],
                        s_lamq[0:1, j * 512:(j + 1) * 512],
                        start=True, stop=True),
                       wait=[("LFD", lfd),
                             ("VS", gA[0] if qq == 0 else gB[0])]
                       if j == 0 else [],
                       inc=[("TS", 1)] if j == 3 else ())
                ts_q = TS()
                V(lambda e, ps=ps, q=q, c=c: e.tensor_tensor(
                    s_E[:, :].rearrange("p (a b) -> p a b", a=16),
                    ps[:, :].rearrange("p (a b) -> p a b", a=16),
                    s_lam[:, c * 64 + q * 16:c * 64 + (q + 1) * 16]
                    .unsqueeze(2).broadcast_to([128, 16, 128]), OP.subtract),
                  wait=[("TS", ts_q)])
                V(lambda e: e.tensor_scalar(s_E[:, :], s_E[:, :], 0.0, None,
                                            OP.min))
                gs_e = 0
                vs_e = VSn()
                if qq == 0:
                    gA[0] = vs_e
                else:
                    gB[0] = vs_e
                SC(lambda e, qq=qq: e.activation(
                    s_L[:, qq * 2048:(qq + 1) * 2048], s_E[:, :], AF.Exp),
                   wait=[("VS", vs_e)])
            ss_l = SSn()
            V(lambda e, half=half, c=c: e.tensor_copy(
                s_dtb16[:, :], s_dtsp[:, c * 64 + half * 32:c * 64 + (half + 1) * 32]))
            V(lambda e: e.tensor_tensor(
                s_L[:, :].rearrange("p (a b) -> p a b", a=32),
                s_L[:, :].rearrange("p (a b) -> p a b", a=32),
                s_gbf[:, :].unsqueeze(1).broadcast_to([128, 32, 128]), OP.mult),
              wait=[("SS", ss_l)])
            V(lambda e: e.tensor_tensor(
                s_L[:, :].rearrange("p (a b) -> p a b", a=32),
                s_L[:, :].rearrange("p (a b) -> p a b", a=32),
                s_dtb16[:, :].unsqueeze(2).broadcast_to([128, 32, 128]), OP.mult))
            vs_stile = VSn()
            V(lambda e, half=half: e.tensor_copy(
                s_hcast[:, :], s_hrun[:, half * 2048:(half + 1) * 2048]))
            vs_hc = VSn()
            for j in range(4):
                TE(lambda e, j=j: e.matmul(
                    psA[:, j * 512:(j + 1) * 512], s_cfm[:, :],
                    s_hcast[:, j * 512:(j + 1) * 512], start=True, stop=True),
                   wait=[("VS", max(vs_hc, gA[0]))] if j == 0 else [],
                   inc=[("TS", 1)] if j == 3 else ())
            ts_inter = TS()
            for hh in range(32):
                h = half * 32 + hh
                TE(lambda e, hh=hh, h=h, first=(hh % 8 == 0), last=(hh % 8 == 7):
                   e.matmul(psB[:, hh * 64:(hh + 1) * 64],
                            s_L[:, hh * 128:(hh + 1) * 128],
                            s_xc[:, h * 64:(h + 1) * 64],
                            start=first, stop=last),
                   wait=[("VS", max(vs_stile, gB[0]))] if hh == 0 else [],
                   inc=[("TS", 1)] if hh == 31 else ())
            ts_intra = TS()
            V(lambda e, half=half, c=c: e.tensor_tensor(
                s_y[:, half * 2048:(half + 1) * 2048].rearrange(
                    "p (a b) -> p a b", a=32),
                psA[:, :].rearrange("p (a b) -> p a b", a=32),
                s_explam[:, c * 64 + half * 32:c * 64 + (half + 1) * 32]
                .unsqueeze(2).broadcast_to([128, 32, 64]), OP.mult),
              wait=[("TS", ts_inter)])
            V(lambda e, half=half: e.tensor_tensor(
                s_y[:, half * 2048:(half + 1) * 2048],
                s_y[:, half * 2048:(half + 1) * 2048], psB[:, :], OP.add),
              wait=[("TS", ts_intra)])
            gA[0] = VSn()
            gB[0] = VSn()
            V(lambda e, half=half, c=c: e.tensor_tensor(
                s_hcast[:, :].rearrange("p (a b) -> p a b", a=32),
                s_xc[:, half * 2048:(half + 1) * 2048].rearrange(
                    "p (a b) -> p a b", a=32),
                s_drep[:, half * 32:(half + 1) * 32].unsqueeze(2)
                .broadcast_to([128, 32, 64]), OP.mult))
            V(lambda e, half=half: e.tensor_tensor(
                s_y[:, half * 2048:(half + 1) * 2048],
                s_y[:, half * 2048:(half + 1) * 2048], s_hcast[:, :], OP.add))
        V(lambda e, c=c: e.tensor_tensor(
            s_xhat[:, :].rearrange("p (a b) -> p a b", a=64),
            s_xc[:, 0:DS].rearrange("p (a b) -> p a b", a=64),
            s_ftm[:, c * 64:(c + 1) * 64].unsqueeze(2).broadcast_to([128, 64, 64]),
            OP.mult))
        vs_xhat = VSn()
        for j in range(8):
            ps = psA if j < 4 else psB
            poff = (j % 4) * 512
            TE(lambda e, ps=ps, poff=poff, j=j:
               e.matmul(ps[:, poff:poff + 512], s_xc[:, DS:DS + 128],
                        s_xhat[:, j * 512:(j + 1) * 512], start=True, stop=True),
               wait=[("VS", max(vs_xhat, gA[0], gB[0]))] if j == 0 else [],
               inc=[("TS", 1)] if j == 7 else ())
        ts_state = TS()
        V(lambda e, c=c: e.tensor_tensor(
            s_hrun[:, :].rearrange("p (a b) -> p a b", a=64),
            s_hrun[:, :].rearrange("p (a b) -> p a b", a=64),
            s_gamrep[:, c * 64:(c + 1) * 64].unsqueeze(2).broadcast_to([128, 64, 64]),
            OP.mult), wait=[("TS", ts_state)])
        V(lambda e: e.tensor_tensor(s_hrun[:, 0:2048], s_hrun[:, 0:2048],
                                    psA[:, :], OP.add))
        V(lambda e: e.tensor_tensor(s_hrun[:, 2048:4096], s_hrun[:, 2048:4096],
                                    psB[:, :], OP.add))
        vs_hupd = VSn()
        SC(lambda e: e.activation(s_zrl[:, :], s_zrl[:, :], AF.Silu),
           wait=[("ZRL", zrl0)])
        ss_silu = SSn()
        V(lambda e: e.tensor_tensor(s_y[:, 0:2048], s_y[:, 0:2048],
                                    s_zrl[:, :], OP.mult),
          wait=[("SS", ss_silu)])
        vs_z0 = VSn()
        DMA("sync", lambda e, c=c: e.dma_start(
            out=s_zrl[:, :], in_=zscr[128 * c:128 * (c + 1), 2048:4096]),
            wait=[("VS", vs_z0), ("SS", SSn())], inc=[("ZRL", 16)])
        zrl1 = CNT["ZRL"]
        SC(lambda e: e.activation(s_zrl[:, :], s_zrl[:, :], AF.Silu),
           wait=[("ZRL", zrl1)])
        ss_silu2 = SSn()
        V(lambda e: e.tensor_tensor(s_y[:, 2048:4096], s_y[:, 2048:4096],
                                    s_zrl[:, :], OP.mult),
          wait=[("SS", ss_silu2)])
        zrl_free[0] = VSn()
        vs_gt = VSn()
        SC(lambda e: e.activation(s_E[:, :], s_y[:, 0:2048], AF.Square),
           wait=[("VS", vs_gt)])
        ss_sq1 = SSn()
        V(lambda e: e.reduce_sum(s_small[:, 100:101], s_E[:, :], axis=AX),
          wait=[("SS", ss_sq1)])
        vs_rs1 = VSn()
        SC(lambda e: e.activation(s_E[:, :], s_y[:, 2048:4096], AF.Square),
           wait=[("VS", vs_rs1)])
        ss_sq2 = SSn()
        V(lambda e: e.reduce_sum(s_small[:, 101:102], s_E[:, :], axis=AX),
          wait=[("SS", ss_sq2)])
        V(lambda e: e.tensor_tensor(s_small[:, 102:103], s_small[:, 100:101],
                                    s_small[:, 101:102], OP.add))
        vs_ss = VSn()
        SC(lambda e: e.activation(s_small[:, 103:104], s_small[:, 102:103],
                                  AF.Sqrt, bias=EPS, scale=1.0 / DS),
           wait=[("VS", vs_ss)])
        ss_sr = SSn()
        V(lambda e: e.reciprocal(s_small[:, 104:105], s_small[:, 103:104]),
          wait=[("SS", ss_sr)])
        V(lambda e: e.scalar_tensor_tensor(s_y[:, :], s_y[:, :],
                                           s_small[:, 104:105], s_normw[:, :],
                                           OP.mult, OP.mult))
        vs_yh = VSn()
        tb_free = {b: ("VS", vs_hupd) for b in range(8)}
        for j in range(32):
            bank = j % 8
            ps = psA if bank < 4 else psB
            poff = (bank % 4) * 512
            gsem, gval = tb_free[bank]
            TE(lambda e, j=j, ps=ps, poff=poff: e.transpose(
                ps[:, poff:poff + 128], s_y[:, j * 128:(j + 1) * 128],
                s_identf[:, :]),
               wait=[("VS", vs_yh), (gsem, gval)], inc=[("TS", 1)])
            ts = TS()
            SC(lambda e, j=j, c=c, ps=ps, poff=poff: e.copy(
                s_stat[:, j * TB + 128 * c:j * TB + 128 * (c + 1)],
                ps[:, poff:poff + 128]),
               wait=[("TS", ts)])
            tb_free[bank] = ("SS", SSn())
        V(lambda e: e.engine_nop(), wait=[("SS", SSn())])
        gA[0] = VSn()
        gB[0] = VSn()
        xc_guard["v"] = VSn()
        xc_guard["s"] = SSn()
        xc_guard["t"] = ts_state
    vs_p5 = VSn()
    ss_p5 = SSn()
    ts_p5 = TS()
    esC2.close()
    esCx.close()

    esC3 = contextlib.ExitStack()
    sbC3 = lambda name, shape, dt: esC3.enter_context(nc.sbuf_tensor(name, shape, dt))
    s_lorab = sbC3("s_lorab", [32, H], BF16)
    s_ow = [sbC3(f"s_ow{p}", [128, 32 * 256], BF16) for p in range(3)]
    s_url = [sbC3(f"s_url{p}", [128, 512], F32) for p in range(2)]
    s_xrl = [sbC3(f"s_xrl{p}", [128, 512], F32) for p in range(2)]
    s_ost = [sbC3(f"s_ost{p}", [128, 512], F32) for p in range(2)]
    DMA("gpsimd", lambda e: e.dma_start(out=s_lorab[:], in_=lorabD),
        wait=[("VS", vs_p5), ("SS", ss_p5)], inc=[("GDM", 16)])
    gdm_lb = CNT["GDM"]
    ow_free = {0: 0, 1: 0, 2: 0}
    url_free = {0: vs_p5, 1: vs_p5}
    xrl_free = {0: vs_p5, 1: vs_p5}
    ost_free = {0: 0, 1: 0}
    obank_free = {b: vs_p5 for b in range(8)}

    for og in range(8):
        owls = []
        obufs = []
        for hf in range(2):
            buf = (2 * og + hf) % 3
            obufs.append(buf)
            DMA("sync", lambda e, og=og, hf=hf, buf=buf: e.dma_start(
                out=s_ow[buf][:, 0:32 * 256].rearrange("p (i c) -> p i c", i=32),
                in_=wsrc(owscr, og * 512 + hf * 256, 256)),
                wait=[("TS", ow_free[buf]), ("VS", vs_p5), ("OCV", ocv_all)],
                inc=[("OWL", 16)])
            owls.append(CNT["OWL"])
        for t in range(8):
            rpar = t % 2
            DMA("sync", lambda e, og=og, t=t, rpar=rpar: e.dma_start(
                out=s_url[rpar][:, :],
                in_=uscr[128 * t:128 * (t + 1), og * 512:(og + 1) * 512]),
                wait=[("VS", url_free[rpar]), ("ZUS", zus_all)], inc=[("URL", 16)])
            DMA("sync", lambda e, og=og, t=t, rpar=rpar: e.dma_start(
                out=s_xrl[rpar][:, :],
                in_=xown[128 * t:128 * (t + 1), og * 512:(og + 1) * 512]),
                wait=[("VS", xrl_free[rpar])], inc=[("XRL", 16)])
            url = CNT["URL"]
            xrl = CNT["XRL"]
            bank = t % 8
            ps = psA if bank < 4 else psB
            poff = (bank % 4) * 512
            for hf in range(2):
                for i in range(32):
                    fa = (hf == 0 and i == 0)
                    w = []
                    if i == 0:
                        w.append(("OWL", owls[hf]))
                    if fa:
                        w += [("VS", obank_free[bank]), ("TS", ts_p5),
                              ("SS", ss_p5)]
                    TE(lambda e, ps=ps, poff=poff, i=i, t=t, hf=hf,
                       buf=obufs[hf], fa=fa:
                       e.matmul(ps[:, poff + hf * 256:poff + hf * 256 + 256],
                                s_stat[:, i * TB + 128 * t:i * TB + 128 * t + 128],
                                s_ow[buf][:, i * 256:(i + 1) * 256],
                                start=fa, stop=False),
                       wait=w)
            for hf in range(2):
                TE(lambda e, ps=ps, poff=poff, t=t, og=og, hf=hf: e.matmul(
                    ps[:, poff + hf * 256:poff + hf * 256 + 256],
                    s_zlt[:, 128 * t:128 * (t + 1)],
                    s_lorab[:, og * 512 + hf * 256:og * 512 + hf * 256 + 256],
                    start=False, stop=True),
                   wait=[("GDM", gdm_lb), ("VS", vs_p2)] if hf == 0 else [],
                   inc=[("TS", 1)] if hf == 1 else ())
            ts_o = TS()
            opar = (og * 8 + t) % 2
            V(lambda e, ps=ps, poff=poff, opar=opar, rpar=rpar:
              e.tensor_tensor(s_ost[opar][:, :], ps[:, poff:poff + 512],
                              s_url[rpar][:, :], OP.add),
              wait=[("TS", ts_o), ("URL", url), ("STO", ost_free[opar])])
            obank_free[bank] = VSn()
            url_free[rpar] = VSn()
            V(lambda e, opar=opar, rpar=rpar: e.tensor_tensor(
                s_ost[opar][:, :], s_ost[opar][:, :], s_xrl[rpar][:, :], OP.add),
              wait=[("XRL", xrl)])
            xrl_free[rpar] = VSn()
            vs_o = VSn()
            DMA("sync", lambda e, og=og, t=t, opar=opar: e.dma_start(
                out=oseg[128 * t:128 * (t + 1), og * 512:(og + 1) * 512],
                in_=s_ost[opar][:, :]),
                wait=[("VS", vs_o)], inc=[("STO", 16)])
            ost_free[opar] = CNT["STO"]
        for _b in obufs:
            ow_free[_b] = TS()

    WAITQ("gpsimd", [("STO", CNT["STO"])])
    esC3.close()
    esC.close()

    with nc.Block() as block:
        @block.sync
        def _(sync):
            for f in QL["sync"]:
                f(sync)

        @block.gpsimd
        def _(g):
            for f in QL["gpsimd"]:
                f(g)

        @block.tensor
        def _(t):
            for f in QL["tensor"]:
                f(t)

        @block.vector
        def _(v):
            for f in QL["vector"]:
                f(v)

        @block.scalar
        def _(s):
            for f in QL["scalar"]:
                f(s)

    es.close()
    _cache["nc"] = nc
    return nc


def _shift_consts():
    mats = []
    for d in (1, 2, 3):
        mats.append(np.eye(128, k=d))
    for d in (1, 2, 3):
        mats.append(np.eye(128, k=d - 128))
    for d in (1, 2, 3):
        m = np.zeros((128, 128))
        for t in range(d):
            m[3 + t - d, t] = 1.0
        mats.append(m)
    return np.ascontiguousarray(
        np.concatenate(mats, axis=1).astype(ml_dtypes.bfloat16))


def kernel(**inputs):
    x = np.asarray(inputs["L0_attn_hidden_states"], np.float32)
    inw = np.asarray(inputs["L0_attn_in_proj_w"], np.float32)
    wup = np.asarray(inputs["L0_ffn_W_up"], np.float32)
    loraA = np.asarray(inputs["L0_ffn_lora_A"], np.float32)
    loraB = np.asarray(inputs["L0_ffn_lora_B"], np.float32)
    gatew = np.asarray(inputs["L0_ffn_gate_w"], np.float32)
    bf = ml_dtypes.bfloat16

    wcat = np.ascontiguousarray(np.concatenate(
        [inw, wup.T, loraA.reshape(32, H).T, gatew.T], axis=1), dtype=np.float32)
    lnw = np.asarray(inputs["L0_attn_ln_w"], np.float32)

    shared = {
        "wcat": wcat,
        "outwD": np.ascontiguousarray(
            np.asarray(inputs["L0_attn_out_proj_w"], np.float32)),
        "lorabD": np.ascontiguousarray(
            loraB.transpose(0, 2, 1).reshape(32, H)),
        "convwT": np.ascontiguousarray(
            np.asarray(inputs["L0_attn_conv_w"], np.float32).T),
        "mupvD": np.asarray(inputs["L0_attn_mup_vector"], np.float32)[None, :],
        "lnwC": np.ascontiguousarray(lnw.reshape(32, 128).T),
        "normwD": np.asarray(inputs["L0_attn_norm_w"], np.float32)[None, :],
        "alogD": np.asarray(inputs["L0_attn_A_log"], np.float32)[None, :],
        "dtbD": np.asarray(inputs["L0_attn_dt_bias"], np.float32)[None, :],
        "dvecD": np.asarray(inputs["L0_attn_D"], np.float32)[None, :],
        "ebiasD": np.asarray(inputs["L0_ffn_expert_bias"], np.float32)[None, :],
        "identbD": np.eye(128, dtype=bf),
        "identfD": np.eye(128, dtype=np.float32),
        "utonesD": np.ascontiguousarray(np.triu(np.ones((128, 128), np.float32))),
        "shiftD": _shift_consts(),
        "ones1D": np.ones((1, 128), np.float32),
    }

    nc = build_program()
    in_maps = []
    for c in range(8):
        b, g = divmod(c, 4)
        own = x[b, g * T:(g + 1) * T]
        prev3 = (np.zeros((3, H), np.float32) if g == 0
                 else x[b, g * T - 3:g * T])
        xseg = np.concatenate([prev3, own], axis=0)
        cm = np.zeros((128, 3), np.float32)
        gm = np.zeros((128, 3), np.float32)
        for j in range(3):
            if j < g:
                cm[:, j] = 1.0
            if 1 <= j <= g - 1:
                gm[:, j] = 1.0
        m = dict(shared)
        m["xsegT"] = np.ascontiguousarray(xseg.T)
        m["xown"] = np.ascontiguousarray(own)
        m["cmaskD"] = cm
        m["gmaskD"] = gm
        m["gmaskiD"] = 1.0 - gm
        in_maps.append(m)

    res = run_bass_kernel_spmd(nc, in_maps, core_ids=list(range(8)))
    out = np.empty((2, 4096, H), np.float32)
    for c in range(8):
        b, g = divmod(c, 4)
        out[b, g * T:(g + 1) * T] = res.results[c]["oseg"]
    return out


# version 15
# speedup vs baseline: 1.2397x; 1.1504x over previous
import contextlib
import numpy as np
import ml_dtypes
import concourse.bass as bass
import concourse.mybir as mybir
from concourse.bass_utils import run_bass_kernel_spmd
from concourse.alu_op_type import AluOpType as OP

F32 = mybir.dt.float32
BF16 = mybir.dt.bfloat16
AF = mybir.ActivationFunctionType
AX = mybir.AxisListType.X

H = 4096
DS = 4096
SN = 128
NH = 64
HD = 64
KC = 4
CONV = DS + 2 * SN
PROJ = 2 * DS + 2 * SN + NH
NCOL = PROJ + H + 32 + 4
T = 1024
NT = 8
TB = 1027
EPS = 1e-5

_cache = {}


def build_program():
    if "nc" in _cache:
        return _cache["nc"]
    nc = bass.Bass("TRN2", target_bir_lowering=False)

    xsegT = nc.dram_tensor("xsegT", [H, TB], F32, kind="ExternalInput").ap()
    xown = nc.dram_tensor("xown", [T, H], F32, kind="ExternalInput").ap()
    wcat = nc.dram_tensor("wcat", [H, NCOL], F32, kind="ExternalInput").ap()
    outwD = nc.dram_tensor("outwD", [DS, H], F32, kind="ExternalInput").ap()
    lorabD = nc.dram_tensor("lorabD", [32, H], F32, kind="ExternalInput").ap()
    convwT = nc.dram_tensor("convwT", [KC, CONV], F32, kind="ExternalInput").ap()
    mupvD = nc.dram_tensor("mupvD", [1, PROJ], F32, kind="ExternalInput").ap()
    lnwC = nc.dram_tensor("lnwC", [128, 32], F32, kind="ExternalInput").ap()
    normwD = nc.dram_tensor("normwD", [1, DS], F32, kind="ExternalInput").ap()
    alogD = nc.dram_tensor("alogD", [1, NH], F32, kind="ExternalInput").ap()
    dtbD = nc.dram_tensor("dtbD", [1, NH], F32, kind="ExternalInput").ap()
    dvecD = nc.dram_tensor("dvecD", [1, NH], F32, kind="ExternalInput").ap()
    ebiasD = nc.dram_tensor("ebiasD", [1, 4], F32, kind="ExternalInput").ap()
    identbD = nc.dram_tensor("identbD", [128, 128], BF16, kind="ExternalInput").ap()
    identfD = nc.dram_tensor("identfD", [128, 128], F32, kind="ExternalInput").ap()
    utonesD = nc.dram_tensor("utonesD", [128, 128], F32, kind="ExternalInput").ap()
    shiftD = nc.dram_tensor("shiftD", [128, 128 * 9], BF16, kind="ExternalInput").ap()
    ones1D = nc.dram_tensor("ones1D", [1, 128], F32, kind="ExternalInput").ap()
    cmaskD = nc.dram_tensor("cmaskD", [128, 3], F32, kind="ExternalInput").ap()
    gmaskD = nc.dram_tensor("gmaskD", [128, 3], F32, kind="ExternalInput").ap()
    gmaskiD = nc.dram_tensor("gmaskiD", [128, 3], F32, kind="ExternalInput").ap()

    oseg = nc.dram_tensor("oseg", [T, H], F32, kind="ExternalOutput").ap()

    convscr = nc.dram_tensor("convscr", [T, CONV], BF16, kind="Internal").ap()
    xnscr = nc.dram_tensor("xnscr", [128, 32 * T], BF16, kind="Internal").ap()
    zscr = nc.dram_tensor("zscr", [T, DS], F32, kind="Internal").ap()
    uscr = nc.dram_tensor("uscr", [T, H], F32, kind="Internal").ap()
    mupwscr = nc.dram_tensor("mupwscr", [128, CONV * KC], BF16, kind="Internal").ap()
    wscr = nc.dram_tensor("wscr", [H, NCOL], BF16, kind="Internal").ap()
    owscr = nc.dram_tensor("owscr", [DS, H], BF16, kind="Internal").ap()
    ccin = nc.dram_tensor("ccin", [128, DS + NH], F32, kind="Internal").ap()
    ccout = nc.dram_tensor("ccout", [512, DS + NH], F32, kind="Internal").ap()

    es = contextlib.ExitStack()
    sbt = lambda name, shape, dt: es.enter_context(nc.sbuf_tensor(name, shape, dt))

    sem_names = ["VS", "SS", "TS", "LDC", "LDX", "LDY", "WCV", "OCV", "WLD", "XNR", "CPL", "ZUS",
                 "XLD", "ZRL", "URL", "XRL", "CC", "GDM", "STO", "LFD", "ROW",
                 "OWL", "GTH", "PUB", "MWS", "MWL", "GS"]
    SEM = {}
    CNT = {}
    QL = {"tensor": [], "vector": [], "scalar": [], "sync": [], "gpsimd": []}

    def emit(eng, fn, wait=(), inc=()):
        wait = tuple((s, v) for (s, v) in wait if v > 0)
        inc = tuple(inc)

        def step(e, _fn=fn, _w=wait, _i=inc):
            for s, v in _w:
                e.wait_ge(SEM[s], v)
            inst = _fn(e)
            for s, a in _i:
                inst = inst.then_inc(SEM[s], a)
        QL[eng].append(step)

    def bump(inc):
        for s, a in inc:
            CNT[s] += a

    def V(fn, wait=(), inc=()):
        w = [("VS", CNT["VS"])] + list(wait)
        i = [("VS", 1)] + list(inc)
        bump(i)
        emit("vector", fn, w, i)

    def SC(fn, wait=(), inc=()):
        w = [("SS", CNT["SS"])] + list(wait)
        i = [("SS", 1)] + list(inc)
        bump(i)
        emit("scalar", fn, w, i)

    def TE(fn, wait=(), inc=()):
        bump(inc)
        emit("tensor", fn, wait, inc)

    def DMA(eng, fn, wait=(), inc=()):
        bump(inc)
        emit(eng, fn, wait, inc)

    def G(fn, wait=(), inc=()):
        w = [("GS", CNT["GS"])] + list(wait)
        i = [("GS", 1)] + list(inc)
        bump(i)
        emit("gpsimd", fn, w, i)

    def WAITQ(eng, wait):
        emit(eng, lambda e: None, wait, ())

    s_identb = sbt("s_identb", [128, 128], BF16)
    s_identf = sbt("s_identf", [128, 128], F32)
    s_ut = sbt("s_ut", [128, 128], F32)
    s_shift = sbt("s_shift", [128, 128 * 9], BF16)
    s_ones1 = sbt("s_ones1", [1, 128], F32)
    s_onescol = sbt("s_onescol", [128, 1], F32)
    s_row64 = sbt("s_row64", [1, 448], F32)
    s_lnwc = sbt("s_lnwc", [128, 32], F32)
    s_dtbrep = sbt("s_dtbrep", [128, NH], F32)
    s_arep = sbt("s_arep", [128, NH], F32)
    s_drep = sbt("s_drep", [128, NH], F32)
    s_ebrep = sbt("s_ebrep", [128, 4], F32)
    s_cmask = sbt("s_cmask", [128, 3], F32)
    s_gmask = sbt("s_gmask", [128, 3], F32)
    s_gmaski = sbt("s_gmaski", [128, 3], F32)
    s_rmsall = sbt("s_rmsall", [128, 16], F32)
    s_dtsp = sbt("s_dtsp", [128, NH * NT], F32)
    s_dtaall = sbt("s_dtaall", [128, NH * NT], F32)
    s_lam = sbt("s_lam", [128, NH * NT], F32)
    s_lamfm = sbt("s_lamfm", [64, 128 * NT], F32)
    s_ftm = sbt("s_ftm", [128, NH * NT], F32)
    s_gamrep = sbt("s_gamrep", [128, NH * NT], F32)
    s_explam = sbt("s_explam", [128, NH * NT], F32)
    s_gtot = sbt("s_gtot", [128, NH], F32)
    s_glog = sbt("s_glog", [128, 4 * NT], F32)
    s_zlora = sbt("s_zlora", [128, 32 * NT], F32)
    s_zlt = sbt("s_zlt", [32, 128 * NT], BF16)
    s_hrun = sbt("s_hrun", [128, DS], F32)
    s_small = sbt("s_small", [128, 128], F32)
    s_normw = sbt("s_normw", [128, DS], BF16)
    s_stat = sbt("s_stat", [128, 32 * TB], BF16)
    s_mupdt = sbt("s_mupdt", [128, NH], F32)
    s_mupz = sbt("s_mupz", [128, DS], BF16)
    s_cnst = sbt("s_cnst", [128, 2], F32)

    psA = es.enter_context(nc.psum_tensor("psA", [128, 2048], F32))
    psB = es.enter_context(nc.psum_tensor("psB", [128, 2048], F32))

    for n in sem_names:
        SEM[n] = es.enter_context(nc.semaphore(n))
        CNT[n] = 0

    TS = lambda: CNT["TS"]
    VSn = lambda: CNT["VS"]
    SSn = lambda: CNT["SS"]

    esA = contextlib.ExitStack()
    sbA = lambda name, shape, dt: esA.enter_context(nc.sbuf_tensor(name, shape, dt))
    s_srow = sbA("s_srow", [1, 5200], F32)
    s_cwp = sbA("s_cwp", [1, 512], F32)
    s_cwq = sbA("s_cwq", [1, 512], F32)
    s_repp = sbA("s_repp", [128, 512], BF16)

    for dst, src in [(s_identb, identbD), (s_identf, identfD), (s_ut, utonesD),
                     (s_shift, shiftD), (s_ones1, ones1D), (s_cmask, cmaskD),
                     (s_gmask, gmaskD), (s_gmaski, gmaskiD), (s_lnwc, lnwC)]:
        DMA("sync", lambda e, d=dst, s=src: e.dma_start(out=d[:], in_=s),
            (), [("LDC", 16)])
    DMA("sync", lambda e: e.dma_start(out=s_row64[0:1, 0:64], in_=dtbD), (), [("LDC", 16)])
    DMA("sync", lambda e: e.dma_start(out=s_row64[0:1, 64:128], in_=alogD), (), [("LDC", 16)])
    DMA("sync", lambda e: e.dma_start(out=s_row64[0:1, 128:192], in_=dvecD), (), [("LDC", 16)])
    DMA("sync", lambda e: e.dma_start(out=s_row64[0:1, 192:196], in_=ebiasD), (), [("LDC", 16)])
    ldc_all = CNT["LDC"]
    for j in range(4):
        DMA("gpsimd", lambda e, j=j: e.dma_start(
            out=owscr[:, j * 1024:(j + 1) * 1024],
            in_=outwD[:, j * 1024:(j + 1) * 1024]),
            (), [("OCV", 16)])
    ocv_all = CNT["OCV"]
    WAITQ("vector", [("LDC", ldc_all)])
    WAITQ("scalar", [("LDC", ldc_all)])
    WAITQ("tensor", [("LDC", ldc_all)])

    V(lambda e: e.memset(s_onescol[:], 1.0))
    V(lambda e: e.memset(s_hrun[:], 0.0))
    V(lambda e: e.memset(s_gtot[:], 1.0))
    V(lambda e: e.memset(s_cnst[:, 0:1], EPS))
    V(lambda e: e.memset(s_cnst[:, 1:2], 1.0 / DS))
    nc.const_aps.register(F32, EPS, s_cnst[:, 0:1])
    nc.const_aps.register(F32, 1.0 / DS, s_cnst[:, 1:2])

    def k1rep(dst_ap, row_ap, width, ps, poff, sc_mul=None, wt=()):
        TE(lambda e, w=width: e.matmul(ps[:, poff:poff + w], s_ones1[0:1, 0:128],
                                       row_ap, start=True, stop=True),
           wait=[("VS", VSn()), ("SS", SSn())] + list(wt), inc=[("TS", 1)])
        ts = TS()
        if sc_mul is None:
            V(lambda e, w=width: e.tensor_copy(dst_ap, ps[:, poff:poff + w]),
              wait=[("TS", ts)])
        else:
            V(lambda e, w=width: e.tensor_scalar(dst_ap, ps[:, poff:poff + w],
                                                 sc_mul, None, OP.mult),
              wait=[("TS", ts)])

    SC(lambda e: e.activation(s_row64[0:1, 256:320], s_row64[0:1, 64:128], AF.Exp))
    k1rep(s_dtbrep[:, :], s_row64[0:1, 0:64], NH, psA, 0)
    k1rep(s_arep[:, :], s_row64[0:1, 256:320], NH, psA, 512, sc_mul=-1.0)
    k1rep(s_drep[:, :], s_row64[0:1, 128:192], NH, psA, 1024)
    k1rep(s_ebrep[:, :], s_row64[0:1, 192:196], 4, psA, 1536)

    mws_g = [0]
    for k in range(KC):
        for j in range(9):
            w = 512 if j < 8 else 256
            off = j * 512
            DMA("sync", lambda e, k=k, off=off, w=w: e.dma_start(
                out=s_cwp[0:1, 0:w], in_=convwT[k:k + 1, off:off + w]),
                wait=[("VS", mws_g[0])], inc=[("LDX", 16)])
            DMA("sync", lambda e, off=off, w=w: e.dma_start(
                out=s_cwq[0:1, 0:w], in_=mupvD[0:1, DS + off:DS + off + w]),
                wait=[("VS", mws_g[0])], inc=[("LDX", 16)])
            ldx = CNT["LDX"]
            V(lambda e, w=w: e.tensor_tensor(s_cwp[0:1, 0:w], s_cwp[0:1, 0:w],
                                             s_cwq[0:1, 0:w], OP.mult),
              wait=[("LDX", ldx)])
            k1rep(s_repp[:, 0:w], s_cwp[0:1, 0:w], w, psA, 0,
                  wt=[("MWS", CNT["MWS"])])
            vs = VSn()
            DMA("sync", lambda e, k=k, off=off, w=w: e.dma_start(
                out=mupwscr[:, k * CONV + off:k * CONV + off + w],
                in_=s_repp[:, 0:w]),
                wait=[("VS", vs)], inc=[("MWS", 16)])
            mws_g[0] = VSn()
    mws_all = CNT["MWS"]

    for j in range(8):
        DMA("sync", lambda e, j=j: e.dma_start(
            out=s_cwp[0:1, 0:512], in_=mupvD[0:1, j * 512:(j + 1) * 512]),
            wait=[("VS", mws_g[0])], inc=[("LDX", 16)])
        ldx = CNT["LDX"]
        WAITQ("tensor", [("LDX", ldx)])
        k1rep(s_mupz[:, j * 512:(j + 1) * 512], s_cwp[0:1, 0:512], 512,
              psA if j % 2 == 0 else psB, 0)
        mws_g[0] = VSn()
    DMA("sync", lambda e: e.dma_start(out=s_cwp[0:1, 0:64],
                                      in_=mupvD[0:1, PROJ - NH:PROJ]),
        wait=[("VS", mws_g[0])], inc=[("LDX", 16)])
    ldx = CNT["LDX"]
    WAITQ("tensor", [("LDX", ldx)])
    k1rep(s_mupdt[:, :], s_cwp[0:1, 0:64], NH, psA, 512)
    for j in range(8):
        DMA("sync", lambda e, j=j: e.dma_start(
            out=s_cwp[0:1, 0:512], in_=normwD[0:1, j * 512:(j + 1) * 512]),
            wait=[("VS", VSn())], inc=[("LDX", 16)])
        ldx = CNT["LDX"]
        WAITQ("tensor", [("LDX", ldx)])
        k1rep(s_normw[:, j * 512:(j + 1) * 512], s_cwp[0:1, 0:512], 512,
              psA if j % 2 == 0 else psB, 1024)

    s_xf = [sbA(f"s_xf{p}", [128, TB], F32) for p in range(4)]
    s_sqf = sbA("s_sqf", [128, TB], F32)
    s_mrep = sbA("s_mrep", [128, TB], F32)
    s_rrep = sbA("s_rrep", [128, TB], F32)
    s_xtmp = sbA("s_xtmp", [128, TB], F32)
    s_xnb = sbA("s_xnb", [128, TB], BF16)
    OM, OV, OR, OU, OT = 0, 1040, 2080, 3120, 4160

    xf_guard = [0, 0, 0, 0]
    sq_bufs = [s_sqf, s_xtmp]
    tsq_end = {}
    for i in range(32):
        qs = "LDX" if i % 2 == 0 else "LDY"
        DMA("sync" if i % 2 == 0 else "scalar", lambda e, i=i: e.dma_start(
            out=s_xf[i % 4][:], in_=xsegT[128 * i:128 * (i + 1), :]),
            wait=[("TS", xf_guard[i % 4])], inc=[(qs, 16)])
        ldx = CNT[qs]
        sq = sq_bufs[i % 2]
        SC(lambda e, i=i, sq=sq: e.activation(sq[:], s_xf[i % 4][:], AF.Square),
           wait=[(qs, ldx), ("TS", tsq_end.get(i - 2, 0))])
        ssq = SSn()
        for c in range(3):
            w = 512 if c < 2 else 3
            TE(lambda e, i=i, c=c, w=w: e.matmul(
                psA[0:1, c * 512:c * 512 + w], s_onescol[:, 0:1],
                s_xf[i % 4][:, c * 512:c * 512 + w], start=(i == 0), stop=(i == 31)),
               wait=[(qs, ldx), ("VS", VSn() if i == 0 else 0)],
               inc=[("TS", 1)])
            TE(lambda e, i=i, c=c, w=w, sq=sq: e.matmul(
                psB[0:1, c * 512:c * 512 + w], s_onescol[:, 0:1],
                sq[:, c * 512:c * 512 + w], start=(i == 0), stop=(i == 31)),
               wait=[("SS", ssq)], inc=[("TS", 1)])
        xf_guard[i % 4] = TS()
        tsq_end[i] = TS()
    ts_stats = TS()
    V(lambda e: e.tensor_scalar(s_srow[0:1, OM:OM + TB], psA[0:1, 0:TB],
                                1.0 / H, None, OP.mult), wait=[("TS", ts_stats)])
    V(lambda e: e.tensor_scalar(s_srow[0:1, OV:OV + TB], psB[0:1, 0:TB],
                                1.0 / H, None, OP.mult))
    V(lambda e: e.tensor_tensor(s_srow[0:1, OT:OT + TB], s_srow[0:1, OM:OM + TB],
                                s_srow[0:1, OM:OM + TB], OP.mult))
    V(lambda e: e.tensor_sub(s_srow[0:1, OV:OV + TB], s_srow[0:1, OV:OV + TB],
                             s_srow[0:1, OT:OT + TB]))
    V(lambda e: e.tensor_scalar(s_srow[0:1, OT:OT + TB], s_srow[0:1, OV:OV + TB],
                                EPS, None, OP.add))
    V(lambda e: e.reciprocal(s_srow[0:1, OR:OR + TB], s_srow[0:1, OT:OT + TB]))
    vs_r2 = VSn()
    SC(lambda e: e.activation(s_srow[0:1, OT:OT + TB], s_srow[0:1, OR:OR + TB],
                              AF.Sqrt), wait=[("VS", vs_r2)])
    ss_r = SSn()
    V(lambda e: e.tensor_tensor(s_srow[0:1, OU:OU + TB], s_srow[0:1, OV:OV + TB],
                                s_srow[0:1, OR:OR + TB], OP.mult),
      wait=[("SS", ss_r)])
    V(lambda e: e.tensor_copy(s_srow[0:1, OR:OR + TB], s_srow[0:1, OT:OT + TB]))
    vs_u = VSn()
    SC(lambda e: e.activation(s_srow[0:1, OT:OT + TB], s_srow[0:1, OU:OU + TB],
                              AF.Sqrt, bias=EPS, scale=1.0), wait=[("VS", vs_u)])
    ss_su = SSn()
    V(lambda e: e.reciprocal(s_srow[0:1, OU:OU + TB], s_srow[0:1, OT:OT + TB]),
      wait=[("SS", ss_su)])
    for c in range(3):
        w = 512 if c < 2 else 3
        k1rep(s_mrep[:, c * 512:c * 512 + w], s_srow[0:1, OM + c * 512:OM + c * 512 + w],
              w, psA, c * 512)
        k1rep(s_rrep[:, c * 512:c * 512 + w], s_srow[0:1, OR + c * 512:OR + c * 512 + w],
              w, psB, c * 512)
    for t in range(9):
        nr = 128 if t < 8 else 3
        so = OU + (3 + 128 * t if t < 8 else 0)
        TE(lambda e, nr=nr, so=so: e.transpose(
            psA[0:nr, 1536:1537], s_srow[0:1, so:so + nr], s_identf[0:1, 0:1]),
           wait=[("VS", VSn())], inc=[("TS", 1)])
        ts = TS()
        V(lambda e, t=t, nr=nr: e.tensor_copy(s_rmsall[0:nr, t:t + 1],
                                              psA[0:nr, 1536:1537]),
          wait=[("TS", ts)])

    vs_sub = {}
    xnr_g = [0]
    for i in range(32):
        qs = "LDX" if i % 2 == 0 else "LDY"
        DMA("sync" if i % 2 == 0 else "scalar", lambda e, i=i: e.dma_start(
            out=s_xf[i % 4][:], in_=xsegT[128 * i:128 * (i + 1), :]),
            wait=[("TS", xf_guard[i % 4]), ("VS", vs_sub.get(i - 4, 0))],
            inc=[(qs, 16)])
        ldx = CNT[qs]
        V(lambda e, i=i: e.tensor_tensor(s_xtmp[:], s_xf[i % 4][:], s_mrep[:],
                                         OP.subtract), wait=[(qs, ldx)])
        vs_sub[i] = VSn()
        V(lambda e: e.tensor_tensor(s_xnb[:], s_xtmp[:], s_rrep[:], OP.mult),
          wait=[("XNR", xnr_g[0])])
        vs_xn = VSn()
        DMA("sync", lambda e, i=i: e.dma_start(
            out=xnscr[:, i * T:(i + 1) * T], in_=s_xnb[:, 3:TB]),
            wait=[("VS", vs_xn)], inc=[("XNR", 16)])
        xnr_g[0] = CNT["XNR"]
        V(lambda e, i=i: e.tensor_scalar(
            s_stat[:, i * TB:(i + 1) * TB], s_xnb[:], s_lnwc[:, i:i + 1],
            None, OP.mult))
    vs_pln = VSn()
    xnr_pln = CNT["XNR"]
    esA.close()

    esB = contextlib.ExitStack()
    sbB = lambda name, shape, dt: esB.enter_context(nc.sbuf_tensor(name, shape, dt))
    s_w = [sbB(f"s_w{p}", [128, 32 * 512], BF16) for p in range(2)]
    s_stg = [sbB(f"s_stg{p}", [128, 512], F32) for p in range(2)]
    s_xt = [[sbB(f"s_xt{k}_{p}", [128, 512], BF16) for p in range(2)]
            for k in range(KC)]
    s_cstg = [sbB(f"s_cstg{p}", [128, 512], BF16) for p in range(2)]
    s_mws = [sbB("s_mws0", [128, KC * 512], BF16)]
    s_dtt = sbB("s_dtt", [128, NH], F32)
    s_dtt2 = sbB("s_dtt2", [128, NH], F32)
    s_dtt3 = sbB("s_dtt3", [128, NH], F32)
    s_p2a = sbB("s_p2a", [128, 64], F32)
    s_p2b = sbB("s_p2b", [128, 32], BF16)

    def wsrc(hbm, c0, cw):
        return hbm.rearrange("(i p) c -> p i c", p=128)[:, :, c0:c0 + cw]

    bank_free = {b: 0 for b in range(8)}
    wbuf_free = {0: 0, 1: 0}
    stg_free = {0: 0, 1: 0}
    stg_cnt = [0]
    cstg_free = {0: 0, 1: 0}
    cstg_cnt = [0]
    xt_free = {0: 0, 1: 0}
    mws_free = {0: vs_pln, 1: vs_pln}
    half_idx = [0]

    def SH(ix):
        return s_shift[:, ix * 128:(ix + 1) * 128]

    def proj_group(g0, gw, conv_order, stat_off_fn, first_wait):
        nhal = (gw + 511) // 512
        bufs = []
        wlds = []
        for hf in range(nhal):
            c0 = g0 + hf * 512
            cw = min(512, g0 + gw - c0)
            buf = half_idx[0] % 2
            half_idx[0] += 1
            bufs.append((buf, c0, cw))
            DMA("gpsimd", lambda e, c0=c0, cw=cw, buf=buf: e.dma_start(
                out=s_w[buf][:, 0:32 * 512].rearrange(
                    "p (i c) -> p i c", i=32)[:, :, 0:cw],
                in_=wsrc(wcat, c0, cw)),
                wait=[("TS", wbuf_free[buf])] + list(first_wait),
                inc=[("WLD", 16)])
            wlds.append(CNT["WLD"])
        is_conv = DS <= g0 < PROJ
        tiles = conv_order if is_conv else list(range(8))
        for t in tiles:
            nr = 128 if t < 8 else 3
            bank = t % 8
            ps = psA if bank < 4 else psB
            poff = (bank % 4) * 512
            for hj, (buf, c0, cw) in enumerate(bufs):
                sub = c0 - g0
                for i in range(32):
                    so = stat_off_fn(i, t)
                    fa = (hj == 0 and i == 0)
                    la = (hj == nhal - 1 and i == 31)
                    w = [("WLD", wlds[hj])]
                    if fa:
                        w.append(("VS", bank_free[bank]))
                    TE(lambda e, nr=nr, ps=ps, poff=poff, sub=sub, cw=cw,
                       so=so, buf=buf, i=i, fa=fa, la=la:
                       e.matmul(ps[0:nr, poff + sub:poff + sub + cw],
                                s_stat[:, so:so + nr],
                                s_w[buf][:, i * 512:i * 512 + cw],
                                start=fa, stop=la),
                       wait=w, inc=[("TS", 1)] if la else ())
            yield t, nr, ps, poff, bank, TS()
        for buf, _, _ in bufs:
            wbuf_free[buf] = TS()

    def evac_z(g0, t, ps, poff, bank, ts_mm):
        par = stg_cnt[0] % 2
        stg_cnt[0] += 1
        V(lambda e, t=t, ps=ps, poff=poff, par=par, g0=g0:
          e.scalar_tensor_tensor(s_stg[par][:, 0:512], ps[:, poff:poff + 512],
                                 s_rmsall[:, t:t + 1], s_mupz[:, g0:g0 + 512],
                                 OP.mult, OP.mult),
          wait=[("TS", ts_mm), ("ZUS", stg_free[par])])
        bank_free[bank] = VSn()
        vs = VSn()
        DMA("sync", lambda e, t=t, par=par, g0=g0: e.dma_start(
            out=zscr[128 * t:128 * (t + 1), g0:g0 + 512], in_=s_stg[par][:, 0:512]),
            wait=[("VS", vs)], inc=[("ZUS", 16)])
        stg_free[par] = CNT["ZUS"]

    def evac_conv(cc0, ccw, t, nr, ps, poff, bank, ts_mm, pos, mpar, mwl):
        par = pos % 2
        for k in range(KC):
            V(lambda e, k=k, ps=ps, poff=poff, ccw=ccw, nr=nr, t=t, par=par,
              mpar=mpar:
              e.scalar_tensor_tensor(
                  s_xt[k][par][0:nr, 0:ccw], ps[0:nr, poff:poff + ccw],
                  (s_rmsall[:, t:t + 1] if t < 8 else s_rmsall[0:3, 8:9]),
                  s_mws[mpar][0:nr, k * 512:k * 512 + ccw],
                  OP.mult, OP.mult),
              wait=([("TS", max(ts_mm, xt_free[par])), ("MWL", mwl)]
                    if k == 0 else []))
        vs_xt = VSn()
        bank_free[bank] = vs_xt
        if t == 8:
            return
        for k in range(KC):
            d = 3 - k
            lhs = s_identb[:, :] if d == 0 else SH(d - 1)
            fa = (k == 0)
            TE(lambda e, lhs=lhs, ps=ps, poff=poff, ccw=ccw, k=k, par=par,
               fa=fa: e.matmul(ps[:, poff:poff + ccw], lhs,
                               s_xt[k][par][:, 0:ccw], start=fa, stop=False),
               wait=[("VS", vs_xt)] if fa else [])
        for k in range(KC - 1):
            d = 3 - k
            lhs = SH(5 + d) if t == 0 else SH(2 + d)
            la = (k == KC - 2)
            TE(lambda e, lhs=lhs, ps=ps, poff=poff, ccw=ccw, k=k, par=par,
               la=la: e.matmul(ps[:, poff:poff + ccw], lhs,
                               s_xt[k][1 - par][:, 0:ccw], start=False, stop=la),
               wait=[], inc=[("TS", 1)] if la else ())
        ts_conv = TS()
        xt_free[1 - par] = ts_conv
        cpar = cstg_cnt[0] % 2
        cstg_cnt[0] += 1
        V(lambda e, ps=ps, poff=poff, ccw=ccw, cpar=cpar:
          e.tensor_copy(s_cstg[cpar][:, 0:ccw], ps[:, poff:poff + ccw]),
          wait=[("TS", ts_conv), ("CPL", cstg_free[cpar])])
        bank_free[bank] = VSn()
        vs = VSn()
        DMA("sync", lambda e, t=t, cc0=cc0, ccw=ccw, cpar=cpar: e.dma_start(
            out=convscr[128 * t:128 * (t + 1), cc0:cc0 + ccw],
            in_=s_cstg[cpar][:, 0:ccw]),
            wait=[("VS", vs)], inc=[("CPL", 16)])
        cstg_free[cpar] = CNT["CPL"]

    def evac_dt(t, ps, poff, ts_mm):
        V(lambda e, t=t, ps=ps, poff=poff: e.scalar_tensor_tensor(
            s_dtt[:, :], ps[:, poff + 256:poff + 320], s_rmsall[:, t:t + 1],
            s_mupdt[:, :], OP.mult, OP.mult), wait=[("TS", ts_mm)])
        V(lambda e: e.tensor_tensor(s_dtt[:, :], s_dtt[:, :], s_dtbrep[:, :],
                                    OP.add))
        vs_z = VSn()
        SC(lambda e: e.activation(s_dtt2[:, :], s_dtt[:, :], AF.Abs),
           wait=[("VS", vs_z)])
        SC(lambda e: e.activation(s_dtt3[:, :], s_dtt2[:, :], AF.Exp, scale=-1.0))
        SC(lambda e: e.activation(s_dtt2[:, :], s_dtt3[:, :], AF.Ln,
                                  bias=1.0, scale=1.0))
        SC(lambda e: e.activation(s_dtt3[:, :], s_dtt[:, :], AF.Relu))
        ss_sp = SSn()
        V(lambda e, t=t: e.tensor_add(s_dtsp[:, t * 64:(t + 1) * 64],
                                      s_dtt2[:, :], s_dtt3[:, :]),
          wait=[("SS", ss_sp)])
        V(lambda e, t=t: e.tensor_tensor(s_dtaall[:, t * 64:(t + 1) * 64],
                                         s_dtsp[:, t * 64:(t + 1) * 64],
                                         s_arep[:, :], OP.mult))

    def attn_stat_off(i, t):
        return i * TB + (3 + 128 * t if t < 8 else 0)

    conv_order = [8] + list(range(8))
    for k in range(KC):
        for p in range(2):
            V(lambda e, k=k, p=p: e.memset(s_xt[k][p][:, :], 0.0),
              wait=[("TS", TS())])
    first_wait = [("VS", vs_pln), ("MWS", mws_all)]
    mcnt = [0]
    for g in range(17):
        g0 = g * 512
        gw = 512 if g < 16 else 320
        is_conv = DS <= g0
        mpar = 0
        mwl = 0
        if is_conv:
            cc0 = g0 - DS
            ccw = min(gw, 512) if g < 16 else 256
            mpar = 0
            mcnt[0] += 1
            DMA("sync", lambda e, cc0=cc0, ccw=ccw, mpar=mpar: e.dma_start(
                out=s_mws[mpar][:, 0:KC * 512].rearrange(
                    "p (k c) -> p k c", k=KC)[:, :, 0:ccw],
                in_=mupwscr[:, :].rearrange(
                    "p (k c) -> p k c", k=KC)[:, :, cc0:cc0 + ccw]),
                wait=[("MWS", mws_all), ("VS", mws_free[mpar])],
                inc=[("MWL", 16)])
            mwl = CNT["MWL"]
        for t, nr, ps, poff, bank, ts_mm in proj_group(
                g0, gw, conv_order, attn_stat_off, first_wait):
            first_wait = []
            if not is_conv:
                evac_z(g0, t, ps, poff, bank, ts_mm)
            else:
                pos = conv_order.index(t)
                evac_conv(g0 - DS, ccw, t, nr, ps, poff, bank, ts_mm, pos,
                          mpar, mwl)
                if g == 16 and t != 8:
                    evac_dt(t, ps, poff, ts_mm)
        if is_conv:
            mws_free[mpar] = VSn()
    ts_p1a = TS()
    cpl_all = CNT["CPL"]
    zus_p1a = CNT["ZUS"]

    DMA("sync", lambda e: e.dma_start(
        out=s_stat[:, 0:32 * TB].rearrange("p (i c) -> p i c", c=TB)[:, :, 0:T],
        in_=xnscr[:, :].rearrange("p (i c) -> p i c", c=T)),
        wait=[("TS", ts_p1a), ("XNR", xnr_pln)], inc=[("XNR", 16)])
    xnr_re = CNT["XNR"]

    def ffn_stat_off(i, t):
        return i * TB + 128 * t

    def evac_u(g0, t, ps, poff, bank, ts_mm):
        par = stg_cnt[0] % 2
        stg_cnt[0] += 1
        u0 = g0 - PROJ
        V(lambda e, ps=ps, poff=poff, par=par:
          e.tensor_copy(s_stg[par][:, 0:512], ps[:, poff:poff + 512]),
          wait=[("TS", ts_mm), ("ZUS", stg_free[par])])
        bank_free[bank] = VSn()
        vs = VSn()
        DMA("sync", lambda e, t=t, par=par, u0=u0: e.dma_start(
            out=uscr[128 * t:128 * (t + 1), u0:u0 + 512],
            in_=s_stg[par][:, 0:512]),
            wait=[("VS", vs)], inc=[("ZUS", 16)])
        stg_free[par] = CNT["ZUS"]

    first_wait = [("XNR", xnr_re)]
    for g in range(8):
        g0 = PROJ + g * 512
        for t, nr, ps, poff, bank, ts_mm in proj_group(
                g0, 512, None, ffn_stat_off, first_wait):
            first_wait = []
            evac_u(g0, t, ps, poff, bank, ts_mm)
    for t, nr, ps, poff, bank, ts_mm in proj_group(
            PROJ + 4096, 36, None, ffn_stat_off, [("XNR", xnr_re)]):
        V(lambda e, ps=ps, poff=poff, t=t:
          e.tensor_copy(s_zlora[:, t * 32:(t + 1) * 32], ps[:, poff:poff + 32]),
          wait=[("TS", ts_mm)])
        V(lambda e, ps=ps, poff=poff, t=t:
          e.tensor_copy(s_glog[:, t * 4:(t + 1) * 4], ps[:, poff + 32:poff + 36]))
        bank_free[bank] = VSn()
    ts_p1b = TS()
    zus_all = CNT["ZUS"]

    for t in range(8):
        sm = s_small
        V(lambda e, t=t: e.tensor_tensor(sm[:, 0:4], s_glog[:, 4 * t:4 * t + 4],
                                         s_ebrep[:, :], OP.add))
        V(lambda e: e.reduce_max(sm[:, 4:5], sm[:, 0:4], axis=AX))
        for eix in range(4):
            V(lambda e, eix=eix: e.tensor_scalar(
                sm[:, 5 + eix:6 + eix], sm[:, eix:eix + 1], sm[:, 4:5], None,
                OP.is_ge))
        V(lambda e: e.tensor_copy(sm[:, 9:10], sm[:, 5:6]))
        V(lambda e: e.tensor_scalar(sm[:, 13:14], sm[:, 9:10], -1.0, 1.0,
                                    OP.mult, OP.add))
        V(lambda e: e.tensor_tensor(sm[:, 10:11], sm[:, 6:7], sm[:, 13:14],
                                    OP.mult))
        V(lambda e: e.tensor_scalar(sm[:, 14:15], sm[:, 10:11], -1.0, 1.0,
                                    OP.mult, OP.add))
        V(lambda e: e.tensor_tensor(sm[:, 14:15], sm[:, 14:15], sm[:, 13:14],
                                    OP.mult))
        V(lambda e: e.tensor_tensor(sm[:, 11:12], sm[:, 7:8], sm[:, 14:15],
                                    OP.mult))
        V(lambda e: e.tensor_tensor(sm[:, 15:16], sm[:, 9:10], sm[:, 10:11],
                                    OP.add))
        V(lambda e: e.tensor_tensor(sm[:, 15:16], sm[:, 15:16], sm[:, 11:12],
                                    OP.add))
        V(lambda e: e.tensor_scalar(sm[:, 12:13], sm[:, 15:16], -1.0, 1.0,
                                    OP.mult, OP.add))
        V(lambda e, t=t: e.tensor_tensor(
            s_p2a[:, 0:32].rearrange("p (a b) -> p a b", a=4),
            s_zlora[:, 32 * t:32 * (t + 1)].rearrange("p (a b) -> p a b", a=4),
            sm[:, 9:13].unsqueeze(2).broadcast_to([128, 4, 8]), OP.mult))
        V(lambda e: e.tensor_copy(s_p2b[:, 0:32], s_p2a[:, 0:32]))
        vs_zt = VSn()
        TE(lambda e: e.transpose(psA[0:32, 0:64].bitcast(BF16)[:, 0:128],
                                 s_p2b[:, 0:32], s_identb[:, :]),
           wait=[("VS", max(vs_zt, bank_free[0]))], inc=[("TS", 1)])
        ts = TS()
        V(lambda e, t=t: e.tensor_copy(s_zlt[:, 128 * t:128 * (t + 1)],
                                       psA[0:32, 0:64].bitcast(BF16)[:, 0:128]),
          wait=[("TS", ts)])
        bank_free[0] = VSn()
    vs_p2 = VSn()
    esB.close()

    esC = contextlib.ExitStack()
    sbC = lambda name, shape, dt: esC.enter_context(nc.sbuf_tensor(name, shape, dt))
    esCx = contextlib.ExitStack()
    s_xc = esCx.enter_context(nc.sbuf_tensor("s_xc", [128, CONV], BF16))
    s_xhat = esCx.enter_context(nc.sbuf_tensor("s_xhat", [128, DS], BF16))
    xc_guard = {"v": vs_p2, "t": 0, "s": 0}

    def load_conv_tile(c):
        DMA("sync", lambda e, c=c: e.dma_start(
            out=s_xc[:, :], in_=convscr[128 * c:128 * (c + 1), :]),
            wait=[("SS", xc_guard["s"]), ("VS", xc_guard["v"]),
                  ("TS", xc_guard["t"]), ("CPL", cpl_all)],
            inc=[("XLD", 16)])
        xld = CNT["XLD"]
        SC(lambda e: e.activation(s_xc[:, :], s_xc[:, :], AF.Silu),
           wait=[("XLD", xld)])
        xc_guard["s"] = SSn()
        return SSn()

    for c in range(NT):
        ss_xc = load_conv_tile(c)
        TE(lambda e, c=c: e.matmul(psA[:, 0:64], s_ut[:, :],
                                   s_dtaall[:, c * 64:(c + 1) * 64],
                                   start=True, stop=True),
           wait=[("VS", bank_free[0])], inc=[("TS", 1)])
        ts_lam = TS()
        V(lambda e, c=c: e.tensor_copy(s_lam[:, c * 64:(c + 1) * 64],
                                       psA[:, 0:64]), wait=[("TS", ts_lam)])
        vs_lam = VSn()
        TE(lambda e, c=c: e.transpose(psA[0:64, 512:640],
                                      s_lam[:, c * 64:(c + 1) * 64],
                                      s_identf[:, :]),
           wait=[("VS", vs_lam)], inc=[("TS", 1)])
        ts_lfm = TS()
        V(lambda e, c=c: e.tensor_copy(s_lamfm[:, c * 128:(c + 1) * 128],
                                       psA[0:64, 512:640]), wait=[("TS", ts_lfm)])
        DMA("sync", lambda e, c=c: e.dma_start(
            out=s_row64[0:1, 320:384], in_=s_lam[127:128, c * 64:(c + 1) * 64]),
            wait=[("VS", vs_lam), ("SS", SSn()), ("TS", TS())],
            inc=[("ROW", 16)])
        roww = CNT["ROW"]
        SC(lambda e: e.activation(s_row64[0:1, 384:448], s_row64[0:1, 320:384],
                                  AF.Exp), wait=[("ROW", roww)])
        ss_gam = SSn()
        TE(lambda e: e.matmul(psA[:, 1024:1088], s_ones1[0:1, 0:128],
                              s_row64[0:1, 320:384], start=True, stop=True),
           wait=[("ROW", roww), ("VS", VSn())], inc=[("TS", 1)])
        ts_lend = TS()
        TE(lambda e: e.matmul(psA[:, 1536:1600], s_ones1[0:1, 0:128],
                              s_row64[0:1, 384:448], start=True, stop=True),
           wait=[("SS", ss_gam)], inc=[("TS", 1)])
        ts_gam = TS()
        V(lambda e, c=c: e.tensor_copy(s_gamrep[:, c * 64:(c + 1) * 64],
                                       psA[:, 1536:1600]), wait=[("TS", ts_gam)])
        V(lambda e, c=c: e.tensor_tensor(s_small[:, 0:64],
                                         psA[:, 1024:1088],
                                         s_lam[:, c * 64:(c + 1) * 64],
                                         OP.subtract),
          wait=[("TS", ts_lend)])
        vs_f1 = VSn()
        SC(lambda e: e.activation(s_small[:, 64:128], s_small[:, 0:64], AF.Exp),
           wait=[("VS", vs_f1)])
        SC(lambda e, c=c: e.activation(s_explam[:, c * 64:(c + 1) * 64],
                                       s_lam[:, c * 64:(c + 1) * 64], AF.Exp))
        ss_f = SSn()
        V(lambda e, c=c: e.tensor_tensor(s_ftm[:, c * 64:(c + 1) * 64],
                                         s_small[:, 64:128],
                                         s_dtsp[:, c * 64:(c + 1) * 64], OP.mult),
          wait=[("SS", ss_f)])
        V(lambda e, c=c: e.tensor_tensor(s_gtot[:, :], s_gtot[:, :],
                                         s_gamrep[:, c * 64:(c + 1) * 64],
                                         OP.mult))
        V(lambda e, c=c: e.tensor_tensor(
            s_xhat[:, :].rearrange("p (a b) -> p a b", a=64),
            s_xc[:, 0:DS].rearrange("p (a b) -> p a b", a=64),
            s_ftm[:, c * 64:(c + 1) * 64].unsqueeze(2).broadcast_to([128, 64, 64]),
            OP.mult), wait=[("SS", ss_xc)])
        vs_xhat = VSn()
        for j in range(8):
            ps = psA if j < 4 else psB
            poff = (j % 4) * 512
            TE(lambda e, ps=ps, poff=poff, j=j:
               e.matmul(ps[:, poff:poff + 512], s_xc[:, DS:DS + 128],
                        s_xhat[:, j * 512:(j + 1) * 512], start=True, stop=True),
               wait=[("VS", vs_xhat)] if j == 0 else [],
               inc=[("TS", 1)] if j == 7 else ())
        ts_state = TS()
        V(lambda e, c=c: e.tensor_tensor(
            s_hrun[:, :].rearrange("p (a b) -> p a b", a=64),
            s_hrun[:, :].rearrange("p (a b) -> p a b", a=64),
            s_gamrep[:, c * 64:(c + 1) * 64].unsqueeze(2).broadcast_to([128, 64, 64]),
            OP.mult), wait=[("TS", ts_state)])
        V(lambda e: e.tensor_tensor(s_hrun[:, 0:2048], s_hrun[:, 0:2048],
                                    psA[:, :], OP.add))
        V(lambda e: e.tensor_tensor(s_hrun[:, 2048:4096], s_hrun[:, 2048:4096],
                                    psB[:, :], OP.add))
        for b in range(8):
            bank_free[b] = VSn()
        xc_guard["v"] = VSn()
        xc_guard["t"] = ts_state

    esC1 = contextlib.ExitStack()
    sbC1 = lambda name, shape, dt: esC1.enter_context(nc.sbuf_tensor(name, shape, dt))
    s_gath = sbC1("s_gath", [128, DS + NH], F32)
    s_prod = sbC1("s_prod", [128, NH], F32)
    s_wj = sbC1("s_wj", [128, NH], F32)
    s_hw = sbC1("s_hw", [128, DS], F32)
    vs_chain = VSn()
    DMA("sync", lambda e: e.dma_start(out=ccin[:, 0:DS], in_=s_hrun[:, :]),
        wait=[("VS", vs_chain)], inc=[("PUB", 16)])
    DMA("sync", lambda e: e.dma_start(out=ccin[:, DS:DS + NH], in_=s_gtot[:, :]),
        wait=[("VS", vs_chain)], inc=[("PUB", 16)])
    DMA("gpsimd", lambda e: e.collective_compute(
        "AllGather", OP.bypass, replica_groups=[[0, 1, 2, 3], [4, 5, 6, 7]],
        ins=[ccin], outs=[ccout]),
        wait=[("PUB", CNT["PUB"]), ("GDM", CNT["GDM"])], inc=[("CC", 1)])
    cc_done = CNT["CC"]
    V(lambda e: e.memset(s_hrun[:, :], 0.0), wait=[("CC", cc_done)])
    V(lambda e: e.memset(s_prod[:, :], 1.0))
    for j in [2, 1, 0]:
        vs_gprev = VSn()
        DMA("sync", lambda e, j=j: e.dma_start(
            out=s_gath[:, :], in_=ccout[128 * j:128 * (j + 1), :]),
            wait=[("CC", cc_done), ("VS", vs_gprev), ("GTH", CNT["GTH"])],
            inc=[("GTH", 16)])
        gth = CNT["GTH"]
        V(lambda e, j=j: e.tensor_scalar(s_wj[:, :], s_prod[:, :],
                                         s_cmask[:, j:j + 1], None, OP.mult))
        V(lambda e, j=j: e.tensor_scalar(s_small[:, 0:64],
                                         s_gath[:, DS:DS + NH],
                                         s_gmask[:, j:j + 1],
                                         s_gmaski[:, j:j + 1], OP.mult, OP.add),
          wait=[("GTH", gth)])
        V(lambda e: e.tensor_tensor(s_prod[:, :], s_prod[:, :], s_small[:, 0:64],
                                    OP.mult))
        V(lambda e, j=j: e.tensor_tensor(
            s_hw[:, :].rearrange("p (a b) -> p a b", a=64),
            s_gath[:, 0:DS].rearrange("p (a b) -> p a b", a=64),
            s_wj[:, :].unsqueeze(2).broadcast_to([128, 64, 64]), OP.mult))
        V(lambda e: e.tensor_tensor(s_hrun[:, :], s_hrun[:, :], s_hw[:, :],
                                    OP.add))
    vs_hinit = VSn()
    esC1.close()

    esC2 = contextlib.ExitStack()
    sbC2 = lambda name, shape, dt: esC2.enter_context(nc.sbuf_tensor(name, shape, dt))
    s_E = sbC2("s_E", [128, 2048], F32)
    s_L = sbC2("s_L", [128, 4096], BF16)
    s_y = sbC2("s_y", [128, DS], F32)
    s_zrl = sbC2("s_zrl", [128, 2048], F32)
    s_hcast = sbC2("s_hcast", [128, 2048], BF16)
    s_bfm = sbC2("s_bfm", [128, 128], BF16)
    s_cfm = sbC2("s_cfm", [128, 128], BF16)
    s_gbf = sbC2("s_gbf", [128, 128], BF16)
    s_lamq = sbC2("s_lamq", [1, 2048], F32)
    s_dtb16 = sbC2("s_dtb16", [128, 32], BF16)
    zrl_free = {0: vs_hinit, 1: vs_hinit}
    gA = [max(bank_free.values())]
    gB = [gA[0]]

    for c in range(NT):
        ss_xc = load_conv_tile(c)
        DMA("sync", lambda e, c=c: e.dma_start(
            out=s_zrl[:, :],
            in_=zscr[128 * c:128 * (c + 1), 0:2048]),
            wait=[("VS", zrl_free[0]), ("ZUS", zus_p1a)], inc=[("ZRL", 16)])
        zrl0 = CNT["ZRL"]
        TE(lambda e: e.transpose(psA[:, 0:64].bitcast(BF16)[:, 0:128],
                                 s_xc[:, DS:DS + 128], s_identb[:, :]),
           wait=[("SS", ss_xc), ("VS", gA[0])], inc=[("TS", 1)])
        ts_b = TS()
        TE(lambda e: e.transpose(psA[:, 512:576].bitcast(BF16)[:, 0:128],
                                 s_xc[:, DS + 128:DS + 256], s_identb[:, :]),
           inc=[("TS", 1)])
        ts_c = TS()
        V(lambda e: e.tensor_copy(s_bfm[:, :], psA[:, 0:64].bitcast(BF16)[:, 0:128]),
          wait=[("TS", ts_b)])
        V(lambda e: e.tensor_copy(s_cfm[:, :], psA[:, 512:576].bitcast(BF16)[:, 0:128]),
          wait=[("TS", ts_c)])
        vs_bc = VSn()
        TE(lambda e: e.matmul(psA[:, 1024:1152], s_bfm[:, :], s_cfm[:, :],
                              start=True, stop=True),
           wait=[("VS", vs_bc)], inc=[("TS", 1)])
        ts_g = TS()
        V(lambda e: e.tensor_tensor(s_gbf[:, :], psA[:, 1024:1152], s_ut[:, :],
                                    OP.mult), wait=[("TS", ts_g)])
        gA[0] = VSn()
        for half in range(2):
            for qq in range(2):
                q = 2 * half + qq
                ps = psA if qq == 0 else psB
                DMA("sync", lambda e, c=c, q=q: e.dma_start(
                    out=s_lamq[0:1, :],
                    in_=s_lamfm[16 * q:16 * (q + 1), c * 128:(c + 1) * 128]),
                    wait=[("LFD", CNT["LFD"]), ("VS", vs_hinit),
                          ("TS", TS())], inc=[("LFD", 16)])
                lfd = CNT["LFD"]
                for j in range(4):
                    TE(lambda e, ps=ps, j=j: e.matmul(
                        ps[:, j * 512:(j + 1) * 512], s_ones1[0:1, 0:128],
                        s_lamq[0:1, j * 512:(j + 1) * 512],
                        start=True, stop=True),
                       wait=[("LFD", lfd),
                             ("VS", gA[0] if qq == 0 else gB[0])]
                       if j == 0 else [],
                       inc=[("TS", 1)] if j == 3 else ())
                ts_q = TS()
                V(lambda e, ps=ps, q=q, c=c: e.tensor_tensor(
                    s_E[:, :].rearrange("p (a b) -> p a b", a=16),
                    ps[:, :].rearrange("p (a b) -> p a b", a=16),
                    s_lam[:, c * 64 + q * 16:c * 64 + (q + 1) * 16]
                    .unsqueeze(2).broadcast_to([128, 16, 128]), OP.subtract),
                  wait=[("TS", ts_q)])
                V(lambda e: e.tensor_scalar(s_E[:, :], s_E[:, :], 0.0, None,
                                            OP.min))
                gs_e = 0
                vs_e = VSn()
                if qq == 0:
                    gA[0] = vs_e
                else:
                    gB[0] = vs_e
                SC(lambda e, qq=qq: e.activation(
                    s_L[:, qq * 2048:(qq + 1) * 2048], s_E[:, :], AF.Exp),
                   wait=[("VS", vs_e)])
            ss_l = SSn()
            V(lambda e, half=half, c=c: e.tensor_copy(
                s_dtb16[:, :], s_dtsp[:, c * 64 + half * 32:c * 64 + (half + 1) * 32]))
            V(lambda e: e.tensor_tensor(
                s_L[:, :].rearrange("p (a b) -> p a b", a=32),
                s_L[:, :].rearrange("p (a b) -> p a b", a=32),
                s_gbf[:, :].unsqueeze(1).broadcast_to([128, 32, 128]), OP.mult),
              wait=[("SS", ss_l)])
            V(lambda e: e.tensor_tensor(
                s_L[:, :].rearrange("p (a b) -> p a b", a=32),
                s_L[:, :].rearrange("p (a b) -> p a b", a=32),
                s_dtb16[:, :].unsqueeze(2).broadcast_to([128, 32, 128]), OP.mult))
            vs_stile = VSn()
            V(lambda e, half=half: e.tensor_copy(
                s_hcast[:, :], s_hrun[:, half * 2048:(half + 1) * 2048]))
            vs_hc = VSn()
            for j in range(4):
                TE(lambda e, j=j: e.matmul(
                    psA[:, j * 512:(j + 1) * 512], s_cfm[:, :],
                    s_hcast[:, j * 512:(j + 1) * 512], start=True, stop=True),
                   wait=[("VS", max(vs_hc, gA[0]))] if j == 0 else [],
                   inc=[("TS", 1)] if j == 3 else ())
            ts_inter = TS()
            for hh in range(32):
                h = half * 32 + hh
                TE(lambda e, hh=hh, h=h, first=(hh % 8 == 0), last=(hh % 8 == 7):
                   e.matmul(psB[:, hh * 64:(hh + 1) * 64],
                            s_L[:, hh * 128:(hh + 1) * 128],
                            s_xc[:, h * 64:(h + 1) * 64],
                            start=first, stop=last),
                   wait=[("VS", max(vs_stile, gB[0]))] if hh == 0 else [],
                   inc=[("TS", 1)] if hh == 31 else ())
            ts_intra = TS()
            V(lambda e, half=half, c=c: e.tensor_tensor(
                s_y[:, half * 2048:(half + 1) * 2048].rearrange(
                    "p (a b) -> p a b", a=32),
                psA[:, :].rearrange("p (a b) -> p a b", a=32),
                s_explam[:, c * 64 + half * 32:c * 64 + (half + 1) * 32]
                .unsqueeze(2).broadcast_to([128, 32, 64]), OP.mult),
              wait=[("TS", ts_inter)])
            V(lambda e, half=half: e.tensor_tensor(
                s_y[:, half * 2048:(half + 1) * 2048],
                s_y[:, half * 2048:(half + 1) * 2048], psB[:, :], OP.add),
              wait=[("TS", ts_intra)])
            gA[0] = VSn()
            gB[0] = VSn()
            V(lambda e, half=half, c=c: e.tensor_tensor(
                s_hcast[:, :].rearrange("p (a b) -> p a b", a=32),
                s_xc[:, half * 2048:(half + 1) * 2048].rearrange(
                    "p (a b) -> p a b", a=32),
                s_drep[:, half * 32:(half + 1) * 32].unsqueeze(2)
                .broadcast_to([128, 32, 64]), OP.mult))
            V(lambda e, half=half: e.tensor_tensor(
                s_y[:, half * 2048:(half + 1) * 2048],
                s_y[:, half * 2048:(half + 1) * 2048], s_hcast[:, :], OP.add))
        V(lambda e, c=c: e.tensor_tensor(
            s_xhat[:, :].rearrange("p (a b) -> p a b", a=64),
            s_xc[:, 0:DS].rearrange("p (a b) -> p a b", a=64),
            s_ftm[:, c * 64:(c + 1) * 64].unsqueeze(2).broadcast_to([128, 64, 64]),
            OP.mult))
        vs_xhat = VSn()
        for j in range(8):
            ps = psA if j < 4 else psB
            poff = (j % 4) * 512
            TE(lambda e, ps=ps, poff=poff, j=j:
               e.matmul(ps[:, poff:poff + 512], s_xc[:, DS:DS + 128],
                        s_xhat[:, j * 512:(j + 1) * 512], start=True, stop=True),
               wait=[("VS", max(vs_xhat, gA[0], gB[0]))] if j == 0 else [],
               inc=[("TS", 1)] if j == 7 else ())
        ts_state = TS()
        V(lambda e, c=c: e.tensor_tensor(
            s_hrun[:, :].rearrange("p (a b) -> p a b", a=64),
            s_hrun[:, :].rearrange("p (a b) -> p a b", a=64),
            s_gamrep[:, c * 64:(c + 1) * 64].unsqueeze(2).broadcast_to([128, 64, 64]),
            OP.mult), wait=[("TS", ts_state)])
        V(lambda e: e.tensor_tensor(s_hrun[:, 0:2048], s_hrun[:, 0:2048],
                                    psA[:, :], OP.add))
        V(lambda e: e.tensor_tensor(s_hrun[:, 2048:4096], s_hrun[:, 2048:4096],
                                    psB[:, :], OP.add))
        vs_hupd = VSn()
        SC(lambda e: e.activation(s_zrl[:, :], s_zrl[:, :], AF.Silu),
           wait=[("ZRL", zrl0)])
        ss_silu = SSn()
        V(lambda e: e.tensor_tensor(s_y[:, 0:2048], s_y[:, 0:2048],
                                    s_zrl[:, :], OP.mult),
          wait=[("SS", ss_silu)])
        vs_z0 = VSn()
        DMA("sync", lambda e, c=c: e.dma_start(
            out=s_zrl[:, :], in_=zscr[128 * c:128 * (c + 1), 2048:4096]),
            wait=[("VS", vs_z0), ("SS", SSn())], inc=[("ZRL", 16)])
        zrl1 = CNT["ZRL"]
        SC(lambda e: e.activation(s_zrl[:, :], s_zrl[:, :], AF.Silu),
           wait=[("ZRL", zrl1)])
        ss_silu2 = SSn()
        V(lambda e: e.tensor_tensor(s_y[:, 2048:4096], s_y[:, 2048:4096],
                                    s_zrl[:, :], OP.mult),
          wait=[("SS", ss_silu2)])
        zrl_free[0] = VSn()
        vs_gt = VSn()
        SC(lambda e: e.activation(s_E[:, :], s_y[:, 0:2048], AF.Square),
           wait=[("VS", vs_gt)])
        ss_sq1 = SSn()
        V(lambda e: e.reduce_sum(s_small[:, 100:101], s_E[:, :], axis=AX),
          wait=[("SS", ss_sq1)])
        vs_rs1 = VSn()
        SC(lambda e: e.activation(s_E[:, :], s_y[:, 2048:4096], AF.Square),
           wait=[("VS", vs_rs1)])
        ss_sq2 = SSn()
        V(lambda e: e.reduce_sum(s_small[:, 101:102], s_E[:, :], axis=AX),
          wait=[("SS", ss_sq2)])
        V(lambda e: e.tensor_tensor(s_small[:, 102:103], s_small[:, 100:101],
                                    s_small[:, 101:102], OP.add))
        vs_ss = VSn()
        SC(lambda e: e.activation(s_small[:, 103:104], s_small[:, 102:103],
                                  AF.Sqrt, bias=EPS, scale=1.0 / DS),
           wait=[("VS", vs_ss)])
        ss_sr = SSn()
        V(lambda e: e.reciprocal(s_small[:, 104:105], s_small[:, 103:104]),
          wait=[("SS", ss_sr)])
        V(lambda e: e.scalar_tensor_tensor(s_y[:, :], s_y[:, :],
                                           s_small[:, 104:105], s_normw[:, :],
                                           OP.mult, OP.mult))
        vs_yh = VSn()
        tb_free = {b: ("VS", vs_hupd) for b in range(8)}
        for j in range(32):
            bank = j % 8
            ps = psA if bank < 4 else psB
            poff = (bank % 4) * 512
            gsem, gval = tb_free[bank]
            TE(lambda e, j=j, ps=ps, poff=poff: e.transpose(
                ps[:, poff:poff + 128], s_y[:, j * 128:(j + 1) * 128],
                s_identf[:, :]),
               wait=[("VS", vs_yh), (gsem, gval)], inc=[("TS", 1)])
            ts = TS()
            SC(lambda e, j=j, c=c, ps=ps, poff=poff: e.copy(
                s_stat[:, j * TB + 128 * c:j * TB + 128 * (c + 1)],
                ps[:, poff:poff + 128]),
               wait=[("TS", ts)])
            tb_free[bank] = ("SS", SSn())
        V(lambda e: e.engine_nop(), wait=[("SS", SSn())])
        gA[0] = VSn()
        gB[0] = VSn()
        xc_guard["v"] = VSn()
        xc_guard["s"] = SSn()
        xc_guard["t"] = ts_state
    vs_p5 = VSn()
    ss_p5 = SSn()
    ts_p5 = TS()
    esC2.close()
    esCx.close()

    esC3 = contextlib.ExitStack()
    sbC3 = lambda name, shape, dt: esC3.enter_context(nc.sbuf_tensor(name, shape, dt))
    s_lorab = sbC3("s_lorab", [32, H], BF16)
    s_ow = [sbC3(f"s_ow{p}", [128, 32 * 256], BF16) for p in range(3)]
    s_url = [sbC3(f"s_url{p}", [128, 512], F32) for p in range(2)]
    s_xrl = [sbC3(f"s_xrl{p}", [128, 512], F32) for p in range(2)]
    s_ost = [sbC3(f"s_ost{p}", [128, 512], F32) for p in range(2)]
    DMA("gpsimd", lambda e: e.dma_start(out=s_lorab[:], in_=lorabD),
        wait=[("VS", vs_p5), ("SS", ss_p5)], inc=[("GDM", 16)])
    gdm_lb = CNT["GDM"]
    ow_free = {0: 0, 1: 0, 2: 0}
    url_free = {0: vs_p5, 1: vs_p5}
    xrl_free = {0: vs_p5, 1: vs_p5}
    ost_free = {0: 0, 1: 0}
    obank_free = {b: vs_p5 for b in range(8)}

    for og in range(8):
        owls = []
        obufs = []
        for hf in range(2):
            buf = (2 * og + hf) % 3
            obufs.append(buf)
            DMA("sync", lambda e, og=og, hf=hf, buf=buf: e.dma_start(
                out=s_ow[buf][:, 0:32 * 256].rearrange("p (i c) -> p i c", i=32),
                in_=wsrc(owscr, og * 512 + hf * 256, 256)),
                wait=[("TS", ow_free[buf]), ("VS", vs_p5), ("OCV", ocv_all)],
                inc=[("OWL", 16)])
            owls.append(CNT["OWL"])
        for t in range(8):
            rpar = t % 2
            DMA("sync", lambda e, og=og, t=t, rpar=rpar: e.dma_start(
                out=s_url[rpar][:, :],
                in_=uscr[128 * t:128 * (t + 1), og * 512:(og + 1) * 512]),
                wait=[("VS", url_free[rpar]), ("ZUS", zus_all)], inc=[("URL", 16)])
            DMA("sync", lambda e, og=og, t=t, rpar=rpar: e.dma_start(
                out=s_xrl[rpar][:, :],
                in_=xown[128 * t:128 * (t + 1), og * 512:(og + 1) * 512]),
                wait=[("VS", xrl_free[rpar])], inc=[("XRL", 16)])
            url = CNT["URL"]
            xrl = CNT["XRL"]
            bank = t % 8
            ps = psA if bank < 4 else psB
            poff = (bank % 4) * 512
            for hf in range(2):
                for i in range(32):
                    fa = (hf == 0 and i == 0)
                    w = []
                    if i == 0:
                        w.append(("OWL", owls[hf]))
                    if fa:
                        w += [("VS", obank_free[bank]), ("TS", ts_p5),
                              ("SS", ss_p5)]
                    TE(lambda e, ps=ps, poff=poff, i=i, t=t, hf=hf,
                       buf=obufs[hf], fa=fa:
                       e.matmul(ps[:, poff + hf * 256:poff + hf * 256 + 256],
                                s_stat[:, i * TB + 128 * t:i * TB + 128 * t + 128],
                                s_ow[buf][:, i * 256:(i + 1) * 256],
                                start=fa, stop=False),
                       wait=w)
            for hf in range(2):
                TE(lambda e, ps=ps, poff=poff, t=t, og=og, hf=hf: e.matmul(
                    ps[:, poff + hf * 256:poff + hf * 256 + 256],
                    s_zlt[:, 128 * t:128 * (t + 1)],
                    s_lorab[:, og * 512 + hf * 256:og * 512 + hf * 256 + 256],
                    start=False, stop=True),
                   wait=[("GDM", gdm_lb), ("VS", vs_p2)] if hf == 0 else [],
                   inc=[("TS", 1)] if hf == 1 else ())
            ts_o = TS()
            opar = (og * 8 + t) % 2
            V(lambda e, ps=ps, poff=poff, opar=opar, rpar=rpar:
              e.tensor_tensor(s_ost[opar][:, :], ps[:, poff:poff + 512],
                              s_url[rpar][:, :], OP.add),
              wait=[("TS", ts_o), ("URL", url), ("STO", ost_free[opar])])
            obank_free[bank] = VSn()
            url_free[rpar] = VSn()
            V(lambda e, opar=opar, rpar=rpar: e.tensor_tensor(
                s_ost[opar][:, :], s_ost[opar][:, :], s_xrl[rpar][:, :], OP.add),
              wait=[("XRL", xrl)])
            xrl_free[rpar] = VSn()
            vs_o = VSn()
            DMA("sync", lambda e, og=og, t=t, opar=opar: e.dma_start(
                out=oseg[128 * t:128 * (t + 1), og * 512:(og + 1) * 512],
                in_=s_ost[opar][:, :]),
                wait=[("VS", vs_o)], inc=[("STO", 16)])
            ost_free[opar] = CNT["STO"]
        for _b in obufs:
            ow_free[_b] = TS()

    WAITQ("gpsimd", [("STO", CNT["STO"])])
    esC3.close()
    esC.close()

    with nc.Block() as block:
        @block.sync
        def _(sync):
            for f in QL["sync"]:
                f(sync)

        @block.gpsimd
        def _(g):
            for f in QL["gpsimd"]:
                f(g)

        @block.tensor
        def _(t):
            for f in QL["tensor"]:
                f(t)

        @block.vector
        def _(v):
            for f in QL["vector"]:
                f(v)

        @block.scalar
        def _(s):
            for f in QL["scalar"]:
                f(s)

    es.close()
    _cache["nc"] = nc
    return nc


def _shift_consts():
    mats = []
    for d in (1, 2, 3):
        mats.append(np.eye(128, k=d))
    for d in (1, 2, 3):
        mats.append(np.eye(128, k=d - 128))
    for d in (1, 2, 3):
        m = np.zeros((128, 128))
        for t in range(d):
            m[3 + t - d, t] = 1.0
        mats.append(m)
    return np.ascontiguousarray(
        np.concatenate(mats, axis=1).astype(ml_dtypes.bfloat16))


def kernel(**inputs):
    x = np.asarray(inputs["L0_attn_hidden_states"], np.float32)
    inw = np.asarray(inputs["L0_attn_in_proj_w"], np.float32)
    wup = np.asarray(inputs["L0_ffn_W_up"], np.float32)
    loraA = np.asarray(inputs["L0_ffn_lora_A"], np.float32)
    loraB = np.asarray(inputs["L0_ffn_lora_B"], np.float32)
    gatew = np.asarray(inputs["L0_ffn_gate_w"], np.float32)
    bf = ml_dtypes.bfloat16

    wcat = np.ascontiguousarray(np.concatenate(
        [inw, wup.T, loraA.reshape(32, H).T, gatew.T], axis=1), dtype=np.float32)
    lnw = np.asarray(inputs["L0_attn_ln_w"], np.float32)

    shared = {
        "wcat": wcat,
        "outwD": np.ascontiguousarray(
            np.asarray(inputs["L0_attn_out_proj_w"], np.float32)),
        "lorabD": np.ascontiguousarray(
            loraB.transpose(0, 2, 1).reshape(32, H)),
        "convwT": np.ascontiguousarray(
            np.asarray(inputs["L0_attn_conv_w"], np.float32).T),
        "mupvD": np.asarray(inputs["L0_attn_mup_vector"], np.float32)[None, :],
        "lnwC": np.ascontiguousarray(lnw.reshape(32, 128).T),
        "normwD": np.asarray(inputs["L0_attn_norm_w"], np.float32)[None, :],
        "alogD": np.asarray(inputs["L0_attn_A_log"], np.float32)[None, :],
        "dtbD": np.asarray(inputs["L0_attn_dt_bias"], np.float32)[None, :],
        "dvecD": np.asarray(inputs["L0_attn_D"], np.float32)[None, :],
        "ebiasD": np.asarray(inputs["L0_ffn_expert_bias"], np.float32)[None, :],
        "identbD": np.eye(128, dtype=bf),
        "identfD": np.eye(128, dtype=np.float32),
        "utonesD": np.ascontiguousarray(np.triu(np.ones((128, 128), np.float32))),
        "shiftD": _shift_consts(),
        "ones1D": np.ones((1, 128), np.float32),
    }

    nc = build_program()
    in_maps = []
    for c in range(8):
        b, g = divmod(c, 4)
        own = x[b, g * T:(g + 1) * T]
        prev3 = (np.zeros((3, H), np.float32) if g == 0
                 else x[b, g * T - 3:g * T])
        xseg = np.concatenate([prev3, own], axis=0)
        cm = np.zeros((128, 3), np.float32)
        gm = np.zeros((128, 3), np.float32)
        for j in range(3):
            if j < g:
                cm[:, j] = 1.0
            if 1 <= j <= g - 1:
                gm[:, j] = 1.0
        m = dict(shared)
        m["xsegT"] = np.ascontiguousarray(xseg.T)
        m["xown"] = np.ascontiguousarray(own)
        m["cmaskD"] = cm
        m["gmaskD"] = gm
        m["gmaskiD"] = 1.0 - gm
        in_maps.append(m)

    res = run_bass_kernel_spmd(nc, in_maps, core_ids=list(range(8)))
    out = np.empty((2, 4096, H), np.float32)
    for c in range(8):
        b, g = divmod(c, 4)
        out[b, g * T:(g + 1) * T] = res.results[c]["oseg"]
    return out


# version 16
# speedup vs baseline: 1.2605x; 1.0168x over previous
import contextlib
import numpy as np
import ml_dtypes
import concourse.bass as bass
import concourse.mybir as mybir
from concourse.bass_utils import run_bass_kernel_spmd
from concourse.alu_op_type import AluOpType as OP

F32 = mybir.dt.float32
BF16 = mybir.dt.bfloat16
AF = mybir.ActivationFunctionType
AX = mybir.AxisListType.X

H = 4096
DS = 4096
SN = 128
NH = 64
HD = 64
KC = 4
CONV = DS + 2 * SN
PROJ = 2 * DS + 2 * SN + NH
NCOL = PROJ + H + 32 + 4
T = 1024
NT = 8
TB = 1027
EPS = 1e-5

_cache = {}


def build_program():
    if "nc" in _cache:
        return _cache["nc"]
    nc = bass.Bass("TRN2", target_bir_lowering=False)

    xsegT = nc.dram_tensor("xsegT", [H, TB], F32, kind="ExternalInput").ap()
    xown = nc.dram_tensor("xown", [T, H], F32, kind="ExternalInput").ap()
    wcat = nc.dram_tensor("wcat", [H, NCOL], F32, kind="ExternalInput").ap()
    outwD = nc.dram_tensor("outwD", [DS, H], F32, kind="ExternalInput").ap()
    lorabD = nc.dram_tensor("lorabD", [32, H], F32, kind="ExternalInput").ap()
    convwT = nc.dram_tensor("convwT", [KC, CONV], F32, kind="ExternalInput").ap()
    mupvD = nc.dram_tensor("mupvD", [1, PROJ], F32, kind="ExternalInput").ap()
    lnwC = nc.dram_tensor("lnwC", [128, 32], F32, kind="ExternalInput").ap()
    normwD = nc.dram_tensor("normwD", [1, DS], F32, kind="ExternalInput").ap()
    alogD = nc.dram_tensor("alogD", [1, NH], F32, kind="ExternalInput").ap()
    dtbD = nc.dram_tensor("dtbD", [1, NH], F32, kind="ExternalInput").ap()
    dvecD = nc.dram_tensor("dvecD", [1, NH], F32, kind="ExternalInput").ap()
    ebiasD = nc.dram_tensor("ebiasD", [1, 4], F32, kind="ExternalInput").ap()
    identbD = nc.dram_tensor("identbD", [128, 128], BF16, kind="ExternalInput").ap()
    identfD = nc.dram_tensor("identfD", [128, 128], F32, kind="ExternalInput").ap()
    utonesD = nc.dram_tensor("utonesD", [128, 128], F32, kind="ExternalInput").ap()
    shiftD = nc.dram_tensor("shiftD", [128, 128 * 9], BF16, kind="ExternalInput").ap()
    ones1D = nc.dram_tensor("ones1D", [1, 128], F32, kind="ExternalInput").ap()
    cmaskD = nc.dram_tensor("cmaskD", [128, 3], F32, kind="ExternalInput").ap()
    gmaskD = nc.dram_tensor("gmaskD", [128, 3], F32, kind="ExternalInput").ap()
    gmaskiD = nc.dram_tensor("gmaskiD", [128, 3], F32, kind="ExternalInput").ap()

    oseg = nc.dram_tensor("oseg", [T, H], F32, kind="ExternalOutput").ap()

    convscr = nc.dram_tensor("convscr", [T, CONV], BF16, kind="Internal").ap()
    xnscr = nc.dram_tensor("xnscr", [128, 32 * T], BF16, kind="Internal").ap()
    zscr = nc.dram_tensor("zscr", [T, DS], F32, kind="Internal").ap()
    uscr = nc.dram_tensor("uscr", [T, H], F32, kind="Internal").ap()
    mupwscr = nc.dram_tensor("mupwscr", [128, CONV * KC], BF16, kind="Internal").ap()
    wscr = nc.dram_tensor("wscr", [H, NCOL], BF16, kind="Internal").ap()
    owscr = nc.dram_tensor("owscr", [DS, H], BF16, kind="Internal").ap()
    ccin = nc.dram_tensor("ccin", [128, DS + NH], F32, kind="Internal").ap()
    ccout = nc.dram_tensor("ccout", [512, DS + NH], F32, kind="Internal").ap()

    es = contextlib.ExitStack()
    sbt = lambda name, shape, dt: es.enter_context(nc.sbuf_tensor(name, shape, dt))

    sem_names = ["VS", "SS", "TS", "LDC", "LDX", "LDY", "WCV", "OCV", "WLD", "XNR", "CPL", "ZUS",
                 "XLD", "ZRL", "URL", "XRL", "CC", "GDM", "STO", "LFD", "ROW",
                 "OWL", "GTH", "PUB", "MWS", "MWL", "GS"]
    SEM = {}
    CNT = {}
    QL = {"tensor": [], "vector": [], "scalar": [], "sync": [], "gpsimd": []}

    def emit(eng, fn, wait=(), inc=()):
        wait = tuple((s, v) for (s, v) in wait if v > 0)
        inc = tuple(inc)

        def step(e, _fn=fn, _w=wait, _i=inc):
            for s, v in _w:
                e.wait_ge(SEM[s], v)
            inst = _fn(e)
            for s, a in _i:
                inst = inst.then_inc(SEM[s], a)
        QL[eng].append(step)

    def bump(inc):
        for s, a in inc:
            CNT[s] += a

    def V(fn, wait=(), inc=()):
        w = [("VS", CNT["VS"])] + list(wait)
        i = [("VS", 1)] + list(inc)
        bump(i)
        emit("vector", fn, w, i)

    def SC(fn, wait=(), inc=()):
        w = [("SS", CNT["SS"])] + list(wait)
        i = [("SS", 1)] + list(inc)
        bump(i)
        emit("scalar", fn, w, i)

    def TE(fn, wait=(), inc=()):
        bump(inc)
        emit("tensor", fn, wait, inc)

    def DMA(eng, fn, wait=(), inc=()):
        bump(inc)
        emit(eng, fn, wait, inc)

    def G(fn, wait=(), inc=()):
        w = [("GS", CNT["GS"])] + list(wait)
        i = [("GS", 1)] + list(inc)
        bump(i)
        emit("gpsimd", fn, w, i)

    def WAITQ(eng, wait):
        emit(eng, lambda e: None, wait, ())

    s_identb = sbt("s_identb", [128, 128], BF16)
    s_identf = sbt("s_identf", [128, 128], F32)
    s_ut = sbt("s_ut", [128, 128], F32)
    s_shift = sbt("s_shift", [128, 128 * 9], BF16)
    s_ones1 = sbt("s_ones1", [1, 128], F32)
    s_onescol = sbt("s_onescol", [128, 1], F32)
    s_row64 = sbt("s_row64", [1, 448], F32)
    s_lnwc = sbt("s_lnwc", [128, 32], F32)
    s_dtbrep = sbt("s_dtbrep", [128, NH], F32)
    s_arep = sbt("s_arep", [128, NH], F32)
    s_drep = sbt("s_drep", [128, NH], F32)
    s_ebrep = sbt("s_ebrep", [128, 4], F32)
    s_cmask = sbt("s_cmask", [128, 3], F32)
    s_gmask = sbt("s_gmask", [128, 3], F32)
    s_gmaski = sbt("s_gmaski", [128, 3], F32)
    s_rmsall = sbt("s_rmsall", [128, 16], F32)
    s_dtsp = sbt("s_dtsp", [128, NH * NT], F32)
    s_dtaall = sbt("s_dtaall", [128, NH * NT], F32)
    s_lam = sbt("s_lam", [128, NH * NT], F32)
    s_lamfm = sbt("s_lamfm", [64, 128 * NT], F32)
    s_ftm = sbt("s_ftm", [128, NH * NT], F32)
    s_gamrep = sbt("s_gamrep", [128, NH * NT], F32)
    s_explam = sbt("s_explam", [128, NH * NT], F32)
    s_gtot = sbt("s_gtot", [128, NH], F32)
    s_glog = sbt("s_glog", [128, 4 * NT], F32)
    s_zlora = sbt("s_zlora", [128, 32 * NT], F32)
    s_zlt = sbt("s_zlt", [32, 128 * NT], BF16)
    s_hrun = sbt("s_hrun", [128, DS], F32)
    s_small = sbt("s_small", [128, 128], F32)
    s_normw = sbt("s_normw", [128, DS], BF16)
    s_stat = sbt("s_stat", [128, 32 * TB], BF16)
    s_mupdt = sbt("s_mupdt", [128, NH], F32)
    s_mupz = sbt("s_mupz", [128, DS], BF16)
    s_cnst = sbt("s_cnst", [128, 2], F32)

    psA = es.enter_context(nc.psum_tensor("psA", [128, 2048], F32))
    psB = es.enter_context(nc.psum_tensor("psB", [128, 2048], F32))

    for n in sem_names:
        SEM[n] = es.enter_context(nc.semaphore(n))
        CNT[n] = 0

    TS = lambda: CNT["TS"]
    VSn = lambda: CNT["VS"]
    SSn = lambda: CNT["SS"]

    esA = contextlib.ExitStack()
    sbA = lambda name, shape, dt: esA.enter_context(nc.sbuf_tensor(name, shape, dt))
    s_srow = sbA("s_srow", [1, 5200], F32)
    s_cwp = sbA("s_cwp", [1, 512], F32)
    s_cwq = sbA("s_cwq", [1, 512], F32)
    s_repp = sbA("s_repp", [128, 512], BF16)

    for dst, src in [(s_identb, identbD), (s_identf, identfD), (s_ut, utonesD),
                     (s_shift, shiftD), (s_ones1, ones1D), (s_cmask, cmaskD),
                     (s_gmask, gmaskD), (s_gmaski, gmaskiD), (s_lnwc, lnwC)]:
        DMA("sync", lambda e, d=dst, s=src: e.dma_start(out=d[:], in_=s),
            (), [("LDC", 16)])
    DMA("sync", lambda e: e.dma_start(out=s_row64[0:1, 0:64], in_=dtbD), (), [("LDC", 16)])
    DMA("sync", lambda e: e.dma_start(out=s_row64[0:1, 64:128], in_=alogD), (), [("LDC", 16)])
    DMA("sync", lambda e: e.dma_start(out=s_row64[0:1, 128:192], in_=dvecD), (), [("LDC", 16)])
    DMA("sync", lambda e: e.dma_start(out=s_row64[0:1, 192:196], in_=ebiasD), (), [("LDC", 16)])
    ldc_all = CNT["LDC"]
    WAITQ("vector", [("LDC", ldc_all)])
    WAITQ("scalar", [("LDC", ldc_all)])
    WAITQ("tensor", [("LDC", ldc_all)])

    V(lambda e: e.memset(s_onescol[:], 1.0))
    V(lambda e: e.memset(s_hrun[:], 0.0))
    V(lambda e: e.memset(s_gtot[:], 1.0))
    V(lambda e: e.memset(s_cnst[:, 0:1], EPS))
    V(lambda e: e.memset(s_cnst[:, 1:2], 1.0 / DS))
    nc.const_aps.register(F32, EPS, s_cnst[:, 0:1])
    nc.const_aps.register(F32, 1.0 / DS, s_cnst[:, 1:2])

    def k1rep(dst_ap, row_ap, width, ps, poff, sc_mul=None, wt=()):
        TE(lambda e, w=width: e.matmul(ps[:, poff:poff + w], s_ones1[0:1, 0:128],
                                       row_ap, start=True, stop=True),
           wait=[("VS", VSn()), ("SS", SSn())] + list(wt), inc=[("TS", 1)])
        ts = TS()
        if sc_mul is None:
            V(lambda e, w=width: e.tensor_copy(dst_ap, ps[:, poff:poff + w]),
              wait=[("TS", ts)])
        else:
            V(lambda e, w=width: e.tensor_scalar(dst_ap, ps[:, poff:poff + w],
                                                 sc_mul, None, OP.mult),
              wait=[("TS", ts)])

    SC(lambda e: e.activation(s_row64[0:1, 256:320], s_row64[0:1, 64:128], AF.Exp))
    k1rep(s_dtbrep[:, :], s_row64[0:1, 0:64], NH, psA, 0)
    k1rep(s_arep[:, :], s_row64[0:1, 256:320], NH, psA, 512, sc_mul=-1.0)
    k1rep(s_drep[:, :], s_row64[0:1, 128:192], NH, psA, 1024)
    k1rep(s_ebrep[:, :], s_row64[0:1, 192:196], 4, psA, 1536)

    mws_g = [0]
    for k in range(KC):
        for j in range(9):
            w = 512 if j < 8 else 256
            off = j * 512
            DMA("sync", lambda e, k=k, off=off, w=w: e.dma_start(
                out=s_cwp[0:1, 0:w], in_=convwT[k:k + 1, off:off + w]),
                wait=[("VS", mws_g[0])], inc=[("LDX", 16)])
            DMA("sync", lambda e, off=off, w=w: e.dma_start(
                out=s_cwq[0:1, 0:w], in_=mupvD[0:1, DS + off:DS + off + w]),
                wait=[("VS", mws_g[0])], inc=[("LDX", 16)])
            ldx = CNT["LDX"]
            V(lambda e, w=w: e.tensor_tensor(s_cwp[0:1, 0:w], s_cwp[0:1, 0:w],
                                             s_cwq[0:1, 0:w], OP.mult),
              wait=[("LDX", ldx)])
            k1rep(s_repp[:, 0:w], s_cwp[0:1, 0:w], w, psA, 0,
                  wt=[("MWS", CNT["MWS"])])
            vs = VSn()
            DMA("sync", lambda e, k=k, off=off, w=w: e.dma_start(
                out=mupwscr[:, k * CONV + off:k * CONV + off + w],
                in_=s_repp[:, 0:w]),
                wait=[("VS", vs)], inc=[("MWS", 16)])
            mws_g[0] = VSn()
    mws_all = CNT["MWS"]

    for j in range(8):
        DMA("sync", lambda e, j=j: e.dma_start(
            out=s_cwp[0:1, 0:512], in_=mupvD[0:1, j * 512:(j + 1) * 512]),
            wait=[("VS", mws_g[0])], inc=[("LDX", 16)])
        ldx = CNT["LDX"]
        WAITQ("tensor", [("LDX", ldx)])
        k1rep(s_mupz[:, j * 512:(j + 1) * 512], s_cwp[0:1, 0:512], 512,
              psA if j % 2 == 0 else psB, 0)
        mws_g[0] = VSn()
    DMA("sync", lambda e: e.dma_start(out=s_cwp[0:1, 0:64],
                                      in_=mupvD[0:1, PROJ - NH:PROJ]),
        wait=[("VS", mws_g[0])], inc=[("LDX", 16)])
    ldx = CNT["LDX"]
    WAITQ("tensor", [("LDX", ldx)])
    k1rep(s_mupdt[:, :], s_cwp[0:1, 0:64], NH, psA, 512)
    for j in range(8):
        DMA("sync", lambda e, j=j: e.dma_start(
            out=s_cwp[0:1, 0:512], in_=normwD[0:1, j * 512:(j + 1) * 512]),
            wait=[("VS", VSn())], inc=[("LDX", 16)])
        ldx = CNT["LDX"]
        WAITQ("tensor", [("LDX", ldx)])
        k1rep(s_normw[:, j * 512:(j + 1) * 512], s_cwp[0:1, 0:512], 512,
              psA if j % 2 == 0 else psB, 1024)

    s_xf = [sbA(f"s_xf{p}", [128, TB], F32) for p in range(4)]
    s_sqf = sbA("s_sqf", [128, TB], F32)
    s_mrep = sbA("s_mrep", [128, TB], F32)
    s_rrep = sbA("s_rrep", [128, TB], F32)
    s_xtmp = sbA("s_xtmp", [128, TB], F32)
    s_xnb = sbA("s_xnb", [128, TB], BF16)
    OM, OV, OR, OU, OT = 0, 1040, 2080, 3120, 4160

    xf_guard = [0, 0, 0, 0]
    sq_bufs = [s_sqf, s_xtmp]
    tsq_end = {}
    for i in range(32):
        qs = "LDX" if i % 2 == 0 else "LDY"
        DMA("sync" if i % 2 == 0 else "scalar", lambda e, i=i: e.dma_start(
            out=s_xf[i % 4][:], in_=xsegT[128 * i:128 * (i + 1), :]),
            wait=[("TS", xf_guard[i % 4])], inc=[(qs, 16)])
        ldx = CNT[qs]
        sq = sq_bufs[i % 2]
        SC(lambda e, i=i, sq=sq: e.activation(sq[:], s_xf[i % 4][:], AF.Square),
           wait=[(qs, ldx), ("TS", tsq_end.get(i - 2, 0))])
        ssq = SSn()
        for c in range(3):
            w = 512 if c < 2 else 3
            TE(lambda e, i=i, c=c, w=w: e.matmul(
                psA[0:1, c * 512:c * 512 + w], s_onescol[:, 0:1],
                s_xf[i % 4][:, c * 512:c * 512 + w], start=(i == 0), stop=(i == 31)),
               wait=[(qs, ldx), ("VS", VSn() if i == 0 else 0)],
               inc=[("TS", 1)])
            TE(lambda e, i=i, c=c, w=w, sq=sq: e.matmul(
                psB[0:1, c * 512:c * 512 + w], s_onescol[:, 0:1],
                sq[:, c * 512:c * 512 + w], start=(i == 0), stop=(i == 31)),
               wait=[("SS", ssq)], inc=[("TS", 1)])
        xf_guard[i % 4] = TS()
        tsq_end[i] = TS()
    ts_stats = TS()
    V(lambda e: e.tensor_scalar(s_srow[0:1, OM:OM + TB], psA[0:1, 0:TB],
                                1.0 / H, None, OP.mult), wait=[("TS", ts_stats)])
    V(lambda e: e.tensor_scalar(s_srow[0:1, OV:OV + TB], psB[0:1, 0:TB],
                                1.0 / H, None, OP.mult))
    V(lambda e: e.tensor_tensor(s_srow[0:1, OT:OT + TB], s_srow[0:1, OM:OM + TB],
                                s_srow[0:1, OM:OM + TB], OP.mult))
    V(lambda e: e.tensor_sub(s_srow[0:1, OV:OV + TB], s_srow[0:1, OV:OV + TB],
                             s_srow[0:1, OT:OT + TB]))
    V(lambda e: e.tensor_scalar(s_srow[0:1, OT:OT + TB], s_srow[0:1, OV:OV + TB],
                                EPS, None, OP.add))
    V(lambda e: e.reciprocal(s_srow[0:1, OR:OR + TB], s_srow[0:1, OT:OT + TB]))
    vs_r2 = VSn()
    SC(lambda e: e.activation(s_srow[0:1, OT:OT + TB], s_srow[0:1, OR:OR + TB],
                              AF.Sqrt), wait=[("VS", vs_r2)])
    ss_r = SSn()
    V(lambda e: e.tensor_tensor(s_srow[0:1, OU:OU + TB], s_srow[0:1, OV:OV + TB],
                                s_srow[0:1, OR:OR + TB], OP.mult),
      wait=[("SS", ss_r)])
    V(lambda e: e.tensor_copy(s_srow[0:1, OR:OR + TB], s_srow[0:1, OT:OT + TB]))
    vs_u = VSn()
    SC(lambda e: e.activation(s_srow[0:1, OT:OT + TB], s_srow[0:1, OU:OU + TB],
                              AF.Sqrt, bias=EPS, scale=1.0), wait=[("VS", vs_u)])
    ss_su = SSn()
    V(lambda e: e.reciprocal(s_srow[0:1, OU:OU + TB], s_srow[0:1, OT:OT + TB]),
      wait=[("SS", ss_su)])
    for c in range(3):
        w = 512 if c < 2 else 3
        k1rep(s_mrep[:, c * 512:c * 512 + w], s_srow[0:1, OM + c * 512:OM + c * 512 + w],
              w, psA, c * 512)
        k1rep(s_rrep[:, c * 512:c * 512 + w], s_srow[0:1, OR + c * 512:OR + c * 512 + w],
              w, psB, c * 512)
    for t in range(9):
        nr = 128 if t < 8 else 3
        so = OU + (3 + 128 * t if t < 8 else 0)
        TE(lambda e, nr=nr, so=so: e.transpose(
            psA[0:nr, 1536:1537], s_srow[0:1, so:so + nr], s_identf[0:1, 0:1]),
           wait=[("VS", VSn())], inc=[("TS", 1)])
        ts = TS()
        V(lambda e, t=t, nr=nr: e.tensor_copy(s_rmsall[0:nr, t:t + 1],
                                              psA[0:nr, 1536:1537]),
          wait=[("TS", ts)])

    vs_sub = {}
    xnr_g = [0]
    for i in range(32):
        qs = "LDX" if i % 2 == 0 else "LDY"
        DMA("sync" if i % 2 == 0 else "scalar", lambda e, i=i: e.dma_start(
            out=s_xf[i % 4][:], in_=xsegT[128 * i:128 * (i + 1), :]),
            wait=[("TS", xf_guard[i % 4]), ("VS", vs_sub.get(i - 4, 0))],
            inc=[(qs, 16)])
        ldx = CNT[qs]
        V(lambda e, i=i: e.tensor_tensor(s_xtmp[:], s_xf[i % 4][:], s_mrep[:],
                                         OP.subtract), wait=[(qs, ldx)])
        vs_sub[i] = VSn()
        V(lambda e: e.tensor_tensor(s_xnb[:], s_xtmp[:], s_rrep[:], OP.mult),
          wait=[("XNR", xnr_g[0])])
        vs_xn = VSn()
        DMA("sync", lambda e, i=i: e.dma_start(
            out=xnscr[:, i * T:(i + 1) * T], in_=s_xnb[:, 3:TB]),
            wait=[("VS", vs_xn)], inc=[("XNR", 16)])
        xnr_g[0] = CNT["XNR"]
        V(lambda e, i=i: e.tensor_scalar(
            s_stat[:, i * TB:(i + 1) * TB], s_xnb[:], s_lnwc[:, i:i + 1],
            None, OP.mult))
    vs_pln = VSn()
    xnr_pln = CNT["XNR"]
    esA.close()

    esB = contextlib.ExitStack()
    sbB = lambda name, shape, dt: esB.enter_context(nc.sbuf_tensor(name, shape, dt))
    s_w = [sbB(f"s_w{p}", [128, 32 * 512], BF16) for p in range(2)]
    s_stg = [sbB(f"s_stg{p}", [128, 512], F32) for p in range(2)]
    s_xt = [[sbB(f"s_xt{k}_{p}", [128, 512], BF16) for p in range(2)]
            for k in range(KC)]
    s_cstg = [sbB(f"s_cstg{p}", [128, 512], BF16) for p in range(2)]
    s_mws = [sbB("s_mws0", [128, KC * 512], BF16)]
    s_dtt = sbB("s_dtt", [128, NH], F32)
    s_dtt2 = sbB("s_dtt2", [128, NH], F32)
    s_dtt3 = sbB("s_dtt3", [128, NH], F32)
    s_p2a = sbB("s_p2a", [128, 64], F32)
    s_p2b = sbB("s_p2b", [128, 32], BF16)

    def wsrc(hbm, c0, cw):
        return hbm.rearrange("(i p) c -> p i c", p=128)[:, :, c0:c0 + cw]

    bank_free = {b: 0 for b in range(8)}
    wbuf_free = {0: 0, 1: 0}
    stg_free = {0: 0, 1: 0}
    stg_cnt = [0]
    cstg_free = {0: 0, 1: 0}
    cstg_cnt = [0]
    xt_free = {0: 0, 1: 0}
    mws_free = {0: vs_pln, 1: vs_pln}
    half_idx = [0]

    def SH(ix):
        return s_shift[:, ix * 128:(ix + 1) * 128]

    def proj_group(g0, gw, conv_order, stat_off_fn, first_wait):
        nhal = (gw + 511) // 512
        bufs = []
        wlds = []
        for hf in range(nhal):
            c0 = g0 + hf * 512
            cw = min(512, g0 + gw - c0)
            buf = half_idx[0] % 2
            half_idx[0] += 1
            bufs.append((buf, c0, cw))
            DMA("gpsimd", lambda e, c0=c0, cw=cw, buf=buf: e.dma_start(
                out=s_w[buf][:, 0:32 * 512].rearrange(
                    "p (i c) -> p i c", i=32)[:, :, 0:cw],
                in_=wsrc(wcat, c0, cw)),
                wait=[("TS", wbuf_free[buf])] + list(first_wait),
                inc=[("WLD", 16)])
            wlds.append(CNT["WLD"])
        is_conv = DS <= g0 < PROJ
        tiles = conv_order if is_conv else list(range(8))
        for t in tiles:
            nr = 128 if t < 8 else 3
            bank = t % 8
            ps = psA if bank < 4 else psB
            poff = (bank % 4) * 512
            for hj, (buf, c0, cw) in enumerate(bufs):
                sub = c0 - g0
                for i in range(32):
                    so = stat_off_fn(i, t)
                    fa = (hj == 0 and i == 0)
                    la = (hj == nhal - 1 and i == 31)
                    w = [("WLD", wlds[hj])]
                    if fa:
                        w.append(("VS", bank_free[bank]))
                    TE(lambda e, nr=nr, ps=ps, poff=poff, sub=sub, cw=cw,
                       so=so, buf=buf, i=i, fa=fa, la=la:
                       e.matmul(ps[0:nr, poff + sub:poff + sub + cw],
                                s_stat[:, so:so + nr],
                                s_w[buf][:, i * 512:i * 512 + cw],
                                start=fa, stop=la),
                       wait=w, inc=[("TS", 1)] if la else ())
            yield t, nr, ps, poff, bank, TS()
        for buf, _, _ in bufs:
            wbuf_free[buf] = TS()

    def evac_z(g0, t, ps, poff, bank, ts_mm):
        par = stg_cnt[0] % 2
        stg_cnt[0] += 1
        V(lambda e, t=t, ps=ps, poff=poff, par=par, g0=g0:
          e.scalar_tensor_tensor(s_stg[par][:, 0:512], ps[:, poff:poff + 512],
                                 s_rmsall[:, t:t + 1], s_mupz[:, g0:g0 + 512],
                                 OP.mult, OP.mult),
          wait=[("TS", ts_mm), ("ZUS", stg_free[par])])
        bank_free[bank] = VSn()
        vs = VSn()
        DMA("sync", lambda e, t=t, par=par, g0=g0: e.dma_start(
            out=zscr[128 * t:128 * (t + 1), g0:g0 + 512], in_=s_stg[par][:, 0:512]),
            wait=[("VS", vs)], inc=[("ZUS", 16)])
        stg_free[par] = CNT["ZUS"]

    def evac_conv(cc0, ccw, t, nr, ps, poff, bank, ts_mm, pos, mpar, mwl):
        par = pos % 2
        for k in range(KC):
            V(lambda e, k=k, ps=ps, poff=poff, ccw=ccw, nr=nr, t=t, par=par,
              mpar=mpar:
              e.scalar_tensor_tensor(
                  s_xt[k][par][0:nr, 0:ccw], ps[0:nr, poff:poff + ccw],
                  (s_rmsall[:, t:t + 1] if t < 8 else s_rmsall[0:3, 8:9]),
                  s_mws[mpar][0:nr, k * 512:k * 512 + ccw],
                  OP.mult, OP.mult),
              wait=([("TS", max(ts_mm, xt_free[par])), ("MWL", mwl)]
                    if k == 0 else []))
        vs_xt = VSn()
        bank_free[bank] = vs_xt
        if t == 8:
            return
        for k in range(KC):
            d = 3 - k
            lhs = s_identb[:, :] if d == 0 else SH(d - 1)
            fa = (k == 0)
            TE(lambda e, lhs=lhs, ps=ps, poff=poff, ccw=ccw, k=k, par=par,
               fa=fa: e.matmul(ps[:, poff:poff + ccw], lhs,
                               s_xt[k][par][:, 0:ccw], start=fa, stop=False),
               wait=[("VS", vs_xt)] if fa else [])
        for k in range(KC - 1):
            d = 3 - k
            lhs = SH(5 + d) if t == 0 else SH(2 + d)
            la = (k == KC - 2)
            TE(lambda e, lhs=lhs, ps=ps, poff=poff, ccw=ccw, k=k, par=par,
               la=la: e.matmul(ps[:, poff:poff + ccw], lhs,
                               s_xt[k][1 - par][:, 0:ccw], start=False, stop=la),
               wait=[], inc=[("TS", 1)] if la else ())
        ts_conv = TS()
        xt_free[1 - par] = ts_conv
        cpar = cstg_cnt[0] % 2
        cstg_cnt[0] += 1
        V(lambda e, ps=ps, poff=poff, ccw=ccw, cpar=cpar:
          e.tensor_copy(s_cstg[cpar][:, 0:ccw], ps[:, poff:poff + ccw]),
          wait=[("TS", ts_conv), ("CPL", cstg_free[cpar])])
        bank_free[bank] = VSn()
        vs = VSn()
        DMA("sync", lambda e, t=t, cc0=cc0, ccw=ccw, cpar=cpar: e.dma_start(
            out=convscr[128 * t:128 * (t + 1), cc0:cc0 + ccw],
            in_=s_cstg[cpar][:, 0:ccw]),
            wait=[("VS", vs)], inc=[("CPL", 16)])
        cstg_free[cpar] = CNT["CPL"]

    def evac_dt(t, ps, poff, ts_mm):
        V(lambda e, t=t, ps=ps, poff=poff: e.scalar_tensor_tensor(
            s_dtt[:, :], ps[:, poff + 256:poff + 320], s_rmsall[:, t:t + 1],
            s_mupdt[:, :], OP.mult, OP.mult), wait=[("TS", ts_mm)])
        V(lambda e: e.tensor_tensor(s_dtt[:, :], s_dtt[:, :], s_dtbrep[:, :],
                                    OP.add))
        vs_z = VSn()
        SC(lambda e: e.activation(s_dtt2[:, :], s_dtt[:, :], AF.Abs),
           wait=[("VS", vs_z)])
        SC(lambda e: e.activation(s_dtt3[:, :], s_dtt2[:, :], AF.Exp, scale=-1.0))
        SC(lambda e: e.activation(s_dtt2[:, :], s_dtt3[:, :], AF.Ln,
                                  bias=1.0, scale=1.0))
        SC(lambda e: e.activation(s_dtt3[:, :], s_dtt[:, :], AF.Relu))
        ss_sp = SSn()
        V(lambda e, t=t: e.tensor_add(s_dtsp[:, t * 64:(t + 1) * 64],
                                      s_dtt2[:, :], s_dtt3[:, :]),
          wait=[("SS", ss_sp)])
        V(lambda e, t=t: e.tensor_tensor(s_dtaall[:, t * 64:(t + 1) * 64],
                                         s_dtsp[:, t * 64:(t + 1) * 64],
                                         s_arep[:, :], OP.mult))

    def attn_stat_off(i, t):
        return i * TB + (3 + 128 * t if t < 8 else 0)

    conv_order = [8] + list(range(8))
    for k in range(KC):
        for p in range(2):
            V(lambda e, k=k, p=p: e.memset(s_xt[k][p][:, :], 0.0),
              wait=[("TS", TS())])
    first_wait = [("VS", vs_pln), ("MWS", mws_all)]
    mcnt = [0]
    for g in range(17):
        g0 = g * 512
        gw = 512 if g < 16 else 320
        is_conv = DS <= g0
        mpar = 0
        mwl = 0
        if is_conv:
            cc0 = g0 - DS
            ccw = min(gw, 512) if g < 16 else 256
            mpar = 0
            mcnt[0] += 1
            DMA("sync", lambda e, cc0=cc0, ccw=ccw, mpar=mpar: e.dma_start(
                out=s_mws[mpar][:, 0:KC * 512].rearrange(
                    "p (k c) -> p k c", k=KC)[:, :, 0:ccw],
                in_=mupwscr[:, :].rearrange(
                    "p (k c) -> p k c", k=KC)[:, :, cc0:cc0 + ccw]),
                wait=[("MWS", mws_all), ("VS", mws_free[mpar])],
                inc=[("MWL", 16)])
            mwl = CNT["MWL"]
        for t, nr, ps, poff, bank, ts_mm in proj_group(
                g0, gw, conv_order, attn_stat_off, first_wait):
            first_wait = []
            if not is_conv:
                evac_z(g0, t, ps, poff, bank, ts_mm)
            else:
                pos = conv_order.index(t)
                evac_conv(g0 - DS, ccw, t, nr, ps, poff, bank, ts_mm, pos,
                          mpar, mwl)
                if g == 16 and t != 8:
                    evac_dt(t, ps, poff, ts_mm)
        if is_conv:
            mws_free[mpar] = VSn()
        if g < 16:
            DMA("gpsimd", lambda e, g=g: e.dma_start(
                out=owscr[:, g * 256:(g + 1) * 256],
                in_=outwD[:, g * 256:(g + 1) * 256]),
                (), [("OCV", 16)])
    ocv_all = CNT["OCV"]
    ts_p1a = TS()
    cpl_all = CNT["CPL"]
    zus_p1a = CNT["ZUS"]

    DMA("sync", lambda e: e.dma_start(
        out=s_stat[:, 0:32 * TB].rearrange("p (i c) -> p i c", c=TB)[:, :, 0:T],
        in_=xnscr[:, :].rearrange("p (i c) -> p i c", c=T)),
        wait=[("TS", ts_p1a), ("XNR", xnr_pln)], inc=[("XNR", 16)])
    xnr_re = CNT["XNR"]

    def ffn_stat_off(i, t):
        return i * TB + 128 * t

    def evac_u(g0, t, ps, poff, bank, ts_mm):
        par = stg_cnt[0] % 2
        stg_cnt[0] += 1
        u0 = g0 - PROJ
        V(lambda e, ps=ps, poff=poff, par=par:
          e.tensor_copy(s_stg[par][:, 0:512], ps[:, poff:poff + 512]),
          wait=[("TS", ts_mm), ("ZUS", stg_free[par])])
        bank_free[bank] = VSn()
        vs = VSn()
        DMA("sync", lambda e, t=t, par=par, u0=u0: e.dma_start(
            out=uscr[128 * t:128 * (t + 1), u0:u0 + 512],
            in_=s_stg[par][:, 0:512]),
            wait=[("VS", vs)], inc=[("ZUS", 16)])
        stg_free[par] = CNT["ZUS"]

    first_wait = [("XNR", xnr_re)]
    for g in range(8):
        g0 = PROJ + g * 512
        for t, nr, ps, poff, bank, ts_mm in proj_group(
                g0, 512, None, ffn_stat_off, first_wait):
            first_wait = []
            evac_u(g0, t, ps, poff, bank, ts_mm)
    for t, nr, ps, poff, bank, ts_mm in proj_group(
            PROJ + 4096, 36, None, ffn_stat_off, [("XNR", xnr_re)]):
        V(lambda e, ps=ps, poff=poff, t=t:
          e.tensor_copy(s_zlora[:, t * 32:(t + 1) * 32], ps[:, poff:poff + 32]),
          wait=[("TS", ts_mm)])
        V(lambda e, ps=ps, poff=poff, t=t:
          e.tensor_copy(s_glog[:, t * 4:(t + 1) * 4], ps[:, poff + 32:poff + 36]))
        bank_free[bank] = VSn()
    ts_p1b = TS()
    zus_all = CNT["ZUS"]

    for t in range(8):
        sm = s_small
        V(lambda e, t=t: e.tensor_tensor(sm[:, 0:4], s_glog[:, 4 * t:4 * t + 4],
                                         s_ebrep[:, :], OP.add))
        V(lambda e: e.reduce_max(sm[:, 4:5], sm[:, 0:4], axis=AX))
        for eix in range(4):
            V(lambda e, eix=eix: e.tensor_scalar(
                sm[:, 5 + eix:6 + eix], sm[:, eix:eix + 1], sm[:, 4:5], None,
                OP.is_ge))
        V(lambda e: e.tensor_copy(sm[:, 9:10], sm[:, 5:6]))
        V(lambda e: e.tensor_scalar(sm[:, 13:14], sm[:, 9:10], -1.0, 1.0,
                                    OP.mult, OP.add))
        V(lambda e: e.tensor_tensor(sm[:, 10:11], sm[:, 6:7], sm[:, 13:14],
                                    OP.mult))
        V(lambda e: e.tensor_scalar(sm[:, 14:15], sm[:, 10:11], -1.0, 1.0,
                                    OP.mult, OP.add))
        V(lambda e: e.tensor_tensor(sm[:, 14:15], sm[:, 14:15], sm[:, 13:14],
                                    OP.mult))
        V(lambda e: e.tensor_tensor(sm[:, 11:12], sm[:, 7:8], sm[:, 14:15],
                                    OP.mult))
        V(lambda e: e.tensor_tensor(sm[:, 15:16], sm[:, 9:10], sm[:, 10:11],
                                    OP.add))
        V(lambda e: e.tensor_tensor(sm[:, 15:16], sm[:, 15:16], sm[:, 11:12],
                                    OP.add))
        V(lambda e: e.tensor_scalar(sm[:, 12:13], sm[:, 15:16], -1.0, 1.0,
                                    OP.mult, OP.add))
        V(lambda e, t=t: e.tensor_tensor(
            s_p2a[:, 0:32].rearrange("p (a b) -> p a b", a=4),
            s_zlora[:, 32 * t:32 * (t + 1)].rearrange("p (a b) -> p a b", a=4),
            sm[:, 9:13].unsqueeze(2).broadcast_to([128, 4, 8]), OP.mult))
        V(lambda e: e.tensor_copy(s_p2b[:, 0:32], s_p2a[:, 0:32]))
        vs_zt = VSn()
        TE(lambda e: e.transpose(psA[0:32, 0:64].bitcast(BF16)[:, 0:128],
                                 s_p2b[:, 0:32], s_identb[:, :]),
           wait=[("VS", max(vs_zt, bank_free[0]))], inc=[("TS", 1)])
        ts = TS()
        V(lambda e, t=t: e.tensor_copy(s_zlt[:, 128 * t:128 * (t + 1)],
                                       psA[0:32, 0:64].bitcast(BF16)[:, 0:128]),
          wait=[("TS", ts)])
        bank_free[0] = VSn()
    vs_p2 = VSn()
    esB.close()

    esC = contextlib.ExitStack()
    sbC = lambda name, shape, dt: esC.enter_context(nc.sbuf_tensor(name, shape, dt))
    esCx = contextlib.ExitStack()
    s_xc = esCx.enter_context(nc.sbuf_tensor("s_xc", [128, CONV], BF16))
    s_xhat = esCx.enter_context(nc.sbuf_tensor("s_xhat", [128, DS], BF16))
    xc_guard = {"v": vs_p2, "t": 0, "s": 0}

    def load_conv_tile(c):
        DMA("sync", lambda e, c=c: e.dma_start(
            out=s_xc[:, :], in_=convscr[128 * c:128 * (c + 1), :]),
            wait=[("SS", xc_guard["s"]), ("VS", xc_guard["v"]),
                  ("TS", xc_guard["t"]), ("CPL", cpl_all)],
            inc=[("XLD", 16)])
        xld = CNT["XLD"]
        SC(lambda e: e.activation(s_xc[:, :], s_xc[:, :], AF.Silu),
           wait=[("XLD", xld)])
        xc_guard["s"] = SSn()
        return SSn()

    for c in range(NT):
        ss_xc = load_conv_tile(c)
        TE(lambda e, c=c: e.matmul(psA[:, 0:64], s_ut[:, :],
                                   s_dtaall[:, c * 64:(c + 1) * 64],
                                   start=True, stop=True),
           wait=[("VS", bank_free[0])], inc=[("TS", 1)])
        ts_lam = TS()
        V(lambda e, c=c: e.tensor_copy(s_lam[:, c * 64:(c + 1) * 64],
                                       psA[:, 0:64]), wait=[("TS", ts_lam)])
        vs_lam = VSn()
        TE(lambda e, c=c: e.transpose(psA[0:64, 512:640],
                                      s_lam[:, c * 64:(c + 1) * 64],
                                      s_identf[:, :]),
           wait=[("VS", vs_lam)], inc=[("TS", 1)])
        ts_lfm = TS()
        V(lambda e, c=c: e.tensor_copy(s_lamfm[:, c * 128:(c + 1) * 128],
                                       psA[0:64, 512:640]), wait=[("TS", ts_lfm)])
        DMA("sync", lambda e, c=c: e.dma_start(
            out=s_row64[0:1, 320:384], in_=s_lam[127:128, c * 64:(c + 1) * 64]),
            wait=[("VS", vs_lam), ("SS", SSn()), ("TS", TS())],
            inc=[("ROW", 16)])
        roww = CNT["ROW"]
        SC(lambda e: e.activation(s_row64[0:1, 384:448], s_row64[0:1, 320:384],
                                  AF.Exp), wait=[("ROW", roww)])
        ss_gam = SSn()
        TE(lambda e: e.matmul(psA[:, 1024:1088], s_ones1[0:1, 0:128],
                              s_row64[0:1, 320:384], start=True, stop=True),
           wait=[("ROW", roww), ("VS", VSn())], inc=[("TS", 1)])
        ts_lend = TS()
        TE(lambda e: e.matmul(psA[:, 1536:1600], s_ones1[0:1, 0:128],
                              s_row64[0:1, 384:448], start=True, stop=True),
           wait=[("SS", ss_gam)], inc=[("TS", 1)])
        ts_gam = TS()
        V(lambda e, c=c: e.tensor_copy(s_gamrep[:, c * 64:(c + 1) * 64],
                                       psA[:, 1536:1600]), wait=[("TS", ts_gam)])
        V(lambda e, c=c: e.tensor_tensor(s_small[:, 0:64],
                                         psA[:, 1024:1088],
                                         s_lam[:, c * 64:(c + 1) * 64],
                                         OP.subtract),
          wait=[("TS", ts_lend)])
        vs_f1 = VSn()
        SC(lambda e: e.activation(s_small[:, 64:128], s_small[:, 0:64], AF.Exp),
           wait=[("VS", vs_f1)])
        SC(lambda e, c=c: e.activation(s_explam[:, c * 64:(c + 1) * 64],
                                       s_lam[:, c * 64:(c + 1) * 64], AF.Exp))
        ss_f = SSn()
        V(lambda e, c=c: e.tensor_tensor(s_ftm[:, c * 64:(c + 1) * 64],
                                         s_small[:, 64:128],
                                         s_dtsp[:, c * 64:(c + 1) * 64], OP.mult),
          wait=[("SS", ss_f)])
        V(lambda e, c=c: e.tensor_tensor(s_gtot[:, :], s_gtot[:, :],
                                         s_gamrep[:, c * 64:(c + 1) * 64],
                                         OP.mult))
        V(lambda e, c=c: e.tensor_tensor(
            s_xhat[:, :].rearrange("p (a b) -> p a b", a=64),
            s_xc[:, 0:DS].rearrange("p (a b) -> p a b", a=64),
            s_ftm[:, c * 64:(c + 1) * 64].unsqueeze(2).broadcast_to([128, 64, 64]),
            OP.mult), wait=[("SS", ss_xc)])
        vs_xhat = VSn()
        for j in range(8):
            ps = psA if j < 4 else psB
            poff = (j % 4) * 512
            TE(lambda e, ps=ps, poff=poff, j=j:
               e.matmul(ps[:, poff:poff + 512], s_xc[:, DS:DS + 128],
                        s_xhat[:, j * 512:(j + 1) * 512], start=True, stop=True),
               wait=[("VS", vs_xhat)] if j == 0 else [],
               inc=[("TS", 1)] if j == 7 else ())
        ts_state = TS()
        V(lambda e, c=c: e.tensor_tensor(
            s_hrun[:, :].rearrange("p (a b) -> p a b", a=64),
            s_hrun[:, :].rearrange("p (a b) -> p a b", a=64),
            s_gamrep[:, c * 64:(c + 1) * 64].unsqueeze(2).broadcast_to([128, 64, 64]),
            OP.mult), wait=[("TS", ts_state)])
        V(lambda e: e.tensor_tensor(s_hrun[:, 0:2048], s_hrun[:, 0:2048],
                                    psA[:, :], OP.add))
        V(lambda e: e.tensor_tensor(s_hrun[:, 2048:4096], s_hrun[:, 2048:4096],
                                    psB[:, :], OP.add))
        for b in range(8):
            bank_free[b] = VSn()
        xc_guard["v"] = VSn()
        xc_guard["t"] = ts_state

    esC1 = contextlib.ExitStack()
    sbC1 = lambda name, shape, dt: esC1.enter_context(nc.sbuf_tensor(name, shape, dt))
    s_gath = sbC1("s_gath", [128, DS + NH], F32)
    s_prod = sbC1("s_prod", [128, NH], F32)
    s_wj = sbC1("s_wj", [128, NH], F32)
    s_hw = sbC1("s_hw", [128, DS], F32)
    vs_chain = VSn()
    DMA("sync", lambda e: e.dma_start(out=ccin[:, 0:DS], in_=s_hrun[:, :]),
        wait=[("VS", vs_chain)], inc=[("PUB", 16)])
    DMA("sync", lambda e: e.dma_start(out=ccin[:, DS:DS + NH], in_=s_gtot[:, :]),
        wait=[("VS", vs_chain)], inc=[("PUB", 16)])
    DMA("gpsimd", lambda e: e.collective_compute(
        "AllGather", OP.bypass, replica_groups=[[0, 1, 2, 3], [4, 5, 6, 7]],
        ins=[ccin], outs=[ccout]),
        wait=[("PUB", CNT["PUB"]), ("GDM", CNT["GDM"])], inc=[("CC", 1)])
    cc_done = CNT["CC"]
    V(lambda e: e.memset(s_hrun[:, :], 0.0), wait=[("CC", cc_done)])
    V(lambda e: e.memset(s_prod[:, :], 1.0))
    for j in [2, 1, 0]:
        vs_gprev = VSn()
        DMA("sync", lambda e, j=j: e.dma_start(
            out=s_gath[:, :], in_=ccout[128 * j:128 * (j + 1), :]),
            wait=[("CC", cc_done), ("VS", vs_gprev), ("GTH", CNT["GTH"])],
            inc=[("GTH", 16)])
        gth = CNT["GTH"]
        V(lambda e, j=j: e.tensor_scalar(s_wj[:, :], s_prod[:, :],
                                         s_cmask[:, j:j + 1], None, OP.mult))
        V(lambda e, j=j: e.tensor_scalar(s_small[:, 0:64],
                                         s_gath[:, DS:DS + NH],
                                         s_gmask[:, j:j + 1],
                                         s_gmaski[:, j:j + 1], OP.mult, OP.add),
          wait=[("GTH", gth)])
        V(lambda e: e.tensor_tensor(s_prod[:, :], s_prod[:, :], s_small[:, 0:64],
                                    OP.mult))
        V(lambda e, j=j: e.tensor_tensor(
            s_hw[:, :].rearrange("p (a b) -> p a b", a=64),
            s_gath[:, 0:DS].rearrange("p (a b) -> p a b", a=64),
            s_wj[:, :].unsqueeze(2).broadcast_to([128, 64, 64]), OP.mult))
        V(lambda e: e.tensor_tensor(s_hrun[:, :], s_hrun[:, :], s_hw[:, :],
                                    OP.add))
    vs_hinit = VSn()
    esC1.close()

    esC2 = contextlib.ExitStack()
    sbC2 = lambda name, shape, dt: esC2.enter_context(nc.sbuf_tensor(name, shape, dt))
    s_E = sbC2("s_E", [128, 2048], F32)
    s_L = sbC2("s_L", [128, 4096], BF16)
    s_y = sbC2("s_y", [128, DS], F32)
    s_zrl = sbC2("s_zrl", [128, 2048], F32)
    s_hcast = sbC2("s_hcast", [128, 2048], BF16)
    s_bfm = sbC2("s_bfm", [128, 128], BF16)
    s_cfm = sbC2("s_cfm", [128, 128], BF16)
    s_gbf = sbC2("s_gbf", [128, 128], BF16)
    s_lamq = sbC2("s_lamq", [1, 2048], F32)
    s_dtb16 = sbC2("s_dtb16", [128, 32], BF16)
    zrl_free = {0: vs_hinit, 1: vs_hinit}
    gA = [max(bank_free.values())]
    gB = [gA[0]]

    for c in range(NT):
        ss_xc = load_conv_tile(c)
        DMA("sync", lambda e, c=c: e.dma_start(
            out=s_zrl[:, :],
            in_=zscr[128 * c:128 * (c + 1), 0:2048]),
            wait=[("VS", zrl_free[0]), ("ZUS", zus_p1a)], inc=[("ZRL", 16)])
        zrl0 = CNT["ZRL"]
        TE(lambda e: e.transpose(psA[:, 0:64].bitcast(BF16)[:, 0:128],
                                 s_xc[:, DS:DS + 128], s_identb[:, :]),
           wait=[("SS", ss_xc), ("VS", gA[0])], inc=[("TS", 1)])
        ts_b = TS()
        TE(lambda e: e.transpose(psA[:, 512:576].bitcast(BF16)[:, 0:128],
                                 s_xc[:, DS + 128:DS + 256], s_identb[:, :]),
           inc=[("TS", 1)])
        ts_c = TS()
        V(lambda e: e.tensor_copy(s_bfm[:, :], psA[:, 0:64].bitcast(BF16)[:, 0:128]),
          wait=[("TS", ts_b)])
        V(lambda e: e.tensor_copy(s_cfm[:, :], psA[:, 512:576].bitcast(BF16)[:, 0:128]),
          wait=[("TS", ts_c)])
        vs_bc = VSn()
        TE(lambda e: e.matmul(psA[:, 1024:1152], s_bfm[:, :], s_cfm[:, :],
                              start=True, stop=True),
           wait=[("VS", vs_bc)], inc=[("TS", 1)])
        ts_g = TS()
        V(lambda e: e.tensor_tensor(s_gbf[:, :], psA[:, 1024:1152], s_ut[:, :],
                                    OP.mult), wait=[("TS", ts_g)])
        gA[0] = VSn()
        for half in range(2):
            for qq in range(2):
                q = 2 * half + qq
                ps = psA if qq == 0 else psB
                DMA("sync", lambda e, c=c, q=q: e.dma_start(
                    out=s_lamq[0:1, :],
                    in_=s_lamfm[16 * q:16 * (q + 1), c * 128:(c + 1) * 128]),
                    wait=[("LFD", CNT["LFD"]), ("VS", vs_hinit),
                          ("TS", TS())], inc=[("LFD", 16)])
                lfd = CNT["LFD"]
                for j in range(4):
                    TE(lambda e, ps=ps, j=j: e.matmul(
                        ps[:, j * 512:(j + 1) * 512], s_ones1[0:1, 0:128],
                        s_lamq[0:1, j * 512:(j + 1) * 512],
                        start=True, stop=True),
                       wait=[("LFD", lfd),
                             ("VS", gA[0] if qq == 0 else gB[0])]
                       if j == 0 else [],
                       inc=[("TS", 1)] if j == 3 else ())
                ts_q = TS()
                V(lambda e, ps=ps, q=q, c=c: e.tensor_tensor(
                    s_E[:, :].rearrange("p (a b) -> p a b", a=16),
                    ps[:, :].rearrange("p (a b) -> p a b", a=16),
                    s_lam[:, c * 64 + q * 16:c * 64 + (q + 1) * 16]
                    .unsqueeze(2).broadcast_to([128, 16, 128]), OP.subtract),
                  wait=[("TS", ts_q)])
                V(lambda e: e.tensor_scalar(s_E[:, :], s_E[:, :], 0.0, None,
                                            OP.min))
                gs_e = 0
                vs_e = VSn()
                if qq == 0:
                    gA[0] = vs_e
                else:
                    gB[0] = vs_e
                SC(lambda e, qq=qq: e.activation(
                    s_L[:, qq * 2048:(qq + 1) * 2048], s_E[:, :], AF.Exp),
                   wait=[("VS", vs_e)])
            ss_l = SSn()
            V(lambda e, half=half, c=c: e.tensor_copy(
                s_dtb16[:, :], s_dtsp[:, c * 64 + half * 32:c * 64 + (half + 1) * 32]))
            V(lambda e: e.tensor_tensor(
                s_L[:, :].rearrange("p (a b) -> p a b", a=32),
                s_L[:, :].rearrange("p (a b) -> p a b", a=32),
                s_gbf[:, :].unsqueeze(1).broadcast_to([128, 32, 128]), OP.mult),
              wait=[("SS", ss_l)])
            V(lambda e: e.tensor_tensor(
                s_L[:, :].rearrange("p (a b) -> p a b", a=32),
                s_L[:, :].rearrange("p (a b) -> p a b", a=32),
                s_dtb16[:, :].unsqueeze(2).broadcast_to([128, 32, 128]), OP.mult))
            vs_stile = VSn()
            V(lambda e, half=half: e.tensor_copy(
                s_hcast[:, :], s_hrun[:, half * 2048:(half + 1) * 2048]))
            vs_hc = VSn()
            for j in range(4):
                TE(lambda e, j=j: e.matmul(
                    psA[:, j * 512:(j + 1) * 512], s_cfm[:, :],
                    s_hcast[:, j * 512:(j + 1) * 512], start=True, stop=True),
                   wait=[("VS", max(vs_hc, gA[0]))] if j == 0 else [],
                   inc=[("TS", 1)] if j == 3 else ())
            ts_inter = TS()
            for hh in range(32):
                h = half * 32 + hh
                TE(lambda e, hh=hh, h=h, first=(hh % 8 == 0), last=(hh % 8 == 7):
                   e.matmul(psB[:, hh * 64:(hh + 1) * 64],
                            s_L[:, hh * 128:(hh + 1) * 128],
                            s_xc[:, h * 64:(h + 1) * 64],
                            start=first, stop=last),
                   wait=[("VS", max(vs_stile, gB[0]))] if hh == 0 else [],
                   inc=[("TS", 1)] if hh == 31 else ())
            ts_intra = TS()
            V(lambda e, half=half, c=c: e.tensor_tensor(
                s_y[:, half * 2048:(half + 1) * 2048].rearrange(
                    "p (a b) -> p a b", a=32),
                psA[:, :].rearrange("p (a b) -> p a b", a=32),
                s_explam[:, c * 64 + half * 32:c * 64 + (half + 1) * 32]
                .unsqueeze(2).broadcast_to([128, 32, 64]), OP.mult),
              wait=[("TS", ts_inter)])
            V(lambda e, half=half: e.tensor_tensor(
                s_y[:, half * 2048:(half + 1) * 2048],
                s_y[:, half * 2048:(half + 1) * 2048], psB[:, :], OP.add),
              wait=[("TS", ts_intra)])
            gA[0] = VSn()
            gB[0] = VSn()
            V(lambda e, half=half, c=c: e.tensor_tensor(
                s_hcast[:, :].rearrange("p (a b) -> p a b", a=32),
                s_xc[:, half * 2048:(half + 1) * 2048].rearrange(
                    "p (a b) -> p a b", a=32),
                s_drep[:, half * 32:(half + 1) * 32].unsqueeze(2)
                .broadcast_to([128, 32, 64]), OP.mult))
            V(lambda e, half=half: e.tensor_tensor(
                s_y[:, half * 2048:(half + 1) * 2048],
                s_y[:, half * 2048:(half + 1) * 2048], s_hcast[:, :], OP.add))
        V(lambda e, c=c: e.tensor_tensor(
            s_xhat[:, :].rearrange("p (a b) -> p a b", a=64),
            s_xc[:, 0:DS].rearrange("p (a b) -> p a b", a=64),
            s_ftm[:, c * 64:(c + 1) * 64].unsqueeze(2).broadcast_to([128, 64, 64]),
            OP.mult))
        vs_xhat = VSn()
        for j in range(8):
            ps = psA if j < 4 else psB
            poff = (j % 4) * 512
            TE(lambda e, ps=ps, poff=poff, j=j:
               e.matmul(ps[:, poff:poff + 512], s_xc[:, DS:DS + 128],
                        s_xhat[:, j * 512:(j + 1) * 512], start=True, stop=True),
               wait=[("VS", max(vs_xhat, gA[0], gB[0]))] if j == 0 else [],
               inc=[("TS", 1)] if j == 7 else ())
        ts_state = TS()
        V(lambda e, c=c: e.tensor_tensor(
            s_hrun[:, :].rearrange("p (a b) -> p a b", a=64),
            s_hrun[:, :].rearrange("p (a b) -> p a b", a=64),
            s_gamrep[:, c * 64:(c + 1) * 64].unsqueeze(2).broadcast_to([128, 64, 64]),
            OP.mult), wait=[("TS", ts_state)])
        V(lambda e: e.tensor_tensor(s_hrun[:, 0:2048], s_hrun[:, 0:2048],
                                    psA[:, :], OP.add))
        V(lambda e: e.tensor_tensor(s_hrun[:, 2048:4096], s_hrun[:, 2048:4096],
                                    psB[:, :], OP.add))
        vs_hupd = VSn()
        SC(lambda e: e.activation(s_zrl[:, :], s_zrl[:, :], AF.Silu),
           wait=[("ZRL", zrl0)])
        ss_silu = SSn()
        V(lambda e: e.tensor_tensor(s_y[:, 0:2048], s_y[:, 0:2048],
                                    s_zrl[:, :], OP.mult),
          wait=[("SS", ss_silu)])
        vs_z0 = VSn()
        DMA("sync", lambda e, c=c: e.dma_start(
            out=s_zrl[:, :], in_=zscr[128 * c:128 * (c + 1), 2048:4096]),
            wait=[("VS", vs_z0), ("SS", SSn())], inc=[("ZRL", 16)])
        zrl1 = CNT["ZRL"]
        SC(lambda e: e.activation(s_zrl[:, :], s_zrl[:, :], AF.Silu),
           wait=[("ZRL", zrl1)])
        ss_silu2 = SSn()
        V(lambda e: e.tensor_tensor(s_y[:, 2048:4096], s_y[:, 2048:4096],
                                    s_zrl[:, :], OP.mult),
          wait=[("SS", ss_silu2)])
        zrl_free[0] = VSn()
        vs_gt = VSn()
        SC(lambda e: e.activation(s_E[:, :], s_y[:, 0:2048], AF.Square),
           wait=[("VS", vs_gt)])
        ss_sq1 = SSn()
        V(lambda e: e.reduce_sum(s_small[:, 100:101], s_E[:, :], axis=AX),
          wait=[("SS", ss_sq1)])
        vs_rs1 = VSn()
        SC(lambda e: e.activation(s_E[:, :], s_y[:, 2048:4096], AF.Square),
           wait=[("VS", vs_rs1)])
        ss_sq2 = SSn()
        V(lambda e: e.reduce_sum(s_small[:, 101:102], s_E[:, :], axis=AX),
          wait=[("SS", ss_sq2)])
        V(lambda e: e.tensor_tensor(s_small[:, 102:103], s_small[:, 100:101],
                                    s_small[:, 101:102], OP.add))
        vs_ss = VSn()
        SC(lambda e: e.activation(s_small[:, 103:104], s_small[:, 102:103],
                                  AF.Sqrt, bias=EPS, scale=1.0 / DS),
           wait=[("VS", vs_ss)])
        ss_sr = SSn()
        V(lambda e: e.reciprocal(s_small[:, 104:105], s_small[:, 103:104]),
          wait=[("SS", ss_sr)])
        V(lambda e: e.scalar_tensor_tensor(s_y[:, :], s_y[:, :],
                                           s_small[:, 104:105], s_normw[:, :],
                                           OP.mult, OP.mult))
        vs_yh = VSn()
        tb_free = {b: ("VS", vs_hupd) for b in range(8)}
        for j in range(32):
            bank = j % 8
            ps = psA if bank < 4 else psB
            poff = (bank % 4) * 512
            gsem, gval = tb_free[bank]
            TE(lambda e, j=j, ps=ps, poff=poff: e.transpose(
                ps[:, poff:poff + 128], s_y[:, j * 128:(j + 1) * 128],
                s_identf[:, :]),
               wait=[("VS", vs_yh), (gsem, gval)], inc=[("TS", 1)])
            ts = TS()
            SC(lambda e, j=j, c=c, ps=ps, poff=poff: e.copy(
                s_stat[:, j * TB + 128 * c:j * TB + 128 * (c + 1)],
                ps[:, poff:poff + 128]),
               wait=[("TS", ts)])
            tb_free[bank] = ("SS", SSn())
        V(lambda e: e.engine_nop(), wait=[("SS", SSn())])
        gA[0] = VSn()
        gB[0] = VSn()
        xc_guard["v"] = VSn()
        xc_guard["s"] = SSn()
        xc_guard["t"] = ts_state
    vs_p5 = VSn()
    ss_p5 = SSn()
    ts_p5 = TS()
    esC2.close()
    esCx.close()

    esC3 = contextlib.ExitStack()
    sbC3 = lambda name, shape, dt: esC3.enter_context(nc.sbuf_tensor(name, shape, dt))
    s_lorab = sbC3("s_lorab", [32, H], BF16)
    s_ow = [sbC3(f"s_ow{p}", [128, 32 * 256], BF16) for p in range(3)]
    s_url = [sbC3(f"s_url{p}", [128, 512], F32) for p in range(2)]
    s_xrl = [sbC3(f"s_xrl{p}", [128, 512], F32) for p in range(2)]
    s_ost = [sbC3(f"s_ost{p}", [128, 512], F32) for p in range(2)]
    DMA("gpsimd", lambda e: e.dma_start(out=s_lorab[:], in_=lorabD),
        wait=[("VS", vs_p5), ("SS", ss_p5)], inc=[("GDM", 16)])
    gdm_lb = CNT["GDM"]
    ow_free = {0: 0, 1: 0, 2: 0}
    url_free = {0: vs_p5, 1: vs_p5}
    xrl_free = {0: vs_p5, 1: vs_p5}
    ost_free = {0: 0, 1: 0}
    obank_free = {b: vs_p5 for b in range(8)}

    for og in range(8):
        owls = []
        obufs = []
        for hf in range(2):
            buf = (2 * og + hf) % 3
            obufs.append(buf)
            DMA("sync", lambda e, og=og, hf=hf, buf=buf: e.dma_start(
                out=s_ow[buf][:, 0:32 * 256].rearrange("p (i c) -> p i c", i=32),
                in_=wsrc(owscr, og * 512 + hf * 256, 256)),
                wait=[("TS", ow_free[buf]), ("VS", vs_p5), ("OCV", ocv_all)],
                inc=[("OWL", 16)])
            owls.append(CNT["OWL"])
        for t in range(8):
            rpar = t % 2
            DMA("sync", lambda e, og=og, t=t, rpar=rpar: e.dma_start(
                out=s_url[rpar][:, :],
                in_=uscr[128 * t:128 * (t + 1), og * 512:(og + 1) * 512]),
                wait=[("VS", url_free[rpar]), ("ZUS", zus_all)], inc=[("URL", 16)])
            DMA("sync", lambda e, og=og, t=t, rpar=rpar: e.dma_start(
                out=s_xrl[rpar][:, :],
                in_=xown[128 * t:128 * (t + 1), og * 512:(og + 1) * 512]),
                wait=[("VS", xrl_free[rpar])], inc=[("XRL", 16)])
            url = CNT["URL"]
            xrl = CNT["XRL"]
            bank = t % 8
            ps = psA if bank < 4 else psB
            poff = (bank % 4) * 512
            for hf in range(2):
                for i in range(32):
                    fa = (hf == 0 and i == 0)
                    w = []
                    if i == 0:
                        w.append(("OWL", owls[hf]))
                    if fa:
                        w += [("VS", obank_free[bank]), ("TS", ts_p5),
                              ("SS", ss_p5)]
                    TE(lambda e, ps=ps, poff=poff, i=i, t=t, hf=hf,
                       buf=obufs[hf], fa=fa:
                       e.matmul(ps[:, poff + hf * 256:poff + hf * 256 + 256],
                                s_stat[:, i * TB + 128 * t:i * TB + 128 * t + 128],
                                s_ow[buf][:, i * 256:(i + 1) * 256],
                                start=fa, stop=False),
                       wait=w)
            for hf in range(2):
                TE(lambda e, ps=ps, poff=poff, t=t, og=og, hf=hf: e.matmul(
                    ps[:, poff + hf * 256:poff + hf * 256 + 256],
                    s_zlt[:, 128 * t:128 * (t + 1)],
                    s_lorab[:, og * 512 + hf * 256:og * 512 + hf * 256 + 256],
                    start=False, stop=True),
                   wait=[("GDM", gdm_lb), ("VS", vs_p2)] if hf == 0 else [],
                   inc=[("TS", 1)] if hf == 1 else ())
            ts_o = TS()
            opar = (og * 8 + t) % 2
            V(lambda e, ps=ps, poff=poff, opar=opar, rpar=rpar:
              e.tensor_tensor(s_ost[opar][:, :], ps[:, poff:poff + 512],
                              s_url[rpar][:, :], OP.add),
              wait=[("TS", ts_o), ("URL", url), ("STO", ost_free[opar])])
            obank_free[bank] = VSn()
            url_free[rpar] = VSn()
            V(lambda e, opar=opar, rpar=rpar: e.tensor_tensor(
                s_ost[opar][:, :], s_ost[opar][:, :], s_xrl[rpar][:, :], OP.add),
              wait=[("XRL", xrl)])
            xrl_free[rpar] = VSn()
            vs_o = VSn()
            DMA("sync", lambda e, og=og, t=t, opar=opar: e.dma_start(
                out=oseg[128 * t:128 * (t + 1), og * 512:(og + 1) * 512],
                in_=s_ost[opar][:, :]),
                wait=[("VS", vs_o)], inc=[("STO", 16)])
            ost_free[opar] = CNT["STO"]
        for _b in obufs:
            ow_free[_b] = TS()

    WAITQ("gpsimd", [("STO", CNT["STO"])])
    esC3.close()
    esC.close()

    with nc.Block() as block:
        @block.sync
        def _(sync):
            for f in QL["sync"]:
                f(sync)

        @block.gpsimd
        def _(g):
            for f in QL["gpsimd"]:
                f(g)

        @block.tensor
        def _(t):
            for f in QL["tensor"]:
                f(t)

        @block.vector
        def _(v):
            for f in QL["vector"]:
                f(v)

        @block.scalar
        def _(s):
            for f in QL["scalar"]:
                f(s)

    es.close()
    _cache["nc"] = nc
    return nc


def _shift_consts():
    mats = []
    for d in (1, 2, 3):
        mats.append(np.eye(128, k=d))
    for d in (1, 2, 3):
        mats.append(np.eye(128, k=d - 128))
    for d in (1, 2, 3):
        m = np.zeros((128, 128))
        for t in range(d):
            m[3 + t - d, t] = 1.0
        mats.append(m)
    return np.ascontiguousarray(
        np.concatenate(mats, axis=1).astype(ml_dtypes.bfloat16))


def kernel(**inputs):
    x = np.asarray(inputs["L0_attn_hidden_states"], np.float32)
    inw = np.asarray(inputs["L0_attn_in_proj_w"], np.float32)
    wup = np.asarray(inputs["L0_ffn_W_up"], np.float32)
    loraA = np.asarray(inputs["L0_ffn_lora_A"], np.float32)
    loraB = np.asarray(inputs["L0_ffn_lora_B"], np.float32)
    gatew = np.asarray(inputs["L0_ffn_gate_w"], np.float32)
    bf = ml_dtypes.bfloat16

    wcat = np.ascontiguousarray(np.concatenate(
        [inw, wup.T, loraA.reshape(32, H).T, gatew.T], axis=1), dtype=np.float32)
    lnw = np.asarray(inputs["L0_attn_ln_w"], np.float32)

    shared = {
        "wcat": wcat,
        "outwD": np.ascontiguousarray(
            np.asarray(inputs["L0_attn_out_proj_w"], np.float32)),
        "lorabD": np.ascontiguousarray(
            loraB.transpose(0, 2, 1).reshape(32, H)),
        "convwT": np.ascontiguousarray(
            np.asarray(inputs["L0_attn_conv_w"], np.float32).T),
        "mupvD": np.asarray(inputs["L0_attn_mup_vector"], np.float32)[None, :],
        "lnwC": np.ascontiguousarray(lnw.reshape(32, 128).T),
        "normwD": np.asarray(inputs["L0_attn_norm_w"], np.float32)[None, :],
        "alogD": np.asarray(inputs["L0_attn_A_log"], np.float32)[None, :],
        "dtbD": np.asarray(inputs["L0_attn_dt_bias"], np.float32)[None, :],
        "dvecD": np.asarray(inputs["L0_attn_D"], np.float32)[None, :],
        "ebiasD": np.asarray(inputs["L0_ffn_expert_bias"], np.float32)[None, :],
        "identbD": np.eye(128, dtype=bf),
        "identfD": np.eye(128, dtype=np.float32),
        "utonesD": np.ascontiguousarray(np.triu(np.ones((128, 128), np.float32))),
        "shiftD": _shift_consts(),
        "ones1D": np.ones((1, 128), np.float32),
    }

    nc = build_program()
    in_maps = []
    for c in range(8):
        b, g = divmod(c, 4)
        own = x[b, g * T:(g + 1) * T]
        prev3 = (np.zeros((3, H), np.float32) if g == 0
                 else x[b, g * T - 3:g * T])
        xseg = np.concatenate([prev3, own], axis=0)
        cm = np.zeros((128, 3), np.float32)
        gm = np.zeros((128, 3), np.float32)
        for j in range(3):
            if j < g:
                cm[:, j] = 1.0
            if 1 <= j <= g - 1:
                gm[:, j] = 1.0
        m = dict(shared)
        m["xsegT"] = np.ascontiguousarray(xseg.T)
        m["xown"] = np.ascontiguousarray(own)
        m["cmaskD"] = cm
        m["gmaskD"] = gm
        m["gmaskiD"] = 1.0 - gm
        in_maps.append(m)

    res = run_bass_kernel_spmd(nc, in_maps, core_ids=list(range(8)))
    out = np.empty((2, 4096, H), np.float32)
    for c in range(8):
        b, g = divmod(c, 4)
        out[b, g * T:(g + 1) * T] = res.results[c]["oseg"]
    return out


# version 17
# speedup vs baseline: 1.3044x; 1.0348x over previous
import contextlib
import numpy as np
import ml_dtypes
import concourse.bass as bass
import concourse.mybir as mybir
from concourse.bass_utils import run_bass_kernel_spmd
from concourse.alu_op_type import AluOpType as OP

F32 = mybir.dt.float32
BF16 = mybir.dt.bfloat16
AF = mybir.ActivationFunctionType
AX = mybir.AxisListType.X

H = 4096
DS = 4096
SN = 128
NH = 64
HD = 64
KC = 4
CONV = DS + 2 * SN
PROJ = 2 * DS + 2 * SN + NH
NCOL = PROJ + H + 32 + 4
T = 1024
NT = 8
TB = 1027
EPS = 1e-5

_cache = {}


def build_program():
    if "nc" in _cache:
        return _cache["nc"]
    nc = bass.Bass("TRN2", target_bir_lowering=False)

    xsegT = nc.dram_tensor("xsegT", [H, TB], F32, kind="ExternalInput").ap()
    xown = nc.dram_tensor("xown", [T, H], F32, kind="ExternalInput").ap()
    wcat = nc.dram_tensor("wcat", [H, NCOL], F32, kind="ExternalInput").ap()
    outwD = nc.dram_tensor("outwD", [DS, H], F32, kind="ExternalInput").ap()
    lorabD = nc.dram_tensor("lorabD", [32, H], F32, kind="ExternalInput").ap()
    convwT = nc.dram_tensor("convwT", [KC, CONV], F32, kind="ExternalInput").ap()
    mupvD = nc.dram_tensor("mupvD", [1, PROJ], F32, kind="ExternalInput").ap()
    lnwC = nc.dram_tensor("lnwC", [128, 32], F32, kind="ExternalInput").ap()
    normwD = nc.dram_tensor("normwD", [1, DS], F32, kind="ExternalInput").ap()
    alogD = nc.dram_tensor("alogD", [1, NH], F32, kind="ExternalInput").ap()
    dtbD = nc.dram_tensor("dtbD", [1, NH], F32, kind="ExternalInput").ap()
    dvecD = nc.dram_tensor("dvecD", [1, NH], F32, kind="ExternalInput").ap()
    ebiasD = nc.dram_tensor("ebiasD", [1, 4], F32, kind="ExternalInput").ap()
    identbD = nc.dram_tensor("identbD", [128, 128], BF16, kind="ExternalInput").ap()
    identfD = nc.dram_tensor("identfD", [128, 128], F32, kind="ExternalInput").ap()
    utonesD = nc.dram_tensor("utonesD", [128, 128], F32, kind="ExternalInput").ap()
    shiftD = nc.dram_tensor("shiftD", [128, 128 * 9], BF16, kind="ExternalInput").ap()
    ones1D = nc.dram_tensor("ones1D", [1, 128], F32, kind="ExternalInput").ap()
    cmaskD = nc.dram_tensor("cmaskD", [128, 3], F32, kind="ExternalInput").ap()
    gmaskD = nc.dram_tensor("gmaskD", [128, 3], F32, kind="ExternalInput").ap()
    gmaskiD = nc.dram_tensor("gmaskiD", [128, 3], F32, kind="ExternalInput").ap()

    oseg = nc.dram_tensor("oseg", [T, H], F32, kind="ExternalOutput").ap()

    convscr = nc.dram_tensor("convscr", [T, CONV], BF16, kind="Internal").ap()
    xnscr = nc.dram_tensor("xnscr", [128, 32 * T], BF16, kind="Internal").ap()
    zscr = nc.dram_tensor("zscr", [T, DS], F32, kind="Internal").ap()
    uscr = nc.dram_tensor("uscr", [T, H], F32, kind="Internal").ap()
    mupwscr = nc.dram_tensor("mupwscr", [128, CONV * KC], BF16, kind="Internal").ap()
    wscr = nc.dram_tensor("wscr", [H, NCOL], BF16, kind="Internal").ap()
    owscr = nc.dram_tensor("owscr", [DS, H], BF16, kind="Internal").ap()
    ccin = nc.dram_tensor("ccin", [128, DS + NH], F32, kind="Internal").ap()
    ccout = nc.dram_tensor("ccout", [512, DS + NH], F32, kind="Internal").ap()

    es = contextlib.ExitStack()
    sbt = lambda name, shape, dt: es.enter_context(nc.sbuf_tensor(name, shape, dt))

    sem_names = ["VS", "SS", "TS", "LDC", "LDX", "LDY", "WCV", "OCV", "WLD", "XNR", "CPL", "ZUS",
                 "XLD", "ZRL", "URL", "XRL", "CC", "GDM", "STO", "LFD", "ROW",
                 "OWL", "GTH", "PUB", "MWS", "MWL", "GS"]
    SEM = {}
    CNT = {}
    QL = {"tensor": [], "vector": [], "scalar": [], "sync": [], "gpsimd": []}

    def emit(eng, fn, wait=(), inc=()):
        wait = tuple((s, v) for (s, v) in wait if v > 0)
        inc = tuple(inc)

        def step(e, _fn=fn, _w=wait, _i=inc):
            for s, v in _w:
                e.wait_ge(SEM[s], v)
            inst = _fn(e)
            for s, a in _i:
                inst = inst.then_inc(SEM[s], a)
        QL[eng].append(step)

    def bump(inc):
        for s, a in inc:
            CNT[s] += a

    def V(fn, wait=(), inc=()):
        w = [("VS", CNT["VS"])] + list(wait)
        i = [("VS", 1)] + list(inc)
        bump(i)
        emit("vector", fn, w, i)

    def SC(fn, wait=(), inc=()):
        w = [("SS", CNT["SS"])] + list(wait)
        i = [("SS", 1)] + list(inc)
        bump(i)
        emit("scalar", fn, w, i)

    def TE(fn, wait=(), inc=()):
        bump(inc)
        emit("tensor", fn, wait, inc)

    def DMA(eng, fn, wait=(), inc=()):
        bump(inc)
        emit(eng, fn, wait, inc)

    def G(fn, wait=(), inc=()):
        w = [("GS", CNT["GS"])] + list(wait)
        i = [("GS", 1)] + list(inc)
        bump(i)
        emit("gpsimd", fn, w, i)

    def WAITQ(eng, wait):
        emit(eng, lambda e: None, wait, ())

    s_identb = sbt("s_identb", [128, 128], BF16)
    s_identf = sbt("s_identf", [128, 128], F32)
    s_ut = sbt("s_ut", [128, 128], F32)
    s_shift = sbt("s_shift", [128, 128 * 9], BF16)
    s_ones1 = sbt("s_ones1", [1, 128], F32)
    s_onescol = sbt("s_onescol", [128, 1], F32)
    s_row64 = sbt("s_row64", [1, 448], F32)
    s_lnwc = sbt("s_lnwc", [128, 32], F32)
    s_dtbrep = sbt("s_dtbrep", [128, NH], F32)
    s_arep = sbt("s_arep", [128, NH], F32)
    s_drep = sbt("s_drep", [128, NH], F32)
    s_ebrep = sbt("s_ebrep", [128, 4], F32)
    s_cmask = sbt("s_cmask", [128, 3], F32)
    s_gmask = sbt("s_gmask", [128, 3], F32)
    s_gmaski = sbt("s_gmaski", [128, 3], F32)
    s_rmsall = sbt("s_rmsall", [128, 16], F32)
    s_dtsp = sbt("s_dtsp", [128, NH * NT], F32)
    s_dtaall = sbt("s_dtaall", [128, NH * NT], F32)
    s_lam = sbt("s_lam", [128, NH * NT], F32)
    s_lamfm = sbt("s_lamfm", [64, 128 * NT], F32)
    s_ftm = sbt("s_ftm", [128, NH * NT], F32)
    s_gamrep = sbt("s_gamrep", [128, NH * NT], F32)
    s_explam = sbt("s_explam", [128, NH * NT], F32)
    s_gtot = sbt("s_gtot", [128, NH], F32)
    s_glog = sbt("s_glog", [128, 4 * NT], F32)
    s_zlora = sbt("s_zlora", [128, 32 * NT], F32)
    s_zlt = sbt("s_zlt", [32, 128 * NT], BF16)
    s_hrun = sbt("s_hrun", [128, DS], F32)
    s_small = sbt("s_small", [128, 128], F32)
    s_normw = sbt("s_normw", [128, DS], BF16)
    s_stat = sbt("s_stat", [128, 32 * TB], BF16)
    s_mupdt = sbt("s_mupdt", [128, NH], F32)
    s_mupz = sbt("s_mupz", [128, DS], BF16)
    s_cnst = sbt("s_cnst", [128, 2], F32)

    psA = es.enter_context(nc.psum_tensor("psA", [128, 2048], F32))
    psB = es.enter_context(nc.psum_tensor("psB", [128, 2048], F32))

    for n in sem_names:
        SEM[n] = es.enter_context(nc.semaphore(n))
        CNT[n] = 0

    TS = lambda: CNT["TS"]
    VSn = lambda: CNT["VS"]
    SSn = lambda: CNT["SS"]

    esA = contextlib.ExitStack()
    sbA = lambda name, shape, dt: esA.enter_context(nc.sbuf_tensor(name, shape, dt))
    s_srow = sbA("s_srow", [1, 5200], F32)
    s_cwp = sbA("s_cwp", [1, 512], F32)
    s_cwq = sbA("s_cwq", [1, 512], F32)
    s_repp = sbA("s_repp", [128, 512], BF16)

    for dst, src in [(s_identb, identbD), (s_identf, identfD), (s_ut, utonesD),
                     (s_shift, shiftD), (s_ones1, ones1D), (s_cmask, cmaskD),
                     (s_gmask, gmaskD), (s_gmaski, gmaskiD), (s_lnwc, lnwC)]:
        DMA("sync", lambda e, d=dst, s=src: e.dma_start(out=d[:], in_=s),
            (), [("LDC", 16)])
    DMA("sync", lambda e: e.dma_start(out=s_row64[0:1, 0:64], in_=dtbD), (), [("LDC", 16)])
    DMA("sync", lambda e: e.dma_start(out=s_row64[0:1, 64:128], in_=alogD), (), [("LDC", 16)])
    DMA("sync", lambda e: e.dma_start(out=s_row64[0:1, 128:192], in_=dvecD), (), [("LDC", 16)])
    DMA("sync", lambda e: e.dma_start(out=s_row64[0:1, 192:196], in_=ebiasD), (), [("LDC", 16)])
    ldc_all = CNT["LDC"]
    WAITQ("vector", [("LDC", ldc_all)])
    WAITQ("scalar", [("LDC", ldc_all)])
    WAITQ("tensor", [("LDC", ldc_all)])

    V(lambda e: e.memset(s_onescol[:], 1.0))
    V(lambda e: e.memset(s_hrun[:], 0.0))
    V(lambda e: e.memset(s_gtot[:], 1.0))
    V(lambda e: e.memset(s_cnst[:, 0:1], EPS))
    V(lambda e: e.memset(s_cnst[:, 1:2], 1.0 / DS))
    nc.const_aps.register(F32, EPS, s_cnst[:, 0:1])
    nc.const_aps.register(F32, 1.0 / DS, s_cnst[:, 1:2])

    def k1rep(dst_ap, row_ap, width, ps, poff, sc_mul=None, wt=()):
        TE(lambda e, w=width: e.matmul(ps[:, poff:poff + w], s_ones1[0:1, 0:128],
                                       row_ap, start=True, stop=True),
           wait=[("VS", VSn()), ("SS", SSn())] + list(wt), inc=[("TS", 1)])
        ts = TS()
        if sc_mul is None:
            V(lambda e, w=width: e.tensor_copy(dst_ap, ps[:, poff:poff + w]),
              wait=[("TS", ts)])
        else:
            V(lambda e, w=width: e.tensor_scalar(dst_ap, ps[:, poff:poff + w],
                                                 sc_mul, None, OP.mult),
              wait=[("TS", ts)])

    SC(lambda e: e.activation(s_row64[0:1, 256:320], s_row64[0:1, 64:128], AF.Exp))
    k1rep(s_dtbrep[:, :], s_row64[0:1, 0:64], NH, psA, 0)
    k1rep(s_arep[:, :], s_row64[0:1, 256:320], NH, psA, 512, sc_mul=-1.0)
    k1rep(s_drep[:, :], s_row64[0:1, 128:192], NH, psA, 1024)
    k1rep(s_ebrep[:, :], s_row64[0:1, 192:196], 4, psA, 1536)

    mws_g = [0]
    for k in range(KC):
        for j in range(9):
            w = 512 if j < 8 else 256
            off = j * 512
            DMA("sync", lambda e, k=k, off=off, w=w: e.dma_start(
                out=s_cwp[0:1, 0:w], in_=convwT[k:k + 1, off:off + w]),
                wait=[("VS", mws_g[0])], inc=[("LDX", 16)])
            DMA("sync", lambda e, off=off, w=w: e.dma_start(
                out=s_cwq[0:1, 0:w], in_=mupvD[0:1, DS + off:DS + off + w]),
                wait=[("VS", mws_g[0])], inc=[("LDX", 16)])
            ldx = CNT["LDX"]
            V(lambda e, w=w: e.tensor_tensor(s_cwp[0:1, 0:w], s_cwp[0:1, 0:w],
                                             s_cwq[0:1, 0:w], OP.mult),
              wait=[("LDX", ldx)])
            k1rep(s_repp[:, 0:w], s_cwp[0:1, 0:w], w, psA, 0,
                  wt=[("MWS", CNT["MWS"])])
            vs = VSn()
            DMA("sync", lambda e, k=k, off=off, w=w: e.dma_start(
                out=mupwscr[:, k * CONV + off:k * CONV + off + w],
                in_=s_repp[:, 0:w]),
                wait=[("VS", vs)], inc=[("MWS", 16)])
            mws_g[0] = VSn()
    mws_all = CNT["MWS"]

    for j in range(8):
        DMA("sync", lambda e, j=j: e.dma_start(
            out=s_cwp[0:1, 0:512], in_=mupvD[0:1, j * 512:(j + 1) * 512]),
            wait=[("VS", mws_g[0])], inc=[("LDX", 16)])
        ldx = CNT["LDX"]
        WAITQ("tensor", [("LDX", ldx)])
        k1rep(s_mupz[:, j * 512:(j + 1) * 512], s_cwp[0:1, 0:512], 512,
              psA if j % 2 == 0 else psB, 0)
        mws_g[0] = VSn()
    DMA("sync", lambda e: e.dma_start(out=s_cwp[0:1, 0:64],
                                      in_=mupvD[0:1, PROJ - NH:PROJ]),
        wait=[("VS", mws_g[0])], inc=[("LDX", 16)])
    ldx = CNT["LDX"]
    WAITQ("tensor", [("LDX", ldx)])
    k1rep(s_mupdt[:, :], s_cwp[0:1, 0:64], NH, psA, 512)
    for j in range(8):
        DMA("sync", lambda e, j=j: e.dma_start(
            out=s_cwp[0:1, 0:512], in_=normwD[0:1, j * 512:(j + 1) * 512]),
            wait=[("VS", VSn())], inc=[("LDX", 16)])
        ldx = CNT["LDX"]
        WAITQ("tensor", [("LDX", ldx)])
        k1rep(s_normw[:, j * 512:(j + 1) * 512], s_cwp[0:1, 0:512], 512,
              psA if j % 2 == 0 else psB, 1024)

    s_xf = [sbA(f"s_xf{p}", [128, TB], F32) for p in range(4)]
    s_sqf = sbA("s_sqf", [128, TB], F32)
    s_mrep = sbA("s_mrep", [128, TB], F32)
    s_rrep = sbA("s_rrep", [128, TB], F32)
    s_xtmp = sbA("s_xtmp", [128, TB], F32)
    s_xnb = sbA("s_xnb", [128, TB], BF16)
    OM, OV, OR, OU, OT = 0, 1040, 2080, 3120, 4160

    xf_guard = [0, 0, 0, 0]
    sq_bufs = [s_sqf, s_xtmp]
    tsq_end = {}
    for i in range(32):
        qs = "LDX" if i % 2 == 0 else "LDY"
        DMA("sync" if i % 2 == 0 else "scalar", lambda e, i=i: e.dma_start(
            out=s_xf[i % 4][:], in_=xsegT[128 * i:128 * (i + 1), :]),
            wait=[("TS", xf_guard[i % 4])], inc=[(qs, 16)])
        ldx = CNT[qs]
        sq = sq_bufs[i % 2]
        SC(lambda e, i=i, sq=sq: e.activation(sq[:], s_xf[i % 4][:], AF.Square),
           wait=[(qs, ldx), ("TS", tsq_end.get(i - 2, 0))])
        ssq = SSn()
        for c in range(3):
            w = 512 if c < 2 else 3
            TE(lambda e, i=i, c=c, w=w: e.matmul(
                psA[0:1, c * 512:c * 512 + w], s_onescol[:, 0:1],
                s_xf[i % 4][:, c * 512:c * 512 + w], start=(i == 0), stop=(i == 31)),
               wait=[(qs, ldx), ("VS", VSn() if i == 0 else 0)],
               inc=[("TS", 1)])
            TE(lambda e, i=i, c=c, w=w, sq=sq: e.matmul(
                psB[0:1, c * 512:c * 512 + w], s_onescol[:, 0:1],
                sq[:, c * 512:c * 512 + w], start=(i == 0), stop=(i == 31)),
               wait=[("SS", ssq)], inc=[("TS", 1)])
        xf_guard[i % 4] = TS()
        tsq_end[i] = TS()
    ts_stats = TS()
    V(lambda e: e.tensor_scalar(s_srow[0:1, OM:OM + TB], psA[0:1, 0:TB],
                                1.0 / H, None, OP.mult), wait=[("TS", ts_stats)])
    V(lambda e: e.tensor_scalar(s_srow[0:1, OV:OV + TB], psB[0:1, 0:TB],
                                1.0 / H, None, OP.mult))
    V(lambda e: e.tensor_tensor(s_srow[0:1, OT:OT + TB], s_srow[0:1, OM:OM + TB],
                                s_srow[0:1, OM:OM + TB], OP.mult))
    V(lambda e: e.tensor_sub(s_srow[0:1, OV:OV + TB], s_srow[0:1, OV:OV + TB],
                             s_srow[0:1, OT:OT + TB]))
    V(lambda e: e.tensor_scalar(s_srow[0:1, OT:OT + TB], s_srow[0:1, OV:OV + TB],
                                EPS, None, OP.add))
    V(lambda e: e.reciprocal(s_srow[0:1, OR:OR + TB], s_srow[0:1, OT:OT + TB]))
    vs_r2 = VSn()
    SC(lambda e: e.activation(s_srow[0:1, OT:OT + TB], s_srow[0:1, OR:OR + TB],
                              AF.Sqrt), wait=[("VS", vs_r2)])
    ss_r = SSn()
    V(lambda e: e.tensor_tensor(s_srow[0:1, OU:OU + TB], s_srow[0:1, OV:OV + TB],
                                s_srow[0:1, OR:OR + TB], OP.mult),
      wait=[("SS", ss_r)])
    V(lambda e: e.tensor_copy(s_srow[0:1, OR:OR + TB], s_srow[0:1, OT:OT + TB]))
    vs_u = VSn()
    SC(lambda e: e.activation(s_srow[0:1, OT:OT + TB], s_srow[0:1, OU:OU + TB],
                              AF.Sqrt, bias=EPS, scale=1.0), wait=[("VS", vs_u)])
    ss_su = SSn()
    V(lambda e: e.reciprocal(s_srow[0:1, OU:OU + TB], s_srow[0:1, OT:OT + TB]),
      wait=[("SS", ss_su)])
    for c in range(3):
        w = 512 if c < 2 else 3
        k1rep(s_mrep[:, c * 512:c * 512 + w], s_srow[0:1, OM + c * 512:OM + c * 512 + w],
              w, psA, c * 512)
        k1rep(s_rrep[:, c * 512:c * 512 + w], s_srow[0:1, OR + c * 512:OR + c * 512 + w],
              w, psB, c * 512)
    for t in range(9):
        nr = 128 if t < 8 else 3
        so = OU + (3 + 128 * t if t < 8 else 0)
        TE(lambda e, nr=nr, so=so: e.transpose(
            psA[0:nr, 1536:1537], s_srow[0:1, so:so + nr], s_identf[0:1, 0:1]),
           wait=[("VS", VSn())], inc=[("TS", 1)])
        ts = TS()
        V(lambda e, t=t, nr=nr: e.tensor_copy(s_rmsall[0:nr, t:t + 1],
                                              psA[0:nr, 1536:1537]),
          wait=[("TS", ts)])

    vs_sub = {}
    xnr_g = [0]
    for i in range(32):
        qs = "LDX" if i % 2 == 0 else "LDY"
        DMA("sync" if i % 2 == 0 else "scalar", lambda e, i=i: e.dma_start(
            out=s_xf[i % 4][:], in_=xsegT[128 * i:128 * (i + 1), :]),
            wait=[("TS", xf_guard[i % 4]), ("VS", vs_sub.get(i - 4, 0))],
            inc=[(qs, 16)])
        ldx = CNT[qs]
        V(lambda e, i=i: e.tensor_tensor(s_xtmp[:], s_xf[i % 4][:], s_mrep[:],
                                         OP.subtract), wait=[(qs, ldx)])
        vs_sub[i] = VSn()
        V(lambda e: e.tensor_tensor(s_xnb[:], s_xtmp[:], s_rrep[:], OP.mult),
          wait=[("XNR", xnr_g[0])])
        vs_xn = VSn()
        DMA("sync", lambda e, i=i: e.dma_start(
            out=xnscr[:, i * T:(i + 1) * T], in_=s_xnb[:, 3:TB]),
            wait=[("VS", vs_xn)], inc=[("XNR", 16)])
        xnr_g[0] = CNT["XNR"]
        V(lambda e, i=i: e.tensor_scalar(
            s_stat[:, i * TB:(i + 1) * TB], s_xnb[:], s_lnwc[:, i:i + 1],
            None, OP.mult))
    vs_pln = VSn()
    xnr_pln = CNT["XNR"]
    esA.close()

    esB = contextlib.ExitStack()
    sbB = lambda name, shape, dt: esB.enter_context(nc.sbuf_tensor(name, shape, dt))
    s_w = [sbB(f"s_w{p}", [128, 32 * 512], BF16) for p in range(2)]
    s_stg = [sbB(f"s_stg{p}", [128, 512], F32) for p in range(2)]
    s_xt = [[sbB(f"s_xt{k}_{p}", [128, 512], BF16) for p in range(2)]
            for k in range(KC)]
    s_cstg = [sbB(f"s_cstg{p}", [128, 512], BF16) for p in range(2)]
    s_mws = [sbB("s_mws0", [128, KC * 512], BF16)]
    s_dtt = sbB("s_dtt", [128, NH], F32)
    s_dtt2 = sbB("s_dtt2", [128, NH], F32)
    s_dtt3 = sbB("s_dtt3", [128, NH], F32)
    s_p2a = sbB("s_p2a", [128, 64], F32)
    s_p2b = sbB("s_p2b", [128, 32], BF16)

    def wsrc(hbm, c0, cw):
        return hbm.rearrange("(i p) c -> p i c", p=128)[:, :, c0:c0 + cw]

    bank_free = {b: 0 for b in range(8)}
    wbuf_free = {0: 0, 1: 0}
    stg_free = {0: 0, 1: 0}
    stg_cnt = [0]
    cstg_free = {0: 0, 1: 0}
    cstg_cnt = [0]
    xt_free = {0: 0, 1: 0}
    mws_free = {0: vs_pln, 1: vs_pln}
    half_idx = [0]

    def SH(ix):
        return s_shift[:, ix * 128:(ix + 1) * 128]

    def proj_group(g0, gw, conv_order, stat_off_fn, first_wait, stat_wait=None):
        nhal = (gw + 511) // 512
        bufs = []
        wlds = []
        for hf in range(nhal):
            c0 = g0 + hf * 512
            cw = min(512, g0 + gw - c0)
            buf = half_idx[0] % 2
            half_idx[0] += 1
            bufs.append((buf, c0, cw))
            DMA("gpsimd", lambda e, c0=c0, cw=cw, buf=buf: e.dma_start(
                out=s_w[buf][:, 0:32 * 512].rearrange(
                    "p (i c) -> p i c", i=32)[:, :, 0:cw],
                in_=wsrc(wcat, c0, cw)),
                wait=[("TS", wbuf_free[buf])] + list(first_wait),
                inc=[("WLD", 16)])
            wlds.append(CNT["WLD"])
        is_conv = DS <= g0 < PROJ
        tiles = conv_order if is_conv else list(range(8))
        for t in tiles:
            nr = 128 if t < 8 else 3
            bank = t % 8
            ps = psA if bank < 4 else psB
            poff = (bank % 4) * 512
            for hj, (buf, c0, cw) in enumerate(bufs):
                sub = c0 - g0
                for i in range(32):
                    so = stat_off_fn(i, t)
                    fa = (hj == 0 and i == 0)
                    la = (hj == nhal - 1 and i == 31)
                    w = [("WLD", wlds[hj])]
                    if stat_wait is not None:
                        w += stat_wait(i)
                    if fa:
                        w.append(("VS", bank_free[bank]))
                    TE(lambda e, nr=nr, ps=ps, poff=poff, sub=sub, cw=cw,
                       so=so, buf=buf, i=i, fa=fa, la=la:
                       e.matmul(ps[0:nr, poff + sub:poff + sub + cw],
                                s_stat[:, so:so + nr],
                                s_w[buf][:, i * 512:i * 512 + cw],
                                start=fa, stop=la),
                       wait=w, inc=[("TS", 1)] if la else ())
            yield t, nr, ps, poff, bank, TS()
        for buf, _, _ in bufs:
            wbuf_free[buf] = TS()

    def evac_z(g0, t, ps, poff, bank, ts_mm):
        par = stg_cnt[0] % 2
        stg_cnt[0] += 1
        V(lambda e, t=t, ps=ps, poff=poff, par=par, g0=g0:
          e.scalar_tensor_tensor(s_stg[par][:, 0:512], ps[:, poff:poff + 512],
                                 s_rmsall[:, t:t + 1], s_mupz[:, g0:g0 + 512],
                                 OP.mult, OP.mult),
          wait=[("TS", ts_mm), ("ZUS", stg_free[par])])
        bank_free[bank] = VSn()
        vs = VSn()
        DMA("sync", lambda e, t=t, par=par, g0=g0: e.dma_start(
            out=zscr[128 * t:128 * (t + 1), g0:g0 + 512], in_=s_stg[par][:, 0:512]),
            wait=[("VS", vs)], inc=[("ZUS", 16)])
        stg_free[par] = CNT["ZUS"]

    def evac_conv(cc0, ccw, t, nr, ps, poff, bank, ts_mm, pos, mpar, mwl):
        par = pos % 2
        for k in range(KC):
            V(lambda e, k=k, ps=ps, poff=poff, ccw=ccw, nr=nr, t=t, par=par,
              mpar=mpar:
              e.scalar_tensor_tensor(
                  s_xt[k][par][0:nr, 0:ccw], ps[0:nr, poff:poff + ccw],
                  (s_rmsall[:, t:t + 1] if t < 8 else s_rmsall[0:3, 8:9]),
                  s_mws[mpar][0:nr, k * 512:k * 512 + ccw],
                  OP.mult, OP.mult),
              wait=([("TS", max(ts_mm, xt_free[par])), ("MWL", mwl)]
                    if k == 0 else []))
        vs_xt = VSn()
        bank_free[bank] = vs_xt
        if t == 8:
            return
        for k in range(KC):
            d = 3 - k
            lhs = s_identb[:, :] if d == 0 else SH(d - 1)
            fa = (k == 0)
            TE(lambda e, lhs=lhs, ps=ps, poff=poff, ccw=ccw, k=k, par=par,
               fa=fa: e.matmul(ps[:, poff:poff + ccw], lhs,
                               s_xt[k][par][:, 0:ccw], start=fa, stop=False),
               wait=[("VS", vs_xt)] if fa else [])
        for k in range(KC - 1):
            d = 3 - k
            lhs = SH(5 + d) if t == 0 else SH(2 + d)
            la = (k == KC - 2)
            TE(lambda e, lhs=lhs, ps=ps, poff=poff, ccw=ccw, k=k, par=par,
               la=la: e.matmul(ps[:, poff:poff + ccw], lhs,
                               s_xt[k][1 - par][:, 0:ccw], start=False, stop=la),
               wait=[], inc=[("TS", 1)] if la else ())
        ts_conv = TS()
        xt_free[1 - par] = ts_conv
        cpar = cstg_cnt[0] % 2
        cstg_cnt[0] += 1
        V(lambda e, ps=ps, poff=poff, ccw=ccw, cpar=cpar:
          e.tensor_copy(s_cstg[cpar][:, 0:ccw], ps[:, poff:poff + ccw]),
          wait=[("TS", ts_conv), ("CPL", cstg_free[cpar])])
        bank_free[bank] = VSn()
        vs = VSn()
        DMA("sync", lambda e, t=t, cc0=cc0, ccw=ccw, cpar=cpar: e.dma_start(
            out=convscr[128 * t:128 * (t + 1), cc0:cc0 + ccw],
            in_=s_cstg[cpar][:, 0:ccw]),
            wait=[("VS", vs)], inc=[("CPL", 16)])
        cstg_free[cpar] = CNT["CPL"]

    def evac_dt(t, ps, poff, ts_mm):
        V(lambda e, t=t, ps=ps, poff=poff: e.scalar_tensor_tensor(
            s_dtt[:, :], ps[:, poff + 256:poff + 320], s_rmsall[:, t:t + 1],
            s_mupdt[:, :], OP.mult, OP.mult), wait=[("TS", ts_mm)])
        V(lambda e: e.tensor_tensor(s_dtt[:, :], s_dtt[:, :], s_dtbrep[:, :],
                                    OP.add))
        vs_z = VSn()
        SC(lambda e: e.activation(s_dtt2[:, :], s_dtt[:, :], AF.Abs),
           wait=[("VS", vs_z)])
        SC(lambda e: e.activation(s_dtt3[:, :], s_dtt2[:, :], AF.Exp, scale=-1.0))
        SC(lambda e: e.activation(s_dtt2[:, :], s_dtt3[:, :], AF.Ln,
                                  bias=1.0, scale=1.0))
        SC(lambda e: e.activation(s_dtt3[:, :], s_dtt[:, :], AF.Relu))
        ss_sp = SSn()
        V(lambda e, t=t: e.tensor_add(s_dtsp[:, t * 64:(t + 1) * 64],
                                      s_dtt2[:, :], s_dtt3[:, :]),
          wait=[("SS", ss_sp)])
        V(lambda e, t=t: e.tensor_tensor(s_dtaall[:, t * 64:(t + 1) * 64],
                                         s_dtsp[:, t * 64:(t + 1) * 64],
                                         s_arep[:, :], OP.mult))

    def attn_stat_off(i, t):
        return i * TB + (3 + 128 * t if t < 8 else 0)

    conv_order = [8] + list(range(8))
    for k in range(KC):
        for p in range(2):
            V(lambda e, k=k, p=p: e.memset(s_xt[k][p][:, :], 0.0),
              wait=[("TS", TS())])
    first_wait = [("VS", vs_pln), ("MWS", mws_all)]
    mcnt = [0]
    for g in range(17):
        g0 = g * 512
        gw = 512 if g < 16 else 320
        is_conv = DS <= g0
        mpar = 0
        mwl = 0
        if is_conv:
            cc0 = g0 - DS
            ccw = min(gw, 512) if g < 16 else 256
            mpar = 0
            mcnt[0] += 1
            DMA("sync", lambda e, cc0=cc0, ccw=ccw, mpar=mpar: e.dma_start(
                out=s_mws[mpar][:, 0:KC * 512].rearrange(
                    "p (k c) -> p k c", k=KC)[:, :, 0:ccw],
                in_=mupwscr[:, :].rearrange(
                    "p (k c) -> p k c", k=KC)[:, :, cc0:cc0 + ccw]),
                wait=[("MWS", mws_all), ("VS", mws_free[mpar])],
                inc=[("MWL", 16)])
            mwl = CNT["MWL"]
        for t, nr, ps, poff, bank, ts_mm in proj_group(
                g0, gw, conv_order, attn_stat_off, first_wait):
            first_wait = []
            if not is_conv:
                evac_z(g0, t, ps, poff, bank, ts_mm)
            else:
                pos = conv_order.index(t)
                evac_conv(g0 - DS, ccw, t, nr, ps, poff, bank, ts_mm, pos,
                          mpar, mwl)
                if g == 16 and t != 8:
                    evac_dt(t, ps, poff, ts_mm)
        if is_conv:
            mws_free[mpar] = VSn()
        if g < 16:
            DMA("gpsimd", lambda e, g=g: e.dma_start(
                out=owscr[:, g * 256:(g + 1) * 256],
                in_=outwD[:, g * 256:(g + 1) * 256]),
                (), [("OCV", 16)])
    ocv_all = CNT["OCV"]
    ts_p1a = TS()
    cpl_all = CNT["CPL"]
    zus_p1a = CNT["ZUS"]

    xnr_need = [0] * 8
    xld_need = [0] * 8
    for cch in range(8):
        DMA("sync" if cch % 2 == 0 else "scalar", lambda e, cch=cch: e.dma_start(
            out=s_stat[:, 0:32 * TB].rearrange(
                "p (i c) -> p i c", c=TB)[:, 4 * cch:4 * cch + 4, 0:T],
            in_=xnscr[:, :].rearrange(
                "p (i c) -> p i c", c=T)[:, 4 * cch:4 * cch + 4, :]),
            wait=[("TS", ts_p1a), ("XNR", xnr_pln)],
            inc=[("XNR" if cch % 2 == 0 else "WCV", 16)])
        for cj in range(cch, 8):
            if cch % 2 == 0:
                xnr_need[cj] = CNT["XNR"]
            else:
                xld_need[cj] = CNT["WCV"]

    def ffn_stat_off(i, t):
        return i * TB + 128 * t

    def evac_u(g0, t, ps, poff, bank, ts_mm):
        par = stg_cnt[0] % 2
        stg_cnt[0] += 1
        u0 = g0 - PROJ
        V(lambda e, ps=ps, poff=poff, par=par:
          e.tensor_copy(s_stg[par][:, 0:512], ps[:, poff:poff + 512]),
          wait=[("TS", ts_mm), ("ZUS", stg_free[par])])
        bank_free[bank] = VSn()
        vs = VSn()
        DMA("sync", lambda e, t=t, par=par, u0=u0: e.dma_start(
            out=uscr[128 * t:128 * (t + 1), u0:u0 + 512],
            in_=s_stg[par][:, 0:512]),
            wait=[("VS", vs)], inc=[("ZUS", 16)])
        stg_free[par] = CNT["ZUS"]

    sw0 = (lambda i: [("XNR", xnr_need[i // 4]), ("WCV", xld_need[i // 4])])
    for g in range(8):
        g0 = PROJ + g * 512
        for t, nr, ps, poff, bank, ts_mm in proj_group(
                g0, 512, None, ffn_stat_off, [],
                stat_wait=sw0 if g == 0 else None):
            evac_u(g0, t, ps, poff, bank, ts_mm)
    for t, nr, ps, poff, bank, ts_mm in proj_group(
            PROJ + 4096, 36, None, ffn_stat_off, []):
        V(lambda e, ps=ps, poff=poff, t=t:
          e.tensor_copy(s_zlora[:, t * 32:(t + 1) * 32], ps[:, poff:poff + 32]),
          wait=[("TS", ts_mm)])
        V(lambda e, ps=ps, poff=poff, t=t:
          e.tensor_copy(s_glog[:, t * 4:(t + 1) * 4], ps[:, poff + 32:poff + 36]))
        bank_free[bank] = VSn()
    ts_p1b = TS()
    zus_all = CNT["ZUS"]

    for t in range(8):
        sm = s_small
        V(lambda e, t=t: e.tensor_tensor(sm[:, 0:4], s_glog[:, 4 * t:4 * t + 4],
                                         s_ebrep[:, :], OP.add))
        V(lambda e: e.reduce_max(sm[:, 4:5], sm[:, 0:4], axis=AX))
        for eix in range(4):
            V(lambda e, eix=eix: e.tensor_scalar(
                sm[:, 5 + eix:6 + eix], sm[:, eix:eix + 1], sm[:, 4:5], None,
                OP.is_ge))
        V(lambda e: e.tensor_copy(sm[:, 9:10], sm[:, 5:6]))
        V(lambda e: e.tensor_scalar(sm[:, 13:14], sm[:, 9:10], -1.0, 1.0,
                                    OP.mult, OP.add))
        V(lambda e: e.tensor_tensor(sm[:, 10:11], sm[:, 6:7], sm[:, 13:14],
                                    OP.mult))
        V(lambda e: e.tensor_scalar(sm[:, 14:15], sm[:, 10:11], -1.0, 1.0,
                                    OP.mult, OP.add))
        V(lambda e: e.tensor_tensor(sm[:, 14:15], sm[:, 14:15], sm[:, 13:14],
                                    OP.mult))
        V(lambda e: e.tensor_tensor(sm[:, 11:12], sm[:, 7:8], sm[:, 14:15],
                                    OP.mult))
        V(lambda e: e.tensor_tensor(sm[:, 15:16], sm[:, 9:10], sm[:, 10:11],
                                    OP.add))
        V(lambda e: e.tensor_tensor(sm[:, 15:16], sm[:, 15:16], sm[:, 11:12],
                                    OP.add))
        V(lambda e: e.tensor_scalar(sm[:, 12:13], sm[:, 15:16], -1.0, 1.0,
                                    OP.mult, OP.add))
        V(lambda e, t=t: e.tensor_tensor(
            s_p2a[:, 0:32].rearrange("p (a b) -> p a b", a=4),
            s_zlora[:, 32 * t:32 * (t + 1)].rearrange("p (a b) -> p a b", a=4),
            sm[:, 9:13].unsqueeze(2).broadcast_to([128, 4, 8]), OP.mult))
        V(lambda e: e.tensor_copy(s_p2b[:, 0:32], s_p2a[:, 0:32]))
        vs_zt = VSn()
        TE(lambda e: e.transpose(psA[0:32, 0:64].bitcast(BF16)[:, 0:128],
                                 s_p2b[:, 0:32], s_identb[:, :]),
           wait=[("VS", max(vs_zt, bank_free[0]))], inc=[("TS", 1)])
        ts = TS()
        V(lambda e, t=t: e.tensor_copy(s_zlt[:, 128 * t:128 * (t + 1)],
                                       psA[0:32, 0:64].bitcast(BF16)[:, 0:128]),
          wait=[("TS", ts)])
        bank_free[0] = VSn()
    vs_p2 = VSn()
    esB.close()

    esC = contextlib.ExitStack()
    sbC = lambda name, shape, dt: esC.enter_context(nc.sbuf_tensor(name, shape, dt))
    esCx = contextlib.ExitStack()
    s_xc = esCx.enter_context(nc.sbuf_tensor("s_xc", [128, CONV], BF16))
    s_xhat = esCx.enter_context(nc.sbuf_tensor("s_xhat", [128, DS], BF16))
    xc_guard = {"v": vs_p2, "t": 0, "s": 0}

    def load_conv_tile(c):
        DMA("sync", lambda e, c=c: e.dma_start(
            out=s_xc[:, :], in_=convscr[128 * c:128 * (c + 1), :]),
            wait=[("SS", xc_guard["s"]), ("VS", xc_guard["v"]),
                  ("TS", xc_guard["t"]), ("CPL", cpl_all)],
            inc=[("XLD", 16)])
        xld = CNT["XLD"]
        SC(lambda e: e.activation(s_xc[:, :], s_xc[:, :], AF.Silu),
           wait=[("XLD", xld)])
        xc_guard["s"] = SSn()
        return SSn()

    for c in range(NT):
        ss_xc = load_conv_tile(c)
        TE(lambda e, c=c: e.matmul(psA[:, 0:64], s_ut[:, :],
                                   s_dtaall[:, c * 64:(c + 1) * 64],
                                   start=True, stop=True),
           wait=[("VS", bank_free[0])], inc=[("TS", 1)])
        ts_lam = TS()
        V(lambda e, c=c: e.tensor_copy(s_lam[:, c * 64:(c + 1) * 64],
                                       psA[:, 0:64]), wait=[("TS", ts_lam)])
        vs_lam = VSn()
        TE(lambda e, c=c: e.transpose(psA[0:64, 512:640],
                                      s_lam[:, c * 64:(c + 1) * 64],
                                      s_identf[:, :]),
           wait=[("VS", vs_lam)], inc=[("TS", 1)])
        ts_lfm = TS()
        V(lambda e, c=c: e.tensor_copy(s_lamfm[:, c * 128:(c + 1) * 128],
                                       psA[0:64, 512:640]), wait=[("TS", ts_lfm)])
        DMA("sync", lambda e, c=c: e.dma_start(
            out=s_row64[0:1, 320:384], in_=s_lam[127:128, c * 64:(c + 1) * 64]),
            wait=[("VS", vs_lam), ("SS", SSn()), ("TS", TS())],
            inc=[("ROW", 16)])
        roww = CNT["ROW"]
        SC(lambda e: e.activation(s_row64[0:1, 384:448], s_row64[0:1, 320:384],
                                  AF.Exp), wait=[("ROW", roww)])
        ss_gam = SSn()
        TE(lambda e: e.matmul(psA[:, 1024:1088], s_ones1[0:1, 0:128],
                              s_row64[0:1, 320:384], start=True, stop=True),
           wait=[("ROW", roww), ("VS", VSn())], inc=[("TS", 1)])
        ts_lend = TS()
        TE(lambda e: e.matmul(psA[:, 1536:1600], s_ones1[0:1, 0:128],
                              s_row64[0:1, 384:448], start=True, stop=True),
           wait=[("SS", ss_gam)], inc=[("TS", 1)])
        ts_gam = TS()
        V(lambda e, c=c: e.tensor_copy(s_gamrep[:, c * 64:(c + 1) * 64],
                                       psA[:, 1536:1600]), wait=[("TS", ts_gam)])
        V(lambda e, c=c: e.tensor_tensor(s_small[:, 0:64],
                                         psA[:, 1024:1088],
                                         s_lam[:, c * 64:(c + 1) * 64],
                                         OP.subtract),
          wait=[("TS", ts_lend)])
        vs_f1 = VSn()
        SC(lambda e: e.activation(s_small[:, 64:128], s_small[:, 0:64], AF.Exp),
           wait=[("VS", vs_f1)])
        SC(lambda e, c=c: e.activation(s_explam[:, c * 64:(c + 1) * 64],
                                       s_lam[:, c * 64:(c + 1) * 64], AF.Exp))
        ss_f = SSn()
        V(lambda e, c=c: e.tensor_tensor(s_ftm[:, c * 64:(c + 1) * 64],
                                         s_small[:, 64:128],
                                         s_dtsp[:, c * 64:(c + 1) * 64], OP.mult),
          wait=[("SS", ss_f)])
        V(lambda e, c=c: e.tensor_tensor(s_gtot[:, :], s_gtot[:, :],
                                         s_gamrep[:, c * 64:(c + 1) * 64],
                                         OP.mult))
        V(lambda e, c=c: e.tensor_tensor(
            s_xhat[:, :].rearrange("p (a b) -> p a b", a=64),
            s_xc[:, 0:DS].rearrange("p (a b) -> p a b", a=64),
            s_ftm[:, c * 64:(c + 1) * 64].unsqueeze(2).broadcast_to([128, 64, 64]),
            OP.mult), wait=[("SS", ss_xc)])
        vs_xhat = VSn()
        for j in range(8):
            ps = psA if j < 4 else psB
            poff = (j % 4) * 512
            TE(lambda e, ps=ps, poff=poff, j=j:
               e.matmul(ps[:, poff:poff + 512], s_xc[:, DS:DS + 128],
                        s_xhat[:, j * 512:(j + 1) * 512], start=True, stop=True),
               wait=[("VS", vs_xhat)] if j == 0 else [],
               inc=[("TS", 1)] if j == 7 else ())
        ts_state = TS()
        V(lambda e, c=c: e.tensor_tensor(
            s_hrun[:, :].rearrange("p (a b) -> p a b", a=64),
            s_hrun[:, :].rearrange("p (a b) -> p a b", a=64),
            s_gamrep[:, c * 64:(c + 1) * 64].unsqueeze(2).broadcast_to([128, 64, 64]),
            OP.mult), wait=[("TS", ts_state)])
        V(lambda e: e.tensor_tensor(s_hrun[:, 0:2048], s_hrun[:, 0:2048],
                                    psA[:, :], OP.add))
        V(lambda e: e.tensor_tensor(s_hrun[:, 2048:4096], s_hrun[:, 2048:4096],
                                    psB[:, :], OP.add))
        for b in range(8):
            bank_free[b] = VSn()
        xc_guard["v"] = VSn()
        xc_guard["t"] = ts_state

    esC1 = contextlib.ExitStack()
    sbC1 = lambda name, shape, dt: esC1.enter_context(nc.sbuf_tensor(name, shape, dt))
    s_gath = sbC1("s_gath", [128, DS + NH], F32)
    s_prod = sbC1("s_prod", [128, NH], F32)
    s_wj = sbC1("s_wj", [128, NH], F32)
    s_hw = sbC1("s_hw", [128, DS], F32)
    vs_chain = VSn()
    DMA("sync", lambda e: e.dma_start(out=ccin[:, 0:DS], in_=s_hrun[:, :]),
        wait=[("VS", vs_chain)], inc=[("PUB", 16)])
    DMA("sync", lambda e: e.dma_start(out=ccin[:, DS:DS + NH], in_=s_gtot[:, :]),
        wait=[("VS", vs_chain)], inc=[("PUB", 16)])
    DMA("gpsimd", lambda e: e.collective_compute(
        "AllGather", OP.bypass, replica_groups=[[0, 1, 2, 3], [4, 5, 6, 7]],
        ins=[ccin], outs=[ccout]),
        wait=[("PUB", CNT["PUB"]), ("GDM", CNT["GDM"])], inc=[("CC", 1)])
    cc_done = CNT["CC"]
    V(lambda e: e.memset(s_hrun[:, :], 0.0), wait=[("CC", cc_done)])
    V(lambda e: e.memset(s_prod[:, :], 1.0))
    for j in [2, 1, 0]:
        vs_gprev = VSn()
        DMA("sync", lambda e, j=j: e.dma_start(
            out=s_gath[:, :], in_=ccout[128 * j:128 * (j + 1), :]),
            wait=[("CC", cc_done), ("VS", vs_gprev), ("GTH", CNT["GTH"])],
            inc=[("GTH", 16)])
        gth = CNT["GTH"]
        V(lambda e, j=j: e.tensor_scalar(s_wj[:, :], s_prod[:, :],
                                         s_cmask[:, j:j + 1], None, OP.mult))
        V(lambda e, j=j: e.tensor_scalar(s_small[:, 0:64],
                                         s_gath[:, DS:DS + NH],
                                         s_gmask[:, j:j + 1],
                                         s_gmaski[:, j:j + 1], OP.mult, OP.add),
          wait=[("GTH", gth)])
        V(lambda e: e.tensor_tensor(s_prod[:, :], s_prod[:, :], s_small[:, 0:64],
                                    OP.mult))
        V(lambda e, j=j: e.tensor_tensor(
            s_hw[:, :].rearrange("p (a b) -> p a b", a=64),
            s_gath[:, 0:DS].rearrange("p (a b) -> p a b", a=64),
            s_wj[:, :].unsqueeze(2).broadcast_to([128, 64, 64]), OP.mult))
        V(lambda e: e.tensor_tensor(s_hrun[:, :], s_hrun[:, :], s_hw[:, :],
                                    OP.add))
    vs_hinit = VSn()
    esC1.close()

    esC2 = contextlib.ExitStack()
    sbC2 = lambda name, shape, dt: esC2.enter_context(nc.sbuf_tensor(name, shape, dt))
    s_E = sbC2("s_E", [128, 2048], F32)
    s_L = sbC2("s_L", [128, 4096], BF16)
    s_y = sbC2("s_y", [128, DS], F32)
    s_zrl = sbC2("s_zrl", [128, 2048], F32)
    s_hcast = sbC2("s_hcast", [128, 2048], BF16)
    s_bfm = sbC2("s_bfm", [128, 128], BF16)
    s_cfm = sbC2("s_cfm", [128, 128], BF16)
    s_gbf = sbC2("s_gbf", [128, 128], BF16)
    s_lamq = sbC2("s_lamq", [1, 2048], F32)
    s_dtb16 = sbC2("s_dtb16", [128, 32], BF16)
    zrl_free = {0: vs_hinit, 1: vs_hinit}
    gA = [max(bank_free.values())]
    gB = [gA[0]]

    for c in range(NT):
        ss_xc = load_conv_tile(c)
        DMA("sync", lambda e, c=c: e.dma_start(
            out=s_zrl[:, :],
            in_=zscr[128 * c:128 * (c + 1), 0:2048]),
            wait=[("VS", zrl_free[0]), ("ZUS", zus_p1a)], inc=[("ZRL", 16)])
        zrl0 = CNT["ZRL"]
        TE(lambda e: e.transpose(psA[:, 0:64].bitcast(BF16)[:, 0:128],
                                 s_xc[:, DS:DS + 128], s_identb[:, :]),
           wait=[("SS", ss_xc), ("VS", gA[0])], inc=[("TS", 1)])
        ts_b = TS()
        TE(lambda e: e.transpose(psA[:, 512:576].bitcast(BF16)[:, 0:128],
                                 s_xc[:, DS + 128:DS + 256], s_identb[:, :]),
           inc=[("TS", 1)])
        ts_c = TS()
        V(lambda e: e.tensor_copy(s_bfm[:, :], psA[:, 0:64].bitcast(BF16)[:, 0:128]),
          wait=[("TS", ts_b)])
        V(lambda e: e.tensor_copy(s_cfm[:, :], psA[:, 512:576].bitcast(BF16)[:, 0:128]),
          wait=[("TS", ts_c)])
        vs_bc = VSn()
        TE(lambda e: e.matmul(psA[:, 1024:1152], s_bfm[:, :], s_cfm[:, :],
                              start=True, stop=True),
           wait=[("VS", vs_bc)], inc=[("TS", 1)])
        ts_g = TS()
        V(lambda e: e.tensor_tensor(s_gbf[:, :], psA[:, 1024:1152], s_ut[:, :],
                                    OP.mult), wait=[("TS", ts_g)])
        gA[0] = VSn()
        for half in range(2):
            for qq in range(2):
                q = 2 * half + qq
                ps = psA if qq == 0 else psB
                DMA("sync", lambda e, c=c, q=q: e.dma_start(
                    out=s_lamq[0:1, :],
                    in_=s_lamfm[16 * q:16 * (q + 1), c * 128:(c + 1) * 128]),
                    wait=[("LFD", CNT["LFD"]), ("VS", vs_hinit),
                          ("TS", TS())], inc=[("LFD", 16)])
                lfd = CNT["LFD"]
                for j in range(4):
                    TE(lambda e, ps=ps, j=j: e.matmul(
                        ps[:, j * 512:(j + 1) * 512], s_ones1[0:1, 0:128],
                        s_lamq[0:1, j * 512:(j + 1) * 512],
                        start=True, stop=True),
                       wait=[("LFD", lfd),
                             ("VS", gA[0] if qq == 0 else gB[0])]
                       if j == 0 else [],
                       inc=[("TS", 1)] if j == 3 else ())
                ts_q = TS()
                V(lambda e, ps=ps, q=q, c=c: e.tensor_tensor(
                    s_E[:, :].rearrange("p (a b) -> p a b", a=16),
                    ps[:, :].rearrange("p (a b) -> p a b", a=16),
                    s_lam[:, c * 64 + q * 16:c * 64 + (q + 1) * 16]
                    .unsqueeze(2).broadcast_to([128, 16, 128]), OP.subtract),
                  wait=[("TS", ts_q)])
                V(lambda e: e.tensor_scalar(s_E[:, :], s_E[:, :], 0.0, None,
                                            OP.min))
                gs_e = 0
                vs_e = VSn()
                if qq == 0:
                    gA[0] = vs_e
                else:
                    gB[0] = vs_e
                SC(lambda e, qq=qq: e.activation(
                    s_L[:, qq * 2048:(qq + 1) * 2048], s_E[:, :], AF.Exp),
                   wait=[("VS", vs_e)])
            ss_l = SSn()
            V(lambda e, half=half, c=c: e.tensor_copy(
                s_dtb16[:, :], s_dtsp[:, c * 64 + half * 32:c * 64 + (half + 1) * 32]))
            V(lambda e: e.tensor_tensor(
                s_L[:, :].rearrange("p (a b) -> p a b", a=32),
                s_L[:, :].rearrange("p (a b) -> p a b", a=32),
                s_gbf[:, :].unsqueeze(1).broadcast_to([128, 32, 128]), OP.mult),
              wait=[("SS", ss_l)])
            V(lambda e: e.tensor_tensor(
                s_L[:, :].rearrange("p (a b) -> p a b", a=32),
                s_L[:, :].rearrange("p (a b) -> p a b", a=32),
                s_dtb16[:, :].unsqueeze(2).broadcast_to([128, 32, 128]), OP.mult))
            vs_stile = VSn()
            V(lambda e, half=half: e.tensor_copy(
                s_hcast[:, :], s_hrun[:, half * 2048:(half + 1) * 2048]))
            vs_hc = VSn()
            for j in range(4):
                TE(lambda e, j=j: e.matmul(
                    psA[:, j * 512:(j + 1) * 512], s_cfm[:, :],
                    s_hcast[:, j * 512:(j + 1) * 512], start=True, stop=True),
                   wait=[("VS", max(vs_hc, gA[0]))] if j == 0 else [],
                   inc=[("TS", 1)] if j == 3 else ())
            ts_inter = TS()
            for hh in range(32):
                h = half * 32 + hh
                TE(lambda e, hh=hh, h=h, first=(hh % 8 == 0), last=(hh % 8 == 7):
                   e.matmul(psB[:, hh * 64:(hh + 1) * 64],
                            s_L[:, hh * 128:(hh + 1) * 128],
                            s_xc[:, h * 64:(h + 1) * 64],
                            start=first, stop=last),
                   wait=[("VS", max(vs_stile, gB[0]))] if hh == 0 else [],
                   inc=[("TS", 1)] if hh == 31 else ())
            ts_intra = TS()
            V(lambda e, half=half, c=c: e.tensor_tensor(
                s_y[:, half * 2048:(half + 1) * 2048].rearrange(
                    "p (a b) -> p a b", a=32),
                psA[:, :].rearrange("p (a b) -> p a b", a=32),
                s_explam[:, c * 64 + half * 32:c * 64 + (half + 1) * 32]
                .unsqueeze(2).broadcast_to([128, 32, 64]), OP.mult),
              wait=[("TS", ts_inter)])
            V(lambda e, half=half: e.tensor_tensor(
                s_y[:, half * 2048:(half + 1) * 2048],
                s_y[:, half * 2048:(half + 1) * 2048], psB[:, :], OP.add),
              wait=[("TS", ts_intra)])
            gA[0] = VSn()
            gB[0] = VSn()
            V(lambda e, half=half, c=c: e.tensor_tensor(
                s_hcast[:, :].rearrange("p (a b) -> p a b", a=32),
                s_xc[:, half * 2048:(half + 1) * 2048].rearrange(
                    "p (a b) -> p a b", a=32),
                s_drep[:, half * 32:(half + 1) * 32].unsqueeze(2)
                .broadcast_to([128, 32, 64]), OP.mult))
            V(lambda e, half=half: e.tensor_tensor(
                s_y[:, half * 2048:(half + 1) * 2048],
                s_y[:, half * 2048:(half + 1) * 2048], s_hcast[:, :], OP.add))
        V(lambda e, c=c: e.tensor_tensor(
            s_xhat[:, :].rearrange("p (a b) -> p a b", a=64),
            s_xc[:, 0:DS].rearrange("p (a b) -> p a b", a=64),
            s_ftm[:, c * 64:(c + 1) * 64].unsqueeze(2).broadcast_to([128, 64, 64]),
            OP.mult))
        vs_xhat = VSn()
        for j in range(8):
            ps = psA if j < 4 else psB
            poff = (j % 4) * 512
            TE(lambda e, ps=ps, poff=poff, j=j:
               e.matmul(ps[:, poff:poff + 512], s_xc[:, DS:DS + 128],
                        s_xhat[:, j * 512:(j + 1) * 512], start=True, stop=True),
               wait=[("VS", max(vs_xhat, gA[0], gB[0]))] if j == 0 else [],
               inc=[("TS", 1)] if j == 7 else ())
        ts_state = TS()
        V(lambda e, c=c: e.tensor_tensor(
            s_hrun[:, :].rearrange("p (a b) -> p a b", a=64),
            s_hrun[:, :].rearrange("p (a b) -> p a b", a=64),
            s_gamrep[:, c * 64:(c + 1) * 64].unsqueeze(2).broadcast_to([128, 64, 64]),
            OP.mult), wait=[("TS", ts_state)])
        V(lambda e: e.tensor_tensor(s_hrun[:, 0:2048], s_hrun[:, 0:2048],
                                    psA[:, :], OP.add))
        V(lambda e: e.tensor_tensor(s_hrun[:, 2048:4096], s_hrun[:, 2048:4096],
                                    psB[:, :], OP.add))
        vs_hupd = VSn()
        SC(lambda e: e.activation(s_zrl[:, :], s_zrl[:, :], AF.Silu),
           wait=[("ZRL", zrl0)])
        ss_silu = SSn()
        V(lambda e: e.tensor_tensor(s_y[:, 0:2048], s_y[:, 0:2048],
                                    s_zrl[:, :], OP.mult),
          wait=[("SS", ss_silu)])
        vs_z0 = VSn()
        DMA("sync", lambda e, c=c: e.dma_start(
            out=s_zrl[:, :], in_=zscr[128 * c:128 * (c + 1), 2048:4096]),
            wait=[("VS", vs_z0), ("SS", SSn())], inc=[("ZRL", 16)])
        zrl1 = CNT["ZRL"]
        SC(lambda e: e.activation(s_zrl[:, :], s_zrl[:, :], AF.Silu),
           wait=[("ZRL", zrl1)])
        ss_silu2 = SSn()
        V(lambda e: e.tensor_tensor(s_y[:, 2048:4096], s_y[:, 2048:4096],
                                    s_zrl[:, :], OP.mult),
          wait=[("SS", ss_silu2)])
        zrl_free[0] = VSn()
        vs_gt = VSn()
        SC(lambda e: e.activation(s_E[:, :], s_y[:, 0:2048], AF.Square),
           wait=[("VS", vs_gt)])
        ss_sq1 = SSn()
        V(lambda e: e.reduce_sum(s_small[:, 100:101], s_E[:, :], axis=AX),
          wait=[("SS", ss_sq1)])
        vs_rs1 = VSn()
        SC(lambda e: e.activation(s_E[:, :], s_y[:, 2048:4096], AF.Square),
           wait=[("VS", vs_rs1)])
        ss_sq2 = SSn()
        V(lambda e: e.reduce_sum(s_small[:, 101:102], s_E[:, :], axis=AX),
          wait=[("SS", ss_sq2)])
        V(lambda e: e.tensor_tensor(s_small[:, 102:103], s_small[:, 100:101],
                                    s_small[:, 101:102], OP.add))
        vs_ss = VSn()
        SC(lambda e: e.activation(s_small[:, 103:104], s_small[:, 102:103],
                                  AF.Sqrt, bias=EPS, scale=1.0 / DS),
           wait=[("VS", vs_ss)])
        ss_sr = SSn()
        V(lambda e: e.reciprocal(s_small[:, 104:105], s_small[:, 103:104]),
          wait=[("SS", ss_sr)])
        V(lambda e: e.scalar_tensor_tensor(s_y[:, :], s_y[:, :],
                                           s_small[:, 104:105], s_normw[:, :],
                                           OP.mult, OP.mult))
        vs_yh = VSn()
        tb_free = {b: ("VS", vs_hupd) for b in range(8)}
        for j in range(32):
            bank = j % 8
            ps = psA if bank < 4 else psB
            poff = (bank % 4) * 512
            gsem, gval = tb_free[bank]
            TE(lambda e, j=j, ps=ps, poff=poff: e.transpose(
                ps[:, poff:poff + 128], s_y[:, j * 128:(j + 1) * 128],
                s_identf[:, :]),
               wait=[("VS", vs_yh), (gsem, gval)], inc=[("TS", 1)])
            ts = TS()
            SC(lambda e, j=j, c=c, ps=ps, poff=poff: e.copy(
                s_stat[:, j * TB + 128 * c:j * TB + 128 * (c + 1)],
                ps[:, poff:poff + 128]),
               wait=[("TS", ts)])
            tb_free[bank] = ("SS", SSn())
        V(lambda e: e.engine_nop(), wait=[("SS", SSn())])
        gA[0] = VSn()
        gB[0] = VSn()
        xc_guard["v"] = VSn()
        xc_guard["s"] = SSn()
        xc_guard["t"] = ts_state
    vs_p5 = VSn()
    ss_p5 = SSn()
    ts_p5 = TS()
    esC2.close()
    esCx.close()

    esC3 = contextlib.ExitStack()
    sbC3 = lambda name, shape, dt: esC3.enter_context(nc.sbuf_tensor(name, shape, dt))
    s_lorab = sbC3("s_lorab", [32, H], BF16)
    s_ow = [sbC3(f"s_ow{p}", [128, 32 * 512], BF16) for p in range(2)]
    s_url = [sbC3(f"s_url{p}", [128, 512], F32) for p in range(2)]
    s_xrl = [sbC3(f"s_xrl{p}", [128, 512], F32) for p in range(2)]
    s_ost = [sbC3(f"s_ost{p}", [128, 512], F32) for p in range(2)]
    DMA("gpsimd", lambda e: e.dma_start(out=s_lorab[:], in_=lorabD),
        wait=[("VS", vs_p5), ("SS", ss_p5)], inc=[("GDM", 16)])
    gdm_lb = CNT["GDM"]
    ow_free = {0: 0, 1: 0, 2: 0}
    url_free = {0: vs_p5, 1: vs_p5}
    xrl_free = {0: vs_p5, 1: vs_p5}
    ost_free = {0: 0, 1: 0}
    obank_free = {b: vs_p5 for b in range(8)}

    for og in range(8):
        buf = og % 2
        DMA("scalar", lambda e, og=og, buf=buf: e.dma_start(
            out=s_ow[buf][:, 0:32 * 512].rearrange("p (i c) -> p i c", i=32),
            in_=wsrc(owscr, og * 512, 512)),
            wait=[("TS", ow_free[buf]), ("VS", vs_p5), ("OCV", ocv_all)],
            inc=[("OWL", 16)])
        owl = CNT["OWL"]
        for t in range(8):
            rpar = t % 2
            DMA("sync", lambda e, og=og, t=t, rpar=rpar: e.dma_start(
                out=s_url[rpar][:, :],
                in_=uscr[128 * t:128 * (t + 1), og * 512:(og + 1) * 512]),
                wait=[("VS", url_free[rpar]), ("ZUS", zus_all)], inc=[("URL", 16)])
            DMA("sync", lambda e, og=og, t=t, rpar=rpar: e.dma_start(
                out=s_xrl[rpar][:, :],
                in_=xown[128 * t:128 * (t + 1), og * 512:(og + 1) * 512]),
                wait=[("VS", xrl_free[rpar])], inc=[("XRL", 16)])
            url = CNT["URL"]
            xrl = CNT["XRL"]
            bank = t % 8
            ps = psA if bank < 4 else psB
            poff = (bank % 4) * 512
            for i in range(32):
                fa = (i == 0)
                w = []
                if fa:
                    w = [("OWL", owl), ("VS", obank_free[bank]),
                         ("TS", ts_p5), ("SS", ss_p5)]
                TE(lambda e, ps=ps, poff=poff, i=i, t=t, buf=buf, fa=fa:
                   e.matmul(ps[:, poff:poff + 512],
                            s_stat[:, i * TB + 128 * t:i * TB + 128 * t + 128],
                            s_ow[buf][:, i * 512:(i + 1) * 512],
                            start=fa, stop=False),
                   wait=w)
            TE(lambda e, ps=ps, poff=poff, t=t, og=og: e.matmul(
                ps[:, poff:poff + 512],
                s_zlt[:, 128 * t:128 * (t + 1)],
                s_lorab[:, og * 512:og * 512 + 512],
                start=False, stop=True),
               wait=[("GDM", gdm_lb), ("VS", vs_p2)] if (og == 0 and t == 0)
               else [],
               inc=[("TS", 1)])
            ts_o = TS()
            opar = (og * 8 + t) % 2
            V(lambda e, ps=ps, poff=poff, opar=opar, rpar=rpar:
              e.tensor_tensor(s_ost[opar][:, :], ps[:, poff:poff + 512],
                              s_url[rpar][:, :], OP.add),
              wait=[("TS", ts_o), ("URL", url), ("STO", ost_free[opar])])
            obank_free[bank] = VSn()
            url_free[rpar] = VSn()
            V(lambda e, opar=opar, rpar=rpar: e.tensor_tensor(
                s_ost[opar][:, :], s_ost[opar][:, :], s_xrl[rpar][:, :], OP.add),
              wait=[("XRL", xrl)])
            xrl_free[rpar] = VSn()
            vs_o = VSn()
            DMA("sync", lambda e, og=og, t=t, opar=opar: e.dma_start(
                out=oseg[128 * t:128 * (t + 1), og * 512:(og + 1) * 512],
                in_=s_ost[opar][:, :]),
                wait=[("VS", vs_o)], inc=[("STO", 16)])
            ost_free[opar] = CNT["STO"]
        ow_free[buf] = TS()

    WAITQ("gpsimd", [("STO", CNT["STO"])])
    esC3.close()
    esC.close()

    with nc.Block() as block:
        @block.sync
        def _(sync):
            for f in QL["sync"]:
                f(sync)

        @block.gpsimd
        def _(g):
            for f in QL["gpsimd"]:
                f(g)

        @block.tensor
        def _(t):
            for f in QL["tensor"]:
                f(t)

        @block.vector
        def _(v):
            for f in QL["vector"]:
                f(v)

        @block.scalar
        def _(s):
            for f in QL["scalar"]:
                f(s)

    es.close()
    _cache["nc"] = nc
    return nc


def _shift_consts():
    mats = []
    for d in (1, 2, 3):
        mats.append(np.eye(128, k=d))
    for d in (1, 2, 3):
        mats.append(np.eye(128, k=d - 128))
    for d in (1, 2, 3):
        m = np.zeros((128, 128))
        for t in range(d):
            m[3 + t - d, t] = 1.0
        mats.append(m)
    return np.ascontiguousarray(
        np.concatenate(mats, axis=1).astype(ml_dtypes.bfloat16))


def kernel(**inputs):
    x = np.asarray(inputs["L0_attn_hidden_states"], np.float32)
    inw = np.asarray(inputs["L0_attn_in_proj_w"], np.float32)
    wup = np.asarray(inputs["L0_ffn_W_up"], np.float32)
    loraA = np.asarray(inputs["L0_ffn_lora_A"], np.float32)
    loraB = np.asarray(inputs["L0_ffn_lora_B"], np.float32)
    gatew = np.asarray(inputs["L0_ffn_gate_w"], np.float32)
    bf = ml_dtypes.bfloat16

    wcat = np.ascontiguousarray(np.concatenate(
        [inw, wup.T, loraA.reshape(32, H).T, gatew.T], axis=1), dtype=np.float32)
    lnw = np.asarray(inputs["L0_attn_ln_w"], np.float32)

    shared = {
        "wcat": wcat,
        "outwD": np.ascontiguousarray(
            np.asarray(inputs["L0_attn_out_proj_w"], np.float32)),
        "lorabD": np.ascontiguousarray(
            loraB.transpose(0, 2, 1).reshape(32, H)),
        "convwT": np.ascontiguousarray(
            np.asarray(inputs["L0_attn_conv_w"], np.float32).T),
        "mupvD": np.asarray(inputs["L0_attn_mup_vector"], np.float32)[None, :],
        "lnwC": np.ascontiguousarray(lnw.reshape(32, 128).T),
        "normwD": np.asarray(inputs["L0_attn_norm_w"], np.float32)[None, :],
        "alogD": np.asarray(inputs["L0_attn_A_log"], np.float32)[None, :],
        "dtbD": np.asarray(inputs["L0_attn_dt_bias"], np.float32)[None, :],
        "dvecD": np.asarray(inputs["L0_attn_D"], np.float32)[None, :],
        "ebiasD": np.asarray(inputs["L0_ffn_expert_bias"], np.float32)[None, :],
        "identbD": np.eye(128, dtype=bf),
        "identfD": np.eye(128, dtype=np.float32),
        "utonesD": np.ascontiguousarray(np.triu(np.ones((128, 128), np.float32))),
        "shiftD": _shift_consts(),
        "ones1D": np.ones((1, 128), np.float32),
    }

    nc = build_program()
    in_maps = []
    for c in range(8):
        b, g = divmod(c, 4)
        own = x[b, g * T:(g + 1) * T]
        prev3 = (np.zeros((3, H), np.float32) if g == 0
                 else x[b, g * T - 3:g * T])
        xseg = np.concatenate([prev3, own], axis=0)
        cm = np.zeros((128, 3), np.float32)
        gm = np.zeros((128, 3), np.float32)
        for j in range(3):
            if j < g:
                cm[:, j] = 1.0
            if 1 <= j <= g - 1:
                gm[:, j] = 1.0
        m = dict(shared)
        m["xsegT"] = np.ascontiguousarray(xseg.T)
        m["xown"] = np.ascontiguousarray(own)
        m["cmaskD"] = cm
        m["gmaskD"] = gm
        m["gmaskiD"] = 1.0 - gm
        in_maps.append(m)

    res = run_bass_kernel_spmd(nc, in_maps, core_ids=list(range(8)))
    out = np.empty((2, 4096, H), np.float32)
    for c in range(8):
        b, g = divmod(c, 4)
        out[b, g * T:(g + 1) * T] = res.results[c]["oseg"]
    return out


# version 21 (final)
# speedup vs baseline: 1.3163x; 1.0091x over previous
import contextlib
import numpy as np
import ml_dtypes
import concourse.bass as bass
import concourse.mybir as mybir
from concourse.bass_utils import run_bass_kernel_spmd
from concourse.alu_op_type import AluOpType as OP

F32 = mybir.dt.float32
BF16 = mybir.dt.bfloat16
AF = mybir.ActivationFunctionType
AX = mybir.AxisListType.X

H = 4096
DS = 4096
SN = 128
NH = 64
HD = 64
KC = 4
CONV = DS + 2 * SN
PROJ = 2 * DS + 2 * SN + NH
NCOL = PROJ + H + 32 + 4
T = 1024
NT = 8
TB = 1027
EPS = 1e-5

_cache = {}


def build_program():
    if "nc" in _cache:
        return _cache["nc"]
    nc = bass.Bass("TRN2", target_bir_lowering=False)

    xsegT = nc.dram_tensor("xsegT", [H, TB], F32, kind="ExternalInput").ap()
    xown = nc.dram_tensor("xown", [T, H], F32, kind="ExternalInput").ap()
    wcat = nc.dram_tensor("wcat", [H, NCOL], F32, kind="ExternalInput").ap()
    outwD = nc.dram_tensor("outwD", [DS, H], F32, kind="ExternalInput").ap()
    lorabD = nc.dram_tensor("lorabD", [32, H], F32, kind="ExternalInput").ap()
    convwT = nc.dram_tensor("convwT", [KC, CONV], F32, kind="ExternalInput").ap()
    mupvD = nc.dram_tensor("mupvD", [1, PROJ], F32, kind="ExternalInput").ap()
    lnwC = nc.dram_tensor("lnwC", [128, 32], F32, kind="ExternalInput").ap()
    normwD = nc.dram_tensor("normwD", [1, DS], F32, kind="ExternalInput").ap()
    alogD = nc.dram_tensor("alogD", [1, NH], F32, kind="ExternalInput").ap()
    dtbD = nc.dram_tensor("dtbD", [1, NH], F32, kind="ExternalInput").ap()
    dvecD = nc.dram_tensor("dvecD", [1, NH], F32, kind="ExternalInput").ap()
    ebiasD = nc.dram_tensor("ebiasD", [1, 4], F32, kind="ExternalInput").ap()
    identbD = nc.dram_tensor("identbD", [128, 128], BF16, kind="ExternalInput").ap()
    identfD = nc.dram_tensor("identfD", [128, 128], F32, kind="ExternalInput").ap()
    utonesD = nc.dram_tensor("utonesD", [128, 128], F32, kind="ExternalInput").ap()
    shiftD = nc.dram_tensor("shiftD", [128, 128 * 9], BF16, kind="ExternalInput").ap()
    ones1D = nc.dram_tensor("ones1D", [1, 128], F32, kind="ExternalInput").ap()
    cmaskD = nc.dram_tensor("cmaskD", [128, 3], F32, kind="ExternalInput").ap()
    gmaskD = nc.dram_tensor("gmaskD", [128, 3], F32, kind="ExternalInput").ap()
    gmaskiD = nc.dram_tensor("gmaskiD", [128, 3], F32, kind="ExternalInput").ap()

    oseg = nc.dram_tensor("oseg", [T, H], F32, kind="ExternalOutput").ap()

    convscr = nc.dram_tensor("convscr", [T, CONV], BF16, kind="Internal").ap()
    xnscr = nc.dram_tensor("xnscr", [128, 32 * T], BF16, kind="Internal").ap()
    zscr = nc.dram_tensor("zscr", [T, DS], F32, kind="Internal").ap()
    uscr = nc.dram_tensor("uscr", [T, H], F32, kind="Internal").ap()
    mupwscr = nc.dram_tensor("mupwscr", [128, CONV * KC], BF16, kind="Internal").ap()
    wscr = nc.dram_tensor("wscr", [H, NCOL], BF16, kind="Internal").ap()
    owscr = nc.dram_tensor("owscr", [DS, H], BF16, kind="Internal").ap()
    ccin = nc.dram_tensor("ccin", [128, DS + NH], F32, kind="Internal").ap()
    ccout = nc.dram_tensor("ccout", [512, DS + NH], F32, kind="Internal").ap()

    es = contextlib.ExitStack()
    sbt = lambda name, shape, dt: es.enter_context(nc.sbuf_tensor(name, shape, dt))

    sem_names = ["VS", "SS", "TS", "LDC", "LDX", "LDY", "WCV", "OCV", "WLD", "XNR", "CPL", "ZUS",
                 "XLD", "ZRL", "URL", "XRL", "CC", "GDM", "STO", "LFD", "ROW",
                 "OWL", "GTH", "PUB", "MWS", "MWL", "GS"]
    SEM = {}
    CNT = {}
    QL = {"tensor": [], "vector": [], "scalar": [], "sync": [], "gpsimd": []}

    def emit(eng, fn, wait=(), inc=()):
        wait = tuple((s, v) for (s, v) in wait if v > 0)
        inc = tuple(inc)

        def step(e, _fn=fn, _w=wait, _i=inc):
            for s, v in _w:
                e.wait_ge(SEM[s], v)
            inst = _fn(e)
            for s, a in _i:
                inst = inst.then_inc(SEM[s], a)
        QL[eng].append(step)

    def bump(inc):
        for s, a in inc:
            CNT[s] += a

    def V(fn, wait=(), inc=()):
        w = [("VS", CNT["VS"])] + list(wait)
        i = [("VS", 1)] + list(inc)
        bump(i)
        emit("vector", fn, w, i)

    def SC(fn, wait=(), inc=()):
        w = [("SS", CNT["SS"])] + list(wait)
        i = [("SS", 1)] + list(inc)
        bump(i)
        emit("scalar", fn, w, i)

    def TE(fn, wait=(), inc=()):
        bump(inc)
        emit("tensor", fn, wait, inc)

    def DMA(eng, fn, wait=(), inc=()):
        bump(inc)
        emit(eng, fn, wait, inc)

    def G(fn, wait=(), inc=()):
        w = [("GS", CNT["GS"])] + list(wait)
        i = [("GS", 1)] + list(inc)
        bump(i)
        emit("gpsimd", fn, w, i)

    def WAITQ(eng, wait):
        emit(eng, lambda e: None, wait, ())

    s_identb = sbt("s_identb", [128, 128], BF16)
    s_identf = sbt("s_identf", [128, 128], F32)
    s_ut = sbt("s_ut", [128, 128], F32)
    s_shift = sbt("s_shift", [128, 128 * 9], BF16)
    s_ones1 = sbt("s_ones1", [1, 128], F32)
    s_onescol = sbt("s_onescol", [128, 1], F32)
    s_row64 = sbt("s_row64", [1, 448], F32)
    s_lnwc = sbt("s_lnwc", [128, 32], F32)
    s_dtbrep = sbt("s_dtbrep", [128, NH], F32)
    s_arep = sbt("s_arep", [128, NH], F32)
    s_drep = sbt("s_drep", [128, NH], F32)
    s_ebrep = sbt("s_ebrep", [128, 4], F32)
    s_cmask = sbt("s_cmask", [128, 3], F32)
    s_gmask = sbt("s_gmask", [128, 3], F32)
    s_gmaski = sbt("s_gmaski", [128, 3], F32)
    s_rmsall = sbt("s_rmsall", [128, 16], F32)
    s_dtsp = sbt("s_dtsp", [128, NH * NT], F32)
    s_dtaall = sbt("s_dtaall", [128, NH * NT], F32)
    s_lam = sbt("s_lam", [128, NH * NT], F32)
    s_lamfm = sbt("s_lamfm", [64, 128 * NT], F32)
    s_ftm = sbt("s_ftm", [128, NH * NT], F32)
    s_gamrep = sbt("s_gamrep", [128, NH * NT], F32)
    s_explam = sbt("s_explam", [128, NH * NT], F32)
    s_gtot = sbt("s_gtot", [128, NH], F32)
    s_glog = sbt("s_glog", [128, 4 * NT], F32)
    s_zlora = sbt("s_zlora", [128, 32 * NT], F32)
    s_zlt = sbt("s_zlt", [32, 128 * NT], BF16)
    s_hrun = sbt("s_hrun", [128, DS], F32)
    s_small = sbt("s_small", [128, 128], F32)
    s_normw = sbt("s_normw", [128, DS], BF16)
    s_stat = sbt("s_stat", [128, 32 * TB], BF16)
    s_mupdt = sbt("s_mupdt", [128, NH], F32)
    s_mupz = sbt("s_mupz", [128, DS], BF16)
    s_cnst = sbt("s_cnst", [128, 2], F32)

    psA = es.enter_context(nc.psum_tensor("psA", [128, 2048], F32))
    psB = es.enter_context(nc.psum_tensor("psB", [128, 2048], F32))

    for n in sem_names:
        SEM[n] = es.enter_context(nc.semaphore(n))
        CNT[n] = 0

    TS = lambda: CNT["TS"]
    VSn = lambda: CNT["VS"]
    SSn = lambda: CNT["SS"]

    esA = contextlib.ExitStack()
    sbA = lambda name, shape, dt: esA.enter_context(nc.sbuf_tensor(name, shape, dt))
    s_srow = sbA("s_srow", [1, 5200], F32)
    s_cwp = sbA("s_cwp", [1, 512], F32)
    s_cwq = sbA("s_cwq", [1, 512], F32)
    s_repp = sbA("s_repp", [128, 512], BF16)

    for dst, src in [(s_identb, identbD), (s_identf, identfD), (s_ut, utonesD),
                     (s_shift, shiftD), (s_ones1, ones1D), (s_cmask, cmaskD),
                     (s_gmask, gmaskD), (s_gmaski, gmaskiD), (s_lnwc, lnwC)]:
        DMA("sync", lambda e, d=dst, s=src: e.dma_start(out=d[:], in_=s),
            (), [("LDC", 16)])
    DMA("sync", lambda e: e.dma_start(out=s_row64[0:1, 0:64], in_=dtbD), (), [("LDC", 16)])
    DMA("sync", lambda e: e.dma_start(out=s_row64[0:1, 64:128], in_=alogD), (), [("LDC", 16)])
    DMA("sync", lambda e: e.dma_start(out=s_row64[0:1, 128:192], in_=dvecD), (), [("LDC", 16)])
    DMA("sync", lambda e: e.dma_start(out=s_row64[0:1, 192:196], in_=ebiasD), (), [("LDC", 16)])
    ldc_all = CNT["LDC"]
    WAITQ("vector", [("LDC", ldc_all)])
    WAITQ("scalar", [("LDC", ldc_all)])
    WAITQ("tensor", [("LDC", ldc_all)])

    V(lambda e: e.memset(s_onescol[:], 1.0))
    V(lambda e: e.memset(s_hrun[:], 0.0))
    V(lambda e: e.memset(s_gtot[:], 1.0))
    V(lambda e: e.memset(s_cnst[:, 0:1], EPS))
    V(lambda e: e.memset(s_cnst[:, 1:2], 1.0 / DS))
    nc.const_aps.register(F32, EPS, s_cnst[:, 0:1])
    nc.const_aps.register(F32, 1.0 / DS, s_cnst[:, 1:2])

    def k1rep(dst_ap, row_ap, width, ps, poff, sc_mul=None, wt=()):
        TE(lambda e, w=width: e.matmul(ps[:, poff:poff + w], s_ones1[0:1, 0:128],
                                       row_ap, start=True, stop=True),
           wait=[("VS", VSn()), ("SS", SSn())] + list(wt), inc=[("TS", 1)])
        ts = TS()
        if sc_mul is None:
            V(lambda e, w=width: e.tensor_copy(dst_ap, ps[:, poff:poff + w]),
              wait=[("TS", ts)])
        else:
            V(lambda e, w=width: e.tensor_scalar(dst_ap, ps[:, poff:poff + w],
                                                 sc_mul, None, OP.mult),
              wait=[("TS", ts)])

    SC(lambda e: e.activation(s_row64[0:1, 256:320], s_row64[0:1, 64:128], AF.Exp))
    k1rep(s_dtbrep[:, :], s_row64[0:1, 0:64], NH, psA, 0)
    k1rep(s_arep[:, :], s_row64[0:1, 256:320], NH, psA, 512, sc_mul=-1.0)
    k1rep(s_drep[:, :], s_row64[0:1, 128:192], NH, psA, 1024)
    k1rep(s_ebrep[:, :], s_row64[0:1, 192:196], 4, psA, 1536)

    mws_g = [0]
    for k in range(KC):
        for j in range(9):
            w = 512 if j < 8 else 256
            off = j * 512
            DMA("sync", lambda e, k=k, off=off, w=w: e.dma_start(
                out=s_cwp[0:1, 0:w], in_=convwT[k:k + 1, off:off + w]),
                wait=[("VS", mws_g[0])], inc=[("LDX", 16)])
            DMA("sync", lambda e, off=off, w=w: e.dma_start(
                out=s_cwq[0:1, 0:w], in_=mupvD[0:1, DS + off:DS + off + w]),
                wait=[("VS", mws_g[0])], inc=[("LDX", 16)])
            ldx = CNT["LDX"]
            V(lambda e, w=w: e.tensor_tensor(s_cwp[0:1, 0:w], s_cwp[0:1, 0:w],
                                             s_cwq[0:1, 0:w], OP.mult),
              wait=[("LDX", ldx)])
            k1rep(s_repp[:, 0:w], s_cwp[0:1, 0:w], w, psA, 0,
                  wt=[("MWS", CNT["MWS"])])
            vs = VSn()
            DMA("sync", lambda e, k=k, off=off, w=w: e.dma_start(
                out=mupwscr[:, k * CONV + off:k * CONV + off + w],
                in_=s_repp[:, 0:w]),
                wait=[("VS", vs)], inc=[("MWS", 16)])
            mws_g[0] = VSn()
    mws_all = CNT["MWS"]

    for j in range(8):
        DMA("sync", lambda e, j=j: e.dma_start(
            out=s_cwp[0:1, 0:512], in_=mupvD[0:1, j * 512:(j + 1) * 512]),
            wait=[("VS", mws_g[0])], inc=[("LDX", 16)])
        ldx = CNT["LDX"]
        WAITQ("tensor", [("LDX", ldx)])
        k1rep(s_mupz[:, j * 512:(j + 1) * 512], s_cwp[0:1, 0:512], 512,
              psA if j % 2 == 0 else psB, 0)
        mws_g[0] = VSn()
    DMA("sync", lambda e: e.dma_start(out=s_cwp[0:1, 0:64],
                                      in_=mupvD[0:1, PROJ - NH:PROJ]),
        wait=[("VS", mws_g[0])], inc=[("LDX", 16)])
    ldx = CNT["LDX"]
    WAITQ("tensor", [("LDX", ldx)])
    k1rep(s_mupdt[:, :], s_cwp[0:1, 0:64], NH, psA, 512)
    for j in range(8):
        DMA("sync", lambda e, j=j: e.dma_start(
            out=s_cwp[0:1, 0:512], in_=normwD[0:1, j * 512:(j + 1) * 512]),
            wait=[("VS", VSn())], inc=[("LDX", 16)])
        ldx = CNT["LDX"]
        WAITQ("tensor", [("LDX", ldx)])
        k1rep(s_normw[:, j * 512:(j + 1) * 512], s_cwp[0:1, 0:512], 512,
              psA if j % 2 == 0 else psB, 1024)

    s_xf = [sbA(f"s_xf{p}", [128, TB], F32) for p in range(4)]
    s_mrep = sbA("s_mrep", [128, TB], F32)
    s_rrep = sbA("s_rrep", [128, TB], F32)
    s_xtmp = sbA("s_xtmp", [128, TB], F32)
    s_xnb = [sbA(f"s_xnb{p}", [128, TB], BF16) for p in range(2)]
    OM, OV, OR, OU, OT = 0, 1040, 2080, 3120, 4160

    xf_guard = [0, 0, 0, 0]
    sq_bufs = [s_mrep, s_rrep]
    tsq_end = {}
    for i in range(32):
        qs = "LDX" if i % 2 == 0 else "LDY"
        DMA("sync" if i % 2 == 0 else "scalar", lambda e, i=i: e.dma_start(
            out=s_xf[i % 4][:], in_=xsegT[128 * i:128 * (i + 1), :]),
            wait=[("TS", xf_guard[i % 4])], inc=[(qs, 16)])
        ldx = CNT[qs]
        sq = sq_bufs[i % 2]
        SC(lambda e, i=i, sq=sq: e.activation(sq[:], s_xf[i % 4][:], AF.Square),
           wait=[(qs, ldx), ("TS", tsq_end.get(i - 2, 0))])
        ssq = SSn()
        for c in range(3):
            w = 512 if c < 2 else 3
            TE(lambda e, i=i, c=c, w=w: e.matmul(
                psA[0:1, c * 512:c * 512 + w], s_onescol[:, 0:1],
                s_xf[i % 4][:, c * 512:c * 512 + w], start=(i == 0), stop=(i == 31)),
               wait=[(qs, ldx), ("VS", VSn() if i == 0 else 0)],
               inc=[("TS", 1)])
            TE(lambda e, i=i, c=c, w=w, sq=sq: e.matmul(
                psB[0:1, c * 512:c * 512 + w], s_onescol[:, 0:1],
                sq[:, c * 512:c * 512 + w], start=(i == 0), stop=(i == 31)),
               wait=[("SS", ssq)], inc=[("TS", 1)])
        xf_guard[i % 4] = TS()
        tsq_end[i] = TS()
    ts_stats = TS()
    V(lambda e: e.tensor_scalar(s_srow[0:1, OM:OM + TB], psA[0:1, 0:TB],
                                1.0 / H, None, OP.mult), wait=[("TS", ts_stats)])
    V(lambda e: e.tensor_scalar(s_srow[0:1, OV:OV + TB], psB[0:1, 0:TB],
                                1.0 / H, None, OP.mult))
    V(lambda e: e.tensor_tensor(s_srow[0:1, OT:OT + TB], s_srow[0:1, OM:OM + TB],
                                s_srow[0:1, OM:OM + TB], OP.mult))
    V(lambda e: e.tensor_sub(s_srow[0:1, OV:OV + TB], s_srow[0:1, OV:OV + TB],
                             s_srow[0:1, OT:OT + TB]))
    V(lambda e: e.tensor_scalar(s_srow[0:1, OT:OT + TB], s_srow[0:1, OV:OV + TB],
                                EPS, None, OP.add))
    V(lambda e: e.reciprocal(s_srow[0:1, OR:OR + TB], s_srow[0:1, OT:OT + TB]))
    vs_r2 = VSn()
    SC(lambda e: e.activation(s_srow[0:1, OT:OT + TB], s_srow[0:1, OR:OR + TB],
                              AF.Sqrt), wait=[("VS", vs_r2)])
    ss_r = SSn()
    V(lambda e: e.tensor_tensor(s_srow[0:1, OU:OU + TB], s_srow[0:1, OV:OV + TB],
                                s_srow[0:1, OR:OR + TB], OP.mult),
      wait=[("SS", ss_r)])
    V(lambda e: e.tensor_copy(s_srow[0:1, OR:OR + TB], s_srow[0:1, OT:OT + TB]))
    vs_u = VSn()
    SC(lambda e: e.activation(s_srow[0:1, OT:OT + TB], s_srow[0:1, OU:OU + TB],
                              AF.Sqrt, bias=EPS, scale=1.0), wait=[("VS", vs_u)])
    ss_su = SSn()
    V(lambda e: e.reciprocal(s_srow[0:1, OU:OU + TB], s_srow[0:1, OT:OT + TB]),
      wait=[("SS", ss_su)])
    for c in range(3):
        w = 512 if c < 2 else 3
        k1rep(s_mrep[:, c * 512:c * 512 + w], s_srow[0:1, OM + c * 512:OM + c * 512 + w],
              w, psA, c * 512)
        k1rep(s_rrep[:, c * 512:c * 512 + w], s_srow[0:1, OR + c * 512:OR + c * 512 + w],
              w, psB, c * 512)
    for t in range(9):
        nr = 128 if t < 8 else 3
        so = OU + (3 + 128 * t if t < 8 else 0)
        TE(lambda e, nr=nr, so=so: e.transpose(
            psA[0:nr, 1536:1537], s_srow[0:1, so:so + nr], s_identf[0:1, 0:1]),
           wait=[("VS", VSn())], inc=[("TS", 1)])
        ts = TS()
        V(lambda e, t=t, nr=nr: e.tensor_copy(s_rmsall[0:nr, t:t + 1],
                                              psA[0:nr, 1536:1537]),
          wait=[("TS", ts)])

    vs_sub = {}
    spill_done = {0: ("XNR", 0), 1: ("WCV", 0)}
    xnr_g = [0]
    for i in range(32):
        qs = "LDX" if i % 2 == 0 else "LDY"
        DMA("sync" if i % 2 == 0 else "scalar", lambda e, i=i: e.dma_start(
            out=s_xf[i % 4][:], in_=xsegT[128 * i:128 * (i + 1), :]),
            wait=[("TS", xf_guard[i % 4]), ("VS", vs_sub.get(i - 4, 0))],
            inc=[(qs, 16)])
        ldx = CNT[qs]
        V(lambda e, i=i: e.tensor_tensor(s_xtmp[:], s_xf[i % 4][:], s_mrep[:],
                                         OP.subtract), wait=[(qs, ldx)])
        vs_sub[i] = VSn()
        xb = s_xnb[i % 2]
        V(lambda e, xb=xb: e.tensor_tensor(xb[:], s_xtmp[:], s_rrep[:], OP.mult),
          wait=[spill_done[i % 2]])
        vs_xn = VSn()
        sq2 = "XNR" if i % 2 == 0 else "WCV"
        DMA("sync" if i % 2 == 0 else "scalar", lambda e, i=i, xb=xb: e.dma_start(
            out=xnscr[:, i * T:(i + 1) * T], in_=xb[:, 3:TB]),
            wait=[("VS", vs_xn)], inc=[(sq2, 16)])
        spill_done[i % 2] = (sq2, CNT[sq2])
        V(lambda e, i=i, xb=xb: e.tensor_scalar(
            s_stat[:, i * TB:(i + 1) * TB], xb[:], s_lnwc[:, i:i + 1],
            None, OP.mult))
    vs_pln = VSn()
    xnr_pln = CNT["XNR"]
    wcv_pln = CNT["WCV"]
    esA.close()

    esB = contextlib.ExitStack()
    sbB = lambda name, shape, dt: esB.enter_context(nc.sbuf_tensor(name, shape, dt))
    s_w = [sbB(f"s_w{p}", [128, 32 * 512], BF16) for p in range(2)]
    s_stg = [sbB(f"s_stg{p}", [128, 512], F32) for p in range(2)]
    s_xt = [[sbB(f"s_xt{k}_{p}", [128, 512], BF16) for p in range(2)]
            for k in range(KC)]
    s_cstg = [sbB(f"s_cstg{p}", [128, 512], BF16) for p in range(2)]
    s_mws = [sbB("s_mws0", [128, KC * 512], BF16)]
    s_dtt = sbB("s_dtt", [128, NH], F32)
    s_dtt2 = sbB("s_dtt2", [128, NH], F32)
    s_dtt3 = sbB("s_dtt3", [128, NH], F32)
    s_p2a = sbB("s_p2a", [128, 64], F32)
    s_p2b = sbB("s_p2b", [128, 32], BF16)

    def wsrc(hbm, c0, cw):
        return hbm.rearrange("(i p) c -> p i c", p=128)[:, :, c0:c0 + cw]

    bank_free = {b: 0 for b in range(8)}
    wbuf_free = {0: 0, 1: 0}
    stg_free = {0: 0, 1: 0}
    stg_cnt = [0]
    cstg_free = {0: 0, 1: 0}
    cstg_cnt = [0]
    xt_free = {0: 0, 1: 0}
    mws_free = {0: vs_pln, 1: vs_pln}
    half_idx = [0]

    def SH(ix):
        return s_shift[:, ix * 128:(ix + 1) * 128]

    def proj_group(g0, gw, conv_order, stat_off_fn, first_wait, stat_wait=None):
        nhal = (gw + 511) // 512
        bufs = []
        wlds = []
        for hf in range(nhal):
            c0 = g0 + hf * 512
            cw = min(512, g0 + gw - c0)
            buf = half_idx[0] % 2
            half_idx[0] += 1
            bufs.append((buf, c0, cw))
            DMA("gpsimd", lambda e, c0=c0, cw=cw, buf=buf: e.dma_start(
                out=s_w[buf][:, 0:32 * 512].rearrange(
                    "p (i c) -> p i c", i=32)[:, :, 0:cw],
                in_=wsrc(wcat, c0, cw)),
                wait=[("TS", wbuf_free[buf])] + list(first_wait),
                inc=[("WLD", 16)])
            wlds.append(CNT["WLD"])
        is_conv = DS <= g0 < PROJ
        tiles = conv_order if is_conv else list(range(8))
        for t in tiles:
            nr = 128 if t < 8 else 3
            bank = t % 8
            ps = psA if bank < 4 else psB
            poff = (bank % 4) * 512
            for hj, (buf, c0, cw) in enumerate(bufs):
                sub = c0 - g0
                for i in range(32):
                    so = stat_off_fn(i, t)
                    fa = (hj == 0 and i == 0)
                    la = (hj == nhal - 1 and i == 31)
                    w = [("WLD", wlds[hj])]
                    if stat_wait is not None:
                        w += stat_wait(i)
                    if fa:
                        w.append(("VS", bank_free[bank]))
                    TE(lambda e, nr=nr, ps=ps, poff=poff, sub=sub, cw=cw,
                       so=so, buf=buf, i=i, fa=fa, la=la:
                       e.matmul(ps[0:nr, poff + sub:poff + sub + cw],
                                s_stat[:, so:so + nr],
                                s_w[buf][:, i * 512:i * 512 + cw],
                                start=fa, stop=la),
                       wait=w, inc=[("TS", 1)] if la else ())
            yield t, nr, ps, poff, bank, TS()
        for buf, _, _ in bufs:
            wbuf_free[buf] = TS()

    def evac_z(g0, t, ps, poff, bank, ts_mm):
        par = stg_cnt[0] % 2
        stg_cnt[0] += 1
        V(lambda e, t=t, ps=ps, poff=poff, par=par, g0=g0:
          e.scalar_tensor_tensor(s_stg[par][:, 0:512], ps[:, poff:poff + 512],
                                 s_rmsall[:, t:t + 1], s_mupz[:, g0:g0 + 512],
                                 OP.mult, OP.mult),
          wait=[("TS", ts_mm), ("ZUS", stg_free[par])])
        bank_free[bank] = VSn()
        vs = VSn()
        DMA("sync", lambda e, t=t, par=par, g0=g0: e.dma_start(
            out=zscr[128 * t:128 * (t + 1), g0:g0 + 512], in_=s_stg[par][:, 0:512]),
            wait=[("VS", vs)], inc=[("ZUS", 16)])
        stg_free[par] = CNT["ZUS"]

    def evac_conv(cc0, ccw, t, nr, ps, poff, bank, ts_mm, pos, mpar, mwl):
        par = pos % 2
        for k in range(KC):
            V(lambda e, k=k, ps=ps, poff=poff, ccw=ccw, nr=nr, t=t, par=par,
              mpar=mpar:
              e.scalar_tensor_tensor(
                  s_xt[k][par][0:nr, 0:ccw], ps[0:nr, poff:poff + ccw],
                  (s_rmsall[:, t:t + 1] if t < 8 else s_rmsall[0:3, 8:9]),
                  s_mws[mpar][0:nr, k * 512:k * 512 + ccw],
                  OP.mult, OP.mult),
              wait=([("TS", max(ts_mm, xt_free[par])), ("MWL", mwl)]
                    if k == 0 else []))
        vs_xt = VSn()
        bank_free[bank] = vs_xt
        if t == 8:
            return
        for k in range(KC):
            d = 3 - k
            lhs = s_identb[:, :] if d == 0 else SH(d - 1)
            fa = (k == 0)
            TE(lambda e, lhs=lhs, ps=ps, poff=poff, ccw=ccw, k=k, par=par,
               fa=fa: e.matmul(ps[:, poff:poff + ccw], lhs,
                               s_xt[k][par][:, 0:ccw], start=fa, stop=False),
               wait=[("VS", vs_xt)] if fa else [])
        for k in range(KC - 1):
            d = 3 - k
            lhs = SH(5 + d) if t == 0 else SH(2 + d)
            la = (k == KC - 2)
            TE(lambda e, lhs=lhs, ps=ps, poff=poff, ccw=ccw, k=k, par=par,
               la=la: e.matmul(ps[:, poff:poff + ccw], lhs,
                               s_xt[k][1 - par][:, 0:ccw], start=False, stop=la),
               wait=[], inc=[("TS", 1)] if la else ())
        ts_conv = TS()
        xt_free[1 - par] = ts_conv
        cpar = cstg_cnt[0] % 2
        cstg_cnt[0] += 1
        V(lambda e, ps=ps, poff=poff, ccw=ccw, cpar=cpar:
          e.tensor_copy(s_cstg[cpar][:, 0:ccw], ps[:, poff:poff + ccw]),
          wait=[("TS", ts_conv), ("CPL", cstg_free[cpar])])
        bank_free[bank] = VSn()
        vs = VSn()
        DMA("sync", lambda e, t=t, cc0=cc0, ccw=ccw, cpar=cpar: e.dma_start(
            out=convscr[128 * t:128 * (t + 1), cc0:cc0 + ccw],
            in_=s_cstg[cpar][:, 0:ccw]),
            wait=[("VS", vs)], inc=[("CPL", 16)])
        cstg_free[cpar] = CNT["CPL"]

    def evac_dt(t, ps, poff, ts_mm):
        V(lambda e, t=t, ps=ps, poff=poff: e.scalar_tensor_tensor(
            s_dtt[:, :], ps[:, poff + 256:poff + 320], s_rmsall[:, t:t + 1],
            s_mupdt[:, :], OP.mult, OP.mult), wait=[("TS", ts_mm)])
        V(lambda e: e.tensor_tensor(s_dtt[:, :], s_dtt[:, :], s_dtbrep[:, :],
                                    OP.add))
        vs_z = VSn()
        SC(lambda e: e.activation(s_dtt2[:, :], s_dtt[:, :], AF.Abs),
           wait=[("VS", vs_z)])
        SC(lambda e: e.activation(s_dtt3[:, :], s_dtt2[:, :], AF.Exp, scale=-1.0))
        SC(lambda e: e.activation(s_dtt2[:, :], s_dtt3[:, :], AF.Ln,
                                  bias=1.0, scale=1.0))
        SC(lambda e: e.activation(s_dtt3[:, :], s_dtt[:, :], AF.Relu))
        ss_sp = SSn()
        V(lambda e, t=t: e.tensor_add(s_dtsp[:, t * 64:(t + 1) * 64],
                                      s_dtt2[:, :], s_dtt3[:, :]),
          wait=[("SS", ss_sp)])
        V(lambda e, t=t: e.tensor_tensor(s_dtaall[:, t * 64:(t + 1) * 64],
                                         s_dtsp[:, t * 64:(t + 1) * 64],
                                         s_arep[:, :], OP.mult))

    def attn_stat_off(i, t):
        return i * TB + (3 + 128 * t if t < 8 else 0)

    conv_order = [8] + list(range(8))
    for k in range(KC):
        for p in range(2):
            V(lambda e, k=k, p=p: e.memset(s_xt[k][p][:, :], 0.0),
              wait=[("TS", TS())])
    first_wait = [("VS", vs_pln), ("MWS", mws_all)]
    mcnt = [0]
    for g in range(17):
        g0 = g * 512
        gw = 512 if g < 16 else 320
        is_conv = DS <= g0
        mpar = 0
        mwl = 0
        if is_conv:
            cc0 = g0 - DS
            ccw = min(gw, 512) if g < 16 else 256
            mpar = 0
            mcnt[0] += 1
            DMA("sync", lambda e, cc0=cc0, ccw=ccw, mpar=mpar: e.dma_start(
                out=s_mws[mpar][:, 0:KC * 512].rearrange(
                    "p (k c) -> p k c", k=KC)[:, :, 0:ccw],
                in_=mupwscr[:, :].rearrange(
                    "p (k c) -> p k c", k=KC)[:, :, cc0:cc0 + ccw]),
                wait=[("MWS", mws_all), ("VS", mws_free[mpar])],
                inc=[("MWL", 16)])
            mwl = CNT["MWL"]
        for t, nr, ps, poff, bank, ts_mm in proj_group(
                g0, gw, conv_order, attn_stat_off, first_wait):
            first_wait = []
            if not is_conv:
                evac_z(g0, t, ps, poff, bank, ts_mm)
            else:
                pos = conv_order.index(t)
                evac_conv(g0 - DS, ccw, t, nr, ps, poff, bank, ts_mm, pos,
                          mpar, mwl)
                if g == 16 and t != 8:
                    evac_dt(t, ps, poff, ts_mm)
        if is_conv:
            mws_free[mpar] = VSn()
        if g < 16:
            DMA("gpsimd", lambda e, g=g: e.dma_start(
                out=owscr[:, g * 256:(g + 1) * 256],
                in_=outwD[:, g * 256:(g + 1) * 256]),
                (), [("OCV", 16)])
    ocv_all = CNT["OCV"]
    ts_p1a = TS()
    cpl_all = CNT["CPL"]
    zus_p1a = CNT["ZUS"]

    xnr_need = [0] * 8
    xld_need = [0] * 8
    for cch in range(8):
        DMA("sync" if cch % 2 == 0 else "scalar", lambda e, cch=cch: e.dma_start(
            out=s_stat[:, 0:32 * TB].rearrange(
                "p (i c) -> p i c", c=TB)[:, 4 * cch:4 * cch + 4, 0:T],
            in_=xnscr[:, :].rearrange(
                "p (i c) -> p i c", c=T)[:, 4 * cch:4 * cch + 4, :]),
            wait=[("TS", ts_p1a), ("XNR", xnr_pln), ("WCV", wcv_pln)],
            inc=[("XNR" if cch % 2 == 0 else "WCV", 16)])
        for cj in range(cch, 8):
            if cch % 2 == 0:
                xnr_need[cj] = CNT["XNR"]
            else:
                xld_need[cj] = CNT["WCV"]

    def ffn_stat_off(i, t):
        return i * TB + 128 * t

    def evac_u(g0, t, ps, poff, bank, ts_mm):
        par = stg_cnt[0] % 2
        stg_cnt[0] += 1
        u0 = g0 - PROJ
        V(lambda e, ps=ps, poff=poff, par=par:
          e.tensor_copy(s_stg[par][:, 0:512], ps[:, poff:poff + 512]),
          wait=[("TS", ts_mm), ("ZUS", stg_free[par])])
        bank_free[bank] = VSn()
        vs = VSn()
        DMA("sync", lambda e, t=t, par=par, u0=u0: e.dma_start(
            out=uscr[128 * t:128 * (t + 1), u0:u0 + 512],
            in_=s_stg[par][:, 0:512]),
            wait=[("VS", vs)], inc=[("ZUS", 16)])
        stg_free[par] = CNT["ZUS"]

    sw0 = (lambda i: [("XNR", xnr_need[i // 4]), ("WCV", xld_need[i // 4])])
    for g in range(8):
        g0 = PROJ + g * 512
        for t, nr, ps, poff, bank, ts_mm in proj_group(
                g0, 512, None, ffn_stat_off, [],
                stat_wait=sw0 if g == 0 else None):
            evac_u(g0, t, ps, poff, bank, ts_mm)
    for t, nr, ps, poff, bank, ts_mm in proj_group(
            PROJ + 4096, 36, None, ffn_stat_off, []):
        V(lambda e, ps=ps, poff=poff, t=t:
          e.tensor_copy(s_zlora[:, t * 32:(t + 1) * 32], ps[:, poff:poff + 32]),
          wait=[("TS", ts_mm)])
        V(lambda e, ps=ps, poff=poff, t=t:
          e.tensor_copy(s_glog[:, t * 4:(t + 1) * 4], ps[:, poff + 32:poff + 36]))
        bank_free[bank] = VSn()
    ts_p1b = TS()
    zus_all = CNT["ZUS"]

    for t in range(8):
        sm = s_small
        V(lambda e, t=t: e.tensor_tensor(sm[:, 0:4], s_glog[:, 4 * t:4 * t + 4],
                                         s_ebrep[:, :], OP.add))
        V(lambda e: e.reduce_max(sm[:, 4:5], sm[:, 0:4], axis=AX))
        for eix in range(4):
            V(lambda e, eix=eix: e.tensor_scalar(
                sm[:, 5 + eix:6 + eix], sm[:, eix:eix + 1], sm[:, 4:5], None,
                OP.is_ge))
        V(lambda e: e.tensor_copy(sm[:, 9:10], sm[:, 5:6]))
        V(lambda e: e.tensor_scalar(sm[:, 13:14], sm[:, 9:10], -1.0, 1.0,
                                    OP.mult, OP.add))
        V(lambda e: e.tensor_tensor(sm[:, 10:11], sm[:, 6:7], sm[:, 13:14],
                                    OP.mult))
        V(lambda e: e.tensor_scalar(sm[:, 14:15], sm[:, 10:11], -1.0, 1.0,
                                    OP.mult, OP.add))
        V(lambda e: e.tensor_tensor(sm[:, 14:15], sm[:, 14:15], sm[:, 13:14],
                                    OP.mult))
        V(lambda e: e.tensor_tensor(sm[:, 11:12], sm[:, 7:8], sm[:, 14:15],
                                    OP.mult))
        V(lambda e: e.tensor_tensor(sm[:, 15:16], sm[:, 9:10], sm[:, 10:11],
                                    OP.add))
        V(lambda e: e.tensor_tensor(sm[:, 15:16], sm[:, 15:16], sm[:, 11:12],
                                    OP.add))
        V(lambda e: e.tensor_scalar(sm[:, 12:13], sm[:, 15:16], -1.0, 1.0,
                                    OP.mult, OP.add))
        V(lambda e, t=t: e.tensor_tensor(
            s_p2a[:, 0:32].rearrange("p (a b) -> p a b", a=4),
            s_zlora[:, 32 * t:32 * (t + 1)].rearrange("p (a b) -> p a b", a=4),
            sm[:, 9:13].unsqueeze(2).broadcast_to([128, 4, 8]), OP.mult))
        V(lambda e: e.tensor_copy(s_p2b[:, 0:32], s_p2a[:, 0:32]))
        vs_zt = VSn()
        TE(lambda e: e.transpose(psA[0:32, 0:64].bitcast(BF16)[:, 0:128],
                                 s_p2b[:, 0:32], s_identb[:, :]),
           wait=[("VS", max(vs_zt, bank_free[0]))], inc=[("TS", 1)])
        ts = TS()
        V(lambda e, t=t: e.tensor_copy(s_zlt[:, 128 * t:128 * (t + 1)],
                                       psA[0:32, 0:64].bitcast(BF16)[:, 0:128]),
          wait=[("TS", ts)])
        bank_free[0] = VSn()
    vs_p2 = VSn()
    esB.close()

    esC = contextlib.ExitStack()
    sbC = lambda name, shape, dt: esC.enter_context(nc.sbuf_tensor(name, shape, dt))
    esCx = contextlib.ExitStack()
    s_xc = esCx.enter_context(nc.sbuf_tensor("s_xc", [128, CONV], BF16))
    s_xhat = esCx.enter_context(nc.sbuf_tensor("s_xhat", [128, DS], BF16))
    xc_guard = {"v": vs_p2, "t": 0, "s": 0}

    def load_conv_tile(c):
        DMA("sync", lambda e, c=c: e.dma_start(
            out=s_xc[:, :], in_=convscr[128 * c:128 * (c + 1), :]),
            wait=[("SS", xc_guard["s"]), ("VS", xc_guard["v"]),
                  ("TS", xc_guard["t"]), ("CPL", cpl_all)],
            inc=[("XLD", 16)])
        xld = CNT["XLD"]
        SC(lambda e: e.activation(s_xc[:, :], s_xc[:, :], AF.Silu),
           wait=[("XLD", xld)])
        xc_guard["s"] = SSn()
        return SSn()

    for c in range(NT):
        ss_xc = load_conv_tile(c)
        TE(lambda e, c=c: e.matmul(psA[:, 0:64], s_ut[:, :],
                                   s_dtaall[:, c * 64:(c + 1) * 64],
                                   start=True, stop=True),
           wait=[("VS", bank_free[0])], inc=[("TS", 1)])
        ts_lam = TS()
        V(lambda e, c=c: e.tensor_copy(s_lam[:, c * 64:(c + 1) * 64],
                                       psA[:, 0:64]), wait=[("TS", ts_lam)])
        vs_lam = VSn()
        TE(lambda e, c=c: e.transpose(psA[0:64, 512:640],
                                      s_lam[:, c * 64:(c + 1) * 64],
                                      s_identf[:, :]),
           wait=[("VS", vs_lam)], inc=[("TS", 1)])
        ts_lfm = TS()
        V(lambda e, c=c: e.tensor_copy(s_lamfm[:, c * 128:(c + 1) * 128],
                                       psA[0:64, 512:640]), wait=[("TS", ts_lfm)])
        DMA("sync", lambda e, c=c: e.dma_start(
            out=s_row64[0:1, 320:384], in_=s_lam[127:128, c * 64:(c + 1) * 64]),
            wait=[("VS", vs_lam), ("SS", SSn()), ("TS", TS())],
            inc=[("ROW", 16)])
        roww = CNT["ROW"]
        SC(lambda e: e.activation(s_row64[0:1, 384:448], s_row64[0:1, 320:384],
                                  AF.Exp), wait=[("ROW", roww)])
        ss_gam = SSn()
        TE(lambda e: e.matmul(psA[:, 1024:1088], s_ones1[0:1, 0:128],
                              s_row64[0:1, 320:384], start=True, stop=True),
           wait=[("ROW", roww), ("VS", VSn())], inc=[("TS", 1)])
        ts_lend = TS()
        TE(lambda e: e.matmul(psA[:, 1536:1600], s_ones1[0:1, 0:128],
                              s_row64[0:1, 384:448], start=True, stop=True),
           wait=[("SS", ss_gam)], inc=[("TS", 1)])
        ts_gam = TS()
        V(lambda e, c=c: e.tensor_copy(s_gamrep[:, c * 64:(c + 1) * 64],
                                       psA[:, 1536:1600]), wait=[("TS", ts_gam)])
        V(lambda e, c=c: e.tensor_tensor(s_small[:, 0:64],
                                         psA[:, 1024:1088],
                                         s_lam[:, c * 64:(c + 1) * 64],
                                         OP.subtract),
          wait=[("TS", ts_lend)])
        vs_f1 = VSn()
        SC(lambda e: e.activation(s_small[:, 64:128], s_small[:, 0:64], AF.Exp),
           wait=[("VS", vs_f1)])
        SC(lambda e, c=c: e.activation(s_explam[:, c * 64:(c + 1) * 64],
                                       s_lam[:, c * 64:(c + 1) * 64], AF.Exp))
        ss_f = SSn()
        V(lambda e, c=c: e.tensor_tensor(s_ftm[:, c * 64:(c + 1) * 64],
                                         s_small[:, 64:128],
                                         s_dtsp[:, c * 64:(c + 1) * 64], OP.mult),
          wait=[("SS", ss_f)])
        V(lambda e, c=c: e.tensor_tensor(s_gtot[:, :], s_gtot[:, :],
                                         s_gamrep[:, c * 64:(c + 1) * 64],
                                         OP.mult))
        V(lambda e, c=c: e.tensor_tensor(
            s_xhat[:, :].rearrange("p (a b) -> p a b", a=64),
            s_xc[:, 0:DS].rearrange("p (a b) -> p a b", a=64),
            s_ftm[:, c * 64:(c + 1) * 64].unsqueeze(2).broadcast_to([128, 64, 64]),
            OP.mult), wait=[("SS", ss_xc)])
        vs_xhat = VSn()
        for j in range(8):
            ps = psA if j < 4 else psB
            poff = (j % 4) * 512
            TE(lambda e, ps=ps, poff=poff, j=j:
               e.matmul(ps[:, poff:poff + 512], s_xc[:, DS:DS + 128],
                        s_xhat[:, j * 512:(j + 1) * 512], start=True, stop=True),
               wait=[("VS", vs_xhat)] if j == 0 else [],
               inc=[("TS", 1)] if j == 7 else ())
        ts_state = TS()
        V(lambda e, c=c: e.tensor_tensor(
            s_hrun[:, :].rearrange("p (a b) -> p a b", a=64),
            s_hrun[:, :].rearrange("p (a b) -> p a b", a=64),
            s_gamrep[:, c * 64:(c + 1) * 64].unsqueeze(2).broadcast_to([128, 64, 64]),
            OP.mult), wait=[("TS", ts_state)])
        V(lambda e: e.tensor_tensor(s_hrun[:, 0:2048], s_hrun[:, 0:2048],
                                    psA[:, :], OP.add))
        V(lambda e: e.tensor_tensor(s_hrun[:, 2048:4096], s_hrun[:, 2048:4096],
                                    psB[:, :], OP.add))
        for b in range(8):
            bank_free[b] = VSn()
        xc_guard["v"] = VSn()
        xc_guard["t"] = ts_state

    esC1 = contextlib.ExitStack()
    sbC1 = lambda name, shape, dt: esC1.enter_context(nc.sbuf_tensor(name, shape, dt))
    s_gath = sbC1("s_gath", [128, DS + NH], F32)
    s_prod = sbC1("s_prod", [128, NH], F32)
    s_wj = sbC1("s_wj", [128, NH], F32)
    s_hw = sbC1("s_hw", [128, DS], F32)
    vs_chain = VSn()
    DMA("sync", lambda e: e.dma_start(out=ccin[:, 0:DS], in_=s_hrun[:, :]),
        wait=[("VS", vs_chain)], inc=[("PUB", 16)])
    DMA("sync", lambda e: e.dma_start(out=ccin[:, DS:DS + NH], in_=s_gtot[:, :]),
        wait=[("VS", vs_chain)], inc=[("PUB", 16)])
    DMA("gpsimd", lambda e: e.collective_compute(
        "AllGather", OP.bypass, replica_groups=[[0, 1, 2, 3], [4, 5, 6, 7]],
        ins=[ccin], outs=[ccout]),
        wait=[("PUB", CNT["PUB"]), ("GDM", CNT["GDM"])], inc=[("CC", 1)])
    cc_done = CNT["CC"]
    V(lambda e: e.memset(s_hrun[:, :], 0.0), wait=[("CC", cc_done)])
    V(lambda e: e.memset(s_prod[:, :], 1.0))
    for j in [2, 1, 0]:
        vs_gprev = VSn()
        DMA("sync", lambda e, j=j: e.dma_start(
            out=s_gath[:, :], in_=ccout[128 * j:128 * (j + 1), :]),
            wait=[("CC", cc_done), ("VS", vs_gprev), ("GTH", CNT["GTH"])],
            inc=[("GTH", 16)])
        gth = CNT["GTH"]
        V(lambda e, j=j: e.tensor_scalar(s_wj[:, :], s_prod[:, :],
                                         s_cmask[:, j:j + 1], None, OP.mult))
        V(lambda e, j=j: e.tensor_scalar(s_small[:, 0:64],
                                         s_gath[:, DS:DS + NH],
                                         s_gmask[:, j:j + 1],
                                         s_gmaski[:, j:j + 1], OP.mult, OP.add),
          wait=[("GTH", gth)])
        V(lambda e: e.tensor_tensor(s_prod[:, :], s_prod[:, :], s_small[:, 0:64],
                                    OP.mult))
        V(lambda e, j=j: e.tensor_tensor(
            s_hw[:, :].rearrange("p (a b) -> p a b", a=64),
            s_gath[:, 0:DS].rearrange("p (a b) -> p a b", a=64),
            s_wj[:, :].unsqueeze(2).broadcast_to([128, 64, 64]), OP.mult))
        V(lambda e: e.tensor_tensor(s_hrun[:, :], s_hrun[:, :], s_hw[:, :],
                                    OP.add))
    vs_hinit = VSn()
    esC1.close()

    esC2 = contextlib.ExitStack()
    sbC2 = lambda name, shape, dt: esC2.enter_context(nc.sbuf_tensor(name, shape, dt))
    s_E = sbC2("s_E", [128, 2048], F32)
    s_L = sbC2("s_L", [128, 4096], BF16)
    s_y = sbC2("s_y", [128, DS], F32)
    s_zrl = sbC2("s_zrl", [128, 2048], F32)
    s_hcast = sbC2("s_hcast", [128, 2048], BF16)
    s_bfm = sbC2("s_bfm", [128, 128], BF16)
    s_cfm = sbC2("s_cfm", [128, 128], BF16)
    s_gbf = sbC2("s_gbf", [128, 128], BF16)
    s_lamq = sbC2("s_lamq", [1, 2048], F32)
    s_dtb16 = sbC2("s_dtb16", [128, 32], BF16)
    zrl_free = {0: vs_hinit, 1: vs_hinit}
    gA = [max(bank_free.values())]
    gB = [gA[0]]

    for c in range(NT):
        ss_xc = load_conv_tile(c)
        DMA("sync", lambda e, c=c: e.dma_start(
            out=s_zrl[:, :],
            in_=zscr[128 * c:128 * (c + 1), 0:2048]),
            wait=[("VS", zrl_free[0]), ("ZUS", zus_p1a)], inc=[("ZRL", 16)])
        zrl0 = CNT["ZRL"]
        TE(lambda e: e.transpose(psA[:, 0:64].bitcast(BF16)[:, 0:128],
                                 s_xc[:, DS:DS + 128], s_identb[:, :]),
           wait=[("SS", ss_xc), ("VS", gA[0])], inc=[("TS", 1)])
        ts_b = TS()
        TE(lambda e: e.transpose(psA[:, 512:576].bitcast(BF16)[:, 0:128],
                                 s_xc[:, DS + 128:DS + 256], s_identb[:, :]),
           inc=[("TS", 1)])
        ts_c = TS()
        V(lambda e: e.tensor_copy(s_bfm[:, :], psA[:, 0:64].bitcast(BF16)[:, 0:128]),
          wait=[("TS", ts_b)])
        V(lambda e: e.tensor_copy(s_cfm[:, :], psA[:, 512:576].bitcast(BF16)[:, 0:128]),
          wait=[("TS", ts_c)])
        vs_bc = VSn()
        TE(lambda e: e.matmul(psA[:, 1024:1152], s_bfm[:, :], s_cfm[:, :],
                              start=True, stop=True),
           wait=[("VS", vs_bc)], inc=[("TS", 1)])
        ts_g = TS()
        V(lambda e: e.tensor_tensor(s_gbf[:, :], psA[:, 1024:1152], s_ut[:, :],
                                    OP.mult), wait=[("TS", ts_g)])
        gA[0] = VSn()
        for half in range(2):
            for qq in range(2):
                q = 2 * half + qq
                ps = psA if qq == 0 else psB
                DMA("sync", lambda e, c=c, q=q: e.dma_start(
                    out=s_lamq[0:1, :],
                    in_=s_lamfm[16 * q:16 * (q + 1), c * 128:(c + 1) * 128]),
                    wait=[("LFD", CNT["LFD"]), ("VS", vs_hinit),
                          ("TS", TS())], inc=[("LFD", 16)])
                lfd = CNT["LFD"]
                for j in range(4):
                    TE(lambda e, ps=ps, j=j: e.matmul(
                        ps[:, j * 512:(j + 1) * 512], s_ones1[0:1, 0:128],
                        s_lamq[0:1, j * 512:(j + 1) * 512],
                        start=True, stop=True),
                       wait=[("LFD", lfd),
                             ("VS", gA[0] if qq == 0 else gB[0])]
                       if j == 0 else [],
                       inc=[("TS", 1)] if j == 3 else ())
                ts_q = TS()
                V(lambda e, ps=ps, q=q, c=c: e.tensor_tensor(
                    s_E[:, :].rearrange("p (a b) -> p a b", a=16),
                    ps[:, :].rearrange("p (a b) -> p a b", a=16),
                    s_lam[:, c * 64 + q * 16:c * 64 + (q + 1) * 16]
                    .unsqueeze(2).broadcast_to([128, 16, 128]), OP.subtract),
                  wait=[("TS", ts_q)])
                V(lambda e: e.tensor_scalar(s_E[:, :], s_E[:, :], 0.0, None,
                                            OP.min))
                gs_e = 0
                vs_e = VSn()
                if qq == 0:
                    gA[0] = vs_e
                else:
                    gB[0] = vs_e
                SC(lambda e, qq=qq: e.activation(
                    s_L[:, qq * 2048:(qq + 1) * 2048], s_E[:, :], AF.Exp),
                   wait=[("VS", vs_e)])
            ss_l = SSn()
            V(lambda e, half=half, c=c: e.tensor_copy(
                s_dtb16[:, :], s_dtsp[:, c * 64 + half * 32:c * 64 + (half + 1) * 32]))
            V(lambda e: e.tensor_tensor(
                s_L[:, :].rearrange("p (a b) -> p a b", a=32),
                s_L[:, :].rearrange("p (a b) -> p a b", a=32),
                s_gbf[:, :].unsqueeze(1).broadcast_to([128, 32, 128]), OP.mult),
              wait=[("SS", ss_l)])
            V(lambda e: e.tensor_tensor(
                s_L[:, :].rearrange("p (a b) -> p a b", a=32),
                s_L[:, :].rearrange("p (a b) -> p a b", a=32),
                s_dtb16[:, :].unsqueeze(2).broadcast_to([128, 32, 128]), OP.mult))
            vs_stile = VSn()
            V(lambda e, half=half: e.tensor_copy(
                s_hcast[:, :], s_hrun[:, half * 2048:(half + 1) * 2048]))
            vs_hc = VSn()
            for j in range(4):
                TE(lambda e, j=j: e.matmul(
                    psA[:, j * 512:(j + 1) * 512], s_cfm[:, :],
                    s_hcast[:, j * 512:(j + 1) * 512], start=True, stop=True),
                   wait=[("VS", max(vs_hc, gA[0]))] if j == 0 else [],
                   inc=[("TS", 1)] if j == 3 else ())
            ts_inter = TS()
            for hh in range(32):
                h = half * 32 + hh
                TE(lambda e, hh=hh, h=h, first=(hh % 8 == 0), last=(hh % 8 == 7):
                   e.matmul(psB[:, hh * 64:(hh + 1) * 64],
                            s_L[:, hh * 128:(hh + 1) * 128],
                            s_xc[:, h * 64:(h + 1) * 64],
                            start=first, stop=last),
                   wait=[("VS", max(vs_stile, gB[0]))] if hh == 0 else [],
                   inc=[("TS", 1)] if hh == 31 else ())
            ts_intra = TS()
            V(lambda e, half=half, c=c: e.tensor_tensor(
                s_y[:, half * 2048:(half + 1) * 2048].rearrange(
                    "p (a b) -> p a b", a=32),
                psA[:, :].rearrange("p (a b) -> p a b", a=32),
                s_explam[:, c * 64 + half * 32:c * 64 + (half + 1) * 32]
                .unsqueeze(2).broadcast_to([128, 32, 64]), OP.mult),
              wait=[("TS", ts_inter)])
            V(lambda e, half=half: e.tensor_tensor(
                s_y[:, half * 2048:(half + 1) * 2048],
                s_y[:, half * 2048:(half + 1) * 2048], psB[:, :], OP.add),
              wait=[("TS", ts_intra)])
            gA[0] = VSn()
            gB[0] = VSn()
            V(lambda e, half=half, c=c: e.tensor_tensor(
                s_hcast[:, :].rearrange("p (a b) -> p a b", a=32),
                s_xc[:, half * 2048:(half + 1) * 2048].rearrange(
                    "p (a b) -> p a b", a=32),
                s_drep[:, half * 32:(half + 1) * 32].unsqueeze(2)
                .broadcast_to([128, 32, 64]), OP.mult))
            V(lambda e, half=half: e.tensor_tensor(
                s_y[:, half * 2048:(half + 1) * 2048],
                s_y[:, half * 2048:(half + 1) * 2048], s_hcast[:, :], OP.add))
        V(lambda e, c=c: e.tensor_tensor(
            s_xhat[:, :].rearrange("p (a b) -> p a b", a=64),
            s_xc[:, 0:DS].rearrange("p (a b) -> p a b", a=64),
            s_ftm[:, c * 64:(c + 1) * 64].unsqueeze(2).broadcast_to([128, 64, 64]),
            OP.mult))
        vs_xhat = VSn()
        for j in range(8):
            ps = psA if j < 4 else psB
            poff = (j % 4) * 512
            TE(lambda e, ps=ps, poff=poff, j=j:
               e.matmul(ps[:, poff:poff + 512], s_xc[:, DS:DS + 128],
                        s_xhat[:, j * 512:(j + 1) * 512], start=True, stop=True),
               wait=[("VS", max(vs_xhat, gA[0], gB[0]))] if j == 0 else [],
               inc=[("TS", 1)] if j == 7 else ())
        ts_state = TS()
        V(lambda e, c=c: e.tensor_tensor(
            s_hrun[:, :].rearrange("p (a b) -> p a b", a=64),
            s_hrun[:, :].rearrange("p (a b) -> p a b", a=64),
            s_gamrep[:, c * 64:(c + 1) * 64].unsqueeze(2).broadcast_to([128, 64, 64]),
            OP.mult), wait=[("TS", ts_state)])
        V(lambda e: e.tensor_tensor(s_hrun[:, 0:2048], s_hrun[:, 0:2048],
                                    psA[:, :], OP.add))
        V(lambda e: e.tensor_tensor(s_hrun[:, 2048:4096], s_hrun[:, 2048:4096],
                                    psB[:, :], OP.add))
        vs_hupd = VSn()
        SC(lambda e: e.activation(s_zrl[:, :], s_zrl[:, :], AF.Silu),
           wait=[("ZRL", zrl0)])
        ss_silu = SSn()
        V(lambda e: e.tensor_tensor(s_y[:, 0:2048], s_y[:, 0:2048],
                                    s_zrl[:, :], OP.mult),
          wait=[("SS", ss_silu)])
        vs_z0 = VSn()
        DMA("sync", lambda e, c=c: e.dma_start(
            out=s_zrl[:, :], in_=zscr[128 * c:128 * (c + 1), 2048:4096]),
            wait=[("VS", vs_z0), ("SS", SSn())], inc=[("ZRL", 16)])
        zrl1 = CNT["ZRL"]
        SC(lambda e: e.activation(s_zrl[:, :], s_zrl[:, :], AF.Silu),
           wait=[("ZRL", zrl1)])
        ss_silu2 = SSn()
        V(lambda e: e.tensor_tensor(s_y[:, 2048:4096], s_y[:, 2048:4096],
                                    s_zrl[:, :], OP.mult),
          wait=[("SS", ss_silu2)])
        zrl_free[0] = VSn()
        vs_gt = VSn()
        SC(lambda e: e.activation(s_E[:, :], s_y[:, 0:2048], AF.Square),
           wait=[("VS", vs_gt)])
        ss_sq1 = SSn()
        V(lambda e: e.reduce_sum(s_small[:, 100:101], s_E[:, :], axis=AX),
          wait=[("SS", ss_sq1)])
        vs_rs1 = VSn()
        SC(lambda e: e.activation(s_E[:, :], s_y[:, 2048:4096], AF.Square),
           wait=[("VS", vs_rs1)])
        ss_sq2 = SSn()
        V(lambda e: e.reduce_sum(s_small[:, 101:102], s_E[:, :], axis=AX),
          wait=[("SS", ss_sq2)])
        V(lambda e: e.tensor_tensor(s_small[:, 102:103], s_small[:, 100:101],
                                    s_small[:, 101:102], OP.add))
        vs_ss = VSn()
        SC(lambda e: e.activation(s_small[:, 103:104], s_small[:, 102:103],
                                  AF.Sqrt, bias=EPS, scale=1.0 / DS),
           wait=[("VS", vs_ss)])
        ss_sr = SSn()
        V(lambda e: e.reciprocal(s_small[:, 104:105], s_small[:, 103:104]),
          wait=[("SS", ss_sr)])
        V(lambda e: e.scalar_tensor_tensor(s_y[:, :], s_y[:, :],
                                           s_small[:, 104:105], s_normw[:, :],
                                           OP.mult, OP.mult))
        vs_yh = VSn()
        tb_free = {b: ("VS", vs_hupd) for b in range(8)}
        for j in range(32):
            bank = j % 8
            ps = psA if bank < 4 else psB
            poff = (bank % 4) * 512
            gsem, gval = tb_free[bank]
            TE(lambda e, j=j, ps=ps, poff=poff: e.transpose(
                ps[:, poff:poff + 128], s_y[:, j * 128:(j + 1) * 128],
                s_identf[:, :]),
               wait=[("VS", vs_yh), (gsem, gval)], inc=[("TS", 1)])
            ts = TS()
            SC(lambda e, j=j, c=c, ps=ps, poff=poff: e.copy(
                s_stat[:, j * TB + 128 * c:j * TB + 128 * (c + 1)],
                ps[:, poff:poff + 128]),
               wait=[("TS", ts)])
            tb_free[bank] = ("SS", SSn())
        V(lambda e: e.engine_nop(), wait=[("SS", SSn())])
        gA[0] = VSn()
        gB[0] = VSn()
        xc_guard["v"] = VSn()
        xc_guard["s"] = SSn()
        xc_guard["t"] = ts_state
    vs_p5 = VSn()
    ss_p5 = SSn()
    ts_p5 = TS()
    esC2.close()
    esCx.close()

    esC3 = contextlib.ExitStack()
    sbC3 = lambda name, shape, dt: esC3.enter_context(nc.sbuf_tensor(name, shape, dt))
    s_lorab = sbC3("s_lorab", [32, H], BF16)
    s_ow = [sbC3(f"s_ow{p}", [128, 32 * 512], BF16) for p in range(2)]
    s_url = [sbC3(f"s_url{p}", [128, 512], F32) for p in range(2)]
    s_xrl = [sbC3(f"s_xrl{p}", [128, 512], F32) for p in range(2)]
    s_ost = [sbC3(f"s_ost{p}", [128, 512], F32) for p in range(2)]
    DMA("gpsimd", lambda e: e.dma_start(out=s_lorab[:], in_=lorabD),
        wait=[("VS", vs_p5), ("SS", ss_p5)], inc=[("GDM", 16)])
    gdm_lb = CNT["GDM"]
    ow_free = {0: 0, 1: 0, 2: 0}
    url_free = {0: vs_p5, 1: vs_p5}
    xrl_free = {0: vs_p5, 1: vs_p5}
    ost_free = {0: 0, 1: 0}
    obank_free = {b: vs_p5 for b in range(8)}

    for og in range(8):
        buf = og % 2
        DMA("scalar", lambda e, og=og, buf=buf: e.dma_start(
            out=s_ow[buf][:, 0:32 * 512].rearrange("p (i c) -> p i c", i=32),
            in_=wsrc(owscr, og * 512, 512)),
            wait=[("TS", ow_free[buf]), ("VS", vs_p5), ("OCV", ocv_all)],
            inc=[("OWL", 16)])
        owl = CNT["OWL"]
        for t in range(8):
            rpar = t % 2
            DMA("sync", lambda e, og=og, t=t, rpar=rpar: e.dma_start(
                out=s_url[rpar][:, :],
                in_=uscr[128 * t:128 * (t + 1), og * 512:(og + 1) * 512]),
                wait=[("VS", url_free[rpar]), ("ZUS", zus_all)], inc=[("URL", 16)])
            DMA("sync", lambda e, og=og, t=t, rpar=rpar: e.dma_start(
                out=s_xrl[rpar][:, :],
                in_=xown[128 * t:128 * (t + 1), og * 512:(og + 1) * 512]),
                wait=[("VS", xrl_free[rpar])], inc=[("XRL", 16)])
            url = CNT["URL"]
            xrl = CNT["XRL"]
            bank = t % 8
            ps = psA if bank < 4 else psB
            poff = (bank % 4) * 512
            for i in range(32):
                fa = (i == 0)
                w = []
                if fa:
                    w = [("OWL", owl), ("VS", obank_free[bank]),
                         ("TS", ts_p5), ("SS", ss_p5)]
                TE(lambda e, ps=ps, poff=poff, i=i, t=t, buf=buf, fa=fa:
                   e.matmul(ps[:, poff:poff + 512],
                            s_stat[:, i * TB + 128 * t:i * TB + 128 * t + 128],
                            s_ow[buf][:, i * 512:(i + 1) * 512],
                            start=fa, stop=False),
                   wait=w)
            TE(lambda e, ps=ps, poff=poff, t=t, og=og: e.matmul(
                ps[:, poff:poff + 512],
                s_zlt[:, 128 * t:128 * (t + 1)],
                s_lorab[:, og * 512:og * 512 + 512],
                start=False, stop=True),
               wait=[("GDM", gdm_lb), ("VS", vs_p2)] if (og == 0 and t == 0)
               else [],
               inc=[("TS", 1)])
            ts_o = TS()
            opar = (og * 8 + t) % 2
            V(lambda e, ps=ps, poff=poff, opar=opar, rpar=rpar:
              e.tensor_tensor(s_ost[opar][:, :], ps[:, poff:poff + 512],
                              s_url[rpar][:, :], OP.add),
              wait=[("TS", ts_o), ("URL", url), ("STO", ost_free[opar])])
            obank_free[bank] = VSn()
            url_free[rpar] = VSn()
            V(lambda e, opar=opar, rpar=rpar: e.tensor_tensor(
                s_ost[opar][:, :], s_ost[opar][:, :], s_xrl[rpar][:, :], OP.add),
              wait=[("XRL", xrl)])
            xrl_free[rpar] = VSn()
            vs_o = VSn()
            DMA("sync", lambda e, og=og, t=t, opar=opar: e.dma_start(
                out=oseg[128 * t:128 * (t + 1), og * 512:(og + 1) * 512],
                in_=s_ost[opar][:, :]),
                wait=[("VS", vs_o)], inc=[("STO", 16)])
            ost_free[opar] = CNT["STO"]
        ow_free[buf] = TS()

    WAITQ("gpsimd", [("STO", CNT["STO"])])
    esC3.close()
    esC.close()

    with nc.Block() as block:
        @block.sync
        def _(sync):
            for f in QL["sync"]:
                f(sync)

        @block.gpsimd
        def _(g):
            for f in QL["gpsimd"]:
                f(g)

        @block.tensor
        def _(t):
            for f in QL["tensor"]:
                f(t)

        @block.vector
        def _(v):
            for f in QL["vector"]:
                f(v)

        @block.scalar
        def _(s):
            for f in QL["scalar"]:
                f(s)

    es.close()
    _cache["nc"] = nc
    return nc


def _shift_consts():
    mats = []
    for d in (1, 2, 3):
        mats.append(np.eye(128, k=d))
    for d in (1, 2, 3):
        mats.append(np.eye(128, k=d - 128))
    for d in (1, 2, 3):
        m = np.zeros((128, 128))
        for t in range(d):
            m[3 + t - d, t] = 1.0
        mats.append(m)
    return np.ascontiguousarray(
        np.concatenate(mats, axis=1).astype(ml_dtypes.bfloat16))


def kernel(**inputs):
    x = np.asarray(inputs["L0_attn_hidden_states"], np.float32)
    inw = np.asarray(inputs["L0_attn_in_proj_w"], np.float32)
    wup = np.asarray(inputs["L0_ffn_W_up"], np.float32)
    loraA = np.asarray(inputs["L0_ffn_lora_A"], np.float32)
    loraB = np.asarray(inputs["L0_ffn_lora_B"], np.float32)
    gatew = np.asarray(inputs["L0_ffn_gate_w"], np.float32)
    bf = ml_dtypes.bfloat16

    wcat = np.ascontiguousarray(np.concatenate(
        [inw, wup.T, loraA.reshape(32, H).T, gatew.T], axis=1), dtype=np.float32)
    lnw = np.asarray(inputs["L0_attn_ln_w"], np.float32)

    shared = {
        "wcat": wcat,
        "outwD": np.ascontiguousarray(
            np.asarray(inputs["L0_attn_out_proj_w"], np.float32)),
        "lorabD": np.ascontiguousarray(
            loraB.transpose(0, 2, 1).reshape(32, H)),
        "convwT": np.ascontiguousarray(
            np.asarray(inputs["L0_attn_conv_w"], np.float32).T),
        "mupvD": np.asarray(inputs["L0_attn_mup_vector"], np.float32)[None, :],
        "lnwC": np.ascontiguousarray(lnw.reshape(32, 128).T),
        "normwD": np.asarray(inputs["L0_attn_norm_w"], np.float32)[None, :],
        "alogD": np.asarray(inputs["L0_attn_A_log"], np.float32)[None, :],
        "dtbD": np.asarray(inputs["L0_attn_dt_bias"], np.float32)[None, :],
        "dvecD": np.asarray(inputs["L0_attn_D"], np.float32)[None, :],
        "ebiasD": np.asarray(inputs["L0_ffn_expert_bias"], np.float32)[None, :],
        "identbD": np.eye(128, dtype=bf),
        "identfD": np.eye(128, dtype=np.float32),
        "utonesD": np.ascontiguousarray(np.triu(np.ones((128, 128), np.float32))),
        "shiftD": _shift_consts(),
        "ones1D": np.ones((1, 128), np.float32),
    }

    nc = build_program()
    in_maps = []
    for c in range(8):
        b, g = divmod(c, 4)
        own = x[b, g * T:(g + 1) * T]
        prev3 = (np.zeros((3, H), np.float32) if g == 0
                 else x[b, g * T - 3:g * T])
        xseg = np.concatenate([prev3, own], axis=0)
        cm = np.zeros((128, 3), np.float32)
        gm = np.zeros((128, 3), np.float32)
        for j in range(3):
            if j < g:
                cm[:, j] = 1.0
            if 1 <= j <= g - 1:
                gm[:, j] = 1.0
        m = dict(shared)
        m["xsegT"] = np.ascontiguousarray(xseg.T)
        m["xown"] = np.ascontiguousarray(own)
        m["cmaskD"] = cm
        m["gmaskD"] = gm
        m["gmaskiD"] = 1.0 - gm
        in_maps.append(m)

    res = run_bass_kernel_spmd(nc, in_maps, core_ids=list(range(8)))
    out = np.empty((2, 4096, H), np.float32)
    for c in range(8):
        b, g = divmod(c, 4)
        out[b, g * T:(g + 1) * T] = res.results[c]["oseg"]
    return out


# version 22
# speedup vs baseline: 1.3361x; 1.0150x over previous
import contextlib
import numpy as np
import ml_dtypes
import concourse.bass as bass
import concourse.mybir as mybir
from concourse.bass_utils import run_bass_kernel_spmd
from concourse.alu_op_type import AluOpType as OP

F32 = mybir.dt.float32
BF16 = mybir.dt.bfloat16
AF = mybir.ActivationFunctionType
AX = mybir.AxisListType.X

H = 4096
DS = 4096
SN = 128
NH = 64
HD = 64
KC = 4
CONV = DS + 2 * SN
PROJ = 2 * DS + 2 * SN + NH
NCOL = PROJ + H + 32 + 4
T = 1024
NT = 8
TB = 1027
EPS = 1e-5

_cache = {}


def build_program():
    if "nc" in _cache:
        return _cache["nc"]
    nc = bass.Bass("TRN2", target_bir_lowering=False)

    xsegT = nc.dram_tensor("xsegT", [H, TB], F32, kind="ExternalInput").ap()
    xown = nc.dram_tensor("xown", [T, H], F32, kind="ExternalInput").ap()
    wcat = nc.dram_tensor("wcat", [H, NCOL], F32, kind="ExternalInput").ap()
    outwD = nc.dram_tensor("outwD", [DS, H], F32, kind="ExternalInput").ap()
    lorabD = nc.dram_tensor("lorabD", [32, H], F32, kind="ExternalInput").ap()
    convwT = nc.dram_tensor("convwT", [KC, CONV], F32, kind="ExternalInput").ap()
    mupvD = nc.dram_tensor("mupvD", [1, PROJ], F32, kind="ExternalInput").ap()
    lnwC = nc.dram_tensor("lnwC", [128, 32], F32, kind="ExternalInput").ap()
    normwD = nc.dram_tensor("normwD", [1, DS], F32, kind="ExternalInput").ap()
    alogD = nc.dram_tensor("alogD", [1, NH], F32, kind="ExternalInput").ap()
    dtbD = nc.dram_tensor("dtbD", [1, NH], F32, kind="ExternalInput").ap()
    dvecD = nc.dram_tensor("dvecD", [1, NH], F32, kind="ExternalInput").ap()
    ebiasD = nc.dram_tensor("ebiasD", [1, 4], F32, kind="ExternalInput").ap()
    identbD = nc.dram_tensor("identbD", [128, 128], BF16, kind="ExternalInput").ap()
    identfD = nc.dram_tensor("identfD", [128, 128], F32, kind="ExternalInput").ap()
    utonesD = nc.dram_tensor("utonesD", [128, 128], F32, kind="ExternalInput").ap()
    shiftD = nc.dram_tensor("shiftD", [128, 128 * 9], BF16, kind="ExternalInput").ap()
    ones1D = nc.dram_tensor("ones1D", [1, 128], F32, kind="ExternalInput").ap()
    cmaskD = nc.dram_tensor("cmaskD", [128, 3], F32, kind="ExternalInput").ap()
    gmaskD = nc.dram_tensor("gmaskD", [128, 3], F32, kind="ExternalInput").ap()
    gmaskiD = nc.dram_tensor("gmaskiD", [128, 3], F32, kind="ExternalInput").ap()

    oseg = nc.dram_tensor("oseg", [T, H], F32, kind="ExternalOutput").ap()

    convscr = nc.dram_tensor("convscr", [T, CONV], BF16, kind="Internal").ap()
    xnscr = nc.dram_tensor("xnscr", [128, 32 * T], BF16, kind="Internal").ap()
    zscr = nc.dram_tensor("zscr", [T, DS], F32, kind="Internal").ap()
    uscr = nc.dram_tensor("uscr", [T, H], F32, kind="Internal").ap()
    mupwscr = nc.dram_tensor("mupwscr", [128, CONV * KC], BF16, kind="Internal").ap()
    wscr = nc.dram_tensor("wscr", [H, NCOL], BF16, kind="Internal").ap()
    owscr = nc.dram_tensor("owscr", [DS, H], BF16, kind="Internal").ap()
    ccin = nc.dram_tensor("ccin", [128, DS + NH], F32, kind="Internal").ap()
    ccout = nc.dram_tensor("ccout", [512, DS + NH], F32, kind="Internal").ap()

    es = contextlib.ExitStack()
    sbt = lambda name, shape, dt: es.enter_context(nc.sbuf_tensor(name, shape, dt))

    sem_names = ["VS", "SS", "TS", "LDC", "LDX", "LDY", "WCV", "OCV", "WLD", "XNR", "CPL", "ZUS",
                 "XLD", "ZRL", "URL", "XRL", "CC", "GDM", "STO", "LFD", "ROW",
                 "OWL", "GTH", "PUB", "MWS", "MWL", "GS"]
    SEM = {}
    CNT = {}
    QL = {"tensor": [], "vector": [], "scalar": [], "sync": [], "gpsimd": []}

    def emit(eng, fn, wait=(), inc=()):
        wait = tuple((s, v) for (s, v) in wait if v > 0)
        inc = tuple(inc)

        def step(e, _fn=fn, _w=wait, _i=inc):
            for s, v in _w:
                e.wait_ge(SEM[s], v)
            inst = _fn(e)
            for s, a in _i:
                inst = inst.then_inc(SEM[s], a)
        QL[eng].append(step)

    def bump(inc):
        for s, a in inc:
            CNT[s] += a

    def V(fn, wait=(), inc=()):
        w = [("VS", CNT["VS"])] + list(wait)
        i = [("VS", 1)] + list(inc)
        bump(i)
        emit("vector", fn, w, i)

    def SC(fn, wait=(), inc=()):
        w = [("SS", CNT["SS"])] + list(wait)
        i = [("SS", 1)] + list(inc)
        bump(i)
        emit("scalar", fn, w, i)

    def TE(fn, wait=(), inc=()):
        bump(inc)
        emit("tensor", fn, wait, inc)

    def DMA(eng, fn, wait=(), inc=()):
        bump(inc)
        emit(eng, fn, wait, inc)

    def G(fn, wait=(), inc=()):
        w = [("GS", CNT["GS"])] + list(wait)
        i = [("GS", 1)] + list(inc)
        bump(i)
        emit("gpsimd", fn, w, i)

    def WAITQ(eng, wait):
        emit(eng, lambda e: None, wait, ())

    s_identb = sbt("s_identb", [128, 128], BF16)
    s_identf = sbt("s_identf", [128, 128], F32)
    s_ut = sbt("s_ut", [128, 128], F32)
    s_shift = sbt("s_shift", [128, 128 * 9], BF16)
    s_ones1 = sbt("s_ones1", [1, 128], F32)
    s_onescol = sbt("s_onescol", [128, 1], F32)
    s_row64 = sbt("s_row64", [1, 448], F32)
    s_lnwc = sbt("s_lnwc", [128, 32], F32)
    s_dtbrep = sbt("s_dtbrep", [128, NH], F32)
    s_arep = sbt("s_arep", [128, NH], F32)
    s_drep = sbt("s_drep", [128, NH], F32)
    s_ebrep = sbt("s_ebrep", [128, 4], F32)
    s_cmask = sbt("s_cmask", [128, 3], F32)
    s_gmask = sbt("s_gmask", [128, 3], F32)
    s_gmaski = sbt("s_gmaski", [128, 3], F32)
    s_rmsall = sbt("s_rmsall", [128, 16], F32)
    s_dtsp = sbt("s_dtsp", [128, NH * NT], F32)
    s_dtaall = sbt("s_dtaall", [128, NH * NT], F32)
    s_lam = sbt("s_lam", [128, NH * NT], F32)
    s_lamfm = sbt("s_lamfm", [64, 128 * NT], F32)
    s_ftm = sbt("s_ftm", [128, NH * NT], F32)
    s_gamrep = sbt("s_gamrep", [128, NH * NT], F32)
    s_explam = sbt("s_explam", [128, NH * NT], F32)
    s_gtot = sbt("s_gtot", [128, NH], F32)
    s_glog = sbt("s_glog", [128, 4 * NT], F32)
    s_zlora = sbt("s_zlora", [128, 32 * NT], F32)
    s_zlt = sbt("s_zlt", [32, 128 * NT], BF16)
    s_hrun = sbt("s_hrun", [128, DS], F32)
    s_small = sbt("s_small", [128, 128], F32)
    s_normw = sbt("s_normw", [128, DS], BF16)
    s_stat = sbt("s_stat", [128, 32 * TB], BF16)
    s_mupdt = sbt("s_mupdt", [128, NH], F32)
    s_mupz = sbt("s_mupz", [128, DS], BF16)
    s_cnst = sbt("s_cnst", [128, 2], F32)

    psA = es.enter_context(nc.psum_tensor("psA", [128, 2048], F32))
    psB = es.enter_context(nc.psum_tensor("psB", [128, 2048], F32))

    for n in sem_names:
        SEM[n] = es.enter_context(nc.semaphore(n))
        CNT[n] = 0

    TS = lambda: CNT["TS"]
    VSn = lambda: CNT["VS"]
    SSn = lambda: CNT["SS"]

    esA = contextlib.ExitStack()
    sbA = lambda name, shape, dt: esA.enter_context(nc.sbuf_tensor(name, shape, dt))
    s_srow = sbA("s_srow", [1, 5200], F32)
    s_cwp = sbA("s_cwp", [1, 512], F32)
    s_cwq = sbA("s_cwq", [1, 512], F32)
    s_repp = sbA("s_repp", [128, 512], BF16)

    for dst, src in [(s_identb, identbD), (s_identf, identfD), (s_ut, utonesD),
                     (s_shift, shiftD), (s_ones1, ones1D), (s_cmask, cmaskD),
                     (s_gmask, gmaskD), (s_gmaski, gmaskiD), (s_lnwc, lnwC)]:
        DMA("sync", lambda e, d=dst, s=src: e.dma_start(out=d[:], in_=s),
            (), [("LDC", 16)])
    DMA("sync", lambda e: e.dma_start(out=s_row64[0:1, 0:64], in_=dtbD), (), [("LDC", 16)])
    DMA("sync", lambda e: e.dma_start(out=s_row64[0:1, 64:128], in_=alogD), (), [("LDC", 16)])
    DMA("sync", lambda e: e.dma_start(out=s_row64[0:1, 128:192], in_=dvecD), (), [("LDC", 16)])
    DMA("sync", lambda e: e.dma_start(out=s_row64[0:1, 192:196], in_=ebiasD), (), [("LDC", 16)])
    ldc_all = CNT["LDC"]
    WAITQ("vector", [("LDC", ldc_all)])
    WAITQ("scalar", [("LDC", ldc_all)])
    WAITQ("tensor", [("LDC", ldc_all)])

    V(lambda e: e.memset(s_onescol[:], 1.0))
    V(lambda e: e.memset(s_hrun[:], 0.0))
    V(lambda e: e.memset(s_gtot[:], 1.0))
    V(lambda e: e.memset(s_cnst[:, 0:1], EPS))
    V(lambda e: e.memset(s_cnst[:, 1:2], 1.0 / DS))
    nc.const_aps.register(F32, EPS, s_cnst[:, 0:1])
    nc.const_aps.register(F32, 1.0 / DS, s_cnst[:, 1:2])

    def k1rep(dst_ap, row_ap, width, ps, poff, sc_mul=None, wt=()):
        TE(lambda e, w=width: e.matmul(ps[:, poff:poff + w], s_ones1[0:1, 0:128],
                                       row_ap, start=True, stop=True),
           wait=[("VS", VSn()), ("SS", SSn())] + list(wt), inc=[("TS", 1)])
        ts = TS()
        if sc_mul is None:
            V(lambda e, w=width: e.tensor_copy(dst_ap, ps[:, poff:poff + w]),
              wait=[("TS", ts)])
        else:
            V(lambda e, w=width: e.tensor_scalar(dst_ap, ps[:, poff:poff + w],
                                                 sc_mul, None, OP.mult),
              wait=[("TS", ts)])

    SC(lambda e: e.activation(s_row64[0:1, 256:320], s_row64[0:1, 64:128], AF.Exp))
    k1rep(s_dtbrep[:, :], s_row64[0:1, 0:64], NH, psA, 0)
    k1rep(s_arep[:, :], s_row64[0:1, 256:320], NH, psA, 512, sc_mul=-1.0)
    k1rep(s_drep[:, :], s_row64[0:1, 128:192], NH, psA, 1024)
    k1rep(s_ebrep[:, :], s_row64[0:1, 192:196], 4, psA, 1536)

    mws_g = [0]
    for k in range(KC):
        for j in range(9):
            w = 512 if j < 8 else 256
            off = j * 512
            DMA("sync", lambda e, k=k, off=off, w=w: e.dma_start(
                out=s_cwp[0:1, 0:w], in_=convwT[k:k + 1, off:off + w]),
                wait=[("VS", mws_g[0])], inc=[("LDX", 16)])
            DMA("sync", lambda e, off=off, w=w: e.dma_start(
                out=s_cwq[0:1, 0:w], in_=mupvD[0:1, DS + off:DS + off + w]),
                wait=[("VS", mws_g[0])], inc=[("LDX", 16)])
            ldx = CNT["LDX"]
            V(lambda e, w=w: e.tensor_tensor(s_cwp[0:1, 0:w], s_cwp[0:1, 0:w],
                                             s_cwq[0:1, 0:w], OP.mult),
              wait=[("LDX", ldx)])
            k1rep(s_repp[:, 0:w], s_cwp[0:1, 0:w], w, psA, 0,
                  wt=[("MWS", CNT["MWS"])])
            vs = VSn()
            DMA("sync", lambda e, k=k, off=off, w=w: e.dma_start(
                out=mupwscr[:, k * CONV + off:k * CONV + off + w],
                in_=s_repp[:, 0:w]),
                wait=[("VS", vs)], inc=[("MWS", 16)])
            mws_g[0] = VSn()
    mws_all = CNT["MWS"]

    for j in range(8):
        DMA("sync", lambda e, j=j: e.dma_start(
            out=s_cwp[0:1, 0:512], in_=mupvD[0:1, j * 512:(j + 1) * 512]),
            wait=[("VS", mws_g[0])], inc=[("LDX", 16)])
        ldx = CNT["LDX"]
        WAITQ("tensor", [("LDX", ldx)])
        k1rep(s_mupz[:, j * 512:(j + 1) * 512], s_cwp[0:1, 0:512], 512,
              psA if j % 2 == 0 else psB, 0)
        mws_g[0] = VSn()
    DMA("sync", lambda e: e.dma_start(out=s_cwp[0:1, 0:64],
                                      in_=mupvD[0:1, PROJ - NH:PROJ]),
        wait=[("VS", mws_g[0])], inc=[("LDX", 16)])
    ldx = CNT["LDX"]
    WAITQ("tensor", [("LDX", ldx)])
    k1rep(s_mupdt[:, :], s_cwp[0:1, 0:64], NH, psA, 512)
    for j in range(8):
        DMA("sync", lambda e, j=j: e.dma_start(
            out=s_cwp[0:1, 0:512], in_=normwD[0:1, j * 512:(j + 1) * 512]),
            wait=[("VS", VSn())], inc=[("LDX", 16)])
        ldx = CNT["LDX"]
        WAITQ("tensor", [("LDX", ldx)])
        k1rep(s_normw[:, j * 512:(j + 1) * 512], s_cwp[0:1, 0:512], 512,
              psA if j % 2 == 0 else psB, 1024)

    s_xf = [sbA(f"s_xf{p}", [128, TB], F32) for p in range(4)]
    s_mrep = sbA("s_mrep", [128, TB], F32)
    s_rrep = sbA("s_rrep", [128, TB], F32)
    s_xtmp = sbA("s_xtmp", [128, TB], F32)
    s_xnb = [sbA(f"s_xnb{p}", [128, TB], BF16) for p in range(2)]
    OM, OV, OR, OU, OT = 0, 1040, 2080, 3120, 4160

    xf_guard = [0, 0, 0, 0]
    sq_bufs = [s_mrep, s_rrep]
    tsq_end = {}
    for i in range(32):
        qs = "LDX" if i % 2 == 0 else "LDY"
        DMA("sync" if i % 2 == 0 else "scalar", lambda e, i=i: e.dma_start(
            out=s_xf[i % 4][:], in_=xsegT[128 * i:128 * (i + 1), :]),
            wait=[("TS", xf_guard[i % 4])], inc=[(qs, 16)])
        ldx = CNT[qs]
        sq = sq_bufs[i % 2]
        SC(lambda e, i=i, sq=sq: e.activation(sq[:], s_xf[i % 4][:], AF.Square),
           wait=[(qs, ldx), ("TS", tsq_end.get(i - 2, 0))])
        ssq = SSn()
        for c in range(3):
            w = 512 if c < 2 else 3
            TE(lambda e, i=i, c=c, w=w: e.matmul(
                psA[0:1, c * 512:c * 512 + w], s_onescol[:, 0:1],
                s_xf[i % 4][:, c * 512:c * 512 + w], start=(i == 0), stop=(i == 31)),
               wait=[(qs, ldx), ("VS", VSn() if i == 0 else 0)],
               inc=[("TS", 1)])
            TE(lambda e, i=i, c=c, w=w, sq=sq: e.matmul(
                psB[0:1, c * 512:c * 512 + w], s_onescol[:, 0:1],
                sq[:, c * 512:c * 512 + w], start=(i == 0), stop=(i == 31)),
               wait=[("SS", ssq)], inc=[("TS", 1)])
        xf_guard[i % 4] = TS()
        tsq_end[i] = TS()
    ts_stats = TS()
    V(lambda e: e.tensor_scalar(s_srow[0:1, OM:OM + TB], psA[0:1, 0:TB],
                                1.0 / H, None, OP.mult), wait=[("TS", ts_stats)])
    V(lambda e: e.tensor_scalar(s_srow[0:1, OV:OV + TB], psB[0:1, 0:TB],
                                1.0 / H, None, OP.mult))
    V(lambda e: e.tensor_tensor(s_srow[0:1, OT:OT + TB], s_srow[0:1, OM:OM + TB],
                                s_srow[0:1, OM:OM + TB], OP.mult))
    V(lambda e: e.tensor_sub(s_srow[0:1, OV:OV + TB], s_srow[0:1, OV:OV + TB],
                             s_srow[0:1, OT:OT + TB]))
    V(lambda e: e.tensor_scalar(s_srow[0:1, OT:OT + TB], s_srow[0:1, OV:OV + TB],
                                EPS, None, OP.add))
    V(lambda e: e.reciprocal(s_srow[0:1, OR:OR + TB], s_srow[0:1, OT:OT + TB]))
    vs_r2 = VSn()
    SC(lambda e: e.activation(s_srow[0:1, OT:OT + TB], s_srow[0:1, OR:OR + TB],
                              AF.Sqrt), wait=[("VS", vs_r2)])
    ss_r = SSn()
    V(lambda e: e.tensor_tensor(s_srow[0:1, OU:OU + TB], s_srow[0:1, OV:OV + TB],
                                s_srow[0:1, OR:OR + TB], OP.mult),
      wait=[("SS", ss_r)])
    V(lambda e: e.tensor_copy(s_srow[0:1, OR:OR + TB], s_srow[0:1, OT:OT + TB]))
    vs_u = VSn()
    SC(lambda e: e.activation(s_srow[0:1, OT:OT + TB], s_srow[0:1, OU:OU + TB],
                              AF.Sqrt, bias=EPS, scale=1.0), wait=[("VS", vs_u)])
    ss_su = SSn()
    V(lambda e: e.reciprocal(s_srow[0:1, OU:OU + TB], s_srow[0:1, OT:OT + TB]),
      wait=[("SS", ss_su)])
    for c in range(3):
        w = 512 if c < 2 else 3
        k1rep(s_mrep[:, c * 512:c * 512 + w], s_srow[0:1, OM + c * 512:OM + c * 512 + w],
              w, psA, c * 512)
        k1rep(s_rrep[:, c * 512:c * 512 + w], s_srow[0:1, OR + c * 512:OR + c * 512 + w],
              w, psB, c * 512)
    for t in range(9):
        nr = 128 if t < 8 else 3
        so = OU + (3 + 128 * t if t < 8 else 0)
        TE(lambda e, nr=nr, so=so: e.transpose(
            psA[0:nr, 1536:1537], s_srow[0:1, so:so + nr], s_identf[0:1, 0:1]),
           wait=[("VS", VSn())], inc=[("TS", 1)])
        ts = TS()
        V(lambda e, t=t, nr=nr: e.tensor_copy(s_rmsall[0:nr, t:t + 1],
                                              psA[0:nr, 1536:1537]),
          wait=[("TS", ts)])

    vs_sub = {}
    spill_done = {0: ("XNR", 0), 1: ("WCV", 0)}
    xnr_g = [0]
    for i in range(32):
        qs = "LDX" if i % 2 == 0 else "LDY"
        DMA("sync" if i % 2 == 0 else "scalar", lambda e, i=i: e.dma_start(
            out=s_xf[i % 4][:], in_=xsegT[128 * i:128 * (i + 1), :]),
            wait=[("TS", xf_guard[i % 4]), ("VS", vs_sub.get(i - 4, 0))],
            inc=[(qs, 16)])
        ldx = CNT[qs]
        V(lambda e, i=i: e.tensor_tensor(s_xtmp[:], s_xf[i % 4][:], s_mrep[:],
                                         OP.subtract), wait=[(qs, ldx)])
        vs_sub[i] = VSn()
        xb = s_xnb[i % 2]
        V(lambda e, xb=xb: e.tensor_tensor(xb[:], s_xtmp[:], s_rrep[:], OP.mult),
          wait=[spill_done[i % 2]])
        vs_xn = VSn()
        sq2 = "XNR" if i % 2 == 0 else "WCV"
        DMA("sync" if i % 2 == 0 else "scalar", lambda e, i=i, xb=xb: e.dma_start(
            out=xnscr[:, i * T:(i + 1) * T], in_=xb[:, 3:TB]),
            wait=[("VS", vs_xn)], inc=[(sq2, 16)])
        spill_done[i % 2] = (sq2, CNT[sq2])
        V(lambda e, i=i, xb=xb: e.tensor_scalar(
            s_stat[:, i * TB:(i + 1) * TB], xb[:], s_lnwc[:, i:i + 1],
            None, OP.mult))
    vs_pln = VSn()
    xnr_pln = CNT["XNR"]
    wcv_pln = CNT["WCV"]
    esA.close()

    esB = contextlib.ExitStack()
    sbB = lambda name, shape, dt: esB.enter_context(nc.sbuf_tensor(name, shape, dt))
    s_w = [sbB(f"s_w{p}", [128, 32 * 512], BF16) for p in range(2)]
    s_stg = [sbB(f"s_stg{p}", [128, 512], F32) for p in range(2)]
    s_xt = [[sbB(f"s_xt{k}_{p}", [128, 512], BF16) for p in range(2)]
            for k in range(KC)]
    s_cstg = [sbB(f"s_cstg{p}", [128, 512], BF16) for p in range(2)]
    s_mws = [sbB("s_mws0", [128, KC * 512], BF16)]
    s_dtt = sbB("s_dtt", [128, NH], F32)
    s_dtt2 = sbB("s_dtt2", [128, NH], F32)
    s_dtt3 = sbB("s_dtt3", [128, NH], F32)
    s_p2a = sbB("s_p2a", [128, 64], F32)
    s_p2b = sbB("s_p2b", [128, 32], BF16)

    def wsrc(hbm, c0, cw):
        return hbm.rearrange("(i p) c -> p i c", p=128)[:, :, c0:c0 + cw]

    bank_free = {b: 0 for b in range(8)}
    wbuf_free = {0: 0, 1: 0}
    stg_free = {0: 0, 1: 0}
    stg_cnt = [0]
    cstg_free = {0: 0, 1: 0}
    cstg_cnt = [0]
    xt_free = {0: 0, 1: 0}
    mws_free = {0: vs_pln, 1: vs_pln}
    half_idx = [0]

    def SH(ix):
        return s_shift[:, ix * 128:(ix + 1) * 128]

    def proj_group(g0, gw, conv_order, stat_off_fn, first_wait, stat_wait=None):
        nhal = (gw + 511) // 512
        bufs = []
        wlds = []
        for hf in range(nhal):
            c0 = g0 + hf * 512
            cw = min(512, g0 + gw - c0)
            buf = half_idx[0] % 2
            half_idx[0] += 1
            bufs.append((buf, c0, cw))
            DMA("gpsimd", lambda e, c0=c0, cw=cw, buf=buf: e.dma_start(
                out=s_w[buf][:, 0:32 * 512].rearrange(
                    "p (i c) -> p i c", i=32)[:, :, 0:cw],
                in_=wsrc(wcat, c0, cw)),
                wait=[("TS", wbuf_free[buf])] + list(first_wait),
                inc=[("WLD", 16)])
            wlds.append(CNT["WLD"])
        is_conv = DS <= g0 < PROJ
        tiles = conv_order if is_conv else list(range(8))
        for t in tiles:
            nr = 128 if t < 8 else 3
            bank = t % 8
            ps = psA if bank < 4 else psB
            poff = (bank % 4) * 512
            for hj, (buf, c0, cw) in enumerate(bufs):
                sub = c0 - g0
                for i in range(32):
                    so = stat_off_fn(i, t)
                    fa = (hj == 0 and i == 0)
                    la = (hj == nhal - 1 and i == 31)
                    w = [("WLD", wlds[hj])]
                    if stat_wait is not None:
                        w += stat_wait(i)
                    if fa:
                        w.append(("VS", bank_free[bank]))
                    TE(lambda e, nr=nr, ps=ps, poff=poff, sub=sub, cw=cw,
                       so=so, buf=buf, i=i, fa=fa, la=la:
                       e.matmul(ps[0:nr, poff + sub:poff + sub + cw],
                                s_stat[:, so:so + nr],
                                s_w[buf][:, i * 512:i * 512 + cw],
                                start=fa, stop=la),
                       wait=w, inc=[("TS", 1)] if la else ())
            yield t, nr, ps, poff, bank, TS()
        for buf, _, _ in bufs:
            wbuf_free[buf] = TS()

    def evac_z(g0, t, ps, poff, bank, ts_mm):
        par = stg_cnt[0] % 2
        stg_cnt[0] += 1
        V(lambda e, t=t, ps=ps, poff=poff, par=par, g0=g0:
          e.scalar_tensor_tensor(s_stg[par][:, 0:512], ps[:, poff:poff + 512],
                                 s_rmsall[:, t:t + 1], s_mupz[:, g0:g0 + 512],
                                 OP.mult, OP.mult),
          wait=[("TS", ts_mm), ("ZUS", stg_free[par])])
        bank_free[bank] = VSn()
        vs = VSn()
        DMA("sync", lambda e, t=t, par=par, g0=g0: e.dma_start(
            out=zscr[128 * t:128 * (t + 1), g0:g0 + 512], in_=s_stg[par][:, 0:512]),
            wait=[("VS", vs)], inc=[("ZUS", 16)])
        stg_free[par] = CNT["ZUS"]

    def evac_conv(cc0, ccw, t, nr, ps, poff, bank, ts_mm, pos, mpar, mwl):
        par = pos % 2
        for k in range(KC):
            V(lambda e, k=k, ps=ps, poff=poff, ccw=ccw, nr=nr, t=t, par=par,
              mpar=mpar:
              e.scalar_tensor_tensor(
                  s_xt[k][par][0:nr, 0:ccw], ps[0:nr, poff:poff + ccw],
                  (s_rmsall[:, t:t + 1] if t < 8 else s_rmsall[0:3, 8:9]),
                  s_mws[mpar][0:nr, k * 512:k * 512 + ccw],
                  OP.mult, OP.mult),
              wait=([("TS", max(ts_mm, xt_free[par])), ("MWL", mwl)]
                    if k == 0 else []))
        vs_xt = VSn()
        bank_free[bank] = vs_xt
        if t == 8:
            return
        for k in range(KC):
            d = 3 - k
            lhs = s_identb[:, :] if d == 0 else SH(d - 1)
            fa = (k == 0)
            TE(lambda e, lhs=lhs, ps=ps, poff=poff, ccw=ccw, k=k, par=par,
               fa=fa: e.matmul(ps[:, poff:poff + ccw], lhs,
                               s_xt[k][par][:, 0:ccw], start=fa, stop=False),
               wait=[("VS", vs_xt)] if fa else [])
        for k in range(KC - 1):
            d = 3 - k
            lhs = SH(5 + d) if t == 0 else SH(2 + d)
            la = (k == KC - 2)
            TE(lambda e, lhs=lhs, ps=ps, poff=poff, ccw=ccw, k=k, par=par,
               la=la: e.matmul(ps[:, poff:poff + ccw], lhs,
                               s_xt[k][1 - par][:, 0:ccw], start=False, stop=la),
               wait=[], inc=[("TS", 1)] if la else ())
        ts_conv = TS()
        xt_free[1 - par] = ts_conv
        cpar = cstg_cnt[0] % 2
        cstg_cnt[0] += 1
        V(lambda e, ps=ps, poff=poff, ccw=ccw, cpar=cpar:
          e.tensor_copy(s_cstg[cpar][:, 0:ccw], ps[:, poff:poff + ccw]),
          wait=[("TS", ts_conv), ("CPL", cstg_free[cpar])])
        bank_free[bank] = VSn()
        vs = VSn()
        DMA("sync", lambda e, t=t, cc0=cc0, ccw=ccw, cpar=cpar: e.dma_start(
            out=convscr[128 * t:128 * (t + 1), cc0:cc0 + ccw],
            in_=s_cstg[cpar][:, 0:ccw]),
            wait=[("VS", vs)], inc=[("CPL", 16)])
        cstg_free[cpar] = CNT["CPL"]

    def evac_dt(t, ps, poff, ts_mm):
        V(lambda e, t=t, ps=ps, poff=poff: e.scalar_tensor_tensor(
            s_dtt[:, :], ps[:, poff + 256:poff + 320], s_rmsall[:, t:t + 1],
            s_mupdt[:, :], OP.mult, OP.mult), wait=[("TS", ts_mm)])
        V(lambda e: e.tensor_tensor(s_dtt[:, :], s_dtt[:, :], s_dtbrep[:, :],
                                    OP.add))
        vs_z = VSn()
        SC(lambda e: e.activation(s_dtt2[:, :], s_dtt[:, :], AF.Abs),
           wait=[("VS", vs_z)])
        SC(lambda e: e.activation(s_dtt3[:, :], s_dtt2[:, :], AF.Exp, scale=-1.0))
        SC(lambda e: e.activation(s_dtt2[:, :], s_dtt3[:, :], AF.Ln,
                                  bias=1.0, scale=1.0))
        SC(lambda e: e.activation(s_dtt3[:, :], s_dtt[:, :], AF.Relu))
        ss_sp = SSn()
        V(lambda e, t=t: e.tensor_add(s_dtsp[:, t * 64:(t + 1) * 64],
                                      s_dtt2[:, :], s_dtt3[:, :]),
          wait=[("SS", ss_sp)])
        V(lambda e, t=t: e.tensor_tensor(s_dtaall[:, t * 64:(t + 1) * 64],
                                         s_dtsp[:, t * 64:(t + 1) * 64],
                                         s_arep[:, :], OP.mult))

    def attn_stat_off(i, t):
        return i * TB + (3 + 128 * t if t < 8 else 0)

    conv_order = [8] + list(range(8))
    for k in range(KC):
        for p in range(2):
            V(lambda e, k=k, p=p: e.memset(s_xt[k][p][:, :], 0.0),
              wait=[("TS", TS())])
    first_wait = [("VS", vs_pln), ("MWS", mws_all)]
    mcnt = [0]
    for g in range(17):
        g0 = g * 512
        gw = 512 if g < 16 else 320
        is_conv = DS <= g0
        mpar = 0
        mwl = 0
        if is_conv:
            cc0 = g0 - DS
            ccw = min(gw, 512) if g < 16 else 256
            mpar = 0
            mcnt[0] += 1
            DMA("sync", lambda e, cc0=cc0, ccw=ccw, mpar=mpar: e.dma_start(
                out=s_mws[mpar][:, 0:KC * 512].rearrange(
                    "p (k c) -> p k c", k=KC)[:, :, 0:ccw],
                in_=mupwscr[:, :].rearrange(
                    "p (k c) -> p k c", k=KC)[:, :, cc0:cc0 + ccw]),
                wait=[("MWS", mws_all), ("VS", mws_free[mpar])],
                inc=[("MWL", 16)])
            mwl = CNT["MWL"]
        for t, nr, ps, poff, bank, ts_mm in proj_group(
                g0, gw, conv_order, attn_stat_off, first_wait):
            first_wait = []
            if not is_conv:
                evac_z(g0, t, ps, poff, bank, ts_mm)
            else:
                pos = conv_order.index(t)
                evac_conv(g0 - DS, ccw, t, nr, ps, poff, bank, ts_mm, pos,
                          mpar, mwl)
                if g == 16 and t != 8:
                    evac_dt(t, ps, poff, ts_mm)
        if is_conv:
            mws_free[mpar] = VSn()
        if g < 16:
            DMA("gpsimd", lambda e, g=g: e.dma_start(
                out=owscr[:, g * 256:(g + 1) * 256],
                in_=outwD[:, g * 256:(g + 1) * 256]),
                (), [("OCV", 16)])
    ocv_all = CNT["OCV"]
    ts_p1a = TS()
    cpl_all = CNT["CPL"]
    zus_p1a = CNT["ZUS"]

    xnr_need = [0] * 8
    xld_need = [0] * 8
    for cch in range(8):
        DMA("sync" if cch % 2 == 0 else "scalar", lambda e, cch=cch: e.dma_start(
            out=s_stat[:, 0:32 * TB].rearrange(
                "p (i c) -> p i c", c=TB)[:, 4 * cch:4 * cch + 4, 0:T],
            in_=xnscr[:, :].rearrange(
                "p (i c) -> p i c", c=T)[:, 4 * cch:4 * cch + 4, :]),
            wait=[("TS", ts_p1a), ("XNR", xnr_pln), ("WCV", wcv_pln)],
            inc=[("XNR" if cch % 2 == 0 else "WCV", 16)])
        for cj in range(cch, 8):
            if cch % 2 == 0:
                xnr_need[cj] = CNT["XNR"]
            else:
                xld_need[cj] = CNT["WCV"]

    def ffn_stat_off(i, t):
        return i * TB + 128 * t

    def evac_u(g0, t, ps, poff, bank, ts_mm):
        par = stg_cnt[0] % 2
        stg_cnt[0] += 1
        u0 = g0 - PROJ
        V(lambda e, ps=ps, poff=poff, par=par:
          e.tensor_copy(s_stg[par][:, 0:512], ps[:, poff:poff + 512]),
          wait=[("TS", ts_mm), ("ZUS", stg_free[par])])
        bank_free[bank] = VSn()
        vs = VSn()
        DMA("sync", lambda e, t=t, par=par, u0=u0: e.dma_start(
            out=uscr[128 * t:128 * (t + 1), u0:u0 + 512],
            in_=s_stg[par][:, 0:512]),
            wait=[("VS", vs)], inc=[("ZUS", 16)])
        stg_free[par] = CNT["ZUS"]

    sw0 = (lambda i: [("XNR", xnr_need[i // 4]), ("WCV", xld_need[i // 4])])
    for g in range(8):
        g0 = PROJ + g * 512
        for t, nr, ps, poff, bank, ts_mm in proj_group(
                g0, 512, None, ffn_stat_off, [],
                stat_wait=sw0 if g == 0 else None):
            evac_u(g0, t, ps, poff, bank, ts_mm)
    for t, nr, ps, poff, bank, ts_mm in proj_group(
            PROJ + 4096, 36, None, ffn_stat_off, []):
        V(lambda e, ps=ps, poff=poff, t=t:
          e.tensor_copy(s_zlora[:, t * 32:(t + 1) * 32], ps[:, poff:poff + 32]),
          wait=[("TS", ts_mm)])
        V(lambda e, ps=ps, poff=poff, t=t:
          e.tensor_copy(s_glog[:, t * 4:(t + 1) * 4], ps[:, poff + 32:poff + 36]))
        bank_free[bank] = VSn()
    ts_p1b = TS()
    zus_all = CNT["ZUS"]

    for t in range(8):
        sm = s_small
        V(lambda e, t=t: e.tensor_tensor(sm[:, 0:4], s_glog[:, 4 * t:4 * t + 4],
                                         s_ebrep[:, :], OP.add))
        V(lambda e: e.reduce_max(sm[:, 4:5], sm[:, 0:4], axis=AX))
        for eix in range(4):
            V(lambda e, eix=eix: e.tensor_scalar(
                sm[:, 5 + eix:6 + eix], sm[:, eix:eix + 1], sm[:, 4:5], None,
                OP.is_ge))
        V(lambda e: e.tensor_copy(sm[:, 9:10], sm[:, 5:6]))
        V(lambda e: e.tensor_scalar(sm[:, 13:14], sm[:, 9:10], -1.0, 1.0,
                                    OP.mult, OP.add))
        V(lambda e: e.tensor_tensor(sm[:, 10:11], sm[:, 6:7], sm[:, 13:14],
                                    OP.mult))
        V(lambda e: e.tensor_scalar(sm[:, 14:15], sm[:, 10:11], -1.0, 1.0,
                                    OP.mult, OP.add))
        V(lambda e: e.tensor_tensor(sm[:, 14:15], sm[:, 14:15], sm[:, 13:14],
                                    OP.mult))
        V(lambda e: e.tensor_tensor(sm[:, 11:12], sm[:, 7:8], sm[:, 14:15],
                                    OP.mult))
        V(lambda e: e.tensor_tensor(sm[:, 15:16], sm[:, 9:10], sm[:, 10:11],
                                    OP.add))
        V(lambda e: e.tensor_tensor(sm[:, 15:16], sm[:, 15:16], sm[:, 11:12],
                                    OP.add))
        V(lambda e: e.tensor_scalar(sm[:, 12:13], sm[:, 15:16], -1.0, 1.0,
                                    OP.mult, OP.add))
        V(lambda e, t=t: e.tensor_tensor(
            s_p2a[:, 0:32].rearrange("p (a b) -> p a b", a=4),
            s_zlora[:, 32 * t:32 * (t + 1)].rearrange("p (a b) -> p a b", a=4),
            sm[:, 9:13].unsqueeze(2).broadcast_to([128, 4, 8]), OP.mult))
        V(lambda e: e.tensor_copy(s_p2b[:, 0:32], s_p2a[:, 0:32]))
        vs_zt = VSn()
        TE(lambda e: e.transpose(psA[0:32, 0:64].bitcast(BF16)[:, 0:128],
                                 s_p2b[:, 0:32], s_identb[:, :]),
           wait=[("VS", max(vs_zt, bank_free[0]))], inc=[("TS", 1)])
        ts = TS()
        V(lambda e, t=t: e.tensor_copy(s_zlt[:, 128 * t:128 * (t + 1)],
                                       psA[0:32, 0:64].bitcast(BF16)[:, 0:128]),
          wait=[("TS", ts)])
        bank_free[0] = VSn()
    vs_p2 = VSn()
    esB.close()

    esC = contextlib.ExitStack()
    sbC = lambda name, shape, dt: esC.enter_context(nc.sbuf_tensor(name, shape, dt))
    esCx = contextlib.ExitStack()
    s_xc = esCx.enter_context(nc.sbuf_tensor("s_xc", [128, CONV], BF16))
    s_xhat = esCx.enter_context(nc.sbuf_tensor("s_xhat", [128, DS], BF16))
    xc_guard = {"v": vs_p2, "t": 0, "s": 0}

    def load_conv_tile(c):
        DMA("sync", lambda e, c=c: e.dma_start(
            out=s_xc[:, :], in_=convscr[128 * c:128 * (c + 1), :]),
            wait=[("SS", xc_guard["s"]), ("VS", xc_guard["v"]),
                  ("TS", xc_guard["t"]), ("CPL", cpl_all)],
            inc=[("XLD", 16)])
        xld = CNT["XLD"]
        SC(lambda e: e.activation(s_xc[:, :], s_xc[:, :], AF.Silu),
           wait=[("XLD", xld)])
        xc_guard["s"] = SSn()
        return SSn()

    for c in range(NT):
        ss_xc = load_conv_tile(c)
        TE(lambda e, c=c: e.matmul(psA[:, 0:64], s_ut[:, :],
                                   s_dtaall[:, c * 64:(c + 1) * 64],
                                   start=True, stop=True),
           wait=[("VS", bank_free[0])], inc=[("TS", 1)])
        ts_lam = TS()
        V(lambda e, c=c: e.tensor_copy(s_lam[:, c * 64:(c + 1) * 64],
                                       psA[:, 0:64]), wait=[("TS", ts_lam)])
        vs_lam = VSn()
        TE(lambda e, c=c: e.transpose(psA[0:64, 512:640],
                                      s_lam[:, c * 64:(c + 1) * 64],
                                      s_identf[:, :]),
           wait=[("VS", vs_lam)], inc=[("TS", 1)])
        ts_lfm = TS()
        V(lambda e, c=c: e.tensor_copy(s_lamfm[:, c * 128:(c + 1) * 128],
                                       psA[0:64, 512:640]), wait=[("TS", ts_lfm)])
        DMA("sync", lambda e, c=c: e.dma_start(
            out=s_row64[0:1, 320:384], in_=s_lam[127:128, c * 64:(c + 1) * 64]),
            wait=[("VS", vs_lam), ("SS", SSn()), ("TS", TS())],
            inc=[("ROW", 16)])
        roww = CNT["ROW"]
        SC(lambda e: e.activation(s_row64[0:1, 384:448], s_row64[0:1, 320:384],
                                  AF.Exp), wait=[("ROW", roww)])
        ss_gam = SSn()
        TE(lambda e: e.matmul(psA[:, 1024:1088], s_ones1[0:1, 0:128],
                              s_row64[0:1, 320:384], start=True, stop=True),
           wait=[("ROW", roww), ("VS", VSn())], inc=[("TS", 1)])
        ts_lend = TS()
        TE(lambda e: e.matmul(psA[:, 1536:1600], s_ones1[0:1, 0:128],
                              s_row64[0:1, 384:448], start=True, stop=True),
           wait=[("SS", ss_gam)], inc=[("TS", 1)])
        ts_gam = TS()
        V(lambda e, c=c: e.tensor_copy(s_gamrep[:, c * 64:(c + 1) * 64],
                                       psA[:, 1536:1600]), wait=[("TS", ts_gam)])
        V(lambda e, c=c: e.tensor_tensor(s_small[:, 0:64],
                                         psA[:, 1024:1088],
                                         s_lam[:, c * 64:(c + 1) * 64],
                                         OP.subtract),
          wait=[("TS", ts_lend)])
        vs_f1 = VSn()
        SC(lambda e: e.activation(s_small[:, 64:128], s_small[:, 0:64], AF.Exp),
           wait=[("VS", vs_f1)])
        SC(lambda e, c=c: e.activation(s_explam[:, c * 64:(c + 1) * 64],
                                       s_lam[:, c * 64:(c + 1) * 64], AF.Exp))
        ss_f = SSn()
        V(lambda e, c=c: e.tensor_tensor(s_ftm[:, c * 64:(c + 1) * 64],
                                         s_small[:, 64:128],
                                         s_dtsp[:, c * 64:(c + 1) * 64], OP.mult),
          wait=[("SS", ss_f)])
        V(lambda e, c=c: e.tensor_tensor(s_gtot[:, :], s_gtot[:, :],
                                         s_gamrep[:, c * 64:(c + 1) * 64],
                                         OP.mult))
        V(lambda e, c=c: e.tensor_tensor(
            s_xhat[:, :].rearrange("p (a b) -> p a b", a=64),
            s_xc[:, 0:DS].rearrange("p (a b) -> p a b", a=64),
            s_ftm[:, c * 64:(c + 1) * 64].unsqueeze(2).broadcast_to([128, 64, 64]),
            OP.mult), wait=[("SS", ss_xc)])
        vs_xhat = VSn()
        for j in range(8):
            ps = psA if j < 4 else psB
            poff = (j % 4) * 512
            TE(lambda e, ps=ps, poff=poff, j=j:
               e.matmul(ps[:, poff:poff + 512], s_xc[:, DS:DS + 128],
                        s_xhat[:, j * 512:(j + 1) * 512], start=True, stop=True),
               wait=[("VS", vs_xhat)] if j == 0 else [],
               inc=[("TS", 1)] if j == 7 else ())
        ts_state = TS()
        V(lambda e, c=c: e.tensor_tensor(
            s_hrun[:, :].rearrange("p (a b) -> p a b", a=64),
            s_hrun[:, :].rearrange("p (a b) -> p a b", a=64),
            s_gamrep[:, c * 64:(c + 1) * 64].unsqueeze(2).broadcast_to([128, 64, 64]),
            OP.mult), wait=[("TS", ts_state)])
        V(lambda e: e.tensor_tensor(s_hrun[:, 0:2048], s_hrun[:, 0:2048],
                                    psA[:, :], OP.add))
        V(lambda e: e.tensor_tensor(s_hrun[:, 2048:4096], s_hrun[:, 2048:4096],
                                    psB[:, :], OP.add))
        for b in range(8):
            bank_free[b] = VSn()
        xc_guard["v"] = VSn()
        xc_guard["t"] = ts_state

    esC1 = contextlib.ExitStack()
    sbC1 = lambda name, shape, dt: esC1.enter_context(nc.sbuf_tensor(name, shape, dt))
    s_gath = sbC1("s_gath", [128, DS + NH], F32)
    s_prod = sbC1("s_prod", [128, NH], F32)
    s_wj = sbC1("s_wj", [128, NH], F32)
    s_hw = sbC1("s_hw", [128, DS], F32)
    vs_chain = VSn()
    DMA("sync", lambda e: e.dma_start(out=ccin[:, 0:DS], in_=s_hrun[:, :]),
        wait=[("VS", vs_chain)], inc=[("PUB", 16)])
    DMA("sync", lambda e: e.dma_start(out=ccin[:, DS:DS + NH], in_=s_gtot[:, :]),
        wait=[("VS", vs_chain)], inc=[("PUB", 16)])
    DMA("gpsimd", lambda e: e.collective_compute(
        "AllGather", OP.bypass, replica_groups=[[0, 1, 2, 3], [4, 5, 6, 7]],
        ins=[ccin], outs=[ccout]),
        wait=[("PUB", CNT["PUB"]), ("GDM", CNT["GDM"])], inc=[("CC", 1)])
    cc_done = CNT["CC"]
    V(lambda e: e.memset(s_hrun[:, :], 0.0), wait=[("CC", cc_done)])
    V(lambda e: e.memset(s_prod[:, :], 1.0))
    for j in [2, 1, 0]:
        vs_gprev = VSn()
        DMA("sync", lambda e, j=j: e.dma_start(
            out=s_gath[:, :], in_=ccout[128 * j:128 * (j + 1), :]),
            wait=[("CC", cc_done), ("VS", vs_gprev), ("GTH", CNT["GTH"])],
            inc=[("GTH", 16)])
        gth = CNT["GTH"]
        V(lambda e, j=j: e.tensor_scalar(s_wj[:, :], s_prod[:, :],
                                         s_cmask[:, j:j + 1], None, OP.mult))
        V(lambda e, j=j: e.tensor_scalar(s_small[:, 0:64],
                                         s_gath[:, DS:DS + NH],
                                         s_gmask[:, j:j + 1],
                                         s_gmaski[:, j:j + 1], OP.mult, OP.add),
          wait=[("GTH", gth)])
        V(lambda e: e.tensor_tensor(s_prod[:, :], s_prod[:, :], s_small[:, 0:64],
                                    OP.mult))
        V(lambda e, j=j: e.tensor_tensor(
            s_hw[:, :].rearrange("p (a b) -> p a b", a=64),
            s_gath[:, 0:DS].rearrange("p (a b) -> p a b", a=64),
            s_wj[:, :].unsqueeze(2).broadcast_to([128, 64, 64]), OP.mult))
        V(lambda e: e.tensor_tensor(s_hrun[:, :], s_hrun[:, :], s_hw[:, :],
                                    OP.add))
    vs_hinit = VSn()
    esC1.close()

    esC2 = contextlib.ExitStack()
    sbC2 = lambda name, shape, dt: esC2.enter_context(nc.sbuf_tensor(name, shape, dt))
    s_E = sbC2("s_E", [128, 2048], F32)
    s_L = sbC2("s_L", [128, 4096], BF16)
    s_y = sbC2("s_y", [128, DS], F32)
    s_zrl = sbC2("s_zrl", [128, 2048], F32)
    s_hcast = sbC2("s_hcast", [128, 2048], BF16)
    s_bfm = sbC2("s_bfm", [128, 128], BF16)
    s_cfm = sbC2("s_cfm", [128, 128], BF16)
    s_gbf = sbC2("s_gbf", [128, 128], BF16)
    s_lamq = sbC2("s_lamq", [1, 2048], F32)
    s_dtb16 = sbC2("s_dtb16", [128, 32], BF16)
    zrl_free = {0: vs_hinit, 1: vs_hinit}
    gA = [max(bank_free.values())]
    gB = [gA[0]]

    for c in range(NT):
        ss_xc = load_conv_tile(c)
        DMA("sync", lambda e, c=c: e.dma_start(
            out=s_zrl[:, :],
            in_=zscr[128 * c:128 * (c + 1), 0:2048]),
            wait=[("VS", zrl_free[0]), ("ZUS", zus_p1a)], inc=[("ZRL", 16)])
        zrl0 = CNT["ZRL"]
        TE(lambda e: e.transpose(psA[:, 0:64].bitcast(BF16)[:, 0:128],
                                 s_xc[:, DS:DS + 128], s_identb[:, :]),
           wait=[("SS", ss_xc), ("VS", gA[0])], inc=[("TS", 1)])
        ts_b = TS()
        TE(lambda e: e.transpose(psA[:, 512:576].bitcast(BF16)[:, 0:128],
                                 s_xc[:, DS + 128:DS + 256], s_identb[:, :]),
           inc=[("TS", 1)])
        ts_c = TS()
        V(lambda e: e.tensor_copy(s_bfm[:, :], psA[:, 0:64].bitcast(BF16)[:, 0:128]),
          wait=[("TS", ts_b)])
        V(lambda e: e.tensor_copy(s_cfm[:, :], psA[:, 512:576].bitcast(BF16)[:, 0:128]),
          wait=[("TS", ts_c)])
        vs_bc = VSn()
        TE(lambda e: e.matmul(psA[:, 1024:1152], s_bfm[:, :], s_cfm[:, :],
                              start=True, stop=True),
           wait=[("VS", vs_bc)], inc=[("TS", 1)])
        ts_g = TS()
        V(lambda e: e.tensor_tensor(s_gbf[:, :], psA[:, 1024:1152], s_ut[:, :],
                                    OP.mult), wait=[("TS", ts_g)])
        gA[0] = VSn()
        for half in range(2):
            for qq in range(2):
                q = 2 * half + qq
                ps = psA if qq == 0 else psB
                DMA("sync", lambda e, c=c, q=q: e.dma_start(
                    out=s_lamq[0:1, :],
                    in_=s_lamfm[16 * q:16 * (q + 1), c * 128:(c + 1) * 128]),
                    wait=[("LFD", CNT["LFD"]), ("VS", vs_hinit),
                          ("TS", TS())], inc=[("LFD", 16)])
                lfd = CNT["LFD"]
                for j in range(4):
                    TE(lambda e, ps=ps, j=j: e.matmul(
                        ps[:, j * 512:(j + 1) * 512], s_ones1[0:1, 0:128],
                        s_lamq[0:1, j * 512:(j + 1) * 512],
                        start=True, stop=True),
                       wait=[("LFD", lfd),
                             ("VS", gA[0] if qq == 0 else gB[0])]
                       if j == 0 else [],
                       inc=[("TS", 1)] if j == 3 else ())
                ts_q = TS()
                V(lambda e, ps=ps, q=q, c=c: e.tensor_tensor(
                    s_E[:, :].rearrange("p (a b) -> p a b", a=16),
                    ps[:, :].rearrange("p (a b) -> p a b", a=16),
                    s_lam[:, c * 64 + q * 16:c * 64 + (q + 1) * 16]
                    .unsqueeze(2).broadcast_to([128, 16, 128]), OP.subtract),
                  wait=[("TS", ts_q)])
                V(lambda e: e.tensor_scalar(s_E[:, :], s_E[:, :], 0.0, None,
                                            OP.min))
                gs_e = 0
                vs_e = VSn()
                if qq == 0:
                    gA[0] = vs_e
                else:
                    gB[0] = vs_e
                SC(lambda e, qq=qq: e.activation(
                    s_L[:, qq * 2048:(qq + 1) * 2048], s_E[:, :], AF.Exp),
                   wait=[("VS", vs_e)])
            ss_l = SSn()
            V(lambda e, half=half, c=c: e.tensor_copy(
                s_dtb16[:, :], s_dtsp[:, c * 64 + half * 32:c * 64 + (half + 1) * 32]))
            V(lambda e: e.tensor_tensor(
                s_L[:, :].rearrange("p (a b) -> p a b", a=32),
                s_L[:, :].rearrange("p (a b) -> p a b", a=32),
                s_gbf[:, :].unsqueeze(1).broadcast_to([128, 32, 128]), OP.mult),
              wait=[("SS", ss_l)])
            V(lambda e: e.tensor_tensor(
                s_L[:, :].rearrange("p (a b) -> p a b", a=32),
                s_L[:, :].rearrange("p (a b) -> p a b", a=32),
                s_dtb16[:, :].unsqueeze(2).broadcast_to([128, 32, 128]), OP.mult))
            vs_stile = VSn()
            V(lambda e, half=half: e.tensor_copy(
                s_hcast[:, :], s_hrun[:, half * 2048:(half + 1) * 2048]))
            vs_hc = VSn()
            for j in range(4):
                TE(lambda e, j=j: e.matmul(
                    psA[:, j * 512:(j + 1) * 512], s_cfm[:, :],
                    s_hcast[:, j * 512:(j + 1) * 512], start=True, stop=True),
                   wait=[("VS", max(vs_hc, gA[0]))] if j == 0 else [],
                   inc=[("TS", 1)] if j == 3 else ())
            ts_inter = TS()
            for hh in range(32):
                h = half * 32 + hh
                TE(lambda e, hh=hh, h=h, first=(hh % 8 == 0), last=(hh % 8 == 7):
                   e.matmul(psB[:, hh * 64:(hh + 1) * 64],
                            s_L[:, hh * 128:(hh + 1) * 128],
                            s_xc[:, h * 64:(h + 1) * 64],
                            start=first, stop=last),
                   wait=[("VS", max(vs_stile, gB[0]))] if hh == 0 else [],
                   inc=[("TS", 1)] if hh == 31 else ())
            ts_intra = TS()
            V(lambda e, half=half, c=c: e.tensor_tensor(
                s_y[:, half * 2048:(half + 1) * 2048].rearrange(
                    "p (a b) -> p a b", a=32),
                psA[:, :].rearrange("p (a b) -> p a b", a=32),
                s_explam[:, c * 64 + half * 32:c * 64 + (half + 1) * 32]
                .unsqueeze(2).broadcast_to([128, 32, 64]), OP.mult),
              wait=[("TS", ts_inter)])
            V(lambda e, half=half: e.tensor_tensor(
                s_y[:, half * 2048:(half + 1) * 2048],
                s_y[:, half * 2048:(half + 1) * 2048], psB[:, :], OP.add),
              wait=[("TS", ts_intra)])
            gA[0] = VSn()
            gB[0] = VSn()
            V(lambda e, half=half, c=c: e.tensor_tensor(
                s_hcast[:, :].rearrange("p (a b) -> p a b", a=32),
                s_xc[:, half * 2048:(half + 1) * 2048].rearrange(
                    "p (a b) -> p a b", a=32),
                s_drep[:, half * 32:(half + 1) * 32].unsqueeze(2)
                .broadcast_to([128, 32, 64]), OP.mult))
            V(lambda e, half=half: e.tensor_tensor(
                s_y[:, half * 2048:(half + 1) * 2048],
                s_y[:, half * 2048:(half + 1) * 2048], s_hcast[:, :], OP.add))
        V(lambda e, c=c: e.tensor_tensor(
            s_xhat[:, :].rearrange("p (a b) -> p a b", a=64),
            s_xc[:, 0:DS].rearrange("p (a b) -> p a b", a=64),
            s_ftm[:, c * 64:(c + 1) * 64].unsqueeze(2).broadcast_to([128, 64, 64]),
            OP.mult))
        vs_xhat = VSn()
        for j in range(8):
            ps = psA if j < 4 else psB
            poff = (j % 4) * 512
            TE(lambda e, ps=ps, poff=poff, j=j:
               e.matmul(ps[:, poff:poff + 512], s_xc[:, DS:DS + 128],
                        s_xhat[:, j * 512:(j + 1) * 512], start=True, stop=True),
               wait=[("VS", max(vs_xhat, gA[0], gB[0]))] if j == 0 else [],
               inc=[("TS", 1)] if j == 7 else ())
        ts_state = TS()
        V(lambda e, c=c: e.tensor_tensor(
            s_hrun[:, :].rearrange("p (a b) -> p a b", a=64),
            s_hrun[:, :].rearrange("p (a b) -> p a b", a=64),
            s_gamrep[:, c * 64:(c + 1) * 64].unsqueeze(2).broadcast_to([128, 64, 64]),
            OP.mult), wait=[("TS", ts_state)])
        V(lambda e: e.tensor_tensor(s_hrun[:, 0:2048], s_hrun[:, 0:2048],
                                    psA[:, :], OP.add))
        V(lambda e: e.tensor_tensor(s_hrun[:, 2048:4096], s_hrun[:, 2048:4096],
                                    psB[:, :], OP.add))
        vs_hupd = VSn()
        SC(lambda e: e.activation(s_zrl[:, :], s_zrl[:, :], AF.Silu),
           wait=[("ZRL", zrl0)])
        ss_silu = SSn()
        V(lambda e: e.tensor_tensor(s_y[:, 0:2048], s_y[:, 0:2048],
                                    s_zrl[:, :], OP.mult),
          wait=[("SS", ss_silu)])
        vs_z0 = VSn()
        DMA("sync", lambda e, c=c: e.dma_start(
            out=s_zrl[:, :], in_=zscr[128 * c:128 * (c + 1), 2048:4096]),
            wait=[("VS", vs_z0), ("SS", SSn())], inc=[("ZRL", 16)])
        zrl1 = CNT["ZRL"]
        SC(lambda e: e.activation(s_zrl[:, :], s_zrl[:, :], AF.Silu),
           wait=[("ZRL", zrl1)])
        ss_silu2 = SSn()
        V(lambda e: e.tensor_tensor(s_y[:, 2048:4096], s_y[:, 2048:4096],
                                    s_zrl[:, :], OP.mult),
          wait=[("SS", ss_silu2)])
        zrl_free[0] = VSn()
        vs_gt = VSn()
        SC(lambda e: e.activation(s_E[:, :], s_y[:, 0:2048], AF.Square),
           wait=[("VS", vs_gt)])
        ss_sq1 = SSn()
        V(lambda e: e.reduce_sum(s_small[:, 100:101], s_E[:, :], axis=AX),
          wait=[("SS", ss_sq1)])
        vs_rs1 = VSn()
        SC(lambda e: e.activation(s_E[:, :], s_y[:, 2048:4096], AF.Square),
           wait=[("VS", vs_rs1)])
        ss_sq2 = SSn()
        V(lambda e: e.reduce_sum(s_small[:, 101:102], s_E[:, :], axis=AX),
          wait=[("SS", ss_sq2)])
        V(lambda e: e.tensor_tensor(s_small[:, 102:103], s_small[:, 100:101],
                                    s_small[:, 101:102], OP.add))
        vs_ss = VSn()
        SC(lambda e: e.activation(s_small[:, 103:104], s_small[:, 102:103],
                                  AF.Sqrt, bias=EPS, scale=1.0 / DS),
           wait=[("VS", vs_ss)])
        ss_sr = SSn()
        V(lambda e: e.reciprocal(s_small[:, 104:105], s_small[:, 103:104]),
          wait=[("SS", ss_sr)])
        V(lambda e: e.scalar_tensor_tensor(s_y[:, :], s_y[:, :],
                                           s_small[:, 104:105], s_normw[:, :],
                                           OP.mult, OP.mult))
        vs_yh = VSn()
        tb_free = {b: ("VS", vs_hupd) for b in range(8)}
        for j in range(32):
            bank = j % 8
            ps = psA if bank < 4 else psB
            poff = (bank % 4) * 512
            gsem, gval = tb_free[bank]
            TE(lambda e, j=j, ps=ps, poff=poff: e.transpose(
                ps[:, poff:poff + 128], s_y[:, j * 128:(j + 1) * 128],
                s_identf[:, :]),
               wait=[("VS", vs_yh), (gsem, gval)], inc=[("TS", 1)])
            ts = TS()
            if bank % 2 == 0:
                SC(lambda e, j=j, c=c, ps=ps, poff=poff: e.copy(
                    s_stat[:, j * TB + 128 * c:j * TB + 128 * (c + 1)],
                    ps[:, poff:poff + 128]),
                   wait=[("TS", ts)])
                tb_free[bank] = ("SS", SSn())
            else:
                V(lambda e, j=j, c=c, ps=ps, poff=poff: e.tensor_copy(
                    s_stat[:, j * TB + 128 * c:j * TB + 128 * (c + 1)],
                    ps[:, poff:poff + 128]),
                  wait=[("TS", ts)])
                tb_free[bank] = ("VS", VSn())
        V(lambda e: e.engine_nop(), wait=[("SS", SSn())])
        gA[0] = VSn()
        gB[0] = VSn()
        xc_guard["v"] = VSn()
        xc_guard["s"] = SSn()
        xc_guard["t"] = ts_state
    vs_p5 = VSn()
    ss_p5 = SSn()
    ts_p5 = TS()
    esC2.close()
    esCx.close()

    esC3 = contextlib.ExitStack()
    sbC3 = lambda name, shape, dt: esC3.enter_context(nc.sbuf_tensor(name, shape, dt))
    s_lorab = sbC3("s_lorab", [32, H], BF16)
    s_ow = [sbC3(f"s_ow{p}", [128, 32 * 512], BF16) for p in range(2)]
    s_url = [sbC3(f"s_url{p}", [128, 512], F32) for p in range(2)]
    s_xrl = [sbC3(f"s_xrl{p}", [128, 512], F32) for p in range(2)]
    s_ost = [sbC3(f"s_ost{p}", [128, 512], F32) for p in range(2)]
    DMA("gpsimd", lambda e: e.dma_start(out=s_lorab[:], in_=lorabD),
        wait=[("VS", vs_p5), ("SS", ss_p5)], inc=[("GDM", 16)])
    gdm_lb = CNT["GDM"]
    ow_free = {0: 0, 1: 0, 2: 0}
    url_free = {0: vs_p5, 1: vs_p5}
    xrl_free = {0: vs_p5, 1: vs_p5}
    ost_free = {0: 0, 1: 0}
    obank_free = {b: vs_p5 for b in range(8)}

    for og in range(8):
        buf = og % 2
        DMA("scalar", lambda e, og=og, buf=buf: e.dma_start(
            out=s_ow[buf][:, 0:32 * 512].rearrange("p (i c) -> p i c", i=32),
            in_=wsrc(owscr, og * 512, 512)),
            wait=[("TS", ow_free[buf]), ("VS", vs_p5), ("OCV", ocv_all)],
            inc=[("OWL", 16)])
        owl = CNT["OWL"]
        for t in range(8):
            rpar = t % 2
            DMA("sync", lambda e, og=og, t=t, rpar=rpar: e.dma_start(
                out=s_url[rpar][:, :],
                in_=uscr[128 * t:128 * (t + 1), og * 512:(og + 1) * 512]),
                wait=[("VS", url_free[rpar]), ("ZUS", zus_all)], inc=[("URL", 16)])
            DMA("sync", lambda e, og=og, t=t, rpar=rpar: e.dma_start(
                out=s_xrl[rpar][:, :],
                in_=xown[128 * t:128 * (t + 1), og * 512:(og + 1) * 512]),
                wait=[("VS", xrl_free[rpar])], inc=[("XRL", 16)])
            url = CNT["URL"]
            xrl = CNT["XRL"]
            bank = t % 8
            ps = psA if bank < 4 else psB
            poff = (bank % 4) * 512
            for i in range(32):
                fa = (i == 0)
                w = []
                if fa:
                    w = [("OWL", owl), ("VS", obank_free[bank]),
                         ("TS", ts_p5), ("SS", ss_p5)]
                TE(lambda e, ps=ps, poff=poff, i=i, t=t, buf=buf, fa=fa:
                   e.matmul(ps[:, poff:poff + 512],
                            s_stat[:, i * TB + 128 * t:i * TB + 128 * t + 128],
                            s_ow[buf][:, i * 512:(i + 1) * 512],
                            start=fa, stop=False),
                   wait=w)
            TE(lambda e, ps=ps, poff=poff, t=t, og=og: e.matmul(
                ps[:, poff:poff + 512],
                s_zlt[:, 128 * t:128 * (t + 1)],
                s_lorab[:, og * 512:og * 512 + 512],
                start=False, stop=True),
               wait=[("GDM", gdm_lb), ("VS", vs_p2)] if (og == 0 and t == 0)
               else [],
               inc=[("TS", 1)])
            ts_o = TS()
            opar = (og * 8 + t) % 2
            V(lambda e, ps=ps, poff=poff, opar=opar, rpar=rpar:
              e.tensor_tensor(s_ost[opar][:, :], ps[:, poff:poff + 512],
                              s_url[rpar][:, :], OP.add),
              wait=[("TS", ts_o), ("URL", url), ("STO", ost_free[opar])])
            obank_free[bank] = VSn()
            url_free[rpar] = VSn()
            V(lambda e, opar=opar, rpar=rpar: e.tensor_tensor(
                s_ost[opar][:, :], s_ost[opar][:, :], s_xrl[rpar][:, :], OP.add),
              wait=[("XRL", xrl)])
            xrl_free[rpar] = VSn()
            vs_o = VSn()
            DMA("sync", lambda e, og=og, t=t, opar=opar: e.dma_start(
                out=oseg[128 * t:128 * (t + 1), og * 512:(og + 1) * 512],
                in_=s_ost[opar][:, :]),
                wait=[("VS", vs_o)], inc=[("STO", 16)])
            ost_free[opar] = CNT["STO"]
        ow_free[buf] = TS()

    WAITQ("gpsimd", [("STO", CNT["STO"])])
    esC3.close()
    esC.close()

    with nc.Block() as block:
        @block.sync
        def _(sync):
            for f in QL["sync"]:
                f(sync)

        @block.gpsimd
        def _(g):
            for f in QL["gpsimd"]:
                f(g)

        @block.tensor
        def _(t):
            for f in QL["tensor"]:
                f(t)

        @block.vector
        def _(v):
            for f in QL["vector"]:
                f(v)

        @block.scalar
        def _(s):
            for f in QL["scalar"]:
                f(s)

    es.close()
    _cache["nc"] = nc
    return nc


def _shift_consts():
    mats = []
    for d in (1, 2, 3):
        mats.append(np.eye(128, k=d))
    for d in (1, 2, 3):
        mats.append(np.eye(128, k=d - 128))
    for d in (1, 2, 3):
        m = np.zeros((128, 128))
        for t in range(d):
            m[3 + t - d, t] = 1.0
        mats.append(m)
    return np.ascontiguousarray(
        np.concatenate(mats, axis=1).astype(ml_dtypes.bfloat16))


def kernel(**inputs):
    x = np.asarray(inputs["L0_attn_hidden_states"], np.float32)
    inw = np.asarray(inputs["L0_attn_in_proj_w"], np.float32)
    wup = np.asarray(inputs["L0_ffn_W_up"], np.float32)
    loraA = np.asarray(inputs["L0_ffn_lora_A"], np.float32)
    loraB = np.asarray(inputs["L0_ffn_lora_B"], np.float32)
    gatew = np.asarray(inputs["L0_ffn_gate_w"], np.float32)
    bf = ml_dtypes.bfloat16

    wcat = np.ascontiguousarray(np.concatenate(
        [inw, wup.T, loraA.reshape(32, H).T, gatew.T], axis=1), dtype=np.float32)
    lnw = np.asarray(inputs["L0_attn_ln_w"], np.float32)

    shared = {
        "wcat": wcat,
        "outwD": np.ascontiguousarray(
            np.asarray(inputs["L0_attn_out_proj_w"], np.float32)),
        "lorabD": np.ascontiguousarray(
            loraB.transpose(0, 2, 1).reshape(32, H)),
        "convwT": np.ascontiguousarray(
            np.asarray(inputs["L0_attn_conv_w"], np.float32).T),
        "mupvD": np.asarray(inputs["L0_attn_mup_vector"], np.float32)[None, :],
        "lnwC": np.ascontiguousarray(lnw.reshape(32, 128).T),
        "normwD": np.asarray(inputs["L0_attn_norm_w"], np.float32)[None, :],
        "alogD": np.asarray(inputs["L0_attn_A_log"], np.float32)[None, :],
        "dtbD": np.asarray(inputs["L0_attn_dt_bias"], np.float32)[None, :],
        "dvecD": np.asarray(inputs["L0_attn_D"], np.float32)[None, :],
        "ebiasD": np.asarray(inputs["L0_ffn_expert_bias"], np.float32)[None, :],
        "identbD": np.eye(128, dtype=bf),
        "identfD": np.eye(128, dtype=np.float32),
        "utonesD": np.ascontiguousarray(np.triu(np.ones((128, 128), np.float32))),
        "shiftD": _shift_consts(),
        "ones1D": np.ones((1, 128), np.float32),
    }

    nc = build_program()
    in_maps = []
    for c in range(8):
        b, g = divmod(c, 4)
        own = x[b, g * T:(g + 1) * T]
        prev3 = (np.zeros((3, H), np.float32) if g == 0
                 else x[b, g * T - 3:g * T])
        xseg = np.concatenate([prev3, own], axis=0)
        cm = np.zeros((128, 3), np.float32)
        gm = np.zeros((128, 3), np.float32)
        for j in range(3):
            if j < g:
                cm[:, j] = 1.0
            if 1 <= j <= g - 1:
                gm[:, j] = 1.0
        m = dict(shared)
        m["xsegT"] = np.ascontiguousarray(xseg.T)
        m["xown"] = np.ascontiguousarray(own)
        m["cmaskD"] = cm
        m["gmaskD"] = gm
        m["gmaskiD"] = 1.0 - gm
        in_maps.append(m)

    res = run_bass_kernel_spmd(nc, in_maps, core_ids=list(range(8)))
    out = np.empty((2, 4096, H), np.float32)
    for c in range(8):
        b, g = divmod(c, 4)
        out[b, g * T:(g + 1) * T] = res.results[c]["oseg"]
    return out
